# version 01
# baseline (speedup 1.0000x reference)
import numpy as np
from contextlib import ExitStack

import concourse.bass as bass
import concourse.mybir as mybir
from concourse.bass import IndirectOffsetOnAxis
from concourse.bass_utils import run_bass_kernel_spmd

F32 = mybir.dt.float32
BF16 = mybir.dt.bfloat16
I32 = mybir.dt.int32
U8 = mybir.dt.uint8
AF = mybir.ActivationFunctionType
ALU = mybir.AluOpType
AX = mybir.AxisListType

NCORES = 8
S = 2048
D = 1024
NH = 16
IS_ = 8192
IR = 4096
E = 8
CAP = 2560
WSL = 1024
NSLOT = E * WSL
SI = S // 128
DC = D // 128
T = NCORES * S

EPS_LN = 1e-5
EPS_RMS_ATTN = 1e-6
EPS_RMS_FFN = float(np.finfo(np.float32).eps)
EPS_Q = 1e-5
BIG = 1e30


class P:
    def __init__(self):
        self.nc = bass.Bass("TRN2", target_bir_lowering=False, detect_race_conditions=False)
        self.es = ExitStack()
        self.scopes = {}
        self._sem = {}
        self.cnt = {}
        self.block = None
        self.dbg = {}

    def open_block(self):
        self.block = self.es.enter_context(self.nc.Block())

    def sem(self, name):
        if name not in self._sem:
            self._sem[name] = self.es.enter_context(self.nc.semaphore(name))
            self.cnt[name] = 0
        return self._sem[name]

    def T(self, fn): self.block.tensor(fn)
    def V(self, fn): self.block.vector(fn)
    def A(self, fn): self.block.scalar(fn)
    def G(self, fn): self.block.gpsimd(fn)
    def S(self, fn): self.block.sync(fn)

    def sbuf(self, name, shape, dt, scope=None):
        stack = self.es if scope is None else self.scopes.setdefault(scope, ExitStack())
        return stack.enter_context(self.nc.sbuf_tensor(name, list(shape), dt))[:, :]

    def close_scope(self, scope):
        if scope in self.scopes:
            self.scopes.pop(scope).close()

    def psum(self, name, shape, dt=F32):
        return self.es.enter_context(self.nc.psum_tensor(name, list(shape), dt))[:, :]

    def close(self):
        for s in list(self.scopes):
            self.close_scope(s)
        self.es.close()
        return self.nc


def build(dbg=()):
    dbg = set(dbg)
    p = P()
    nc = p.nc

    xb = nc.dram_tensor("xb", [S, D], F32, kind="ExternalInput").ap()
    xfull = nc.dram_tensor("xfull", [T, D], F32, kind="ExternalInput").ap()
    w_in = {n: nc.dram_tensor(n, [D, D], F32, kind="ExternalInput").ap()
            for n in ("wq", "wk", "wv", "wo")}
    rwin = nc.dram_tensor("rwin", [E, D], F32, kind="ExternalInput").ap()
    sg_d = nc.dram_tensor("sg", [D, IS_], F32, kind="ExternalInput").ap()
    su_d = nc.dram_tensor("su", [D, IS_], F32, kind="ExternalInput").ap()
    sd_d = nc.dram_tensor("sd", [IS_, D], F32, kind="ExternalInput").ap()
    rg_d = nc.dram_tensor("rg", [E, D, IR], F32, kind="ExternalInput").ap()
    ru_d = nc.dram_tensor("ru", [E, D, IR], F32, kind="ExternalInput").ap()
    rd_d = nc.dram_tensor("rd", [E, IR, D], F32, kind="ExternalInput").ap()
    idenf = nc.dram_tensor("idenf", [128, 128], F32, kind="ExternalInput").ap()
    sup16_d = nc.dram_tensor("sup16", [128, 128], BF16, kind="ExternalInput").ap()
    e16all_d = nc.dram_tensor("e16all", [128, 128], BF16, kind="ExternalInput").ap()
    e16t_d = nc.dram_tensor("e16t", [16, 1024], BF16, kind="ExternalInput").ap()
    maskrep_d = nc.dram_tensor("maskrep", [1, NCORES * E], F32, kind="ExternalInput").ap()
    out = nc.dram_tensor("out", [S, D], F32, kind="ExternalOutput").ap()

    xf16 = nc.dram_tensor("xf16", [T, D], BF16).ap()
    xnb_d = nc.dram_tensor("xnb_d", [S, D], BF16).ap()
    xq_d = nc.dram_tensor("xq_d", [S, D], BF16).ap()
    xqo_d = nc.dram_tensor("xqo_d", [S, D], BF16).ap()
    w16_d = {n: nc.dram_tensor(n + "16d", [D, D], BF16).ap()
             for n in ("wq", "wk", "wv", "wo")}
    acc_d = nc.dram_tensor("acc_d", [S, D], F32).ap()
    xg16 = nc.dram_tensor("xg16", [NSLOT + 128, D], BF16).ap()
    eout_d = nc.dram_tensor("eout_d", [NSLOT, D], F32).ap()
    sums_d = nc.dram_tensor("sums_d", [1, NH * S], F32).ap()
    sd16_d = nc.dram_tensor("sd16_d", [IS_, D], BF16).ap()
    ctxT_d = nc.dram_tensor("ctxT_d", [D, S], F32).ap()
    mix_d = nc.dram_tensor("mix_d", [S, 64], F32).ap()

    def dbgout(tag, shape, dt=F32):
        if tag in dbg:
            ap = nc.dram_tensor("dbg_" + tag, list(shape), dt, kind="ExternalOutput").ap()
            p.dbg[tag] = ap
            return ap
        return None

    pb = [p.psum(f"pb{i}", [128, 512]) for i in range(8)]

    IDENF = p.sbuf("idenf_s", [128, 128], F32)
    C16 = p.sbuf("c16", [128, 392], BF16)
    SUP = C16[:, 0:128]
    E16A = C16[:, 128:256]
    ONESF16 = C16[:, 256:384]
    ONE16 = C16[:, 384:385]
    ONEROW = p.sbuf("onerow", [1, 128], BF16)
    E16T = p.sbuf("e16t_s", [16, 1024], BF16)

    rowbuf = p.sbuf("rowbuf", [128, 96], F32)
    MEAN = rowbuf[:, 0:16]; RSTD = rowbuf[:, 16:32]; RS2R = rowbuf[:, 32:48]
    UQ = rowbuf[:, 48:64]; UO = rowbuf[:, 64:80]
    RALL = p.sbuf("rall", [128, 256], F32)
    BIASC = p.sbuf("biasc", [128, 16], F32)
    GSC = p.sbuf("gsc", [128, 4], F32)
    SW = p.sbuf("swt", [1, 16], F32)
    CNT = p.sbuf("cntall", [1, NCORES * E + E], F32)
    CB = CNT[:, NCORES * E:NCORES * E + E]
    MSK = p.sbuf("mskrep", [1, NCORES * E], F32)
    CBC = p.sbuf("cbc", [128, E], F32)
    SBC = p.sbuf("sbc", [128, E], F32)
    tv = p.sbuf("tvs", [128, 96], F32)
    EPSC = p.sbuf("epsc", [128, 4], F32)

    LOG = p.sbuf("plLOG", [128, 128], F32)
    F1 = p.sbuf("plF1", [128, 128], F32)
    F2 = p.sbuf("plF2", [128, 128], F32)
    L2 = p.sbuf("plL2", [128, 128], F32)
    LSR = p.sbuf("plLSR", [128, 128], F32)
    VAL = p.sbuf("plVAL", [128, 128], F32)
    CCM = p.sbuf("plCCM", [128, 128], F32)
    PSC = p.sbuf("plPSC", [128, 16 * 14], F32)
    def ps(i): return PSC[:, 16 * i:16 * (i + 1)]
    PU = p.sbuf("planesu", [128, 16 * 3], U8)
    def pu(i): return PU[:, 16 * i:16 * (i + 1)]
    SIGI = p.sbuf("sigi", [128, 16 * 4], I32)
    def sigi(i): return SIGI[:, 16 * i:16 * (i + 1)]
    WV = p.sbuf("plWV", [128, 32], F32)
    rw_s = p.sbuf("rw_s", [E, D], F32)
    rwT = p.sbuf("rwT", [128, DC * E], F32)
    LH = p.sbuf("lh", [128, DC * 18], BF16)
    rwlo = p.sbuf("rwlo", [128, E], F32)
    SEST = p.sbuf("sest", [E, 128], F32)
    SER16 = p.sbuf("ser16", [1, E], BF16)
    SEL16 = p.sbuf("sel16", [128, 16], BF16)
    CB16 = p.sbuf("cb16", [1, E], BF16)
    ONERF = p.sbuf("onerf", [1, 128], F32)
    ONECF = p.sbuf("onecf", [128, 1], F32)

    dS = p.sem("dS"); dG = p.sem("dG"); dA = p.sem("dA")
    cT = p.sem("cT"); cV = p.sem("cV"); cA = p.sem("cA"); cP = p.sem("cP")
    SEMNAMES = ["dS", "dG", "dA", "cT", "cV", "cA", "cP"]

    def bump(name, k=1):
        p.cnt[name] += k
        return p.cnt[name]

    def join_all():
        for nm, emitfn in (("cT", p.T), ("cV", p.V), ("cA", p.A), ("cP", p.G)):
            bump(nm)
            def f(eng, nm=nm):
                eng.drain().then_inc(p.sem(nm), 1)
            emitfn(f)
        snap = {nm: p.cnt[nm] for nm in SEMNAMES}
        for emitfn in (p.T, p.V, p.A, p.G, p.S):
            def f(eng, snap=snap):
                for nm, v in snap.items():
                    if v:
                        eng.wait_ge(p.sem(nm), v)
            emitfn(f)

    def chv(eng, ins):
        ins.then_inc(p.sem("cW"), 1)
        eng.wait_ge(p.sem("cW"), bump("cW"))

    def cha(eng, ins):
        ins.then_inc(p.sem("cX"), 1)
        eng.wait_ge(p.sem("cX"), bump("cX"))

    p.open_block()

    xn = p.sbuf("xn", [128, SI * D], F32, scope="sx")
    scrA = p.sbuf("scrA", [128, D], F32, scope="sx")
    scrB = p.sbuf("scrB", [128, D], F32, scope="sx")
    ZI = p.sbuf("zi", [128, D], I32, scope="sx")
    logT = p.sbuf("logT", [E, S], F32, scope="sx")
    xnT32 = p.sbuf("xnT32", [128, 512], F32, scope="sx")
    xq16 = p.sbuf("xq16", [128, SI * D], BF16, scope="sq")
    xnb = p.sbuf("xnb", [128, SI * D], BF16, scope="sq")
    def xn_t(ci): return xn[:, ci * D:(ci + 1) * D]
    def xq_t(ci): return xq16[:, ci * D:(ci + 1) * D]
    def xnb_t(ci): return xnb[:, ci * D:(ci + 1) * D]

    def s0_s(eng):
        eng.dma_start(out=IDENF, in_=idenf).then_inc(dS, 16)
        eng.dma_start(out=SUP, in_=sup16_d).then_inc(dS, 16)
        eng.dma_start(out=E16A, in_=e16all_d).then_inc(dS, 16)
        eng.dma_start(out=E16T, in_=e16t_d).then_inc(dS, 16)
        eng.dma_start(out=MSK, in_=maskrep_d).then_inc(dS, 16)
        eng.dma_start(out=rw_s, in_=rwin).then_inc(dS, 16)
        for ci in range(SI):
            eng.dma_start(out=xn_t(ci), in_=xb[ci * 128:(ci + 1) * 128, :]).then_inc(dS, 16)
    bump("dS", 16 * (6 + SI)); p.S(s0_s)
    x_dS = p.cnt["dS"]

    def s0_v(eng):
        eng.memset(ONE16, 1.0)
        eng.memset(ONEROW, 1.0)
        eng.memset(ONESF16, 1.0)
        eng.memset(ps(13), 0.0)
        eng.memset(ps(12), float(NSLOT))
        eng.memset(EPSC[:, 0:1], EPS_LN)
        eng.memset(EPSC[:, 1:2], EPS_RMS_ATTN)
        eng.memset(EPSC[:, 2:3], EPS_RMS_FFN)
        eng.memset(EPSC[:, 3:4], 0.0)
        chv(eng, eng.engine_nop())
        eng.drain().then_inc(cV, 1)
    p.V(s0_v); bump("cV")

    NFC = 32
    rows_per = T // NFC
    def s0_g(eng):
        for k in range(NFC):
            eng.dma_start(out=xf16[k * rows_per:(k + 1) * rows_per, :],
                          in_=xfull[k * rows_per:(k + 1) * rows_per, :]).then_inc(dG, 16)
        for k in range(16):
            eng.dma_start(out=sd16_d[k * 512:(k + 1) * 512, :],
                          in_=sd_d[k * 512:(k + 1) * 512, :]).then_inc(dG, 16)
    bump("dG", 16 * (NFC + 16)); p.G(s0_g)
    xf16_dG = p.cnt["dG"]

    for ci in range(SI):
        m = MEAN[:, ci:ci + 1]; r = RSTD[:, ci:ci + 1]; uq = UQ[:, ci:ci + 1]
        c0 = tv[:, 0:1]; c1 = tv[:, 1:2]; c2 = tv[:, 2:3]; c3 = tv[:, 3:4]
        c4 = tv[:, 4:5]; c5 = tv[:, 5:6]; c6 = tv[:, 6:7]; c7 = tv[:, 7:8]
        c8 = tv[:, 8:9]; c9 = tv[:, 9:10]; c10 = tv[:, 10:11]; c11 = tv[:, 11:12]

        def v1(eng, ci=ci, m=m, c0=c0, c1=c1, c2=c2):
            if ci == 0:
                eng.wait_ge(dS, x_dS)
            x = xn_t(ci)
            chv(eng, eng.scalar_tensor_tensor(scrA, x, 1.0, x, ALU.mult, ALU.mult))
            chv(eng, eng.reduce_sum(c0, scrA, axis=AX.X))
            chv(eng, eng.reduce_sum(m, x, axis=AX.X))
            chv(eng, eng.tensor_scalar_mul(m, m, 1.0 / D))
            chv(eng, eng.tensor_scalar_mul(c0, c0, 1.0 / D))
            chv(eng, eng.tensor_mul(c1, m, m))
            chv(eng, eng.tensor_sub(c2, c0, c1))
            chv(eng, eng.engine_nop())
            eng.drain().then_inc(cV, 1)
        p.V(v1); bump("cV")

        def a1(eng, c2=c2, c3=c3):
            eng.wait_ge(cV, p.cnt["cV"])
            cha(eng, eng.activation(c3, c2, AF.Sqrt, bias=EPSC[:, 0:1], scale=1.0))
            eng.drain().then_inc(cA, 1)
        p.A(a1); bump("cA")

        def v2(eng, r=r, c3=c3, c4=c4, m=m):
            eng.wait_ge(cA, p.cnt["cA"])
            chv(eng, eng.reciprocal(r, c3))
            chv(eng, eng.scalar_tensor_tensor(c4, m, -1.0, r, ALU.mult, ALU.mult))
            chv(eng, eng.engine_nop())
            eng.drain().then_inc(cV, 1)
        p.V(v2); bump("cV")

        def v3(eng, ci=ci, c4=c4, c5=c5, c6=c6, r=r, c2=c2):
            x = xn_t(ci)
            chv(eng, eng.tensor_scalar(x, x, r, c4, ALU.mult, ALU.add))
            eng.tensor_copy(xnb_t(ci), x)
            chv(eng, eng.reduce_max(c5, x, axis=AX.X, apply_absolute_value=True))
            chv(eng, eng.tensor_mul(c6, c2, r))
            chv(eng, eng.tensor_mul(c6, c6, r))
            eng.drain().then_inc(cV, 1)
        p.V(v3); bump("cV")

        def a3(eng, c6=c6, c7=c7, c8=c8):
            eng.wait_ge(cV, p.cnt["cV"])
            cha(eng, eng.activation(c7, c6, AF.Sqrt, bias=EPSC[:, 1:2], scale=1.0))
            cha(eng, eng.activation(c8, c6, AF.Sqrt, bias=EPSC[:, 2:3], scale=1.0))
            eng.drain().then_inc(cA, 1)
        p.A(a3); bump("cA")

        def v4(eng, ci=ci, c5=c5, c7=c7, c8=c8, c9=c9, c10=c10, c11=c11, uq=uq, r=r):
            eng.wait_ge(cA, p.cnt["cA"])
            x = xn_t(ci)
            chv(eng, eng.reciprocal(c9, c7))
            chv(eng, eng.reciprocal(c10, c8))
            chv(eng, eng.tensor_copy(RS2R[:, ci:ci + 1], c10))
            chv(eng, eng.tensor_mul(c10, c9, c5))
            chv(eng, eng.tensor_scalar_max(c10, c10, EPS_Q))
            chv(eng, eng.tensor_scalar_mul(uq, c10, 1.0 / 127.0))
            chv(eng, eng.reciprocal(c11, c10))
            chv(eng, eng.scalar_tensor_tensor(c9, c9, 127.0, c11, ALU.mult, ALU.mult))
            chv(eng, eng.tensor_scalar_mul(scrB, x, c9))
            chv(eng, eng.tensor_copy(ZI, scrB))
            chv(eng, eng.tensor_scalar_mul(xq_t(ci), ZI, uq))
            chv(eng, eng.engine_nop())
            eng.drain().then_inc(cV, 1)
        p.V(v4); bump("cV")
    ln_cV = p.cnt["cV"]

    def s1_wb(eng):
        eng.wait_ge(cV, ln_cV)
        for ci in range(SI):
            eng.dma_start(out=xnb_d[ci * 128:(ci + 1) * 128, :], in_=xnb_t(ci)).then_inc(dA, 16)
            eng.dma_start(out=xq_d[ci * 128:(ci + 1) * 128, :], in_=xq_t(ci)).then_inc(dA, 16)
    p.A(s1_wb); bump("dA", 16 * 2 * SI)

    d = dbgout("xn", [S, D])
    if d is not None:
        def f(eng, d=d):
            eng.wait_ge(cV, ln_cV)
            for ci in range(SI):
                eng.dma_start(out=d[ci * 128:(ci + 1) * 128, :], in_=xn_t(ci)).then_inc(dA, 16)
        p.A(f); bump("dA", 16 * SI)
    d = dbgout("rowb", [128, 96])
    if d is not None:
        def f_rb(eng, d=d):
            eng.wait_ge(cV, ln_cV)
            eng.dma_start(out=d, in_=rowbuf).then_inc(dA, 16)
        p.A(f_rb); bump("dA", 16)
    d = dbgout("xq", [S, D], BF16)
    if d is not None:
        def f_xq(eng, d=d):
            eng.wait_ge(cV, ln_cV)
            for ci in range(SI):
                eng.dma_start(out=d[ci * 128:(ci + 1) * 128, :], in_=xq_t(ci)).then_inc(dA, 16)
        p.A(f_xq); bump("dA", 16 * SI)

    for dc in range(DC):
        def t_rw(eng, dc=dc):
            if dc == 0:
                eng.wait_ge(dS, x_dS)
            eng.wait_ge(cV, p.cnt["cV"])
            eng.transpose(pb[6][:, 0:E], rw_s[:, dc * 128:(dc + 1) * 128], IDENF[0:E, 0:E]).then_inc(cT, 1)
        p.T(t_rw); bump("cT")
        def v_rw(eng, dc=dc):
            eng.wait_ge(cT, p.cnt["cT"])
            chv(eng, eng.tensor_copy(rwT[:, dc * E:(dc + 1) * E], pb[6][:, 0:E]))
            eng.drain().then_inc(cV, 1)
        p.V(v_rw); bump("cV")

    for nb in range(S // 512):
        for dc in range(DC):
            for q in range(4):
                ci = nb * 4 + q
                bank = 6 + (q % 2)
                def t_tr(eng, ci=ci, dc=dc, bank=bank):
                    eng.wait_ge(cV, p.cnt["cV"])
                    eng.transpose(pb[bank][:, 0:128], xn_t(ci)[:, dc * 128:(dc + 1) * 128],
                                  IDENF).then_inc(cT, 1)
                p.T(t_tr); bump("cT")
                def v_tr(eng, q=q, bank=bank):
                    eng.wait_ge(cT, p.cnt["cT"])
                    chv(eng, eng.tensor_copy(xnT32[:, q * 128:(q + 1) * 128], pb[bank][:, 0:128]))
                    eng.drain().then_inc(cV, 1)
                p.V(v_tr); bump("cV")
            def t_mm(eng, dc=dc):
                eng.wait_ge(cV, p.cnt["cV"])
                eng.matmul(pb[5][0:E, :], rwT[:, dc * E:(dc + 1) * E], xnT32,
                           start=(dc == 0), stop=(dc == DC - 1)).then_inc(cT, 1)
            p.T(t_mm); bump("cT")
        def v_ev(eng, nb=nb):
            eng.wait_ge(cT, p.cnt["cT"])
            chv(eng, eng.tensor_copy(logT[:, nb * 512:(nb + 1) * 512], pb[5][0:E, :]))
            eng.drain().then_inc(cV, 1)
        p.V(v_ev); bump("cV")

    for ci in range(SI):
        def t_pl(eng, ci=ci):
            eng.wait_ge(cV, p.cnt["cV"])
            eng.transpose(pb[6][:, 0:E], logT[:, ci * 128:(ci + 1) * 128], IDENF[0:E, 0:E]).then_inc(cT, 1)
        p.T(t_pl); bump("cT")
        def v_pl(eng, ci=ci):
            eng.wait_ge(cT, p.cnt["cT"])
            chv(eng, eng.tensor_scalar_mul(LOG[:, ci::16], pb[6][0:128, 0:E],
                                           RS2R[:, ci:ci + 1]))
            chv(eng, eng.engine_nop())
            eng.drain().then_inc(cV, 1)
        p.V(v_pl); bump("cV")

    d = dbgout("logits", [128, 128])
    if d is not None:
        def f(eng, d=d):
            eng.wait_ge(cV, p.cnt["cV"])
            eng.dma_start(out=d, in_=LOG).then_inc(dA, 16)
        p.A(f); bump("dA", 16)

    def v_lh(eng):
        eng.wait_ge(cV, p.cnt["cV"])
        for dc in range(DC):
            h = LH[:, dc * 18:dc * 18 + 9]
            lo = LH[:, dc * 18 + 9:dc * 18 + 18]
            eng.tensor_copy(h[:, 0:1], ONE16)
            eng.memset(lo[:, 0:1], 0.0)
            chv(eng, eng.tensor_copy(h[:, 1:9], rwT[:, dc * E:(dc + 1) * E]))
            chv(eng, eng.tensor_copy(rwlo, h[:, 1:9]))
            chv(eng, eng.tensor_sub(rwlo, rwT[:, dc * E:(dc + 1) * E], rwlo))
            chv(eng, eng.tensor_copy(lo[:, 1:9], rwlo))
        chv(eng, eng.engine_nop())
        eng.drain().then_inc(cV, 1)
    p.V(v_lh); bump("cV")

    def v_se(eng):
        chv(eng, eng.memset(SEST, 0.0))
        chv(eng, eng.reduce_sum(SEST[:, 0:1], rw_s, axis=AX.X))
        chv(eng, eng.engine_nop())
        eng.drain().then_inc(cV, 1)
    p.V(v_se); bump("cV")
    def t_se(eng):
        eng.wait_ge(cV, p.cnt["cV"])
        eng.transpose(pb[6][:, 0:E], SEST, IDENF[0:E, 0:E]).then_inc(cT, 1)
    p.T(t_se); bump("cT")
    def v_se2(eng):
        eng.wait_ge(cT, p.cnt["cT"])
        chv(eng, eng.tensor_copy(SER16, pb[6][0:1, 0:E]))
        eng.drain().then_inc(cV, 1)
    p.V(v_se2); bump("cV")
    def t_se2(eng):
        eng.wait_ge(cV, p.cnt["cV"])
        eng.matmul(pb[6][:, 0:E], ONEROW, SER16, start=True, stop=True).then_inc(cT, 1)
    p.T(t_se2); bump("cT")
    def v_se3(eng):
        eng.wait_ge(cT, p.cnt["cT"])
        chv(eng, eng.tensor_copy(SBC, pb[6][:, 0:E]))
        eng.drain().then_inc(cV, 1)
    p.V(v_se3); bump("cV")

    xhiT = p.sbuf("xhiT", [128, DC * S], BF16, scope="sF")
    fLb = p.sbuf("flb", [9, S], F32, scope="sF")
    FLO = p.sbuf("flo", [128, SI * 9], F32, scope="sF")
    fsel = p.sbuf("fsel", [128, 16 * 9], F32, scope="sF")
    fsel16 = p.sbuf("fsel16", [128, E], BF16, scope="sF")

    for b in range(NCORES):
        for dc in range(DC):
            def s_tr(eng, b=b, dc=dc):
                if b == 0 and dc == 0:
                    eng.wait_ge(dG, xf16_dG)
                if dc == 0:
                    eng.wait_ge(cV, p.cnt["cV"])
                eng.dma_start(out=xhiT[:, dc * S:(dc + 1) * S],
                              in_=xf16[b * S:(b + 1) * S, dc * 128:(dc + 1) * 128],
                              transpose=True).then_inc(dS, 16)
                eng.wait_ge(dS, bump("dS", 16))
            p.S(s_tr)
        for nb in range(4):
            for dc in range(DC):
                def t_mm(eng, nb=nb, dc=dc):
                    if dc == 0:
                        eng.wait_ge(dS, p.cnt["dS"])
                        eng.wait_ge(cV, p.cnt["cV"])
                    rhs = xhiT[:, dc * S + nb * 512: dc * S + (nb + 1) * 512]
                    eng.matmul(pb[4][0:9, :], LH[:, dc * 18:dc * 18 + 9], rhs,
                               start=(dc == 0), stop=False)
                    eng.matmul(pb[4][0:9, :], LH[:, dc * 18 + 9:dc * 18 + 18], rhs,
                               start=False, stop=(dc == DC - 1)).then_inc(cT, 1)
                p.T(t_mm); bump("cT")
            def v_ev(eng, nb=nb):
                eng.wait_ge(cT, p.cnt["cT"])
                chv(eng, eng.tensor_copy(fLb[:, nb * 512:(nb + 1) * 512], pb[4][0:9, :]))
                eng.drain().then_inc(cV, 1)
            p.V(v_ev); bump("cV")
        for ci in range(SI):
            def t_pl(eng, ci=ci):
                eng.wait_ge(cV, p.cnt["cV"])
                eng.transpose(pb[6][:, 0:9], fLb[:, ci * 128:(ci + 1) * 128], IDENF[0:9, 0:9]).then_inc(cT, 1)
            p.T(t_pl); bump("cT")
            def v_pl(eng, ci=ci):
                eng.wait_ge(cT, p.cnt["cT"])
                chv(eng, eng.tensor_copy(FLO[:, ci * 9:(ci + 1) * 9], pb[6][0:128, 0:9]))
                chv(eng, eng.engine_nop())
                eng.drain().then_inc(cV, 1)
            p.V(v_pl); bump("cV")
        def v_f(eng, b=b):
            eng.wait_ge(cV, p.cnt["cV"])
            eng.wait_ge(cT, p.cnt["cT"])
            mpl = ps(9); m1 = ps(10); m2 = ps(11)
            chv(eng, eng.tensor_scalar_mul(mpl, FLO[:, 0::9], 1.0 / D))
            for e in range(E):
                lf = FLO[:, 1 + e::9]
                chv(eng, eng.tensor_scalar_mul(ps(0), mpl, SBC[:, e:e + 1]))
                chv(eng, eng.tensor_sub(lf, lf, ps(0)))
            chv(eng, eng.tensor_max(m1, FLO[:, 1::9], FLO[:, 2::9]))
            for e in range(2, E):
                chv(eng, eng.tensor_max(m1, m1, FLO[:, 1 + e::9]))
            for e in range(E):
                lf = FLO[:, 1 + e::9]
                sc = fsel[:, e * 16:(e + 1) * 16]
                chv(eng, eng.tensor_tensor(sc, lf, m1, ALU.is_ge))
                chv(eng, eng.scalar_tensor_tensor(sc, sc, -BIG, lf, ALU.mult, ALU.add))
            chv(eng, eng.tensor_max(m2, fsel[:, 0:16], fsel[:, 16:32]))
            for e in range(2, E):
                chv(eng, eng.tensor_max(m2, m2, fsel[:, e * 16:(e + 1) * 16]))
            for e in range(E):
                lf = FLO[:, 1 + e::9]
                sc = fsel[:, e * 16:(e + 1) * 16]
                chv(eng, eng.tensor_tensor(sc, lf, m2, ALU.is_ge))
                chv(eng, eng.reduce_sum(fsel[:, 128 + e:129 + e], sc, axis=AX.X))
            chv(eng, eng.tensor_copy(fsel16, fsel[:, 128:128 + E]))
            chv(eng, eng.engine_nop())
            eng.drain().then_inc(cV, 1)
        p.V(v_f); bump("cV")
        def t_cnt(eng):
            eng.wait_ge(cV, p.cnt["cV"])
            eng.matmul(pb[6][0:1, 0:E], ONE16, fsel16, start=True, stop=True).then_inc(cT, 1)
        p.T(t_cnt); bump("cT")
        def v_cnt(eng, b=b):
            eng.wait_ge(cT, p.cnt["cT"])
            chv(eng, eng.tensor_copy(CNT[:, b * E:(b + 1) * E], pb[6][0:1, 0:E]))
            chv(eng, eng.engine_nop())
            eng.drain().then_inc(cV, 1)
        p.V(v_cnt); bump("cV")

    def v_cb(eng):
        eng.wait_ge(cV, p.cnt["cV"])
        tr = tv[0:1, 32:32 + NCORES * E]
        chv(eng, eng.tensor_mul(tr, CNT[:, 0:NCORES * E], MSK))
        for e in range(E):
            chv(eng, eng.reduce_sum(CB[:, e:e + 1], tr[:, e::E], axis=AX.X))
        chv(eng, eng.tensor_copy(CB16, CB))
        chv(eng, eng.engine_nop())
        eng.drain().then_inc(cV, 1)
    p.V(v_cb); bump("cV")
    def t_cb(eng):
        eng.wait_ge(cV, p.cnt["cV"])
        eng.matmul(pb[6][:, 0:E], ONEROW, CB16, start=True, stop=True).then_inc(cT, 1)
    p.T(t_cb); bump("cT")
    def v_cb2(eng):
        eng.wait_ge(cT, p.cnt["cT"])
        chv(eng, eng.tensor_copy(CBC, pb[6][:, 0:E]))
        eng.drain().then_inc(cV, 1)
    p.V(v_cb2); bump("cV")

    d = dbgout("cnt", [1, NCORES * E + E])
    if d is not None:
        def f(eng, d=d):
            eng.wait_ge(cV, p.cnt["cV"])
            eng.dma_start(out=d, in_=CNT).then_inc(dA, 16)
        p.A(f); bump("dA", 16)

    def Le(e): return LOG[:, e * 16:(e + 1) * 16]

    def v_top2(eng):
        eng.wait_ge(cV, p.cnt["cV"])
        m1 = ps(0); run = ps(11); t = ps(9); t2 = ps(10)
        chv(eng, eng.tensor_max(m1, Le(0), Le(1)))
        for e in range(2, E):
            chv(eng, eng.tensor_max(m1, m1, Le(e)))
        chv(eng, eng.memset(run, 0.0))
        for e in range(E):
            f1 = F1[:, e * 16:(e + 1) * 16]
            chv(eng, eng.tensor_tensor(t, Le(e), m1, ALU.is_ge))
            chv(eng, eng.tensor_mul(t2, t, run))
            chv(eng, eng.tensor_sub(f1, t, t2))
            chv(eng, eng.tensor_max(run, run, t))
        for e in range(E):
            f1 = F1[:, e * 16:(e + 1) * 16]
            l2 = L2[:, e * 16:(e + 1) * 16]
            chv(eng, eng.scalar_tensor_tensor(l2, f1, -BIG, Le(e), ALU.mult, ALU.add))
        m2 = ps(1)
        chv(eng, eng.tensor_max(m2, L2[:, 0:16], L2[:, 16:32]))
        for e in range(2, E):
            chv(eng, eng.tensor_max(m2, m2, L2[:, e * 16:(e + 1) * 16]))
        chv(eng, eng.memset(run, 0.0))
        for e in range(E):
            f2 = F2[:, e * 16:(e + 1) * 16]
            l2 = L2[:, e * 16:(e + 1) * 16]
            chv(eng, eng.tensor_tensor(t, l2, m2, ALU.is_ge))
            chv(eng, eng.tensor_mul(t2, t, run))
            chv(eng, eng.tensor_sub(f2, t, t2))
            chv(eng, eng.tensor_max(run, run, t))
        e1 = ps(2); e2 = ps(3)
        chv(eng, eng.memset(e1, 0.0))
        chv(eng, eng.memset(e2, 0.0))
        for e in range(1, E):
            chv(eng, eng.scalar_tensor_tensor(e1, F1[:, e * 16:(e + 1) * 16], float(e), e1,
                                              ALU.mult, ALU.add))
            chv(eng, eng.scalar_tensor_tensor(e2, F2[:, e * 16:(e + 1) * 16], float(e), e2,
                                              ALU.mult, ALU.add))
        chv(eng, eng.tensor_sub(ps(9), ps(0), ps(1)))
        chv(eng, eng.engine_nop())
        eng.drain().then_inc(cV, 1)
    p.V(v_top2); bump("cV")

    def a_sig(eng):
        eng.wait_ge(cV, p.cnt["cV"])
        cha(eng, eng.activation(ps(4), ps(9), AF.Sigmoid, bias=0.0, scale=1.0))
        eng.drain().then_inc(cA, 1)
    p.A(a_sig); bump("cA")

    for e in range(E):
        def v_rk(eng, e=e):
            if e == 0:
                eng.wait_ge(cV, p.cnt["cV"])
            else:
                eng.wait_ge(cT, p.cnt["cT"])
            sel = VAL[:, e * 16:(e + 1) * 16]
            chv(eng, eng.tensor_add(sel, F1[:, e * 16:(e + 1) * 16], F2[:, e * 16:(e + 1) * 16]))
            chv(eng, eng.tensor_copy(SEL16, sel))
            chv(eng, eng.engine_nop())
            eng.drain().then_inc(cV, 1)
        p.V(v_rk); bump("cV")
        def t_rk(eng, e=e):
            eng.wait_ge(cV, p.cnt["cV"])
            eng.matmul(pb[6][:, 0:16], SUP, SEL16, start=True, stop=True)
            eng.matmul(pb[7][:, 0:16], ONESF16, SEL16, start=True, stop=True).then_inc(cT, 1)
        p.T(t_rk); bump("cT")
        def v_rk2(eng, e=e):
            eng.wait_ge(cT, p.cnt["cT"])
            lsr = LSR[:, e * 16:(e + 1) * 16]
            ccm = CCM[:, e * 16:(e + 1) * 16]
            eng.tensor_copy(lsr, pb[6][:, 0:16])
            eng.tensor_copy(ccm, pb[7][:, 0:16])
            chv(eng, eng.engine_nop())
            eng.drain().then_inc(cV, 1)
        p.V(v_rk2); bump("cV")
    def v_rk3(eng):
        eng.wait_ge(cV, p.cnt["cV"])
        for e in range(E):
            lsr = LSR[:, e * 16:(e + 1) * 16]
            ccm = CCM[:, e * 16:(e + 1) * 16]
            for ci in range(1, SI):
                chv(eng, eng.tensor_add(ccm[:, ci:ci + 1], ccm[:, ci:ci + 1], ccm[:, ci - 1:ci]))
            for ci in range(1, SI):
                chv(eng, eng.tensor_add(lsr[:, ci:ci + 1], lsr[:, ci:ci + 1], ccm[:, ci - 1:ci]))
        chv(eng, eng.engine_nop())
        eng.drain().then_inc(cV, 1)
    p.V(v_rk3); bump("cV")

    def v_sigma(eng):
        eng.wait_ge(cV, p.cnt["cV"])
        eng.wait_ge(cA, p.cnt["cA"])
        t = ps(9); t2 = ps(10)
        for e in range(E):
            sel = VAL[:, e * 16:(e + 1) * 16]
            lsr = LSR[:, e * 16:(e + 1) * 16]
            chv(eng, eng.tensor_scalar_add(t, lsr, CBC[:, e:e + 1]))
            chv(eng, eng.tensor_scalar(t, t, float(CAP), None, ALU.is_lt))
            chv(eng, eng.tensor_mul(sel, sel, t))
        lsr1 = ps(5); lsr2 = ps(6); v1 = ps(7); v2 = ps(8)
        chv(eng, eng.memset(lsr1, 0.0)); chv(eng, eng.memset(lsr2, 0.0))
        chv(eng, eng.memset(v1, 0.0)); chv(eng, eng.memset(v2, 0.0))
        for e in range(E):
            f1 = F1[:, e * 16:(e + 1) * 16]; f2 = F2[:, e * 16:(e + 1) * 16]
            lsr = LSR[:, e * 16:(e + 1) * 16]; val = VAL[:, e * 16:(e + 1) * 16]
            chv(eng, eng.tensor_mul(t, f1, lsr)); chv(eng, eng.tensor_add(lsr1, lsr1, t))
            chv(eng, eng.tensor_mul(t, f2, lsr)); chv(eng, eng.tensor_add(lsr2, lsr2, t))
            chv(eng, eng.tensor_mul(t, f1, val)); chv(eng, eng.tensor_add(v1, v1, t))
            chv(eng, eng.tensor_mul(t, f2, val)); chv(eng, eng.tensor_add(v2, v2, t))
        chv(eng, eng.tensor_mul(WV[:, 0:16], ps(4), v1))
        chv(eng, eng.engine_nop())
        eng.drain().then_inc(cV, 1)
    p.V(v_sigma); bump("cV")

    def v_w2(eng):
        t = ps(9)
        chv(eng, eng.tensor_scalar(t, ps(4), -1.0, 1.0, ALU.mult, ALU.add))
        chv(eng, eng.tensor_mul(WV[:, 16:32], t, ps(8)))
        chv(eng, eng.engine_nop())
        eng.drain().then_inc(cV, 1)
    p.V(v_w2); bump("cV")

    def v_sig2(eng):
        t = ps(9); t2 = ps(10)
        for j, (ee, ll, vv, so, go) in enumerate(
                ((ps(2), ps(5), ps(7), 0, 2), (ps(3), ps(6), ps(8), 1, 3))):
            chv(eng, eng.tensor_scalar_min(t, ll, float(WSL - 1)))
            chv(eng, eng.scalar_tensor_tensor(t, ee, float(WSL), t, ALU.mult, ALU.add))
            chv(eng, eng.tensor_copy(pu(0), vv))
            chv(eng, eng.select(t2, pu(0), t, ps(12)))
            chv(eng, eng.tensor_copy(sigi(so), t2))
            chv(eng, eng.select(t2, pu(0), t, ps(13)))
            chv(eng, eng.tensor_copy(sigi(go), t2))
        chv(eng, eng.engine_nop())
        eng.drain().then_inc(cV, 1)
    p.V(v_sig2); bump("cV")

    d = dbgout("tables", [128, 16 * 8])
    if d is not None:
        def f(eng, d=d):
            eng.wait_ge(cV, p.cnt["cV"])
            eng.dma_start(out=d[:, 0:16], in_=ps(2)).then_inc(dA, 16)
            eng.dma_start(out=d[:, 16:32], in_=ps(3)).then_inc(dA, 16)
            eng.dma_start(out=d[:, 32:48], in_=ps(5)).then_inc(dA, 16)
            eng.dma_start(out=d[:, 48:64], in_=ps(6)).then_inc(dA, 16)
            eng.dma_start(out=d[:, 64:80], in_=ps(7)).then_inc(dA, 16)
            eng.dma_start(out=d[:, 80:96], in_=ps(8)).then_inc(dA, 16)
            eng.dma_start(out=d[:, 96:112], in_=WV[:, 0:16]).then_inc(dA, 16)
            eng.dma_start(out=d[:, 112:128], in_=WV[:, 16:32]).then_inc(dA, 16)
        p.A(f); bump("dA", 16 * 8)

    join_all()
    p.close_scope("sF")
    p.close_scope("sq")
    p.close_scope("sx")

    _emit_rest(p, locals())

    nc_final = p.close()
    return nc_final, p.dbg


def _emit_rest(p, g):
    nc = p.nc
    pb = g["pb"]
    IDENF = g["IDENF"]; ONEROW = g["ONEROW"]; E16A = g["E16A"]; ONE16 = g["ONE16"]
    RALL = g["RALL"]; BIASC = g["BIASC"]; GSC = g["GSC"]; SW = g["SW"]
    UO = g["UO"]; tv = g["tv"]; WV = g["WV"]; EPSC2 = g["EPSC"]
    sigi = g["sigi"]
    w_in = g["w_in"]; w16_d = g["w16_d"]
    xq_d = g["xq_d"]; xqo_d = g["xqo_d"]; xnb_d = g["xnb_d"]; xb = g["xb"]
    acc_d = g["acc_d"]; sums_d = g["sums_d"]; ctxT_d = g["ctxT_d"]
    mix_d = g["mix_d"]; E16T = g["E16T"]
    xg16 = g["xg16"]; eout_d = g["eout_d"]; out = g["out"]
    rg_d = g["rg_d"]; ru_d = g["ru_d"]; rd_d = g["rd_d"]
    sg_d = g["sg_d"]; su_d = g["su_d"]; sd16_d = g["sd16_d"]
    dbgout = g["dbgout"]; bump = g["bump"]; join_all = g["join_all"]; chv = g["chv"]; cha = g["cha"]
    dS, dG, dA, cT, cV, cA, cP = (p.sem(n) for n in
                                  ("dS", "dG", "dA", "cT", "cV", "cA", "cP"))

    w32 = p.sbuf("w32", [128, DC * D], F32, scope="wq")
    qscr = p.sbuf("qscr", [128, D], F32, scope="wq")
    ZI2 = p.sbuf("zi2", [128, D], I32, scope="wq")
    w16b = p.sbuf("w16b", [128, D], BF16, scope="wq")
    ONERF = g["ONERF"]; ONECF = g["ONECF"]
    asum = tv[:, 64:65]
    ared = tv[:, 65:66]
    swcol = tv[:, 66:67]

    def v_onerf(eng):
        eng.memset(ONERF, 1.0)
        eng.memset(ONECF, 1.0)
        chv(eng, eng.engine_nop())
        eng.drain().then_inc(cV, 1)
    p.V(v_onerf); bump("cV")

    for wi, wn in enumerate(("wq", "wk", "wv", "wo")):
        def s_w(eng, wn=wn):
            eng.wait_ge(cV, p.cnt["cV"])
            for r in range(DC):
                eng.dma_start(out=w32[:, r * D:(r + 1) * D],
                              in_=w_in[wn][r * 128:(r + 1) * 128, :]).then_inc(dS, 16)
        p.S(s_w); bump("dS", 16 * DC)

        def v_as(eng):
            eng.wait_ge(dS, p.cnt["dS"])
            chv(eng, eng.memset(asum, 0.0))
            for r in range(DC):
                chv(eng, eng.reduce_sum(ared, w32[:, r * D:(r + 1) * D], axis=AX.X,
                                        apply_absolute_value=True))
                chv(eng, eng.tensor_add(asum, asum, ared))
            chv(eng, eng.engine_nop())
            eng.drain().then_inc(cV, 1)
        p.V(v_as); bump("cV")

        def t_tot(eng):
            eng.wait_ge(cV, p.cnt["cV"])
            eng.matmul(pb[6][0:1, 0:1], asum, ONECF, start=True, stop=True).then_inc(cT, 1)
        p.T(t_tot); bump("cT")

        def v_sw(eng, wi=wi):
            eng.wait_ge(cT, p.cnt["cT"])
            sws = SW[:, wi:wi + 1]
            chv(eng, eng.tensor_scalar_mul(sws, pb[6][0:1, 0:1], 1.0 / (D * D)))
            chv(eng, eng.tensor_scalar_max(sws, sws, 1e-5))
            chv(eng, eng.reciprocal(sws, sws))
            chv(eng, eng.engine_nop())
            eng.drain().then_inc(cV, 1)
        p.V(v_sw); bump("cV")

        def t_bc(eng, wi=wi):
            eng.wait_ge(cV, p.cnt["cV"])
            eng.matmul(pb[6][:, 0:1], ONERF, SW[:, wi:wi + 1], start=True, stop=True).then_inc(cT, 1)
        p.T(t_bc); bump("cT")

        def v_bc(eng):
            eng.wait_ge(cT, p.cnt["cT"])
            chv(eng, eng.tensor_copy(swcol, pb[6][:, 0:1]))
            eng.drain().then_inc(cV, 1)
        p.V(v_bc); bump("cV")

        for r in range(DC):
            def v_q(eng, r=r):
                if r == 0:
                    eng.wait_ge(cV, p.cnt["cV"])
                if r > 0:
                    eng.wait_ge(dA, p.cnt["dA"])
                chv(eng, eng.tensor_scalar_mul(qscr, w32[:, r * D:(r + 1) * D], swcol))
                chv(eng, eng.tensor_copy(ZI2, qscr))
                chv(eng, eng.tensor_scalar(w16b, ZI2, -1.0, 1.0, ALU.max, ALU.min))
                chv(eng, eng.engine_nop())
                eng.drain().then_inc(cV, 1)
            p.V(v_q); bump("cV")
            def a_wb(eng, wn=wn, r=r):
                eng.wait_ge(cV, p.cnt["cV"])
                eng.dma_start(out=w16_d[wn][r * 128:(r + 1) * 128, :], in_=w16b).then_inc(dA, 16)
            p.A(a_wb); bump("dA", 16)

    def v_gs(eng):
        eng.wait_ge(dA, p.cnt["dA"])
        a = tv[0:1, 70:71]; b = tv[0:1, 71:72]
        chv(eng, eng.tensor_mul(a, SW[:, 0:1], SW[:, 1:2]))
        chv(eng, eng.tensor_scalar_mul(a, a, 8.0))
        chv(eng, eng.reciprocal(a, a))
        chv(eng, eng.tensor_copy(tv[0:1, 72:73], a))
        chv(eng, eng.reciprocal(b, SW[:, 2:3]))
        chv(eng, eng.tensor_copy(tv[0:1, 73:74], b))
        chv(eng, eng.reciprocal(b, SW[:, 3:4]))
        chv(eng, eng.tensor_copy(tv[0:1, 74:75], b))
        chv(eng, eng.engine_nop())
        eng.drain().then_inc(cV, 1)
    p.V(v_gs); bump("cV")
    for i in range(3):
        def t_g(eng, i=i):
            eng.wait_ge(cV, p.cnt["cV"])
            eng.matmul(pb[6][:, 0:1], ONERF, tv[0:1, 72 + i:73 + i], start=True, stop=True).then_inc(cT, 1)
        p.T(t_g); bump("cT")
        def v_g(eng, i=i):
            eng.wait_ge(cT, p.cnt["cT"])
            chv(eng, eng.tensor_copy(GSC[:, i:i + 1], pb[6][:, 0:1]))
            eng.drain().then_inc(cV, 1)
        p.V(v_g); bump("cV")

    join_all()
    p.close_scope("wq")

    qT = p.sbuf("qT", [128, DC * S], BF16, scope="attB")
    kT = p.sbuf("kT", [128, DC * S], BF16, scope="attB")
    vT = p.sbuf("vT", [128, DC * S], BF16, scope="attB")
    SF = p.sbuf("sfull", [16, S], F32, scope="attB")
    xqT = p.sbuf("xqT", [128, DC * S], BF16, scope="attW")
    wqT = p.sbuf("wqT", [128, DC * D], BF16, scope="attW")
    wkT = p.sbuf("wkT", [128, DC * D], BF16, scope="attW")
    wvT = p.sbuf("wvT", [128, DC * D], BF16, scope="attW")

    def s_tp(eng):
        eng.wait_ge(dA, p.cnt["dA"])
        for dc in range(DC):
            eng.dma_start(out=xqT[:, dc * S:(dc + 1) * S],
                          in_=xq_d[:, dc * 128:(dc + 1) * 128], transpose=True).then_inc(dS, 16)
            eng.wait_ge(dS, bump("dS", 16))
        for wbuf, wn in ((wqT, "wq"), (wkT, "wk"), (wvT, "wv")):
            for dc in range(DC):
                eng.dma_start(out=wbuf[:, dc * D:(dc + 1) * D],
                              in_=w16_d[wn][:, dc * 128:(dc + 1) * 128],
                              transpose=True).then_inc(dS, 16)
                eng.wait_ge(dS, bump("dS", 16))
    p.S(s_tp)
    tp_dS = p.cnt["dS"]

    for which, wbuf, dst, gcol in (("q", wqT, qT, None), ("k", wkT, kT, 0), ("v", wvT, vT, 1)):
        for ec in range(DC):
            for nb in range(4):
                bank = (nb % 2) * 2 + (0 if which != "k" else 1)
                def t_p(eng, ec=ec, nb=nb, wbuf=wbuf, bank=bank):
                    if ec == 0 and nb == 0:
                        eng.wait_ge(dS, tp_dS)
                    eng.wait_ge(cV, p.cnt["cV"])
                    for dc in range(DC):
                        mm = eng.matmul(pb[bank], wbuf[:, dc * D + ec * 128: dc * D + (ec + 1) * 128],
                                        xqT[:, dc * S + nb * 512: dc * S + (nb + 1) * 512],
                                        start=(dc == 0), stop=(dc == DC - 1))
                    mm.then_inc(cT, 1)
                p.T(t_p); bump("cT")
                def v_p(eng, ec=ec, nb=nb, dst=dst, gcol=gcol, bank=bank):
                    eng.wait_ge(cT, p.cnt["cT"])
                    o = dst[:, ec * S + nb * 512: ec * S + (nb + 1) * 512]
                    if gcol is None:
                        chv(eng, eng.tensor_copy(o, pb[bank]))
                    else:
                        chv(eng, eng.tensor_scalar_mul(o, pb[bank], GSC[:, gcol:gcol + 1]))
                    eng.drain().then_inc(cV, 1)
                p.V(v_p); bump("cV")

    join_all()
    p.close_scope("attW")

    krep = p.sbuf("krep", [128, 512], BF16, scope="hmx")
    vrep = p.sbuf("vrep", [128, 512], BF16, scope="hmx")
    PROD = p.sbuf("prod", [128, 2 * 512], BF16, scope="hmx")
    SCt = [p.sbuf(f"sct{t}", [16, 512], F32, scope="hmx") for t in range(16)]
    EBt = [p.sbuf(f"ebt{t}", [16, 512], BF16, scope="hmx") for t in range(16)]
    MX = p.sbuf("mx", [16, 512], F32, scope="hmx")
    TMPA = p.sbuf("tmpa", [128, 2 * 512], BF16, scope="hmx")
    ctxa = p.sbuf("ctxa", [128, DC * 512], F32, scope="hmx")

    for sb in range(4):
        for t in range(16):
            def v_kr(eng, sb=sb, t=t):
                eng.wait_ge(cT, p.cnt["cT"])
                src_ = kT[:, (t // 2) * S + sb * 512:(t // 2) * S + (sb + 1) * 512]
                base = (t % 2) * 64
                chv(eng, eng.tensor_copy(krep[0:64, :], src_[base:base + 64, :]))
                chv(eng, eng.tensor_copy(krep[64:128, :], src_[base:base + 64, :]))
                eng.drain().then_inc(cV, 1)
            p.V(v_kr); bump("cV")
            for c in range(DC):
                def v_pr(eng, sb=sb, t=t, c=c):
                    if c >= 2:
                        eng.wait_ge(cT, p.cnt["cT"] - 1)
                    chv(eng, eng.tensor_mul(PROD[:, (c % 2) * 512:(c % 2 + 1) * 512],
                                            qT[:, c * S + sb * 512: c * S + (sb + 1) * 512], krep))
                    eng.drain().then_inc(cV, 1)
                p.V(v_pr); bump("cV")
                def t_sc(eng, t=t, c=c):
                    eng.wait_ge(cV, p.cnt["cV"])
                    eng.matmul(pb[4][0:16, :], E16A[:, c * 16:(c + 1) * 16],
                               PROD[:, (c % 2) * 512:(c % 2 + 1) * 512],
                               start=(c == 0), stop=(c == DC - 1)).then_inc(cT, 1)
                p.T(t_sc); bump("cT")
            def v_sc(eng, t=t):
                eng.wait_ge(cT, p.cnt["cT"])
                chv(eng, eng.tensor_copy(SCt[t], pb[4][0:16, :]))
                eng.drain().then_inc(cV, 1)
            p.V(v_sc); bump("cV")
        def v_sm(eng, sb=sb):
            chv(eng, eng.tensor_max(MX, SCt[0], SCt[1]))
            for t in range(2, 16):
                chv(eng, eng.tensor_max(MX, MX, SCt[t]))
            for t in range(16):
                chv(eng, eng.tensor_sub(SCt[t], SCt[t], MX))
            eng.drain().then_inc(cV, 1)
        p.V(v_sm); bump("cV")
        def a_ex(eng, sb=sb):
            eng.wait_ge(cV, p.cnt["cV"])
            for t in range(16):
                cha(eng, eng.activation(SCt[t], SCt[t], AF.Exp))
            eng.drain().then_inc(cA, 1)
        p.A(a_ex); bump("cA")
        def v_ss(eng, sb=sb):
            eng.wait_ge(cA, p.cnt["cA"])
            s_ = SF[:, sb * 512:(sb + 1) * 512]
            chv(eng, eng.tensor_add(s_, SCt[0], SCt[1]))
            for t in range(2, 16):
                chv(eng, eng.tensor_add(s_, s_, SCt[t]))
            for t in range(16):
                chv(eng, eng.tensor_copy(EBt[t], SCt[t]))
            eng.drain().then_inc(cV, 1)
        p.V(v_ss); bump("cV")
        def g_cz(eng, sb=sb):
            eng.wait_ge(cP, p.cnt["cP"])
            eng.wait_ge(dA, p.cnt["dA"])
            eng.memset(ctxa, 0.0)
            eng.drain().then_inc(cP, 1)
        p.G(g_cz); bump("cP")
        for t in range(16):
            def v_vr(eng, sb=sb, t=t):
                eng.wait_ge(cP, p.cnt["cP"])
                src_ = vT[:, (t // 2) * S + sb * 512:(t // 2) * S + (sb + 1) * 512]
                base = (t % 2) * 64
                chv(eng, eng.tensor_copy(vrep[0:64, :], src_[base:base + 64, :]))
                chv(eng, eng.tensor_copy(vrep[64:128, :], src_[base:base + 64, :]))
                eng.drain().then_inc(cV, 1)
            p.V(v_vr); bump("cV")
            for c in range(DC):
                def t_bc(eng, t=t, c=c):
                    eng.wait_ge(cV, p.cnt["cV"])
                    if c == 0:
                        eng.wait_ge(cP, p.cnt["cP"])
                    eng.matmul(pb[5 + c % 2][:, :], E16T[:, c * 128:(c + 1) * 128],
                               EBt[t], start=True, stop=True).then_inc(cT, 1)
                p.T(t_bc); bump("cT")
                def v_mu(eng, t=t, c=c):
                    eng.wait_ge(cT, p.cnt["cT"])
                    eng.wait_ge(cP, max(p.cnt["cP"] - 1, 0))
                    chv(eng, eng.tensor_mul(TMPA[:, (c % 2) * 512:(c % 2 + 1) * 512],
                                            pb[5 + c % 2], vrep))
                    eng.drain().then_inc(cV, 1)
                p.V(v_mu); bump("cV")
                def g_ad(eng, t=t, c=c):
                    eng.wait_ge(cV, p.cnt["cV"])
                    a = ctxa[:, c * 512:(c + 1) * 512]
                    eng.tensor_add(a, a, TMPA[:, (c % 2) * 512:(c % 2 + 1) * 512])
                    eng.drain().then_inc(cP, 1)
                p.G(g_ad); bump("cP")
        def a_cw2(eng, sb=sb):
            eng.wait_ge(cP, p.cnt["cP"])
            for c in range(DC):
                eng.dma_start(out=ctxT_d[c * 128:(c + 1) * 128, sb * 512:(sb + 1) * 512],
                              in_=ctxa[:, c * 512:(c + 1) * 512]).then_inc(dA, 16)
        p.A(a_cw2); bump("dA", 16 * DC)

    for tci in range(SI):
        def t_r(eng, tci=tci):
            if tci == 0:
                eng.wait_ge(cV, p.cnt["cV"])
            eng.wait_ge(cV, p.cnt["cV"])
            eng.transpose(pb[6][:, 0:16], SF[:, tci * 128:(tci + 1) * 128], IDENF[0:16, 0:16]).then_inc(cT, 1)
        p.T(t_r); bump("cT")
        def v_r(eng, tci=tci):
            eng.wait_ge(cT, p.cnt["cT"])
            chv(eng, eng.reciprocal(RALL[:, tci * 16:(tci + 1) * 16], pb[6][:, 0:16]))
            eng.drain().then_inc(cV, 1)
        p.V(v_r); bump("cV")

    join_all()
    p.close_scope("hmx")
    p.close_scope("attB")

    cblk = p.sbuf("cblk", [64, S], F32, scope="mix")
    mbuf = p.sbuf("mbuf", [128, 2 * 64], F32, scope="mix")
    ctok = p.sbuf("ctok", [128, D], F32, scope="mix")
    zoq = p.sbuf("zoq", [128, D], F32, scope="mix")
    ZI3 = p.sbuf("zi3", [128, D], I32, scope="mix")
    xqo16 = p.sbuf("xqo16", [128, D], BF16, scope="mix")

    for ci in range(SI):
        def s_cb(eng, ci=ci):
            eng.wait_ge(dA, p.cnt["dA"])
            eng.wait_ge(cT, p.cnt["cT"])
            eng.dma_start(out=cblk, in_=ctxT_d[ci * 64:(ci + 1) * 64, :]).then_inc(dS, 16)
        p.S(s_cb); bump("dS", 16)
        for tci in range(SI):
            def t_mt(eng, ci=ci, tci=tci):
                if tci == 0:
                    eng.wait_ge(dS, p.cnt["dS"])
                eng.wait_ge(cV, max(p.cnt["cV"] - 1, 1))
                eng.transpose(pb[6 + tci % 2][:, 0:64],
                              cblk[:, tci * 128:(tci + 1) * 128],
                              IDENF[0:64, 0:64]).then_inc(cT, 1)
            p.T(t_mt); bump("cT")
            def v_mt(eng, ci=ci, tci=tci):
                eng.wait_ge(cT, p.cnt["cT"])
                eng.wait_ge(dA, max(p.cnt["dA"] - 1, 0))
                chv(eng, eng.tensor_scalar_mul(mbuf[:, (tci % 2) * 64:(tci % 2 + 1) * 64],
                                               pb[6 + tci % 2][:, 0:64],
                                               RALL[:, tci * 16 + ci:tci * 16 + ci + 1]))
                eng.drain().then_inc(cV, 1)
            p.V(v_mt); bump("cV")
            def a_mw(eng, ci=ci, tci=tci):
                eng.wait_ge(cV, p.cnt["cV"])
                eng.dma_start(out=mix_d[tci * 128:(tci + 1) * 128, :],
                              in_=mbuf[:, (tci % 2) * 64:(tci % 2 + 1) * 64]).then_inc(dA, 16)
            p.A(a_mw); bump("dA", 16)
        def s_mr(eng, ci=ci):
            eng.wait_ge(dA, p.cnt["dA"])
            eng.wait_ge(cV, p.cnt["cV"])
            eng.dma_start(out=ctok.rearrange("p (j dh) -> p j dh", dh=64),
                          in_=mix_d.rearrange("(pp j) dh -> pp j dh", j=16)).then_inc(dS, 16)
        p.S(s_mr); bump("dS", 16)
        c0 = tv[:, 80:81]; c1 = tv[:, 81:82]; c2 = tv[:, 82:83]; c3 = tv[:, 83:84]
        uo = UO[:, ci:ci + 1]
        def v_cq(eng, ci=ci, c0=c0, c1=c1):
            eng.wait_ge(dS, p.cnt["dS"])
            eng.wait_ge(dA, p.cnt["dA"])
            chv(eng, eng.scalar_tensor_tensor(zoq, ctok, 1.0, ctok, ALU.mult, ALU.mult))
            chv(eng, eng.reduce_sum(c0, zoq, axis=AX.X))
            chv(eng, eng.reduce_max(c1, ctok, axis=AX.X, apply_absolute_value=True))
            chv(eng, eng.tensor_scalar_mul(c0, c0, 1.0 / D))
            eng.drain().then_inc(cV, 1)
        p.V(v_cq); bump("cV")
        def a_cq(eng, c0=c0, c2=c2):
            eng.wait_ge(cV, p.cnt["cV"])
            cha(eng, eng.activation(c2, c0, AF.Sqrt, bias=EPSC2[:, 1:2], scale=1.0))
            eng.drain().then_inc(cA, 1)
        p.A(a_cq); bump("cA")
        def v_cq2(eng, ci=ci, c1=c1, c2=c2, c3=c3, uo=uo):
            eng.wait_ge(cA, p.cnt["cA"])
            chv(eng, eng.reciprocal(c2, c2))
            chv(eng, eng.tensor_mul(c3, c2, c1))
            chv(eng, eng.tensor_scalar_max(c3, c3, EPS_Q))
            chv(eng, eng.tensor_scalar_mul(uo, c3, 1.0 / 127.0))
            chv(eng, eng.reciprocal(c3, c3))
            chv(eng, eng.scalar_tensor_tensor(c2, c2, 127.0, c3, ALU.mult, ALU.mult))
            chv(eng, eng.tensor_scalar_mul(zoq, ctok, c2))
            chv(eng, eng.tensor_copy(ZI3, zoq))
            chv(eng, eng.tensor_scalar_mul(xqo16, ZI3, uo))
            eng.drain().then_inc(cV, 1)
        p.V(v_cq2); bump("cV")
        def a_cw(eng, ci=ci):
            eng.wait_ge(cV, p.cnt["cV"])
            eng.dma_start(out=xqo_d[ci * 128:(ci + 1) * 128, :], in_=xqo16).then_inc(dA, 16)
        p.A(a_cw); bump("dA", 16)

    join_all()
    p.close_scope("mix")

    xqoT = p.sbuf("xqoT", [128, DC * S], BF16, scope="wo")
    woT = p.sbuf("woT", [128, DC * D], BF16, scope="wo")
    xrld = p.sbuf("xrld", [128, D], F32, scope="wo")
    acct = p.sbuf("acct", [128, D], F32, scope="wo")

    def s_wo(eng):
        eng.wait_ge(dA, p.cnt["dA"])
        for dc in range(DC):
            eng.dma_start(out=xqoT[:, dc * S:(dc + 1) * S],
                          in_=xqo_d[:, dc * 128:(dc + 1) * 128], transpose=True).then_inc(dS, 16)
            eng.wait_ge(dS, bump("dS", 16))
        for dc in range(DC):
            eng.dma_start(out=woT[:, dc * D:(dc + 1) * D],
                          in_=w16_d["wo"][:, dc * 128:(dc + 1) * 128],
                          transpose=True).then_inc(dS, 16)
            eng.wait_ge(dS, bump("dS", 16))
    p.S(s_wo)
    wo_dS = p.cnt["dS"]

    for si in range(SI):
        def s_x(eng, si=si):
            eng.wait_ge(cV, p.cnt["cV"])
            eng.dma_start(out=xrld, in_=xb[si * 128:(si + 1) * 128, :]).then_inc(dS, 16)
        p.S(s_x); bump("dS", 16)
        def t_wo(eng, si=si):
            if si == 0:
                eng.wait_ge(dS, wo_dS)
            eng.wait_ge(cV, p.cnt["cV"])
            for dc in range(DC):
                eng.matmul(pb[0], xqoT[:, dc * S + si * 128: dc * S + (si + 1) * 128],
                           woT[:, dc * D: dc * D + 512],
                           start=(dc == 0), stop=(dc == DC - 1))
            for dc in range(DC):
                mm = eng.matmul(pb[1], xqoT[:, dc * S + si * 128: dc * S + (si + 1) * 128],
                                woT[:, dc * D + 512: (dc + 1) * D],
                                start=(dc == 0), stop=(dc == DC - 1))
            mm.then_inc(cT, 1)
        p.T(t_wo); bump("cT")
        def v_wo(eng, si=si):
            eng.wait_ge(cT, p.cnt["cT"])
            eng.wait_ge(dS, p.cnt["dS"])
            eng.wait_ge(dA, p.cnt["dA"])
            eng.scalar_tensor_tensor(acct[:, 0:512], pb[0], GSC[:, 2:3], xrld[:, 0:512],
                                     ALU.mult, ALU.add)
            eng.scalar_tensor_tensor(acct[:, 512:1024], pb[1], GSC[:, 2:3], xrld[:, 512:1024],
                                     ALU.mult, ALU.add)
            chv(eng, eng.engine_nop())
            eng.drain().then_inc(cV, 1)
        p.V(v_wo); bump("cV")
        def a_acc(eng, si=si):
            eng.wait_ge(cV, p.cnt["cV"])
            eng.dma_start(out=acc_d[si * 128:(si + 1) * 128, :], in_=acct).then_inc(dA, 16)
        p.A(a_acc); bump("dA", 16)

    d = dbgout("xqo", [S, D], BF16)
    if d is not None:
        def f_xqo(eng, d=d):
            eng.wait_ge(dA, p.cnt["dA"])
            eng.dma_start(out=d, in_=xqo_d).then_inc(dS, 16)
        p.S(f_xqo); bump("dS", 16)
    d = dbgout("ctxT", [D, S])
    if d is not None:
        def f_ctxT(eng, d=d):
            eng.wait_ge(dA, p.cnt["dA"])
            for k in range(8):
                eng.dma_start(out=d[k * 128:(k + 1) * 128, :],
                              in_=ctxT_d[k * 128:(k + 1) * 128, :]).then_inc(dS, 16)
        p.S(f_ctxT); bump("dS", 16 * 8)
    d = dbgout("sums", [1, NH * S])
    if d is not None:
        def f_sums(eng, d=d):
            eng.wait_ge(dA, p.cnt["dA"])
            eng.dma_start(out=d, in_=sums_d).then_inc(dS, 16)
        p.S(f_sums); bump("dS", 16)
    d = dbgout("accA", [S, D])
    if d is not None:
        def f_accA(eng, d=d):
            eng.wait_ge(dA, p.cnt["dA"])
            eng.dma_start(out=d, in_=acc_d).then_inc(dS, 16)
        p.S(f_accA); bump("dS", 16)
    join_all()
    p.close_scope("wo")

    ztile = p.sbuf("ztile", [128, D], BF16, scope="moe")
    xnbt = p.sbuf("xnbt", [128, D], BF16, scope="moe")
    xgT = p.sbuf("xgT", [128, DC * WSL], BF16, scope="moe")
    rg16 = [p.sbuf(f"rg16_{i}", [128, 4096], BF16, scope="moe") for i in range(2)]
    ru16 = [p.sbuf(f"ru16_{i}", [128, 4096], BF16, scope="moe") for i in range(2)]
    h_e = p.sbuf("h_e", [128, 32 * WSL], BF16, scope="moe")
    rd16 = p.sbuf("rd16", [128, 32 * D], BF16, scope="moe")
    sgt = p.sbuf("sgt", [128, 1024], F32, scope="moe")
    eoutb = p.sbuf("eoutb", [128, D], F32, scope="moe")

    def g_z(eng):
        eng.memset(ztile, 0.0).then_inc(p.sem("cP"), 1)
        eng.wait_ge(p.sem("cP"), bump("cP"))
        for r in range(NSLOT // 128 + 1):
            eng.dma_start(out=xg16[r * 128:(r + 1) * 128, :], in_=ztile).then_inc(dG, 16)
    bump("dG", 16 * (NSLOT // 128 + 1)); p.G(g_z)

    def g_sc(eng):
        eng.wait_ge(dG, p.cnt["dG"])
        for ci in range(SI):
            eng.dma_start(out=xnbt, in_=xnb_d[ci * 128:(ci + 1) * 128, :]).then_inc(dG, 16)
            eng.wait_ge(dG, bump("dG", 16))
            for jj in range(2):
                eng.indirect_dma_start(
                    out=xg16, out_offset=IndirectOffsetOnAxis(sigi(jj)[:, ci:ci + 1], 0),
                    in_=xnbt, in_offset=None).then_inc(dG, 16)
                bump("dG", 16)
            eng.wait_ge(dG, p.cnt["dG"])
    p.G(g_sc)
    scat_dG = p.cnt["dG"]

    for e in range(E):
        def s_xg(eng, e=e):
            if e == 0:
                eng.wait_ge(dG, scat_dG)
            eng.wait_ge(cT, p.cnt["cT"])
            for dc in range(DC):
                eng.dma_start(out=xgT[:, dc * WSL:(dc + 1) * WSL],
                              in_=xg16[e * WSL:(e + 1) * WSL, dc * 128:(dc + 1) * 128],
                              transpose=True).then_inc(dS, 16)
                eng.wait_ge(dS, bump("dS", 16))
        p.S(s_xg)
        def g_rd(eng, e=e):
            eng.wait_ge(cT, p.cnt["cT"])
            for ic in range(32):
                eng.dma_start(out=rd16[:, ic * D:(ic + 1) * D],
                              in_=rd_d[e, ic * 128:(ic + 1) * 128, :]).then_inc(dG, 16)
        p.G(g_rd); bump("dG", 16 * 32)
        xg_dS = p.cnt["dS"]

        for icc in range(DC):
            buf = icc % 2
            def g_w(eng, e=e, icc=icc, buf=buf):
                if icc >= 2:
                    eng.wait_ge(cT, p.cnt["cT"])
                eng.dma_start(out=rg16[buf].rearrange("p (c i) -> p c i", i=512),
                              in_=rg_d[e, :, icc * 512:(icc + 1) * 512].rearrange(
                                  "(c p) i -> p c i", p=128)).then_inc(dG, 16)
                eng.dma_start(out=ru16[buf].rearrange("p (c i) -> p c i", i=512),
                              in_=ru_d[e, :, icc * 512:(icc + 1) * 512].rearrange(
                                  "(c p) i -> p c i", p=128)).then_inc(dG, 16)
            p.G(g_w); bump("dG", 32)
            for ib in range(4):
                for nbs in range(2):
                    kk = (icc * 8 + ib * 2 + nbs) % 2
                    def t_gu(eng, e=e, icc=icc, ib=ib, nbs=nbs, buf=buf, kk=kk):
                        if ib == 0 and nbs == 0:
                            eng.wait_ge(dG, p.cnt["dG"])
                            if icc == 0:
                                eng.wait_ge(dS, xg_dS)
                        if p.cnt["cV"] >= 1:
                            eng.wait_ge(cV, max(p.cnt["cV"] - 1, 1))
                        for dc in range(DC):
                            eng.matmul(pb[2 * kk], rg16[buf][:, dc * 512 + ib * 128: dc * 512 + (ib + 1) * 128],
                                       xgT[:, dc * WSL + nbs * 512: dc * WSL + (nbs + 1) * 512],
                                       start=(dc == 0), stop=(dc == DC - 1))
                        for dc in range(DC):
                            mm = eng.matmul(pb[2 * kk + 1], ru16[buf][:, dc * 512 + ib * 128: dc * 512 + (ib + 1) * 128],
                                            xgT[:, dc * WSL + nbs * 512: dc * WSL + (nbs + 1) * 512],
                                            start=(dc == 0), stop=(dc == DC - 1))
                        mm.then_inc(cT, 1)
                    p.T(t_gu); bump("cT")
                    def a_si(eng, kk=kk):
                        eng.wait_ge(cT, p.cnt["cT"])
                        if p.cnt["cV"] >= 1:
                            eng.wait_ge(cV, max(p.cnt["cV"] - 1, 1))
                        cha(eng, eng.activation(sgt[:, kk * 512:(kk + 1) * 512], pb[2 * kk], AF.Silu))
                        eng.drain().then_inc(cA, 1)
                    p.A(a_si); bump("cA")
                    def v_h(eng, e=e, icc=icc, ib=ib, nbs=nbs, kk=kk):
                        eng.wait_ge(cA, p.cnt["cA"])
                        ic32 = icc * 4 + ib
                        eng.tensor_mul(h_e[:, ic32 * WSL + nbs * 512: ic32 * WSL + (nbs + 1) * 512],
                                       sgt[:, kk * 512:(kk + 1) * 512], pb[2 * kk + 1])
                        chv(eng, eng.engine_nop())
                        eng.drain().then_inc(cV, 1)
                    p.V(v_h); bump("cV")
        for sc in range(8):
            def t_dn(eng, e=e, sc=sc):
                if sc == 0:
                    eng.wait_ge(dG, p.cnt["dG"])
                eng.wait_ge(cV, p.cnt["cV"])
                for ic in range(32):
                    eng.matmul(pb[4], h_e[:, ic * WSL + sc * 128: ic * WSL + (sc + 1) * 128],
                               rd16[:, ic * D: ic * D + 512],
                               start=(ic == 0), stop=(ic == 31))
                for ic in range(32):
                    mm = eng.matmul(pb[5], h_e[:, ic * WSL + sc * 128: ic * WSL + (sc + 1) * 128],
                                    rd16[:, ic * D + 512: (ic + 1) * D],
                                    start=(ic == 0), stop=(ic == 31))
                mm.then_inc(cT, 1)
            p.T(t_dn); bump("cT")
            def v_dn(eng):
                eng.wait_ge(cT, p.cnt["cT"])
                eng.wait_ge(dA, p.cnt["dA"])
                eng.tensor_copy(eoutb[:, 0:512], pb[4])
                eng.tensor_copy(eoutb[:, 512:1024], pb[5])
                chv(eng, eng.engine_nop())
                eng.drain().then_inc(cV, 1)
            p.V(v_dn); bump("cV")
            def a_eo(eng, e=e, sc=sc):
                eng.wait_ge(cV, p.cnt["cV"])
                eng.dma_start(out=eout_d[e * WSL + sc * 128: e * WSL + (sc + 1) * 128, :],
                              in_=eoutb).then_inc(dA, 16)
            p.A(a_eo); bump("dA", 16)

    join_all()
    p.close_scope("moe")

    xnT16 = p.sbuf("xnT16", [128, DC * S], BF16, scope="shc")
    sg16 = [p.sbuf(f"sg16_{i}", [128, 4096], BF16, scope="shc") for i in range(2)]
    su16 = [p.sbuf(f"su16_{i}", [128, 4096], BF16, scope="shc") for i in range(2)]
    h_sb = p.sbuf("h_sb", [128, 64 * 512], BF16, scope="shc")
    sdb = [p.sbuf(f"sdb_{i}", [128, D], BF16, scope="shc") for i in range(2)]
    shT = p.sbuf("shT", [128, 8 * 512], F32, scope="shc")
    sgt2 = p.sbuf("sgt2", [128, 1024], F32, scope="shc")
    acc4 = p.sbuf("acc4", [128, 4 * D], F32, scope="shc")

    def s_xnt(eng):
        for dc in range(DC):
            eng.dma_start(out=xnT16[:, dc * S:(dc + 1) * S],
                          in_=xnb_d[:, dc * 128:(dc + 1) * 128], transpose=True).then_inc(dS, 16)
            eng.wait_ge(dS, bump("dS", 16))
    p.S(s_xnt)
    xnt_dS = p.cnt["dS"]

    for sb in range(4):
        for icc in range(16):
            buf = icc % 2
            def g_sg(eng, icc=icc, buf=buf):
                if icc >= 2:
                    eng.wait_ge(cT, p.cnt["cT"])
                eng.dma_start(out=sg16[buf].rearrange("p (c i) -> p c i", i=512),
                              in_=sg_d[:, icc * 512:(icc + 1) * 512].rearrange(
                                  "(c p) i -> p c i", p=128)).then_inc(dG, 16)
                eng.dma_start(out=su16[buf].rearrange("p (c i) -> p c i", i=512),
                              in_=su_d[:, icc * 512:(icc + 1) * 512].rearrange(
                                  "(c p) i -> p c i", p=128)).then_inc(dG, 16)
            p.G(g_sg); bump("dG", 32)
            for ib in range(4):
                kk = (icc * 4 + ib) % 2
                def t_gu(eng, sb=sb, icc=icc, ib=ib, buf=buf, kk=kk):
                    if ib == 0:
                        eng.wait_ge(dG, p.cnt["dG"])
                        if icc == 0:
                            if sb == 0:
                                eng.wait_ge(dS, xnt_dS)
                            eng.wait_ge(cV, p.cnt["cV"])
                    if p.cnt["cV"] >= 1:
                        eng.wait_ge(cV, max(p.cnt["cV"] - 1, 1))
                    for dc in range(DC):
                        eng.matmul(pb[2 * kk], sg16[buf][:, dc * 512 + ib * 128: dc * 512 + (ib + 1) * 128],
                                   xnT16[:, dc * S + sb * 512: dc * S + (sb + 1) * 512],
                                   start=(dc == 0), stop=(dc == DC - 1))
                    for dc in range(DC):
                        mm = eng.matmul(pb[2 * kk + 1], su16[buf][:, dc * 512 + ib * 128: dc * 512 + (ib + 1) * 128],
                                        xnT16[:, dc * S + sb * 512: dc * S + (sb + 1) * 512],
                                        start=(dc == 0), stop=(dc == DC - 1))
                    mm.then_inc(cT, 1)
                p.T(t_gu); bump("cT")
                def a_si(eng, kk=kk):
                    eng.wait_ge(cT, p.cnt["cT"])
                    if p.cnt["cV"] >= 1:
                        eng.wait_ge(cV, max(p.cnt["cV"] - 1, 1))
                    cha(eng, eng.activation(sgt2[:, kk * 512:(kk + 1) * 512], pb[2 * kk], AF.Silu))
                    eng.drain().then_inc(cA, 1)
                p.A(a_si); bump("cA")
                def v_h(eng, icc=icc, ib=ib, kk=kk):
                    eng.wait_ge(cA, p.cnt["cA"])
                    ic64 = icc * 4 + ib
                    eng.tensor_mul(h_sb[:, ic64 * 512:(ic64 + 1) * 512],
                                   sgt2[:, kk * 512:(kk + 1) * 512], pb[2 * kk + 1])
                    chv(eng, eng.engine_nop())
                    eng.drain().then_inc(cV, 1)
                p.V(v_h); bump("cV")
        for ic64 in range(64):
            buf = ic64 % 2
            def g_sd(eng, ic64=ic64, buf=buf):
                if ic64 >= 2:
                    eng.wait_ge(cT, p.cnt["cT"])
                eng.dma_start(out=sdb[buf],
                              in_=sd16_d[ic64 * 128:(ic64 + 1) * 128, :]).then_inc(dG, 16)
            p.G(g_sd); bump("dG", 16)
            def t_dn(eng, sb=sb, ic64=ic64, buf=buf):
                eng.wait_ge(dG, p.cnt["dG"])
                if ic64 == 0:
                    eng.wait_ge(cV, p.cnt["cV"])
                for dm in range(8):
                    mm = eng.matmul(pb[dm], sdb[buf][:, dm * 128:(dm + 1) * 128],
                                    h_sb[:, ic64 * 512:(ic64 + 1) * 512],
                                    start=(ic64 == 0), stop=(ic64 == 63))
                mm.then_inc(cT, 1)
            p.T(t_dn); bump("cT")
        def v_sh(eng):
            eng.wait_ge(cT, p.cnt["cT"])
            for dm in range(8):
                eng.tensor_copy(shT[:, dm * 512:(dm + 1) * 512], pb[dm])
            chv(eng, eng.engine_nop())
            eng.drain().then_inc(cV, 1)
        p.V(v_sh); bump("cV")
        def s_ac(eng, sb=sb):
            eng.wait_ge(cV, p.cnt["cV"])
            eng.wait_ge(dA, p.cnt["dA"])
            for q in range(4):
                eng.dma_start(out=acc4[:, q * D:(q + 1) * D],
                              in_=acc_d[sb * 512 + q * 128: sb * 512 + (q + 1) * 128, :]).then_inc(dS, 16)
        p.S(s_ac); bump("dS", 64)
        for q in range(4):
            for dm in range(8):
                bank = dm % 2
                def t_tr(eng, q=q, dm=dm, bank=bank):
                    if q == 0 and dm == 0:
                        eng.wait_ge(dS, p.cnt["dS"])
                    if dm >= 2 or q > 0:
                        eng.wait_ge(cV, p.cnt["cV"] - 1)
                    eng.transpose(pb[bank][:, 0:128],
                                  shT[:, dm * 512 + q * 128: dm * 512 + (q + 1) * 128],
                                  IDENF).then_inc(cT, 1)
                p.T(t_tr); bump("cT")
                def v_ad(eng, q=q, dm=dm, bank=bank):
                    eng.wait_ge(cT, p.cnt["cT"])
                    a = acc4[:, q * D + dm * 128: q * D + (dm + 1) * 128]
                    eng.tensor_add(a, a, pb[bank][:, 0:128])
                    chv(eng, eng.engine_nop())
                    eng.drain().then_inc(cV, 1)
                p.V(v_ad); bump("cV")
        def a_st(eng, sb=sb):
            eng.wait_ge(cV, p.cnt["cV"])
            for q in range(4):
                eng.dma_start(out=acc_d[sb * 512 + q * 128: sb * 512 + (q + 1) * 128, :],
                              in_=acc4[:, q * D:(q + 1) * D]).then_inc(dA, 16)
        p.A(a_st); bump("dA", 64)

    d = dbgout("accS", [S, D])
    if d is not None:
        def f_accS(eng, d=d):
            eng.wait_ge(dA, p.cnt["dA"])
            eng.dma_start(out=d, in_=acc_d).then_inc(dS, 16)
        p.S(f_accS); bump("dS", 16)
    d = dbgout("eout", [NSLOT, D])
    if d is not None:
        def f_eout(eng, d=d):
            eng.wait_ge(dA, p.cnt["dA"])
            for k in range(8):
                eng.dma_start(out=d[k * WSL:(k + 1) * WSL, :],
                              in_=eout_d[k * WSL:(k + 1) * WSL, :]).then_inc(dS, 16)
        p.S(f_eout); bump("dS", 16 * 8)
    join_all()
    p.close_scope("shc")

    accf = p.sbuf("accf", [128, D], F32, scope="fin")
    g1b = p.sbuf("g1b", [128, D], F32, scope="fin")
    g2b = p.sbuf("g2b", [128, D], F32, scope="fin")

    for ci in range(SI):
        def s_af(eng, ci=ci):
            eng.wait_ge(dS, p.cnt["dS"])
            eng.dma_start(out=accf, in_=acc_d[ci * 128:(ci + 1) * 128, :]).then_inc(dS, 16)
        p.S(s_af); bump("dS", 16)
        def g_g(eng, ci=ci):
            eng.wait_ge(cV, p.cnt["cV"])
            eng.indirect_dma_start(out=g1b, out_offset=None, in_=eout_d,
                                   in_offset=IndirectOffsetOnAxis(sigi(2)[:, ci:ci + 1], 0)).then_inc(dG, 16)
            eng.indirect_dma_start(out=g2b, out_offset=None, in_=eout_d,
                                   in_offset=IndirectOffsetOnAxis(sigi(3)[:, ci:ci + 1], 0)).then_inc(dG, 16)
        p.G(g_g); bump("dG", 32)
        def v_fin(eng, ci=ci):
            eng.wait_ge(dS, p.cnt["dS"])
            eng.wait_ge(dG, p.cnt["dG"])
            eng.scalar_tensor_tensor(accf, g1b, WV[:, ci:ci + 1], accf, ALU.mult, ALU.add)
            eng.scalar_tensor_tensor(accf, g2b, WV[:, 16 + ci:17 + ci], accf, ALU.mult, ALU.add)
            chv(eng, eng.engine_nop())
            eng.drain().then_inc(cV, 1)
        p.V(v_fin); bump("cV")
        def s_out(eng, ci=ci):
            eng.wait_ge(cV, p.cnt["cV"])
            eng.dma_start(out=out[ci * 128:(ci + 1) * 128, :], in_=accf).then_inc(dS, 16)
        p.S(s_out); bump("dS", 16)

    def s_end(eng):
        eng.wait_ge(dS, p.cnt["dS"])
    p.S(s_end)
    p.close_scope("fin")


_NC_CACHE = {}


def _consts():
    iden = np.eye(128, dtype=np.float32)
    sup = np.triu(np.ones((128, 128), np.float32), 1)
    e16 = np.zeros((128, 128), np.float32)
    for c in range(8):
        for pp in range(128):
            m = 2 * c + (1 if pp >= 64 else 0)
            e16[pp, c * 16 + m] = 1.0
    e16t = np.zeros((16, 1024), np.float32)
    for c in range(8):
        for mm in range(128):
            e16t[2 * c + mm // 64, c * 128 + mm] = 1.0
    import ml_dtypes
    return (iden, sup.astype(ml_dtypes.bfloat16), e16.astype(ml_dtypes.bfloat16),
            e16t.astype(ml_dtypes.bfloat16))


def kernel(**inputs):
    import ml_dtypes
    x = np.asarray(inputs["L0_attn_x"], np.float32)
    xflat = np.ascontiguousarray(x.reshape(T, D))
    iden, sup16, e16, e16t = _consts()
    if "nc" not in _NC_CACHE:
        _NC_CACHE["nc"], _NC_CACHE["dbg"] = build()
    nc = _NC_CACHE["nc"]
    in_maps = []
    for c in range(NCORES):
        mask = np.zeros((1, NCORES * E), np.float32)
        mask[0, :c * E] = 1.0
        in_maps.append(dict(
            xb=np.ascontiguousarray(xflat[c * S:(c + 1) * S]),
            xfull=xflat,
            wq=np.asarray(inputs["L0_attn_wq"], np.float32),
            wk=np.asarray(inputs["L0_attn_wk"], np.float32),
            wv=np.asarray(inputs["L0_attn_wv"], np.float32),
            wo=np.asarray(inputs["L0_attn_wo"], np.float32),
            rwin=np.asarray(inputs["L0_ffn_router_w"], np.float32),
            sg=np.asarray(inputs["L0_ffn_sg"], np.float32),
            su=np.asarray(inputs["L0_ffn_su"], np.float32),
            sd=np.asarray(inputs["L0_ffn_sd"], np.float32),
            rg=np.asarray(inputs["L0_ffn_rg"], np.float32),
            ru=np.asarray(inputs["L0_ffn_ru"], np.float32),
            rd=np.asarray(inputs["L0_ffn_rd"], np.float32),
            idenf=iden, sup16=sup16, e16all=e16, e16t=e16t, maskrep=mask,
        ))
    res = run_bass_kernel_spmd(nc, in_maps, core_ids=list(range(NCORES)))
    outs = [res.results[c]["out"] for c in range(NCORES)]
    return np.stack(outs, 0).astype(np.float32)


# version 02
# speedup vs baseline: 1.0710x; 1.0710x over previous
import numpy as np
from contextlib import ExitStack

import concourse.bass as bass
import concourse.mybir as mybir
from concourse.bass import IndirectOffsetOnAxis
from concourse.bass_utils import run_bass_kernel_spmd

F32 = mybir.dt.float32
BF16 = mybir.dt.bfloat16
I32 = mybir.dt.int32
U8 = mybir.dt.uint8
AF = mybir.ActivationFunctionType
ALU = mybir.AluOpType
AX = mybir.AxisListType

NCORES = 8
S = 2048
D = 1024
NH = 16
IS_ = 8192
IR = 4096
E = 8
CAP = 2560
WSL = 768
NSLOT = E * WSL
SI = S // 128
DC = D // 128
T = NCORES * S

EPS_LN = 1e-5
EPS_RMS_ATTN = 1e-6
EPS_RMS_FFN = float(np.finfo(np.float32).eps)
EPS_Q = 1e-5
BIG = 1e30


class P:
    def __init__(self):
        self.nc = bass.Bass("TRN2", target_bir_lowering=False, detect_race_conditions=False)
        self.es = ExitStack()
        self.scopes = {}
        self._sem = {}
        self.cnt = {}
        self.block = None
        self.dbg = {}

    def open_block(self):
        self.block = self.es.enter_context(self.nc.Block())

    def sem(self, name):
        if name not in self._sem:
            self._sem[name] = self.es.enter_context(self.nc.semaphore(name))
            self.cnt[name] = 0
        return self._sem[name]

    def T(self, fn): self.block.tensor(fn)
    def V(self, fn): self.block.vector(fn)
    def A(self, fn): self.block.scalar(fn)
    def G(self, fn): self.block.gpsimd(fn)
    def S(self, fn): self.block.sync(fn)

    def sbuf(self, name, shape, dt, scope=None):
        stack = self.es if scope is None else self.scopes.setdefault(scope, ExitStack())
        return stack.enter_context(self.nc.sbuf_tensor(name, list(shape), dt))[:, :]

    def close_scope(self, scope):
        if scope in self.scopes:
            self.scopes.pop(scope).close()

    def psum(self, name, shape, dt=F32):
        return self.es.enter_context(self.nc.psum_tensor(name, list(shape), dt))[:, :]

    def close(self):
        for s in list(self.scopes):
            self.close_scope(s)
        self.es.close()
        return self.nc


def build(dbg=()):
    dbg = set(dbg)
    p = P()
    nc = p.nc

    xb = nc.dram_tensor("xb", [S, D], F32, kind="ExternalInput").ap()
    xfull = nc.dram_tensor("xfull", [T, D], F32, kind="ExternalInput").ap()
    w_in = {n: nc.dram_tensor(n, [D, D], F32, kind="ExternalInput").ap()
            for n in ("wq", "wk", "wv", "wo")}
    rwin = nc.dram_tensor("rwin", [E, D], F32, kind="ExternalInput").ap()
    sg_d = nc.dram_tensor("sg", [D, IS_], F32, kind="ExternalInput").ap()
    su_d = nc.dram_tensor("su", [D, IS_], F32, kind="ExternalInput").ap()
    sd_d = nc.dram_tensor("sd", [IS_, D], F32, kind="ExternalInput").ap()
    rg_d = nc.dram_tensor("rg", [E, D, IR], F32, kind="ExternalInput").ap()
    ru_d = nc.dram_tensor("ru", [E, D, IR], F32, kind="ExternalInput").ap()
    rd_d = nc.dram_tensor("rd", [E, IR, D], F32, kind="ExternalInput").ap()
    idenf = nc.dram_tensor("idenf", [128, 128], F32, kind="ExternalInput").ap()
    sup16_d = nc.dram_tensor("sup16", [128, 128], BF16, kind="ExternalInput").ap()
    e16all_d = nc.dram_tensor("e16all", [128, 128], BF16, kind="ExternalInput").ap()
    e16t_d = nc.dram_tensor("e16t", [16, 1024], BF16, kind="ExternalInput").ap()
    maskrep_d = nc.dram_tensor("maskrep", [1, NCORES * E], F32, kind="ExternalInput").ap()
    out = nc.dram_tensor("out", [S, D], F32, kind="ExternalOutput").ap()

    xf16 = nc.dram_tensor("xf16", [T, D], BF16).ap()
    xnb_d = nc.dram_tensor("xnb_d", [S, D], BF16).ap()
    xq_d = nc.dram_tensor("xq_d", [S, D], BF16).ap()
    xqo_d = nc.dram_tensor("xqo_d", [S, D], BF16).ap()
    w16_d = {n: nc.dram_tensor(n + "16d", [D, D], BF16).ap()
             for n in ("wq", "wk", "wv", "wo")}
    acc_d = nc.dram_tensor("acc_d", [S, D], F32).ap()
    xg16 = nc.dram_tensor("xg16", [NSLOT + 128, D], BF16).ap()
    eout_d = nc.dram_tensor("eout_d", [NSLOT + 1, D], F32).ap()
    sums_d = nc.dram_tensor("sums_d", [1, NH * S], F32).ap()
    sd16_d = nc.dram_tensor("sd16_d", [IS_, D], BF16).ap()
    sgb16_d = nc.dram_tensor("sgb16_d", [16, 128, 8 * 512], BF16).ap()
    sub16_d = nc.dram_tensor("sub16_d", [16, 128, 8 * 512], BF16).ap()
    ctxT_d = nc.dram_tensor("ctxT_d", [D, S], F32).ap()
    mix_d = nc.dram_tensor("mix_d", [S, 64], F32).ap()

    def dbgout(tag, shape, dt=F32):
        if tag in dbg:
            ap = nc.dram_tensor("dbg_" + tag, list(shape), dt, kind="ExternalOutput").ap()
            p.dbg[tag] = ap
            return ap
        return None

    pb = [p.psum(f"pb{i}", [128, 512]) for i in range(8)]

    IDENF = p.sbuf("idenf_s", [128, 128], F32)
    C16 = p.sbuf("c16", [128, 392], BF16)
    SUP = C16[:, 0:128]
    E16A = C16[:, 128:256]
    ONESF16 = C16[:, 256:384]
    ONE16 = C16[:, 384:385]
    ONEROW = p.sbuf("onerow", [1, 128], BF16)
    E16T = p.sbuf("e16t_s", [16, 1024], BF16)
    IDENB = p.sbuf("idenb", [128, 128], BF16)
    ztile = p.sbuf("ztile_p", [128, D], BF16)
    xnbt = p.sbuf("xnbt_p", [128, D], BF16)

    rowbuf = p.sbuf("rowbuf", [128, 96], F32)
    MEAN = rowbuf[:, 0:16]; RSTD = rowbuf[:, 16:32]; RS2R = rowbuf[:, 32:48]
    UQ = rowbuf[:, 48:64]; UO = rowbuf[:, 64:80]
    RALL = p.sbuf("rall", [128, 256], F32)
    BIASC = p.sbuf("biasc", [128, 16], F32)
    GSC = p.sbuf("gsc", [128, 4], F32)
    SW = p.sbuf("swt", [1, 16], F32)
    CNT = p.sbuf("cntall", [1, NCORES * E + E], F32)
    CB = CNT[:, NCORES * E:NCORES * E + E]
    MSK = p.sbuf("mskrep", [1, NCORES * E], F32)
    CBC = p.sbuf("cbc", [128, E], F32)
    SBC = p.sbuf("sbc", [128, E], F32)
    tv = p.sbuf("tvs", [128, 96], F32)
    EPSC = p.sbuf("epsc", [128, 4], F32)

    LOG = p.sbuf("plLOG", [128, 128], F32)
    F1 = p.sbuf("plF1", [128, 128], F32)
    F2 = p.sbuf("plF2", [128, 128], F32)
    L2 = p.sbuf("plL2", [128, 128], F32)
    LSR = p.sbuf("plLSR", [128, 128], F32)
    VAL = p.sbuf("plVAL", [128, 128], F32)
    CCM = p.sbuf("plCCM", [128, 128], F32)
    PSC = p.sbuf("plPSC", [128, 16 * 14], F32)
    def ps(i): return PSC[:, 16 * i:16 * (i + 1)]
    PU = p.sbuf("planesu", [128, 16 * 3], U8)
    def pu(i): return PU[:, 16 * i:16 * (i + 1)]
    SIGI = p.sbuf("sigi", [128, 16 * 4], I32)
    def sigi(i): return SIGI[:, 16 * i:16 * (i + 1)]
    WV = p.sbuf("plWV", [128, 32], F32)
    rw_s = p.sbuf("rw_s", [E, D], F32)
    rwT = p.sbuf("rwT", [128, DC * E], F32)
    LH = p.sbuf("lh", [128, DC * 18], BF16)
    rwlo = p.sbuf("rwlo", [128, E], F32)
    SEST = p.sbuf("sest", [E, 128], F32)
    SER16 = p.sbuf("ser16", [1, E], BF16)
    SEL16 = p.sbuf("sel16", [128, 16], BF16)
    CB16 = p.sbuf("cb16", [1, E], BF16)
    ONERF = p.sbuf("onerf", [1, 128], F32)
    ONECF = p.sbuf("onecf", [128, 1], F32)

    dS = p.sem("dS"); dG = p.sem("dG"); dA = p.sem("dA")
    cT = p.sem("cT"); cV = p.sem("cV"); cA = p.sem("cA"); cP = p.sem("cP")
    SEMNAMES = ["dS", "dG", "dA", "cT", "cV", "cA", "cP"]

    def bump(name, k=1):
        p.cnt[name] += k
        return p.cnt[name]

    def join_all():
        for nm, emitfn in (("cT", p.T), ("cV", p.V), ("cA", p.A), ("cP", p.G)):
            bump(nm)
            def f(eng, nm=nm):
                eng.drain().then_inc(p.sem(nm), 1)
            emitfn(f)
        snap = {nm: p.cnt[nm] for nm in SEMNAMES}
        for emitfn in (p.T, p.V, p.A, p.G, p.S):
            def f(eng, snap=snap):
                for nm, v in snap.items():
                    if v:
                        eng.wait_ge(p.sem(nm), v)
            emitfn(f)

    def chv(eng, ins):
        ins.then_inc(p.sem("cW"), 1)
        eng.wait_ge(p.sem("cW"), bump("cW"))

    def cha(eng, ins):
        ins.then_inc(p.sem("cX"), 1)
        eng.wait_ge(p.sem("cX"), bump("cX"))

    p.open_block()

    xn = p.sbuf("xn", [128, SI * D], F32, scope="sx")
    scrA = p.sbuf("scrA", [128, D], F32, scope="sx")
    scrB = p.sbuf("scrB", [128, D], F32, scope="sx")
    ZI = p.sbuf("zi", [128, D], I32, scope="sx")
    logTs = p.sbuf("logTs", [E, 512], F32, scope="sx")
    xnT32 = p.sbuf("xnT32", [128, 512], F32, scope="sx")
    xq16 = p.sbuf("xq16", [128, SI * D], BF16, scope="sq")
    xnb = p.sbuf("xnb", [128, SI * D], BF16, scope="sq")
    def xn_t(ci): return xn[:, ci * D:(ci + 1) * D]
    def xq_t(ci): return xq16[:, ci * D:(ci + 1) * D]
    def xnb_t(ci): return xnb[:, ci * D:(ci + 1) * D]

    def s0_s(eng):
        eng.dma_start(out=IDENF, in_=idenf).then_inc(dS, 16)
        eng.dma_start(out=SUP, in_=sup16_d).then_inc(dS, 16)
        eng.dma_start(out=E16A, in_=e16all_d).then_inc(dS, 16)
        eng.dma_start(out=E16T, in_=e16t_d).then_inc(dS, 16)
        eng.dma_start(out=MSK, in_=maskrep_d).then_inc(dS, 16)
        eng.dma_start(out=rw_s, in_=rwin).then_inc(dS, 16)
        for ci in range(SI):
            eng.dma_start(out=xn_t(ci), in_=xb[ci * 128:(ci + 1) * 128, :]).then_inc(dS, 16)
    bump("dS", 16 * (6 + SI)); p.S(s0_s)
    x_dS = p.cnt["dS"]

    def s0_v(eng):
        eng.memset(ONE16, 1.0)
        eng.memset(ONEROW, 1.0)
        eng.memset(ONESF16, 1.0)
        eng.memset(ps(13), 0.0)
        eng.memset(ps(12), float(NSLOT))
        eng.memset(EPSC[:, 0:1], EPS_LN)
        eng.memset(EPSC[:, 1:2], EPS_RMS_ATTN)
        eng.memset(EPSC[:, 2:3], EPS_RMS_FFN)
        eng.memset(EPSC[:, 3:4], 0.0)
        chv(eng, eng.engine_nop())
        eng.drain().then_inc(cV, 1)
    p.V(s0_v); bump("cV")

    NFC = 32
    rows_per = T // NFC
    def s0_g(eng):
        for k in range(NFC):
            eng.dma_start(out=xf16[k * rows_per:(k + 1) * rows_per, :],
                          in_=xfull[k * rows_per:(k + 1) * rows_per, :]).then_inc(dG, 16)
    bump("dG", 16 * NFC); p.G(s0_g)
    xf16_dG = p.cnt["dG"]

    for ci in range(SI):
        m = MEAN[:, ci:ci + 1]; r = RSTD[:, ci:ci + 1]; uq = UQ[:, ci:ci + 1]
        c0 = tv[:, 0:1]; c1 = tv[:, 1:2]; c2 = tv[:, 2:3]; c3 = tv[:, 3:4]
        c4 = tv[:, 4:5]; c5 = tv[:, 5:6]; c6 = tv[:, 6:7]; c7 = tv[:, 7:8]
        c8 = tv[:, 8:9]; c9 = tv[:, 9:10]; c10 = tv[:, 10:11]; c11 = tv[:, 11:12]

        def v1(eng, ci=ci, m=m, c0=c0, c1=c1, c2=c2):
            if ci == 0:
                eng.wait_ge(dS, x_dS)
            x = xn_t(ci)
            chv(eng, eng.scalar_tensor_tensor(scrA, x, 1.0, x, ALU.mult, ALU.mult))
            chv(eng, eng.reduce_sum(c0, scrA, axis=AX.X))
            chv(eng, eng.reduce_sum(m, x, axis=AX.X))
            chv(eng, eng.tensor_scalar_mul(m, m, 1.0 / D))
            chv(eng, eng.tensor_scalar_mul(c0, c0, 1.0 / D))
            chv(eng, eng.tensor_mul(c1, m, m))
            chv(eng, eng.tensor_sub(c2, c0, c1))
            chv(eng, eng.engine_nop())
            eng.drain().then_inc(cV, 1)
        p.V(v1); bump("cV")

        def a1(eng, c2=c2, c3=c3):
            eng.wait_ge(cV, p.cnt["cV"])
            cha(eng, eng.activation(c3, c2, AF.Sqrt, bias=EPSC[:, 0:1], scale=1.0))
            eng.drain().then_inc(cA, 1)
        p.A(a1); bump("cA")

        def v2(eng, r=r, c3=c3, c4=c4, m=m):
            eng.wait_ge(cA, p.cnt["cA"])
            chv(eng, eng.reciprocal(r, c3))
            chv(eng, eng.scalar_tensor_tensor(c4, m, -1.0, r, ALU.mult, ALU.mult))
            chv(eng, eng.engine_nop())
            eng.drain().then_inc(cV, 1)
        p.V(v2); bump("cV")

        def v3(eng, ci=ci, c4=c4, c5=c5, c6=c6, r=r, c2=c2):
            x = xn_t(ci)
            chv(eng, eng.tensor_scalar(x, x, r, c4, ALU.mult, ALU.add))
            eng.tensor_copy(xnb_t(ci), x)
            chv(eng, eng.reduce_max(c5, x, axis=AX.X, apply_absolute_value=True))
            chv(eng, eng.tensor_mul(c6, c2, r))
            chv(eng, eng.tensor_mul(c6, c6, r))
            eng.drain().then_inc(cV, 1)
        p.V(v3); bump("cV")

        def a3(eng, c6=c6, c7=c7, c8=c8):
            eng.wait_ge(cV, p.cnt["cV"])
            cha(eng, eng.activation(c7, c6, AF.Sqrt, bias=EPSC[:, 1:2], scale=1.0))
            cha(eng, eng.activation(c8, c6, AF.Sqrt, bias=EPSC[:, 2:3], scale=1.0))
            eng.drain().then_inc(cA, 1)
        p.A(a3); bump("cA")

        def v4(eng, ci=ci, c5=c5, c7=c7, c8=c8, c9=c9, c10=c10, c11=c11, uq=uq, r=r):
            eng.wait_ge(cA, p.cnt["cA"])
            x = xn_t(ci)
            chv(eng, eng.reciprocal(c9, c7))
            chv(eng, eng.reciprocal(c10, c8))
            chv(eng, eng.tensor_copy(RS2R[:, ci:ci + 1], c10))
            chv(eng, eng.tensor_mul(c10, c9, c5))
            chv(eng, eng.tensor_scalar_max(c10, c10, EPS_Q))
            chv(eng, eng.tensor_scalar_mul(uq, c10, 1.0 / 127.0))
            chv(eng, eng.reciprocal(c11, c10))
            chv(eng, eng.scalar_tensor_tensor(c9, c9, 127.0, c11, ALU.mult, ALU.mult))
            chv(eng, eng.tensor_scalar_mul(scrB, x, c9))
            chv(eng, eng.tensor_copy(ZI, scrB))
            chv(eng, eng.tensor_scalar_mul(xq_t(ci), ZI, uq))
            chv(eng, eng.engine_nop())
            eng.drain().then_inc(cV, 1)
        p.V(v4); bump("cV")
    ln_cV = p.cnt["cV"]

    def s1_wb(eng):
        eng.wait_ge(cV, ln_cV)
        for ci in range(SI):
            eng.dma_start(out=xnb_d[ci * 128:(ci + 1) * 128, :], in_=xnb_t(ci)).then_inc(dA, 16)
            eng.dma_start(out=xq_d[ci * 128:(ci + 1) * 128, :], in_=xq_t(ci)).then_inc(dA, 16)
    p.A(s1_wb); bump("dA", 16 * 2 * SI)

    d = dbgout("xn", [S, D])
    if d is not None:
        def f(eng, d=d):
            eng.wait_ge(cV, ln_cV)
            for ci in range(SI):
                eng.dma_start(out=d[ci * 128:(ci + 1) * 128, :], in_=xn_t(ci)).then_inc(dA, 16)
        p.A(f); bump("dA", 16 * SI)
    d = dbgout("rowb", [128, 96])
    if d is not None:
        def f_rb(eng, d=d):
            eng.wait_ge(cV, ln_cV)
            eng.dma_start(out=d, in_=rowbuf).then_inc(dA, 16)
        p.A(f_rb); bump("dA", 16)
    d = dbgout("xq", [S, D], BF16)
    if d is not None:
        def f_xq(eng, d=d):
            eng.wait_ge(cV, ln_cV)
            for ci in range(SI):
                eng.dma_start(out=d[ci * 128:(ci + 1) * 128, :], in_=xq_t(ci)).then_inc(dA, 16)
        p.A(f_xq); bump("dA", 16 * SI)

    for dc in range(DC):
        def t_rw(eng, dc=dc):
            if dc == 0:
                eng.wait_ge(dS, x_dS)
            eng.wait_ge(cV, p.cnt["cV"])
            eng.transpose(pb[6][:, 0:E], rw_s[:, dc * 128:(dc + 1) * 128], IDENF[0:E, 0:E]).then_inc(cT, 1)
        p.T(t_rw); bump("cT")
        def v_rw(eng, dc=dc):
            eng.wait_ge(cT, p.cnt["cT"])
            chv(eng, eng.tensor_copy(rwT[:, dc * E:(dc + 1) * E], pb[6][:, 0:E]))
            eng.drain().then_inc(cV, 1)
        p.V(v_rw); bump("cV")

    for nb in range(S // 512):
        for dc in range(DC):
            for q in range(4):
                ci = nb * 4 + q
                bank = 6 + (q % 2)
                def t_tr(eng, ci=ci, dc=dc, bank=bank):
                    eng.wait_ge(cV, p.cnt["cV"])
                    eng.transpose(pb[bank][:, 0:128], xn_t(ci)[:, dc * 128:(dc + 1) * 128],
                                  IDENF).then_inc(cT, 1)
                p.T(t_tr); bump("cT")
                def v_tr(eng, q=q, bank=bank):
                    eng.wait_ge(cT, p.cnt["cT"])
                    chv(eng, eng.tensor_copy(xnT32[:, q * 128:(q + 1) * 128], pb[bank][:, 0:128]))
                    eng.drain().then_inc(cV, 1)
                p.V(v_tr); bump("cV")
            def t_mm(eng, dc=dc):
                eng.wait_ge(cV, p.cnt["cV"])
                eng.matmul(pb[5][0:E, :], rwT[:, dc * E:(dc + 1) * E], xnT32,
                           start=(dc == 0), stop=(dc == DC - 1)).then_inc(cT, 1)
            p.T(t_mm); bump("cT")
        def v_ev(eng, nb=nb):
            eng.wait_ge(cT, p.cnt["cT"])
            chv(eng, eng.tensor_copy(logTs, pb[5][0:E, :]))
            eng.drain().then_inc(cV, 1)
        p.V(v_ev); bump("cV")
        for q in range(4):
            ci = nb * 4 + q
            def t_pl(eng, q=q):
                eng.wait_ge(cV, p.cnt["cV"])
                eng.transpose(pb[6][:, 0:E], logTs[:, q * 128:(q + 1) * 128],
                              IDENF[0:E, 0:E]).then_inc(cT, 1)
            p.T(t_pl); bump("cT")
            def v_pl(eng, ci=ci):
                eng.wait_ge(cT, p.cnt["cT"])
                chv(eng, eng.tensor_scalar_mul(LOG[:, ci::16], pb[6][0:128, 0:E],
                                               RS2R[:, ci:ci + 1]))
                chv(eng, eng.engine_nop())
                eng.drain().then_inc(cV, 1)
            p.V(v_pl); bump("cV")

    d = dbgout("logits", [128, 128])
    if d is not None:
        def f(eng, d=d):
            eng.wait_ge(cV, p.cnt["cV"])
            eng.dma_start(out=d, in_=LOG).then_inc(dA, 16)
        p.A(f); bump("dA", 16)

    def v_lh(eng):
        eng.wait_ge(cV, p.cnt["cV"])
        for dc in range(DC):
            h = LH[:, dc * 18:dc * 18 + 9]
            lo = LH[:, dc * 18 + 9:dc * 18 + 18]
            eng.tensor_copy(h[:, 0:1], ONE16)
            eng.memset(lo[:, 0:1], 0.0)
            chv(eng, eng.tensor_copy(h[:, 1:9], rwT[:, dc * E:(dc + 1) * E]))
            chv(eng, eng.tensor_copy(rwlo, h[:, 1:9]))
            chv(eng, eng.tensor_sub(rwlo, rwT[:, dc * E:(dc + 1) * E], rwlo))
            chv(eng, eng.tensor_copy(lo[:, 1:9], rwlo))
        chv(eng, eng.engine_nop())
        eng.drain().then_inc(cV, 1)
    p.V(v_lh); bump("cV")

    def v_se(eng):
        chv(eng, eng.memset(SEST, 0.0))
        chv(eng, eng.reduce_sum(SEST[:, 0:1], rw_s, axis=AX.X))
        chv(eng, eng.engine_nop())
        eng.drain().then_inc(cV, 1)
    p.V(v_se); bump("cV")
    def t_se(eng):
        eng.wait_ge(cV, p.cnt["cV"])
        eng.transpose(pb[6][:, 0:E], SEST, IDENF[0:E, 0:E]).then_inc(cT, 1)
    p.T(t_se); bump("cT")
    def v_se2(eng):
        eng.wait_ge(cT, p.cnt["cT"])
        chv(eng, eng.tensor_copy(SER16, pb[6][0:1, 0:E]))
        eng.drain().then_inc(cV, 1)
    p.V(v_se2); bump("cV")
    def t_se2(eng):
        eng.wait_ge(cV, p.cnt["cV"])
        eng.matmul(pb[6][:, 0:E], ONEROW, SER16, start=True, stop=True).then_inc(cT, 1)
    p.T(t_se2); bump("cT")
    def v_se3(eng):
        eng.wait_ge(cT, p.cnt["cT"])
        chv(eng, eng.tensor_copy(SBC, pb[6][:, 0:E]))
        eng.drain().then_inc(cV, 1)
    p.V(v_se3); bump("cV")

    xhiT = p.sbuf("xhiT", [128, DC * S], BF16, scope="sF")
    fLb = p.sbuf("flb", [9, S], F32, scope="sF")
    FLO = p.sbuf("flo", [128, SI * 9], F32, scope="sF")
    fsel = p.sbuf("fsel", [128, 16 * 9], F32, scope="sF")
    fsel16 = p.sbuf("fsel16", [128, E], BF16, scope="sF")

    for b in range(NCORES):
        for dc in range(DC):
            def s_tr(eng, b=b, dc=dc):
                if b == 0 and dc == 0:
                    eng.wait_ge(dG, xf16_dG)
                if dc == 0:
                    eng.wait_ge(cV, p.cnt["cV"])
                eng.dma_start(out=xhiT[:, dc * S:(dc + 1) * S],
                              in_=xf16[b * S:(b + 1) * S, dc * 128:(dc + 1) * 128],
                              transpose=True).then_inc(dS, 16)
                eng.wait_ge(dS, bump("dS", 16))
            p.S(s_tr)
        for nb in range(4):
            for dc in range(DC):
                def t_mm(eng, nb=nb, dc=dc):
                    if dc == 0:
                        eng.wait_ge(dS, p.cnt["dS"])
                        eng.wait_ge(cV, p.cnt["cV"])
                    rhs = xhiT[:, dc * S + nb * 512: dc * S + (nb + 1) * 512]
                    eng.matmul(pb[4][0:9, :], LH[:, dc * 18:dc * 18 + 9], rhs,
                               start=(dc == 0), stop=False)
                    eng.matmul(pb[4][0:9, :], LH[:, dc * 18 + 9:dc * 18 + 18], rhs,
                               start=False, stop=(dc == DC - 1)).then_inc(cT, 1)
                p.T(t_mm); bump("cT")
            def v_ev(eng, nb=nb):
                eng.wait_ge(cT, p.cnt["cT"])
                chv(eng, eng.tensor_copy(fLb[:, nb * 512:(nb + 1) * 512], pb[4][0:9, :]))
                eng.drain().then_inc(cV, 1)
            p.V(v_ev); bump("cV")
        for ci in range(SI):
            def t_pl(eng, ci=ci):
                eng.wait_ge(cV, p.cnt["cV"])
                eng.transpose(pb[6][:, 0:9], fLb[:, ci * 128:(ci + 1) * 128], IDENF[0:9, 0:9]).then_inc(cT, 1)
            p.T(t_pl); bump("cT")
            def v_pl(eng, ci=ci):
                eng.wait_ge(cT, p.cnt["cT"])
                chv(eng, eng.tensor_copy(FLO[:, ci * 9:(ci + 1) * 9], pb[6][0:128, 0:9]))
                chv(eng, eng.engine_nop())
                eng.drain().then_inc(cV, 1)
            p.V(v_pl); bump("cV")
        def v_f(eng, b=b):
            eng.wait_ge(cV, p.cnt["cV"])
            eng.wait_ge(cT, p.cnt["cT"])
            mpl = ps(9); m1 = ps(10); m2 = ps(11)
            chv(eng, eng.tensor_scalar_mul(mpl, FLO[:, 0::9], 1.0 / D))
            for e in range(E):
                lf = FLO[:, 1 + e::9]
                chv(eng, eng.tensor_scalar_mul(ps(0), mpl, SBC[:, e:e + 1]))
                chv(eng, eng.tensor_sub(lf, lf, ps(0)))
            chv(eng, eng.tensor_max(m1, FLO[:, 1::9], FLO[:, 2::9]))
            for e in range(2, E):
                chv(eng, eng.tensor_max(m1, m1, FLO[:, 1 + e::9]))
            for e in range(E):
                lf = FLO[:, 1 + e::9]
                sc = fsel[:, e * 16:(e + 1) * 16]
                chv(eng, eng.tensor_tensor(sc, lf, m1, ALU.is_ge))
                chv(eng, eng.scalar_tensor_tensor(sc, sc, -BIG, lf, ALU.mult, ALU.add))
            chv(eng, eng.tensor_max(m2, fsel[:, 0:16], fsel[:, 16:32]))
            for e in range(2, E):
                chv(eng, eng.tensor_max(m2, m2, fsel[:, e * 16:(e + 1) * 16]))
            for e in range(E):
                lf = FLO[:, 1 + e::9]
                sc = fsel[:, e * 16:(e + 1) * 16]
                chv(eng, eng.tensor_tensor(sc, lf, m2, ALU.is_ge))
                chv(eng, eng.reduce_sum(fsel[:, 128 + e:129 + e], sc, axis=AX.X))
            chv(eng, eng.tensor_copy(fsel16, fsel[:, 128:128 + E]))
            chv(eng, eng.engine_nop())
            eng.drain().then_inc(cV, 1)
        p.V(v_f); bump("cV")
        def t_cnt(eng):
            eng.wait_ge(cV, p.cnt["cV"])
            eng.matmul(pb[6][0:1, 0:E], ONE16, fsel16, start=True, stop=True).then_inc(cT, 1)
        p.T(t_cnt); bump("cT")
        def v_cnt(eng, b=b):
            eng.wait_ge(cT, p.cnt["cT"])
            chv(eng, eng.tensor_copy(CNT[:, b * E:(b + 1) * E], pb[6][0:1, 0:E]))
            chv(eng, eng.engine_nop())
            eng.drain().then_inc(cV, 1)
        p.V(v_cnt); bump("cV")

    def v_cb(eng):
        eng.wait_ge(cV, p.cnt["cV"])
        tr = tv[0:1, 32:32 + NCORES * E]
        chv(eng, eng.tensor_mul(tr, CNT[:, 0:NCORES * E], MSK))
        for e in range(E):
            chv(eng, eng.reduce_sum(CB[:, e:e + 1], tr[:, e::E], axis=AX.X))
        chv(eng, eng.tensor_copy(CB16, CB))
        chv(eng, eng.engine_nop())
        eng.drain().then_inc(cV, 1)
    p.V(v_cb); bump("cV")
    def t_cb(eng):
        eng.wait_ge(cV, p.cnt["cV"])
        eng.matmul(pb[6][:, 0:E], ONEROW, CB16, start=True, stop=True).then_inc(cT, 1)
    p.T(t_cb); bump("cT")
    def v_cb2(eng):
        eng.wait_ge(cT, p.cnt["cT"])
        chv(eng, eng.tensor_copy(CBC, pb[6][:, 0:E]))
        eng.drain().then_inc(cV, 1)
    p.V(v_cb2); bump("cV")

    d = dbgout("cnt", [1, NCORES * E + E])
    if d is not None:
        def f(eng, d=d):
            eng.wait_ge(cV, p.cnt["cV"])
            eng.dma_start(out=d, in_=CNT).then_inc(dA, 16)
        p.A(f); bump("dA", 16)

    def Le(e): return LOG[:, e * 16:(e + 1) * 16]

    def v_top2(eng):
        eng.wait_ge(cV, p.cnt["cV"])
        m1 = ps(0); run = ps(11); t = ps(9); t2 = ps(10)
        chv(eng, eng.tensor_max(m1, Le(0), Le(1)))
        for e in range(2, E):
            chv(eng, eng.tensor_max(m1, m1, Le(e)))
        chv(eng, eng.memset(run, 0.0))
        for e in range(E):
            f1 = F1[:, e * 16:(e + 1) * 16]
            chv(eng, eng.tensor_tensor(t, Le(e), m1, ALU.is_ge))
            chv(eng, eng.tensor_mul(t2, t, run))
            chv(eng, eng.tensor_sub(f1, t, t2))
            chv(eng, eng.tensor_max(run, run, t))
        for e in range(E):
            f1 = F1[:, e * 16:(e + 1) * 16]
            l2 = L2[:, e * 16:(e + 1) * 16]
            chv(eng, eng.scalar_tensor_tensor(l2, f1, -BIG, Le(e), ALU.mult, ALU.add))
        m2 = ps(1)
        chv(eng, eng.tensor_max(m2, L2[:, 0:16], L2[:, 16:32]))
        for e in range(2, E):
            chv(eng, eng.tensor_max(m2, m2, L2[:, e * 16:(e + 1) * 16]))
        chv(eng, eng.memset(run, 0.0))
        for e in range(E):
            f2 = F2[:, e * 16:(e + 1) * 16]
            l2 = L2[:, e * 16:(e + 1) * 16]
            chv(eng, eng.tensor_tensor(t, l2, m2, ALU.is_ge))
            chv(eng, eng.tensor_mul(t2, t, run))
            chv(eng, eng.tensor_sub(f2, t, t2))
            chv(eng, eng.tensor_max(run, run, t))
        e1 = ps(2); e2 = ps(3)
        chv(eng, eng.memset(e1, 0.0))
        chv(eng, eng.memset(e2, 0.0))
        for e in range(1, E):
            chv(eng, eng.scalar_tensor_tensor(e1, F1[:, e * 16:(e + 1) * 16], float(e), e1,
                                              ALU.mult, ALU.add))
            chv(eng, eng.scalar_tensor_tensor(e2, F2[:, e * 16:(e + 1) * 16], float(e), e2,
                                              ALU.mult, ALU.add))
        chv(eng, eng.tensor_sub(ps(9), ps(0), ps(1)))
        chv(eng, eng.engine_nop())
        eng.drain().then_inc(cV, 1)
    p.V(v_top2); bump("cV")

    def a_sig(eng):
        eng.wait_ge(cV, p.cnt["cV"])
        cha(eng, eng.activation(ps(4), ps(9), AF.Sigmoid, bias=0.0, scale=1.0))
        eng.drain().then_inc(cA, 1)
    p.A(a_sig); bump("cA")

    for e in range(E):
        def v_rk(eng, e=e):
            if e == 0:
                eng.wait_ge(cV, p.cnt["cV"])
            else:
                eng.wait_ge(cT, p.cnt["cT"])
            sel = VAL[:, e * 16:(e + 1) * 16]
            chv(eng, eng.tensor_add(sel, F1[:, e * 16:(e + 1) * 16], F2[:, e * 16:(e + 1) * 16]))
            chv(eng, eng.tensor_copy(SEL16, sel))
            chv(eng, eng.engine_nop())
            eng.drain().then_inc(cV, 1)
        p.V(v_rk); bump("cV")
        def t_rk(eng, e=e):
            eng.wait_ge(cV, p.cnt["cV"])
            eng.matmul(pb[6][:, 0:16], SUP, SEL16, start=True, stop=True)
            eng.matmul(pb[7][:, 0:16], ONESF16, SEL16, start=True, stop=True).then_inc(cT, 1)
        p.T(t_rk); bump("cT")
        def v_rk2(eng, e=e):
            eng.wait_ge(cT, p.cnt["cT"])
            lsr = LSR[:, e * 16:(e + 1) * 16]
            ccm = CCM[:, e * 16:(e + 1) * 16]
            eng.tensor_copy(lsr, pb[6][:, 0:16])
            eng.tensor_copy(ccm, pb[7][:, 0:16])
            chv(eng, eng.engine_nop())
            eng.drain().then_inc(cV, 1)
        p.V(v_rk2); bump("cV")
    def v_rk3(eng):
        eng.wait_ge(cV, p.cnt["cV"])
        for e in range(E):
            lsr = LSR[:, e * 16:(e + 1) * 16]
            ccm = CCM[:, e * 16:(e + 1) * 16]
            for ci in range(1, SI):
                chv(eng, eng.tensor_add(ccm[:, ci:ci + 1], ccm[:, ci:ci + 1], ccm[:, ci - 1:ci]))
            for ci in range(1, SI):
                chv(eng, eng.tensor_add(lsr[:, ci:ci + 1], lsr[:, ci:ci + 1], ccm[:, ci - 1:ci]))
        chv(eng, eng.engine_nop())
        eng.drain().then_inc(cV, 1)
    p.V(v_rk3); bump("cV")

    def v_sigma(eng):
        eng.wait_ge(cV, p.cnt["cV"])
        eng.wait_ge(cA, p.cnt["cA"])
        t = ps(9); t2 = ps(10)
        for e in range(E):
            sel = VAL[:, e * 16:(e + 1) * 16]
            lsr = LSR[:, e * 16:(e + 1) * 16]
            chv(eng, eng.tensor_scalar_add(t, lsr, CBC[:, e:e + 1]))
            chv(eng, eng.tensor_scalar(t, t, float(CAP), None, ALU.is_lt))
            chv(eng, eng.tensor_mul(sel, sel, t))
        lsr1 = ps(5); lsr2 = ps(6); v1 = ps(7); v2 = ps(8)
        chv(eng, eng.memset(lsr1, 0.0)); chv(eng, eng.memset(lsr2, 0.0))
        chv(eng, eng.memset(v1, 0.0)); chv(eng, eng.memset(v2, 0.0))
        for e in range(E):
            f1 = F1[:, e * 16:(e + 1) * 16]; f2 = F2[:, e * 16:(e + 1) * 16]
            lsr = LSR[:, e * 16:(e + 1) * 16]; val = VAL[:, e * 16:(e + 1) * 16]
            chv(eng, eng.tensor_mul(t, f1, lsr)); chv(eng, eng.tensor_add(lsr1, lsr1, t))
            chv(eng, eng.tensor_mul(t, f2, lsr)); chv(eng, eng.tensor_add(lsr2, lsr2, t))
            chv(eng, eng.tensor_mul(t, f1, val)); chv(eng, eng.tensor_add(v1, v1, t))
            chv(eng, eng.tensor_mul(t, f2, val)); chv(eng, eng.tensor_add(v2, v2, t))
        chv(eng, eng.tensor_mul(WV[:, 0:16], ps(4), v1))
        chv(eng, eng.engine_nop())
        eng.drain().then_inc(cV, 1)
    p.V(v_sigma); bump("cV")

    def v_w2(eng):
        t = ps(9)
        chv(eng, eng.tensor_scalar(t, ps(4), -1.0, 1.0, ALU.mult, ALU.add))
        chv(eng, eng.tensor_mul(WV[:, 16:32], t, ps(8)))
        chv(eng, eng.engine_nop())
        eng.drain().then_inc(cV, 1)
    p.V(v_w2); bump("cV")

    def v_sig2(eng):
        t = ps(9); t2 = ps(10)
        for j, (ee, ll, vv, so, go) in enumerate(
                ((ps(2), ps(5), ps(7), 0, 2), (ps(3), ps(6), ps(8), 1, 3))):
            chv(eng, eng.tensor_scalar_min(t, ll, float(WSL - 1)))
            chv(eng, eng.scalar_tensor_tensor(t, ee, float(WSL), t, ALU.mult, ALU.add))
            chv(eng, eng.tensor_copy(pu(0), vv))
            chv(eng, eng.select(t2, pu(0), t, ps(12)))
            chv(eng, eng.tensor_copy(sigi(so), t2))
            chv(eng, eng.select(t2, pu(0), t, ps(12)))
            chv(eng, eng.tensor_copy(sigi(go), t2))
        chv(eng, eng.engine_nop())
        eng.drain().then_inc(cV, 1)
    p.V(v_sig2); bump("cV")

    d = dbgout("tables", [128, 16 * 8])
    if d is not None:
        def f(eng, d=d):
            eng.wait_ge(cV, p.cnt["cV"])
            eng.dma_start(out=d[:, 0:16], in_=ps(2)).then_inc(dA, 16)
            eng.dma_start(out=d[:, 16:32], in_=ps(3)).then_inc(dA, 16)
            eng.dma_start(out=d[:, 32:48], in_=ps(5)).then_inc(dA, 16)
            eng.dma_start(out=d[:, 48:64], in_=ps(6)).then_inc(dA, 16)
            eng.dma_start(out=d[:, 64:80], in_=ps(7)).then_inc(dA, 16)
            eng.dma_start(out=d[:, 80:96], in_=ps(8)).then_inc(dA, 16)
            eng.dma_start(out=d[:, 96:112], in_=WV[:, 0:16]).then_inc(dA, 16)
            eng.dma_start(out=d[:, 112:128], in_=WV[:, 16:32]).then_inc(dA, 16)
        p.A(f); bump("dA", 16 * 8)

    join_all()
    p.close_scope("sF")
    p.close_scope("sq")
    p.close_scope("sx")

    _emit_rest(p, locals())

    nc_final = p.close()
    return nc_final, p.dbg


def _emit_rest(p, g):
    nc = p.nc
    pb = g["pb"]
    IDENF = g["IDENF"]; ONEROW = g["ONEROW"]; E16A = g["E16A"]; ONE16 = g["ONE16"]
    RALL = g["RALL"]; BIASC = g["BIASC"]; GSC = g["GSC"]; SW = g["SW"]
    UO = g["UO"]; tv = g["tv"]; WV = g["WV"]; EPSC2 = g["EPSC"]
    sigi = g["sigi"]
    w_in = g["w_in"]; w16_d = g["w16_d"]
    xq_d = g["xq_d"]; xqo_d = g["xqo_d"]; xnb_d = g["xnb_d"]; xb = g["xb"]
    acc_d = g["acc_d"]; sums_d = g["sums_d"]; ctxT_d = g["ctxT_d"]
    mix_d = g["mix_d"]; E16T = g["E16T"]; IDENB = g["IDENB"]
    ztile = g["ztile"]; xnbt = g["xnbt"]
    sgb16_d = g["sgb16_d"]; sub16_d = g["sub16_d"]; sd_d = g["sd_d"]
    xg16 = g["xg16"]; eout_d = g["eout_d"]; out = g["out"]
    rg_d = g["rg_d"]; ru_d = g["ru_d"]; rd_d = g["rd_d"]
    sg_d = g["sg_d"]; su_d = g["su_d"]; sd16_d = g["sd16_d"]
    dbgout = g["dbgout"]; bump = g["bump"]; join_all = g["join_all"]; chv = g["chv"]; cha = g["cha"]
    dS, dG, dA, cT, cV, cA, cP = (p.sem(n) for n in
                                  ("dS", "dG", "dA", "cT", "cV", "cA", "cP"))

    def g_z(eng):
        eng.memset(ztile, 0.0).then_inc(p.sem("cP"), 1)
        eng.wait_ge(p.sem("cP"), bump("cP"))
        for r in range(NSLOT // 128 + 1):
            eng.dma_start(out=xg16[r * 128:(r + 1) * 128, :], in_=ztile).then_inc(dG, 16)
    bump("dG", 16 * (NSLOT // 128 + 1)); p.G(g_z)

    def g_cast(eng):
        for k in range(16):
            eng.dma_start(out=sd16_d[k * 512:(k + 1) * 512, :],
                          in_=sd_d[k * 512:(k + 1) * 512, :]).then_inc(dG, 16)
        for icc in range(16):
            eng.dma_start(out=sgb16_d[icc].rearrange("p (c i) -> p c i", i=512),
                          in_=sg_d[:, icc * 512:(icc + 1) * 512].rearrange(
                              "(c p) i -> p c i", p=128)).then_inc(dG, 16)
            eng.dma_start(out=sub16_d[icc].rearrange("p (c i) -> p c i", i=512),
                          in_=su_d[:, icc * 512:(icc + 1) * 512].rearrange(
                              "(c p) i -> p c i", p=128)).then_inc(dG, 16)
    bump("dG", 16 * 48); p.G(g_cast)

    def g_sc(eng):
        eng.wait_ge(dG, p.cnt["dG"])
        eng.wait_ge(dA, p.cnt["dA"])
        eng.wait_ge(cV, p.cnt["cV"])
        for ci in range(SI):
            eng.dma_start(out=xnbt, in_=xnb_d[ci * 128:(ci + 1) * 128, :]).then_inc(dG, 16)
            eng.wait_ge(dG, bump("dG", 16))
            for jj in range(2):
                eng.indirect_dma_start(
                    out=xg16, out_offset=IndirectOffsetOnAxis(sigi(jj)[:, ci:ci + 1], 0),
                    in_=xnbt, in_offset=None).then_inc(dG, 16)
                bump("dG", 16)
            eng.wait_ge(dG, p.cnt["dG"])
    p.G(g_sc)
    scat_dG = p.cnt["dG"]

    w32 = p.sbuf("w32", [128, DC * D], F32, scope="wq")
    qscr = p.sbuf("qscr", [128, D], F32, scope="wq")
    ZI2 = p.sbuf("zi2", [128, D], I32, scope="wq")
    w16b = p.sbuf("w16b", [128, D], BF16, scope="wq")
    ONERF = g["ONERF"]; ONECF = g["ONECF"]
    asum = tv[:, 64:65]
    ared = tv[:, 65:66]
    swcol = tv[:, 66:67]

    def v_onerf(eng):
        eng.memset(ONERF, 1.0)
        eng.memset(ONECF, 1.0)
        eng.tensor_copy(IDENB, IDENF)
        chv(eng, eng.engine_nop())
        eng.drain().then_inc(cV, 1)
    p.V(v_onerf); bump("cV")

    for wi, wn in enumerate(("wq", "wk", "wv", "wo")):
        def s_w(eng, wn=wn):
            eng.wait_ge(cV, p.cnt["cV"])
            for r in range(DC):
                eng.dma_start(out=w32[:, r * D:(r + 1) * D],
                              in_=w_in[wn][r * 128:(r + 1) * 128, :]).then_inc(dS, 16)
        p.S(s_w); bump("dS", 16 * DC)

        def v_as(eng):
            eng.wait_ge(dS, p.cnt["dS"])
            chv(eng, eng.memset(asum, 0.0))
            for r in range(DC):
                chv(eng, eng.reduce_sum(ared, w32[:, r * D:(r + 1) * D], axis=AX.X,
                                        apply_absolute_value=True))
                chv(eng, eng.tensor_add(asum, asum, ared))
            chv(eng, eng.engine_nop())
            eng.drain().then_inc(cV, 1)
        p.V(v_as); bump("cV")

        def t_tot(eng):
            eng.wait_ge(cV, p.cnt["cV"])
            eng.matmul(pb[6][0:1, 0:1], asum, ONECF, start=True, stop=True).then_inc(cT, 1)
        p.T(t_tot); bump("cT")

        def v_sw(eng, wi=wi):
            eng.wait_ge(cT, p.cnt["cT"])
            sws = SW[:, wi:wi + 1]
            chv(eng, eng.tensor_scalar_mul(sws, pb[6][0:1, 0:1], 1.0 / (D * D)))
            chv(eng, eng.tensor_scalar_max(sws, sws, 1e-5))
            chv(eng, eng.reciprocal(sws, sws))
            chv(eng, eng.engine_nop())
            eng.drain().then_inc(cV, 1)
        p.V(v_sw); bump("cV")

        def t_bc(eng, wi=wi):
            eng.wait_ge(cV, p.cnt["cV"])
            eng.matmul(pb[6][:, 0:1], ONERF, SW[:, wi:wi + 1], start=True, stop=True).then_inc(cT, 1)
        p.T(t_bc); bump("cT")

        def v_bc(eng):
            eng.wait_ge(cT, p.cnt["cT"])
            chv(eng, eng.tensor_copy(swcol, pb[6][:, 0:1]))
            eng.drain().then_inc(cV, 1)
        p.V(v_bc); bump("cV")

        for r in range(DC):
            def v_q(eng, r=r):
                if r == 0:
                    eng.wait_ge(cV, p.cnt["cV"])
                if r > 0:
                    eng.wait_ge(dA, p.cnt["dA"])
                chv(eng, eng.tensor_scalar_mul(qscr, w32[:, r * D:(r + 1) * D], swcol))
                chv(eng, eng.tensor_copy(ZI2, qscr))
                chv(eng, eng.tensor_scalar(w16b, ZI2, -1.0, 1.0, ALU.max, ALU.min))
                chv(eng, eng.engine_nop())
                eng.drain().then_inc(cV, 1)
            p.V(v_q); bump("cV")
            def a_wb(eng, wn=wn, r=r):
                eng.wait_ge(cV, p.cnt["cV"])
                eng.dma_start(out=w16_d[wn][r * 128:(r + 1) * 128, :], in_=w16b).then_inc(dA, 16)
            p.A(a_wb); bump("dA", 16)

    def v_gs(eng):
        eng.wait_ge(dA, p.cnt["dA"])
        a = tv[0:1, 70:71]; b = tv[0:1, 71:72]
        chv(eng, eng.tensor_mul(a, SW[:, 0:1], SW[:, 1:2]))
        chv(eng, eng.tensor_scalar_mul(a, a, 8.0))
        chv(eng, eng.reciprocal(a, a))
        chv(eng, eng.tensor_copy(tv[0:1, 72:73], a))
        chv(eng, eng.reciprocal(b, SW[:, 2:3]))
        chv(eng, eng.tensor_copy(tv[0:1, 73:74], b))
        chv(eng, eng.reciprocal(b, SW[:, 3:4]))
        chv(eng, eng.tensor_copy(tv[0:1, 74:75], b))
        chv(eng, eng.engine_nop())
        eng.drain().then_inc(cV, 1)
    p.V(v_gs); bump("cV")
    for i in range(3):
        def t_g(eng, i=i):
            eng.wait_ge(cV, p.cnt["cV"])
            eng.matmul(pb[6][:, 0:1], ONERF, tv[0:1, 72 + i:73 + i], start=True, stop=True).then_inc(cT, 1)
        p.T(t_g); bump("cT")
        def v_g(eng, i=i):
            eng.wait_ge(cT, p.cnt["cT"])
            chv(eng, eng.tensor_copy(GSC[:, i:i + 1], pb[6][:, 0:1]))
            eng.drain().then_inc(cV, 1)
        p.V(v_g); bump("cV")

    join_all()
    p.close_scope("wq")

    qT = p.sbuf("qT", [128, DC * S], BF16, scope="attB")
    kT = p.sbuf("kT", [128, DC * S], BF16, scope="attB")
    vT = p.sbuf("vT", [128, DC * S], BF16, scope="attB")
    SF = p.sbuf("sfull", [16, S], F32, scope="attB")
    xqT = p.sbuf("xqT", [128, DC * S], BF16, scope="attW")
    wqT = p.sbuf("wqT", [128, DC * D], BF16, scope="attW")
    wkT = p.sbuf("wkT", [128, DC * D], BF16, scope="attW")
    wvT = p.sbuf("wvT", [128, DC * D], BF16, scope="attW")

    def s_tp(eng):
        eng.wait_ge(dA, p.cnt["dA"])
        for dc in range(DC):
            eng.dma_start(out=xqT[:, dc * S:(dc + 1) * S],
                          in_=xq_d[:, dc * 128:(dc + 1) * 128], transpose=True).then_inc(dS, 16)
            eng.wait_ge(dS, bump("dS", 16))
        for wbuf, wn in ((wqT, "wq"), (wkT, "wk"), (wvT, "wv")):
            for dc in range(DC):
                eng.dma_start(out=wbuf[:, dc * D:(dc + 1) * D],
                              in_=w16_d[wn][:, dc * 128:(dc + 1) * 128],
                              transpose=True).then_inc(dS, 16)
                eng.wait_ge(dS, bump("dS", 16))
    p.S(s_tp)
    tp_dS = p.cnt["dS"]

    for which, wbuf, dst, gcol in (("q", wqT, qT, None), ("k", wkT, kT, 0), ("v", wvT, vT, 1)):
        for ec in range(DC):
            for nb in range(4):
                bank = (nb % 2) * 2 + (0 if which != "k" else 1)
                def t_p(eng, ec=ec, nb=nb, wbuf=wbuf, bank=bank):
                    if ec == 0 and nb == 0:
                        eng.wait_ge(dS, tp_dS)
                    eng.wait_ge(cV, p.cnt["cV"])
                    for dc in range(DC):
                        mm = eng.matmul(pb[bank], wbuf[:, dc * D + ec * 128: dc * D + (ec + 1) * 128],
                                        xqT[:, dc * S + nb * 512: dc * S + (nb + 1) * 512],
                                        start=(dc == 0), stop=(dc == DC - 1))
                    mm.then_inc(cT, 1)
                p.T(t_p); bump("cT")
                def v_p(eng, ec=ec, nb=nb, dst=dst, gcol=gcol, bank=bank):
                    eng.wait_ge(cT, p.cnt["cT"])
                    o = dst[:, ec * S + nb * 512: ec * S + (nb + 1) * 512]
                    if gcol is None:
                        chv(eng, eng.tensor_copy(o, pb[bank]))
                    else:
                        chv(eng, eng.tensor_scalar_mul(o, pb[bank], GSC[:, gcol:gcol + 1]))
                    eng.drain().then_inc(cV, 1)
                p.V(v_p); bump("cV")

    join_all()
    p.close_scope("attW")

    krep = p.sbuf("krep", [128, 512], BF16, scope="hmx")
    vrepA = p.sbuf("vrepa", [128, 16 * 512], BF16, scope="hmx")
    PROD = p.sbuf("prod", [128, 2 * 512], BF16, scope="hmx")
    SCt = [p.sbuf(f"sct{t}", [16, 512], F32, scope="hmx") for t in range(16)]
    EBt = [p.sbuf(f"ebt{t}", [16, 512], BF16, scope="hmx") for t in range(16)]
    MX = p.sbuf("mx", [16, 512], F32, scope="hmx")
    TMPA = p.sbuf("tmpa", [128, 2 * 512], BF16, scope="hmx")
    ctxc = p.sbuf("ctxc", [128, 2 * 512], F32, scope="hmx")

    for sb in range(4):
        for t in range(16):
            def v_kr(eng, sb=sb, t=t):
                eng.wait_ge(cT, p.cnt["cT"])
                src_ = kT[:, (t // 2) * S + sb * 512:(t // 2) * S + (sb + 1) * 512]
                base = (t % 2) * 64
                chv(eng, eng.tensor_copy(krep[0:64, :], src_[base:base + 64, :]))
                chv(eng, eng.tensor_copy(krep[64:128, :], src_[base:base + 64, :]))
                eng.drain().then_inc(cV, 1)
            p.V(v_kr); bump("cV")
            for c in range(DC):
                def v_pr(eng, sb=sb, t=t, c=c):
                    if c >= 2:
                        eng.wait_ge(cT, p.cnt["cT"] - 1)
                    chv(eng, eng.tensor_mul(PROD[:, (c % 2) * 512:(c % 2 + 1) * 512],
                                            qT[:, c * S + sb * 512: c * S + (sb + 1) * 512], krep))
                    eng.drain().then_inc(cV, 1)
                p.V(v_pr); bump("cV")
                def t_sc(eng, t=t, c=c):
                    eng.wait_ge(cV, p.cnt["cV"])
                    eng.matmul(pb[4][0:16, :], E16A[:, c * 16:(c + 1) * 16],
                               PROD[:, (c % 2) * 512:(c % 2 + 1) * 512],
                               start=(c == 0), stop=(c == DC - 1)).then_inc(cT, 1)
                p.T(t_sc); bump("cT")
            def v_sc(eng, t=t):
                eng.wait_ge(cT, p.cnt["cT"])
                chv(eng, eng.tensor_copy(SCt[t], pb[4][0:16, :]))
                eng.drain().then_inc(cV, 1)
            p.V(v_sc); bump("cV")
        def v_sm(eng, sb=sb):
            chv(eng, eng.tensor_max(MX, SCt[0], SCt[1]))
            for t in range(2, 16):
                chv(eng, eng.tensor_max(MX, MX, SCt[t]))
            for t in range(16):
                chv(eng, eng.tensor_sub(SCt[t], SCt[t], MX))
            eng.drain().then_inc(cV, 1)
        p.V(v_sm); bump("cV")
        def a_ex(eng, sb=sb):
            eng.wait_ge(cV, p.cnt["cV"])
            for t in range(16):
                cha(eng, eng.activation(SCt[t], SCt[t], AF.Exp))
            eng.drain().then_inc(cA, 1)
        p.A(a_ex); bump("cA")
        def v_ss(eng, sb=sb):
            eng.wait_ge(cA, p.cnt["cA"])
            s_ = SF[:, sb * 512:(sb + 1) * 512]
            chv(eng, eng.tensor_add(s_, SCt[0], SCt[1]))
            for t in range(2, 16):
                chv(eng, eng.tensor_add(s_, s_, SCt[t]))
            for t in range(16):
                chv(eng, eng.tensor_copy(EBt[t], SCt[t]))
            eng.drain().then_inc(cV, 1)
        p.V(v_ss); bump("cV")
        def v_vra(eng, sb=sb):
            for t in range(16):
                src_ = vT[:, (t // 2) * S + sb * 512:(t // 2) * S + (sb + 1) * 512]
                base = (t % 2) * 64
                chv(eng, eng.tensor_copy(vrepA[0:64, t * 512:(t + 1) * 512],
                                         src_[base:base + 64, :]))
                chv(eng, eng.tensor_copy(vrepA[64:128, t * 512:(t + 1) * 512],
                                         src_[base:base + 64, :]))
            eng.drain().then_inc(cV, 1)
        p.V(v_vra); bump("cV")
        for c in range(DC):
            for t in range(16):
                def t_bc(eng, c=c, t=t):
                    eng.wait_ge(cV, max(p.cnt["cV"] - 1, 1))
                    eng.matmul(pb[5 + t % 2][:, :], E16T[:, c * 128:(c + 1) * 128],
                               EBt[t], start=True, stop=True).then_inc(cT, 1)
                p.T(t_bc); bump("cT")
                def v_mu(eng, c=c, t=t):
                    eng.wait_ge(cT, p.cnt["cT"])
                    chv(eng, eng.tensor_mul(TMPA[:, (t % 2) * 512:(t % 2 + 1) * 512],
                                            pb[5 + t % 2], vrepA[:, t * 512:(t + 1) * 512]))
                    eng.drain().then_inc(cV, 1)
                p.V(v_mu); bump("cV")
                def t_ac(eng, c=c, t=t):
                    eng.wait_ge(cV, p.cnt["cV"])
                    eng.matmul(pb[7][:, :], IDENB,
                               TMPA[:, (t % 2) * 512:(t % 2 + 1) * 512],
                               start=(t == 0), stop=(t == 15)).then_inc(cT, 1)
                p.T(t_ac); bump("cT")
            def v_ev(eng, sb=sb, c=c):
                eng.wait_ge(cT, p.cnt["cT"])
                eng.wait_ge(dA, max(p.cnt["dA"] - 1, 0))
                chv(eng, eng.tensor_copy(ctxc[:, (c % 2) * 512:(c % 2 + 1) * 512], pb[7]))
                eng.drain().then_inc(cV, 1)
            p.V(v_ev); bump("cV")
            def a_cw2(eng, sb=sb, c=c):
                eng.wait_ge(cV, p.cnt["cV"])
                eng.dma_start(out=ctxT_d[c * 128:(c + 1) * 128, sb * 512:(sb + 1) * 512],
                              in_=ctxc[:, (c % 2) * 512:(c % 2 + 1) * 512]).then_inc(dA, 16)
            p.A(a_cw2); bump("dA", 16)

    for tci in range(SI):
        def t_r(eng, tci=tci):
            if tci == 0:
                eng.wait_ge(cV, p.cnt["cV"])
            eng.wait_ge(cV, p.cnt["cV"])
            eng.transpose(pb[6][:, 0:16], SF[:, tci * 128:(tci + 1) * 128], IDENF[0:16, 0:16]).then_inc(cT, 1)
        p.T(t_r); bump("cT")
        def v_r(eng, tci=tci):
            eng.wait_ge(cT, p.cnt["cT"])
            chv(eng, eng.reciprocal(RALL[:, tci * 16:(tci + 1) * 16], pb[6][:, 0:16]))
            eng.drain().then_inc(cV, 1)
        p.V(v_r); bump("cV")

    join_all()
    p.close_scope("hmx")
    p.close_scope("attB")

    cblk = p.sbuf("cblk", [64, S], F32, scope="mix")
    mbuf = p.sbuf("mbuf", [128, 2 * 64], F32, scope="mix")
    ctok = p.sbuf("ctok", [128, D], F32, scope="mix")
    zoq = p.sbuf("zoq", [128, D], F32, scope="mix")
    ZI3 = p.sbuf("zi3", [128, D], I32, scope="mix")
    xqo16 = p.sbuf("xqo16", [128, D], BF16, scope="mix")

    for ci in range(SI):
        def s_cb(eng, ci=ci):
            eng.wait_ge(dA, p.cnt["dA"])
            eng.wait_ge(cT, p.cnt["cT"])
            eng.dma_start(out=cblk, in_=ctxT_d[ci * 64:(ci + 1) * 64, :]).then_inc(dS, 16)
        p.S(s_cb); bump("dS", 16)
        for tci in range(SI):
            def t_mt(eng, ci=ci, tci=tci):
                if tci == 0:
                    eng.wait_ge(dS, p.cnt["dS"])
                eng.wait_ge(cV, max(p.cnt["cV"] - 1, 1))
                eng.transpose(pb[6 + tci % 2][:, 0:64],
                              cblk[:, tci * 128:(tci + 1) * 128],
                              IDENF[0:64, 0:64]).then_inc(cT, 1)
            p.T(t_mt); bump("cT")
            def v_mt(eng, ci=ci, tci=tci):
                eng.wait_ge(cT, p.cnt["cT"])
                eng.wait_ge(dA, max(p.cnt["dA"] - 1, 0))
                chv(eng, eng.tensor_scalar_mul(mbuf[:, (tci % 2) * 64:(tci % 2 + 1) * 64],
                                               pb[6 + tci % 2][:, 0:64],
                                               RALL[:, tci * 16 + ci:tci * 16 + ci + 1]))
                eng.drain().then_inc(cV, 1)
            p.V(v_mt); bump("cV")
            def a_mw(eng, ci=ci, tci=tci):
                eng.wait_ge(cV, p.cnt["cV"])
                eng.dma_start(out=mix_d[tci * 128:(tci + 1) * 128, :],
                              in_=mbuf[:, (tci % 2) * 64:(tci % 2 + 1) * 64]).then_inc(dA, 16)
            p.A(a_mw); bump("dA", 16)
        def s_mr(eng, ci=ci):
            eng.wait_ge(dA, p.cnt["dA"])
            eng.wait_ge(cV, p.cnt["cV"])
            eng.dma_start(out=ctok.rearrange("p (j dh) -> p j dh", dh=64),
                          in_=mix_d.rearrange("(pp j) dh -> pp j dh", j=16)).then_inc(dS, 16)
        p.S(s_mr); bump("dS", 16)
        c0 = tv[:, 80:81]; c1 = tv[:, 81:82]; c2 = tv[:, 82:83]; c3 = tv[:, 83:84]
        uo = UO[:, ci:ci + 1]
        def v_cq(eng, ci=ci, c0=c0, c1=c1):
            eng.wait_ge(dS, p.cnt["dS"])
            eng.wait_ge(dA, p.cnt["dA"])
            chv(eng, eng.scalar_tensor_tensor(zoq, ctok, 1.0, ctok, ALU.mult, ALU.mult))
            chv(eng, eng.reduce_sum(c0, zoq, axis=AX.X))
            chv(eng, eng.reduce_max(c1, ctok, axis=AX.X, apply_absolute_value=True))
            chv(eng, eng.tensor_scalar_mul(c0, c0, 1.0 / D))
            eng.drain().then_inc(cV, 1)
        p.V(v_cq); bump("cV")
        def a_cq(eng, c0=c0, c2=c2):
            eng.wait_ge(cV, p.cnt["cV"])
            cha(eng, eng.activation(c2, c0, AF.Sqrt, bias=EPSC2[:, 1:2], scale=1.0))
            eng.drain().then_inc(cA, 1)
        p.A(a_cq); bump("cA")
        def v_cq2(eng, ci=ci, c1=c1, c2=c2, c3=c3, uo=uo):
            eng.wait_ge(cA, p.cnt["cA"])
            chv(eng, eng.reciprocal(c2, c2))
            chv(eng, eng.tensor_mul(c3, c2, c1))
            chv(eng, eng.tensor_scalar_max(c3, c3, EPS_Q))
            chv(eng, eng.tensor_scalar_mul(uo, c3, 1.0 / 127.0))
            chv(eng, eng.reciprocal(c3, c3))
            chv(eng, eng.scalar_tensor_tensor(c2, c2, 127.0, c3, ALU.mult, ALU.mult))
            chv(eng, eng.tensor_scalar_mul(zoq, ctok, c2))
            chv(eng, eng.tensor_copy(ZI3, zoq))
            chv(eng, eng.tensor_scalar_mul(xqo16, ZI3, uo))
            eng.drain().then_inc(cV, 1)
        p.V(v_cq2); bump("cV")
        def a_cw(eng, ci=ci):
            eng.wait_ge(cV, p.cnt["cV"])
            eng.dma_start(out=xqo_d[ci * 128:(ci + 1) * 128, :], in_=xqo16).then_inc(dA, 16)
        p.A(a_cw); bump("dA", 16)

    join_all()
    p.close_scope("mix")

    xqoT = p.sbuf("xqoT", [128, DC * S], BF16, scope="wo")
    woT = p.sbuf("woT", [128, DC * D], BF16, scope="wo")
    xrld = p.sbuf("xrld", [128, D], F32, scope="wo")
    acct = p.sbuf("acct", [128, D], F32, scope="wo")

    def s_wo(eng):
        eng.wait_ge(dA, p.cnt["dA"])
        for dc in range(DC):
            eng.dma_start(out=xqoT[:, dc * S:(dc + 1) * S],
                          in_=xqo_d[:, dc * 128:(dc + 1) * 128], transpose=True).then_inc(dS, 16)
            eng.wait_ge(dS, bump("dS", 16))
        for dc in range(DC):
            eng.dma_start(out=woT[:, dc * D:(dc + 1) * D],
                          in_=w16_d["wo"][:, dc * 128:(dc + 1) * 128],
                          transpose=True).then_inc(dS, 16)
            eng.wait_ge(dS, bump("dS", 16))
    p.S(s_wo)
    wo_dS = p.cnt["dS"]

    for si in range(SI):
        def s_x(eng, si=si):
            eng.wait_ge(cV, p.cnt["cV"])
            eng.dma_start(out=xrld, in_=xb[si * 128:(si + 1) * 128, :]).then_inc(dS, 16)
        p.S(s_x); bump("dS", 16)
        def t_wo(eng, si=si):
            if si == 0:
                eng.wait_ge(dS, wo_dS)
            eng.wait_ge(cV, p.cnt["cV"])
            for dc in range(DC):
                eng.matmul(pb[0], xqoT[:, dc * S + si * 128: dc * S + (si + 1) * 128],
                           woT[:, dc * D: dc * D + 512],
                           start=(dc == 0), stop=(dc == DC - 1))
            for dc in range(DC):
                mm = eng.matmul(pb[1], xqoT[:, dc * S + si * 128: dc * S + (si + 1) * 128],
                                woT[:, dc * D + 512: (dc + 1) * D],
                                start=(dc == 0), stop=(dc == DC - 1))
            mm.then_inc(cT, 1)
        p.T(t_wo); bump("cT")
        def v_wo(eng, si=si):
            eng.wait_ge(cT, p.cnt["cT"])
            eng.wait_ge(dS, p.cnt["dS"])
            eng.wait_ge(dA, p.cnt["dA"])
            eng.scalar_tensor_tensor(acct[:, 0:512], pb[0], GSC[:, 2:3], xrld[:, 0:512],
                                     ALU.mult, ALU.add)
            eng.scalar_tensor_tensor(acct[:, 512:1024], pb[1], GSC[:, 2:3], xrld[:, 512:1024],
                                     ALU.mult, ALU.add)
            chv(eng, eng.engine_nop())
            eng.drain().then_inc(cV, 1)
        p.V(v_wo); bump("cV")
        def a_acc(eng, si=si):
            eng.wait_ge(cV, p.cnt["cV"])
            eng.dma_start(out=acc_d[si * 128:(si + 1) * 128, :], in_=acct).then_inc(dA, 16)
        p.A(a_acc); bump("dA", 16)

    d = dbgout("xqo", [S, D], BF16)
    if d is not None:
        def f_xqo(eng, d=d):
            eng.wait_ge(dA, p.cnt["dA"])
            eng.dma_start(out=d, in_=xqo_d).then_inc(dS, 16)
        p.S(f_xqo); bump("dS", 16)
    d = dbgout("ctxT", [D, S])
    if d is not None:
        def f_ctxT(eng, d=d):
            eng.wait_ge(dA, p.cnt["dA"])
            for k in range(8):
                eng.dma_start(out=d[k * 128:(k + 1) * 128, :],
                              in_=ctxT_d[k * 128:(k + 1) * 128, :]).then_inc(dS, 16)
        p.S(f_ctxT); bump("dS", 16 * 8)
    d = dbgout("sums", [1, NH * S])
    if d is not None:
        def f_sums(eng, d=d):
            eng.wait_ge(dA, p.cnt["dA"])
            eng.dma_start(out=d, in_=sums_d).then_inc(dS, 16)
        p.S(f_sums); bump("dS", 16)
    d = dbgout("accA", [S, D])
    if d is not None:
        def f_accA(eng, d=d):
            eng.wait_ge(dA, p.cnt["dA"])
            eng.dma_start(out=d, in_=acc_d).then_inc(dS, 16)
        p.S(f_accA); bump("dS", 16)
    join_all()
    p.close_scope("wo")

    xgT = p.sbuf("xgT", [128, DC * WSL], BF16, scope="moe")
    rg16 = [p.sbuf(f"rg16_{i}", [128, 4096], BF16, scope="moe") for i in range(2)]
    ru16 = [p.sbuf(f"ru16_{i}", [128, 4096], BF16, scope="moe") for i in range(2)]
    h_e = p.sbuf("h_e", [128, 32 * WSL], BF16, scope="moe")
    rd16 = p.sbuf("rd16", [128, 32 * D], BF16, scope="moe")
    sgt = p.sbuf("sgt", [128, 1024], F32, scope="moe")
    eoutb = p.sbuf("eoutb", [128, D], F32, scope="moe")

    NBW = ((0, 512), (512, 256))

    for e in range(E):
        def s_xg(eng, e=e):
            if e == 0:
                eng.wait_ge(dG, scat_dG)
            eng.wait_ge(cT, p.cnt["cT"])
            for dc in range(DC):
                eng.dma_start(out=xgT[:, dc * WSL:(dc + 1) * WSL],
                              in_=xg16[e * WSL:(e + 1) * WSL, dc * 128:(dc + 1) * 128],
                              transpose=True).then_inc(dS, 16)
                eng.wait_ge(dS, bump("dS", 16))
        p.S(s_xg)
        def g_rd(eng, e=e):
            eng.wait_ge(cT, p.cnt["cT"])
            for ic in range(32):
                eng.dma_start(out=rd16[:, ic * D:(ic + 1) * D],
                              in_=rd_d[e, ic * 128:(ic + 1) * 128, :]).then_inc(dG, 16)
        p.G(g_rd); bump("dG", 16 * 32)
        xg_dS = p.cnt["dS"]

        for icc in range(DC):
            buf = icc % 2
            def g_w(eng, e=e, icc=icc, buf=buf):
                if icc >= 2:
                    eng.wait_ge(cT, p.cnt["cT"])
                eng.dma_start(out=rg16[buf].rearrange("p (c i) -> p c i", i=512),
                              in_=rg_d[e, :, icc * 512:(icc + 1) * 512].rearrange(
                                  "(c p) i -> p c i", p=128)).then_inc(dG, 16)
                eng.dma_start(out=ru16[buf].rearrange("p (c i) -> p c i", i=512),
                              in_=ru_d[e, :, icc * 512:(icc + 1) * 512].rearrange(
                                  "(c p) i -> p c i", p=128)).then_inc(dG, 16)
            p.G(g_w); bump("dG", 32)
            for ib in range(4):
                recs = []
                for nbs in range(2):
                    off, w = NBW[nbs]
                    kk = (icc * 8 + ib * 2 + nbs) % 2
                    def t_gu(eng, e=e, icc=icc, ib=ib, nbs=nbs, buf=buf, kk=kk, off=off, w=w):
                        if ib == 0 and nbs == 0:
                            eng.wait_ge(dG, p.cnt["dG"])
                            if icc == 0:
                                eng.wait_ge(dS, xg_dS)
                        if nbs == 0:
                            if p.cnt["cV"] >= 1:
                                eng.wait_ge(cV, max(p.cnt["cV"] - 1, 1))
                        else:
                            eng.wait_ge(cV, p.cnt["cV"])
                        for dc in range(DC):
                            eng.matmul(pb[2 * kk][:, 0:w],
                                       rg16[buf][:, dc * 512 + ib * 128: dc * 512 + (ib + 1) * 128],
                                       xgT[:, dc * WSL + off: dc * WSL + off + w],
                                       start=(dc == 0), stop=(dc == DC - 1))
                        for dc in range(DC):
                            mm = eng.matmul(pb[2 * kk + 1][:, 0:w],
                                            ru16[buf][:, dc * 512 + ib * 128: dc * 512 + (ib + 1) * 128],
                                            xgT[:, dc * WSL + off: dc * WSL + off + w],
                                            start=(dc == 0), stop=(dc == DC - 1))
                        mm.then_inc(cT, 1)
                    p.T(t_gu); bump("cT")
                    recs.append((p.cnt["cT"], kk, off, w))
                cvb = p.cnt["cV"]
                def a_si(eng, recs=recs, cvb=cvb):
                    for j, (ct, kk, off, w) in enumerate(recs):
                        eng.wait_ge(cT, ct)
                        if cvb - 1 + j >= 1:
                            eng.wait_ge(cV, cvb - 1 + j)
                        cha(eng, eng.activation(sgt[:, kk * 512: kk * 512 + w],
                                                pb[2 * kk][:, 0:w], AF.Silu))
                        eng.drain().then_inc(cA, 1)
                p.A(a_si); bump("cA", 2)
                cab = p.cnt["cA"]
                for j, (ct, kk, off, w) in enumerate(recs):
                    def v_h(eng, icc=icc, ib=ib, kk=kk, off=off, w=w, ca=cab - 1 + j):
                        eng.wait_ge(cA, ca)
                        ic32 = icc * 4 + ib
                        eng.tensor_mul(h_e[:, ic32 * WSL + off: ic32 * WSL + off + w],
                                       sgt[:, kk * 512: kk * 512 + w], pb[2 * kk + 1][:, 0:w])
                        chv(eng, eng.engine_nop())
                        eng.drain().then_inc(cV, 1)
                    p.V(v_h); bump("cV")
        for sc in range(WSL // 128):
            def t_dn(eng, e=e, sc=sc):
                if sc == 0:
                    eng.wait_ge(dG, p.cnt["dG"])
                eng.wait_ge(cV, p.cnt["cV"])
                for ic in range(32):
                    eng.matmul(pb[4], h_e[:, ic * WSL + sc * 128: ic * WSL + (sc + 1) * 128],
                               rd16[:, ic * D: ic * D + 512],
                               start=(ic == 0), stop=(ic == 31))
                for ic in range(32):
                    mm = eng.matmul(pb[5], h_e[:, ic * WSL + sc * 128: ic * WSL + (sc + 1) * 128],
                                    rd16[:, ic * D + 512: (ic + 1) * D],
                                    start=(ic == 0), stop=(ic == 31))
                mm.then_inc(cT, 1)
            p.T(t_dn); bump("cT")
            def v_dn(eng):
                eng.wait_ge(cT, p.cnt["cT"])
                eng.wait_ge(dA, p.cnt["dA"])
                eng.tensor_copy(eoutb[:, 0:512], pb[4])
                eng.tensor_copy(eoutb[:, 512:1024], pb[5])
                chv(eng, eng.engine_nop())
                eng.drain().then_inc(cV, 1)
            p.V(v_dn); bump("cV")
            def a_eo(eng, e=e, sc=sc):
                eng.wait_ge(cV, p.cnt["cV"])
                eng.dma_start(out=eout_d[e * WSL + sc * 128: e * WSL + (sc + 1) * 128, :],
                              in_=eoutb).then_inc(dA, 16)
            p.A(a_eo); bump("dA", 16)

    def v_z0(eng):
        eng.wait_ge(dA, p.cnt["dA"])
        eng.memset(eoutb[0:1, :], 0.0)
        chv(eng, eng.engine_nop())
        eng.drain().then_inc(cV, 1)
    p.V(v_z0); bump("cV")
    def a_z0(eng):
        eng.wait_ge(cV, p.cnt["cV"])
        eng.dma_start(out=eout_d[NSLOT:NSLOT + 1, :], in_=eoutb[0:1, :]).then_inc(dA, 16)
    p.A(a_z0); bump("dA", 16)

    join_all()
    p.close_scope("moe")

    xnT16 = p.sbuf("xnT16", [128, DC * S], BF16, scope="shc")
    sg16 = [p.sbuf(f"sg16_{i}", [128, 4096], BF16, scope="shc") for i in range(2)]
    su16 = [p.sbuf(f"su16_{i}", [128, 4096], BF16, scope="shc") for i in range(2)]
    h_sb = p.sbuf("h_sb", [128, 64 * 512], BF16, scope="shc")
    sdb = [p.sbuf(f"sdb_{i}", [128, D], BF16, scope="shc") for i in range(2)]
    shT = p.sbuf("shT", [128, 8 * 512], F32, scope="shc")
    sgt2 = p.sbuf("sgt2", [128, 1024], F32, scope="shc")
    acc4 = p.sbuf("acc4", [128, 4 * D], F32, scope="shc")

    def s_xnt(eng):
        for dc in range(DC):
            eng.dma_start(out=xnT16[:, dc * S:(dc + 1) * S],
                          in_=xnb_d[:, dc * 128:(dc + 1) * 128], transpose=True).then_inc(dS, 16)
            eng.wait_ge(dS, bump("dS", 16))
    p.S(s_xnt)
    xnt_dS = p.cnt["dS"]

    for sb in range(4):
        for icc in range(16):
            buf = icc % 2
            def g_sg(eng, icc=icc, buf=buf):
                if icc >= 2:
                    eng.wait_ge(cT, p.cnt["cT"])
                eng.dma_start(out=sg16[buf], in_=sgb16_d[icc]).then_inc(dG, 16)
                eng.dma_start(out=su16[buf], in_=sub16_d[icc]).then_inc(dG, 16)
            p.G(g_sg); bump("dG", 32)
            for ibp in range(2):
                recs = []
                for jj in range(2):
                    ib = ibp * 2 + jj
                    kk = (icc * 4 + ib) % 2
                    def t_gu(eng, sb=sb, icc=icc, ib=ib, buf=buf, kk=kk, jj=jj):
                        if ib == 0:
                            eng.wait_ge(dG, p.cnt["dG"])
                            if icc == 0:
                                if sb == 0:
                                    eng.wait_ge(dS, xnt_dS)
                                eng.wait_ge(cV, p.cnt["cV"])
                        if jj == 0:
                            if p.cnt["cV"] >= 1:
                                eng.wait_ge(cV, max(p.cnt["cV"] - 1, 1))
                        else:
                            eng.wait_ge(cV, p.cnt["cV"])
                        for dc in range(DC):
                            eng.matmul(pb[2 * kk], sg16[buf][:, dc * 512 + ib * 128: dc * 512 + (ib + 1) * 128],
                                       xnT16[:, dc * S + sb * 512: dc * S + (sb + 1) * 512],
                                       start=(dc == 0), stop=(dc == DC - 1))
                        for dc in range(DC):
                            mm = eng.matmul(pb[2 * kk + 1], su16[buf][:, dc * 512 + ib * 128: dc * 512 + (ib + 1) * 128],
                                            xnT16[:, dc * S + sb * 512: dc * S + (sb + 1) * 512],
                                            start=(dc == 0), stop=(dc == DC - 1))
                        mm.then_inc(cT, 1)
                    p.T(t_gu); bump("cT")
                    recs.append((p.cnt["cT"], ib, kk))
                cvb = p.cnt["cV"]
                def a_si(eng, recs=recs, cvb=cvb):
                    for j, (ct, ib, kk) in enumerate(recs):
                        eng.wait_ge(cT, ct)
                        if cvb - 1 + j >= 1:
                            eng.wait_ge(cV, cvb - 1 + j)
                        cha(eng, eng.activation(sgt2[:, kk * 512:(kk + 1) * 512],
                                                pb[2 * kk], AF.Silu))
                        eng.drain().then_inc(cA, 1)
                p.A(a_si); bump("cA", 2)
                cab = p.cnt["cA"]
                for j, (ct, ib, kk) in enumerate(recs):
                    def v_h(eng, icc=icc, ib=ib, kk=kk, ca=cab - 1 + j):
                        eng.wait_ge(cA, ca)
                        ic64 = icc * 4 + ib
                        eng.tensor_mul(h_sb[:, ic64 * 512:(ic64 + 1) * 512],
                                       sgt2[:, kk * 512:(kk + 1) * 512], pb[2 * kk + 1])
                        chv(eng, eng.engine_nop())
                        eng.drain().then_inc(cV, 1)
                    p.V(v_h); bump("cV")
        for ic64 in range(64):
            buf = ic64 % 2
            def g_sd(eng, ic64=ic64, buf=buf):
                if ic64 >= 2:
                    eng.wait_ge(cT, p.cnt["cT"])
                eng.dma_start(out=sdb[buf],
                              in_=sd16_d[ic64 * 128:(ic64 + 1) * 128, :]).then_inc(dG, 16)
            p.G(g_sd); bump("dG", 16)
            def t_dn(eng, sb=sb, ic64=ic64, buf=buf):
                eng.wait_ge(dG, p.cnt["dG"])
                if ic64 == 0:
                    eng.wait_ge(cV, p.cnt["cV"])
                for dm in range(8):
                    mm = eng.matmul(pb[dm], sdb[buf][:, dm * 128:(dm + 1) * 128],
                                    h_sb[:, ic64 * 512:(ic64 + 1) * 512],
                                    start=(ic64 == 0), stop=(ic64 == 63))
                mm.then_inc(cT, 1)
            p.T(t_dn); bump("cT")
        def v_sh(eng):
            eng.wait_ge(cT, p.cnt["cT"])
            for dm in range(8):
                eng.tensor_copy(shT[:, dm * 512:(dm + 1) * 512], pb[dm])
            chv(eng, eng.engine_nop())
            eng.drain().then_inc(cV, 1)
        p.V(v_sh); bump("cV")
        def s_ac(eng, sb=sb):
            eng.wait_ge(cV, p.cnt["cV"])
            eng.wait_ge(dA, p.cnt["dA"])
            for q in range(4):
                eng.dma_start(out=acc4[:, q * D:(q + 1) * D],
                              in_=acc_d[sb * 512 + q * 128: sb * 512 + (q + 1) * 128, :]).then_inc(dS, 16)
        p.S(s_ac); bump("dS", 64)
        for q in range(4):
            for dm in range(8):
                bank = dm % 2
                def t_tr(eng, q=q, dm=dm, bank=bank):
                    if q == 0 and dm == 0:
                        eng.wait_ge(dS, p.cnt["dS"])
                    if dm >= 2 or q > 0:
                        eng.wait_ge(cV, p.cnt["cV"] - 1)
                    eng.transpose(pb[bank][:, 0:128],
                                  shT[:, dm * 512 + q * 128: dm * 512 + (q + 1) * 128],
                                  IDENF).then_inc(cT, 1)
                p.T(t_tr); bump("cT")
                def v_ad(eng, q=q, dm=dm, bank=bank):
                    eng.wait_ge(cT, p.cnt["cT"])
                    a = acc4[:, q * D + dm * 128: q * D + (dm + 1) * 128]
                    eng.tensor_add(a, a, pb[bank][:, 0:128])
                    chv(eng, eng.engine_nop())
                    eng.drain().then_inc(cV, 1)
                p.V(v_ad); bump("cV")
        def a_st(eng, sb=sb):
            eng.wait_ge(cV, p.cnt["cV"])
            for q in range(4):
                eng.dma_start(out=acc_d[sb * 512 + q * 128: sb * 512 + (q + 1) * 128, :],
                              in_=acc4[:, q * D:(q + 1) * D]).then_inc(dA, 16)
        p.A(a_st); bump("dA", 64)

    d = dbgout("accS", [S, D])
    if d is not None:
        def f_accS(eng, d=d):
            eng.wait_ge(dA, p.cnt["dA"])
            eng.dma_start(out=d, in_=acc_d).then_inc(dS, 16)
        p.S(f_accS); bump("dS", 16)
    d = dbgout("eout", [NSLOT, D])
    if d is not None:
        def f_eout(eng, d=d):
            eng.wait_ge(dA, p.cnt["dA"])
            for k in range(8):
                eng.dma_start(out=d[k * WSL:(k + 1) * WSL, :],
                              in_=eout_d[k * WSL:(k + 1) * WSL, :]).then_inc(dS, 16)
        p.S(f_eout); bump("dS", 16 * 8)
    join_all()
    p.close_scope("shc")

    accf = p.sbuf("accf", [128, D], F32, scope="fin")
    g1b = p.sbuf("g1b", [128, D], F32, scope="fin")
    g2b = p.sbuf("g2b", [128, D], F32, scope="fin")

    for ci in range(SI):
        def s_af(eng, ci=ci):
            eng.wait_ge(dS, p.cnt["dS"])
            eng.dma_start(out=accf, in_=acc_d[ci * 128:(ci + 1) * 128, :]).then_inc(dS, 16)
        p.S(s_af); bump("dS", 16)
        def g_g(eng, ci=ci):
            eng.wait_ge(cV, p.cnt["cV"])
            eng.indirect_dma_start(out=g1b, out_offset=None, in_=eout_d,
                                   in_offset=IndirectOffsetOnAxis(sigi(2)[:, ci:ci + 1], 0)).then_inc(dG, 16)
            eng.indirect_dma_start(out=g2b, out_offset=None, in_=eout_d,
                                   in_offset=IndirectOffsetOnAxis(sigi(3)[:, ci:ci + 1], 0)).then_inc(dG, 16)
        p.G(g_g); bump("dG", 32)
        def v_fin(eng, ci=ci):
            eng.wait_ge(dS, p.cnt["dS"])
            eng.wait_ge(dG, p.cnt["dG"])
            eng.scalar_tensor_tensor(accf, g1b, WV[:, ci:ci + 1], accf, ALU.mult, ALU.add)
            eng.scalar_tensor_tensor(accf, g2b, WV[:, 16 + ci:17 + ci], accf, ALU.mult, ALU.add)
            chv(eng, eng.engine_nop())
            eng.drain().then_inc(cV, 1)
        p.V(v_fin); bump("cV")
        def s_out(eng, ci=ci):
            eng.wait_ge(cV, p.cnt["cV"])
            eng.dma_start(out=out[ci * 128:(ci + 1) * 128, :], in_=accf).then_inc(dS, 16)
        p.S(s_out); bump("dS", 16)

    def s_end(eng):
        eng.wait_ge(dS, p.cnt["dS"])
    p.S(s_end)
    p.close_scope("fin")


_NC_CACHE = {}


def _consts():
    iden = np.eye(128, dtype=np.float32)
    sup = np.triu(np.ones((128, 128), np.float32), 1)
    e16 = np.zeros((128, 128), np.float32)
    for c in range(8):
        for pp in range(128):
            m = 2 * c + (1 if pp >= 64 else 0)
            e16[pp, c * 16 + m] = 1.0
    e16t = np.zeros((16, 1024), np.float32)
    for c in range(8):
        for mm in range(128):
            e16t[2 * c + mm // 64, c * 128 + mm] = 1.0
    import ml_dtypes
    return (iden, sup.astype(ml_dtypes.bfloat16), e16.astype(ml_dtypes.bfloat16),
            e16t.astype(ml_dtypes.bfloat16))


def kernel(**inputs):
    import ml_dtypes
    x = np.asarray(inputs["L0_attn_x"], np.float32)
    xflat = np.ascontiguousarray(x.reshape(T, D))
    iden, sup16, e16, e16t = _consts()
    if "nc" not in _NC_CACHE:
        _NC_CACHE["nc"], _NC_CACHE["dbg"] = build()
    nc = _NC_CACHE["nc"]
    in_maps = []
    for c in range(NCORES):
        mask = np.zeros((1, NCORES * E), np.float32)
        mask[0, :c * E] = 1.0
        in_maps.append(dict(
            xb=np.ascontiguousarray(xflat[c * S:(c + 1) * S]),
            xfull=xflat,
            wq=np.asarray(inputs["L0_attn_wq"], np.float32),
            wk=np.asarray(inputs["L0_attn_wk"], np.float32),
            wv=np.asarray(inputs["L0_attn_wv"], np.float32),
            wo=np.asarray(inputs["L0_attn_wo"], np.float32),
            rwin=np.asarray(inputs["L0_ffn_router_w"], np.float32),
            sg=np.asarray(inputs["L0_ffn_sg"], np.float32),
            su=np.asarray(inputs["L0_ffn_su"], np.float32),
            sd=np.asarray(inputs["L0_ffn_sd"], np.float32),
            rg=np.asarray(inputs["L0_ffn_rg"], np.float32),
            ru=np.asarray(inputs["L0_ffn_ru"], np.float32),
            rd=np.asarray(inputs["L0_ffn_rd"], np.float32),
            idenf=iden, sup16=sup16, e16all=e16, e16t=e16t, maskrep=mask,
        ))
    res = run_bass_kernel_spmd(nc, in_maps, core_ids=list(range(NCORES)))
    outs = [res.results[c]["out"] for c in range(NCORES)]
    return np.stack(outs, 0).astype(np.float32)


# version 03
# speedup vs baseline: 1.0934x; 1.0209x over previous
import numpy as np
from contextlib import ExitStack

import concourse.bass as bass
import concourse.mybir as mybir
from concourse.bass import IndirectOffsetOnAxis
from concourse.bass_utils import run_bass_kernel_spmd

F32 = mybir.dt.float32
BF16 = mybir.dt.bfloat16
I32 = mybir.dt.int32
U8 = mybir.dt.uint8
AF = mybir.ActivationFunctionType
ALU = mybir.AluOpType
AX = mybir.AxisListType

NCORES = 8
S = 2048
D = 1024
NH = 16
IS_ = 8192
IR = 4096
E = 8
CAP = 2560
WSL = 768
NSLOT = E * WSL
SI = S // 128
DC = D // 128
T = NCORES * S

EPS_LN = 1e-5
EPS_RMS_ATTN = 1e-6
EPS_RMS_FFN = float(np.finfo(np.float32).eps)
EPS_Q = 1e-5
BIG = 1e30


class P:
    def __init__(self):
        self.nc = bass.Bass("TRN2", target_bir_lowering=False, detect_race_conditions=False)
        self.es = ExitStack()
        self.scopes = {}
        self._sem = {}
        self.cnt = {}
        self.block = None
        self.dbg = {}

    def open_block(self):
        self.block = self.es.enter_context(self.nc.Block())

    def sem(self, name):
        if name not in self._sem:
            self._sem[name] = self.es.enter_context(self.nc.semaphore(name))
            self.cnt[name] = 0
        return self._sem[name]

    def T(self, fn): self.block.tensor(fn)
    def V(self, fn): self.block.vector(fn)
    def A(self, fn): self.block.scalar(fn)
    def G(self, fn): self.block.gpsimd(fn)
    def S(self, fn): self.block.sync(fn)

    def sbuf(self, name, shape, dt, scope=None):
        stack = self.es if scope is None else self.scopes.setdefault(scope, ExitStack())
        return stack.enter_context(self.nc.sbuf_tensor(name, list(shape), dt))[:, :]

    def close_scope(self, scope):
        if scope in self.scopes:
            self.scopes.pop(scope).close()

    def psum(self, name, shape, dt=F32):
        return self.es.enter_context(self.nc.psum_tensor(name, list(shape), dt))[:, :]

    def close(self):
        for s in list(self.scopes):
            self.close_scope(s)
        self.es.close()
        return self.nc


def build(dbg=()):
    dbg = set(dbg)
    p = P()
    nc = p.nc

    xb = nc.dram_tensor("xb", [S, D], F32, kind="ExternalInput").ap()
    xfull = nc.dram_tensor("xfull", [T, D], F32, kind="ExternalInput").ap()
    w_in = {n: nc.dram_tensor(n, [D, D], F32, kind="ExternalInput").ap()
            for n in ("wq", "wk", "wv", "wo")}
    rwin = nc.dram_tensor("rwin", [E, D], F32, kind="ExternalInput").ap()
    sg_d = nc.dram_tensor("sg", [D, IS_], F32, kind="ExternalInput").ap()
    su_d = nc.dram_tensor("su", [D, IS_], F32, kind="ExternalInput").ap()
    sd_d = nc.dram_tensor("sd", [IS_, D], F32, kind="ExternalInput").ap()
    rg_d = nc.dram_tensor("rg", [E, D, IR], F32, kind="ExternalInput").ap()
    ru_d = nc.dram_tensor("ru", [E, D, IR], F32, kind="ExternalInput").ap()
    rd_d = nc.dram_tensor("rd", [E, IR, D], F32, kind="ExternalInput").ap()
    idenf = nc.dram_tensor("idenf", [128, 128], F32, kind="ExternalInput").ap()
    sup16_d = nc.dram_tensor("sup16", [128, 128], BF16, kind="ExternalInput").ap()
    e16all_d = nc.dram_tensor("e16all", [128, 128], BF16, kind="ExternalInput").ap()
    e16t_d = nc.dram_tensor("e16t", [16, 1024], BF16, kind="ExternalInput").ap()
    maskrep_d = nc.dram_tensor("maskrep", [1, NCORES * E], F32, kind="ExternalInput").ap()
    out = nc.dram_tensor("out", [S, D], F32, kind="ExternalOutput").ap()

    xf16 = nc.dram_tensor("xf16", [T, D], BF16).ap()
    xnb_d = nc.dram_tensor("xnb_d", [S, D], BF16).ap()
    xq_d = nc.dram_tensor("xq_d", [S, D], BF16).ap()
    xqo_d = nc.dram_tensor("xqo_d", [S, D], BF16).ap()
    w16_d = {n: nc.dram_tensor(n + "16d", [D, D], BF16).ap()
             for n in ("wq", "wk", "wv", "wo")}
    acc_d = nc.dram_tensor("acc_d", [S, D], F32).ap()
    xg16 = nc.dram_tensor("xg16", [NSLOT + 128, D], BF16).ap()
    eout_d = nc.dram_tensor("eout_d", [NSLOT + 1, D], F32).ap()
    sums_d = nc.dram_tensor("sums_d", [1, NH * S], F32).ap()
    sd16_d = nc.dram_tensor("sd16_d", [IS_, D], BF16).ap()
    sgb16_d = nc.dram_tensor("sgb16_d", [16, 128, 8 * 512], BF16).ap()
    sub16_d = nc.dram_tensor("sub16_d", [16, 128, 8 * 512], BF16).ap()
    ctxT_d = nc.dram_tensor("ctxT_d", [D, S], F32).ap()
    mix_d = nc.dram_tensor("mix_d", [S, 64], F32).ap()

    def dbgout(tag, shape, dt=F32):
        if tag in dbg:
            ap = nc.dram_tensor("dbg_" + tag, list(shape), dt, kind="ExternalOutput").ap()
            p.dbg[tag] = ap
            return ap
        return None

    pb = [p.psum(f"pb{i}", [128, 512]) for i in range(8)]

    IDENF = p.sbuf("idenf_s", [128, 128], F32)
    C16 = p.sbuf("c16", [128, 392], BF16)
    SUP = C16[:, 0:128]
    E16A = C16[:, 128:256]
    ONESF16 = C16[:, 256:384]
    ONE16 = C16[:, 384:385]
    ONEROW = p.sbuf("onerow", [1, 128], BF16)
    E16T = p.sbuf("e16t_s", [16, 1024], BF16)
    IDENB = p.sbuf("idenb", [128, 128], BF16)
    ztile = p.sbuf("ztile_p", [128, D], BF16)
    xnbt = p.sbuf("xnbt_p", [128, D], BF16)

    rowbuf = p.sbuf("rowbuf", [128, 96], F32)
    MEAN = rowbuf[:, 0:16]; RSTD = rowbuf[:, 16:32]; RS2R = rowbuf[:, 32:48]
    UQ = rowbuf[:, 48:64]; UO = rowbuf[:, 64:80]
    RALL = p.sbuf("rall", [128, 256], F32)
    BIASC = p.sbuf("biasc", [128, 16], F32)
    GSC = p.sbuf("gsc", [128, 4], F32)
    SW = p.sbuf("swt", [1, 16], F32)
    CNT = p.sbuf("cntall", [1, NCORES * E + E], F32)
    CB = CNT[:, NCORES * E:NCORES * E + E]
    MSK = p.sbuf("mskrep", [1, NCORES * E], F32)
    CBC = p.sbuf("cbc", [128, E], F32)
    SBC = p.sbuf("sbc", [128, E], F32)
    tv = p.sbuf("tvs", [128, 96], F32)
    EPSC = p.sbuf("epsc", [128, 4], F32)

    LOG = p.sbuf("plLOG", [128, 128], F32)
    F1 = p.sbuf("plF1", [128, 128], F32)
    F2 = p.sbuf("plF2", [128, 128], F32)
    L2 = p.sbuf("plL2", [128, 128], F32)
    LSR = p.sbuf("plLSR", [128, 128], F32)
    VAL = p.sbuf("plVAL", [128, 128], F32)
    CCM = p.sbuf("plCCM", [128, 128], F32)
    PSC = p.sbuf("plPSC", [128, 16 * 14], F32)
    def ps(i): return PSC[:, 16 * i:16 * (i + 1)]
    PU = p.sbuf("planesu", [128, 16 * 3], U8)
    def pu(i): return PU[:, 16 * i:16 * (i + 1)]
    SIGI = p.sbuf("sigi", [128, 16 * 4], I32)
    def sigi(i): return SIGI[:, 16 * i:16 * (i + 1)]
    WV = p.sbuf("plWV", [128, 32], F32)
    rw_s = p.sbuf("rw_s", [E, D], F32)
    rwT = p.sbuf("rwT", [128, DC * E], F32)
    LH = p.sbuf("lh", [128, DC * 18], BF16)
    rwlo = p.sbuf("rwlo", [128, E], F32)
    SEST = p.sbuf("sest", [E, 128], F32)
    SER16 = p.sbuf("ser16", [1, E], BF16)
    SEL16 = p.sbuf("sel16", [128, 16], BF16)
    CB16 = p.sbuf("cb16", [1, E], BF16)
    ONERF = p.sbuf("onerf", [1, 128], F32)
    ONECF = p.sbuf("onecf", [128, 1], F32)

    dS = p.sem("dS"); dG = p.sem("dG"); dA = p.sem("dA")
    cT = p.sem("cT"); cV = p.sem("cV"); cA = p.sem("cA"); cP = p.sem("cP")
    SEMNAMES = ["dS", "dG", "dA", "cT", "cV", "cA", "cP"]

    def bump(name, k=1):
        p.cnt[name] += k
        return p.cnt[name]

    def join_all():
        for nm, emitfn in (("cT", p.T), ("cV", p.V), ("cA", p.A), ("cP", p.G)):
            bump(nm)
            def f(eng, nm=nm):
                eng.drain().then_inc(p.sem(nm), 1)
            emitfn(f)
        snap = {nm: p.cnt[nm] for nm in SEMNAMES}
        for emitfn in (p.T, p.V, p.A, p.G, p.S):
            def f(eng, snap=snap):
                for nm, v in snap.items():
                    if v:
                        eng.wait_ge(p.sem(nm), v)
            emitfn(f)

    def chv(eng, ins):
        ins.then_inc(p.sem("cW"), 1)
        eng.wait_ge(p.sem("cW"), bump("cW"))

    def cha(eng, ins):
        ins.then_inc(p.sem("cX"), 1)
        eng.wait_ge(p.sem("cX"), bump("cX"))

    p.open_block()

    xn = p.sbuf("xn", [128, SI * D], F32, scope="sx")
    scrA = p.sbuf("scrA", [128, D], F32, scope="sx")
    scrB = p.sbuf("scrB", [128, D], F32, scope="sx")
    ZI = p.sbuf("zi", [128, D], I32, scope="sx")
    logTs = p.sbuf("logTs", [E, 512], F32, scope="sx")
    xnT32 = p.sbuf("xnT32", [128, 512], F32, scope="sx")
    xq16 = p.sbuf("xq16", [128, SI * D], BF16, scope="sq")
    xnb = p.sbuf("xnb", [128, SI * D], BF16, scope="sq")
    def xn_t(ci): return xn[:, ci * D:(ci + 1) * D]
    def xq_t(ci): return xq16[:, ci * D:(ci + 1) * D]
    def xnb_t(ci): return xnb[:, ci * D:(ci + 1) * D]

    def s0_s(eng):
        eng.dma_start(out=IDENF, in_=idenf).then_inc(dS, 16)
        eng.dma_start(out=SUP, in_=sup16_d).then_inc(dS, 16)
        eng.dma_start(out=E16A, in_=e16all_d).then_inc(dS, 16)
        eng.dma_start(out=E16T, in_=e16t_d).then_inc(dS, 16)
        eng.dma_start(out=MSK, in_=maskrep_d).then_inc(dS, 16)
        eng.dma_start(out=rw_s, in_=rwin).then_inc(dS, 16)
        for ci in range(SI):
            eng.dma_start(out=xn_t(ci), in_=xb[ci * 128:(ci + 1) * 128, :]).then_inc(dS, 16)
    bump("dS", 16 * (6 + SI)); p.S(s0_s)
    x_dS = p.cnt["dS"]

    def s0_v(eng):
        eng.memset(ONE16, 1.0)
        eng.memset(ONEROW, 1.0)
        eng.memset(ONESF16, 1.0)
        eng.memset(ps(13), 0.0)
        eng.memset(ps(12), float(NSLOT))
        eng.memset(EPSC[:, 0:1], EPS_LN)
        eng.memset(EPSC[:, 1:2], EPS_RMS_ATTN)
        eng.memset(EPSC[:, 2:3], EPS_RMS_FFN)
        eng.memset(EPSC[:, 3:4], 0.0)
        chv(eng, eng.engine_nop())
        eng.drain().then_inc(cV, 1)
    p.V(s0_v); bump("cV")

    NFC = 32
    rows_per = T // NFC
    def s0_g(eng):
        for k in range(NFC):
            eng.dma_start(out=xf16[k * rows_per:(k + 1) * rows_per, :],
                          in_=xfull[k * rows_per:(k + 1) * rows_per, :]).then_inc(dG, 16)
    bump("dG", 16 * NFC); p.G(s0_g)
    xf16_dG = p.cnt["dG"]

    for ci in range(SI):
        m = MEAN[:, ci:ci + 1]; r = RSTD[:, ci:ci + 1]; uq = UQ[:, ci:ci + 1]
        c0 = tv[:, 0:1]; c1 = tv[:, 1:2]; c2 = tv[:, 2:3]; c3 = tv[:, 3:4]
        c4 = tv[:, 4:5]; c5 = tv[:, 5:6]; c6 = tv[:, 6:7]; c7 = tv[:, 7:8]
        c8 = tv[:, 8:9]; c9 = tv[:, 9:10]; c10 = tv[:, 10:11]; c11 = tv[:, 11:12]

        def v1(eng, ci=ci, m=m, c0=c0, c1=c1, c2=c2):
            if ci == 0:
                eng.wait_ge(dS, x_dS)
            x = xn_t(ci)
            chv(eng, eng.scalar_tensor_tensor(scrA, x, 1.0, x, ALU.mult, ALU.mult))
            chv(eng, eng.reduce_sum(c0, scrA, axis=AX.X))
            chv(eng, eng.reduce_sum(m, x, axis=AX.X))
            chv(eng, eng.tensor_scalar_mul(m, m, 1.0 / D))
            chv(eng, eng.tensor_scalar_mul(c0, c0, 1.0 / D))
            chv(eng, eng.tensor_mul(c1, m, m))
            chv(eng, eng.tensor_sub(c2, c0, c1))
            chv(eng, eng.engine_nop())
            eng.drain().then_inc(cV, 1)
        p.V(v1); bump("cV")

        def a1(eng, c2=c2, c3=c3):
            eng.wait_ge(cV, p.cnt["cV"])
            cha(eng, eng.activation(c3, c2, AF.Sqrt, bias=EPSC[:, 0:1], scale=1.0))
            eng.drain().then_inc(cA, 1)
        p.A(a1); bump("cA")

        def v2(eng, r=r, c3=c3, c4=c4, m=m):
            eng.wait_ge(cA, p.cnt["cA"])
            chv(eng, eng.reciprocal(r, c3))
            chv(eng, eng.scalar_tensor_tensor(c4, m, -1.0, r, ALU.mult, ALU.mult))
            chv(eng, eng.engine_nop())
            eng.drain().then_inc(cV, 1)
        p.V(v2); bump("cV")

        def v3(eng, ci=ci, c4=c4, c5=c5, c6=c6, r=r, c2=c2):
            x = xn_t(ci)
            chv(eng, eng.tensor_scalar(x, x, r, c4, ALU.mult, ALU.add))
            eng.tensor_copy(xnb_t(ci), x)
            chv(eng, eng.reduce_max(c5, x, axis=AX.X, apply_absolute_value=True))
            chv(eng, eng.tensor_mul(c6, c2, r))
            chv(eng, eng.tensor_mul(c6, c6, r))
            eng.drain().then_inc(cV, 1)
        p.V(v3); bump("cV")

        def a3(eng, c6=c6, c7=c7, c8=c8):
            eng.wait_ge(cV, p.cnt["cV"])
            cha(eng, eng.activation(c7, c6, AF.Sqrt, bias=EPSC[:, 1:2], scale=1.0))
            cha(eng, eng.activation(c8, c6, AF.Sqrt, bias=EPSC[:, 2:3], scale=1.0))
            eng.drain().then_inc(cA, 1)
        p.A(a3); bump("cA")

        def v4(eng, ci=ci, c5=c5, c7=c7, c8=c8, c9=c9, c10=c10, c11=c11, uq=uq, r=r):
            eng.wait_ge(cA, p.cnt["cA"])
            x = xn_t(ci)
            chv(eng, eng.reciprocal(c9, c7))
            chv(eng, eng.reciprocal(c10, c8))
            chv(eng, eng.tensor_copy(RS2R[:, ci:ci + 1], c10))
            chv(eng, eng.tensor_mul(c10, c9, c5))
            chv(eng, eng.tensor_scalar_max(c10, c10, EPS_Q))
            chv(eng, eng.tensor_scalar_mul(uq, c10, 1.0 / 127.0))
            chv(eng, eng.reciprocal(c11, c10))
            chv(eng, eng.scalar_tensor_tensor(c9, c9, 127.0, c11, ALU.mult, ALU.mult))
            chv(eng, eng.tensor_scalar_mul(scrB, x, c9))
            chv(eng, eng.tensor_copy(ZI, scrB))
            chv(eng, eng.tensor_scalar_mul(xq_t(ci), ZI, uq))
            chv(eng, eng.engine_nop())
            eng.drain().then_inc(cV, 1)
        p.V(v4); bump("cV")
    ln_cV = p.cnt["cV"]

    def s1_wb(eng):
        eng.wait_ge(cV, ln_cV)
        for ci in range(SI):
            eng.dma_start(out=xnb_d[ci * 128:(ci + 1) * 128, :], in_=xnb_t(ci)).then_inc(dA, 16)
            eng.dma_start(out=xq_d[ci * 128:(ci + 1) * 128, :], in_=xq_t(ci)).then_inc(dA, 16)
    p.A(s1_wb); bump("dA", 16 * 2 * SI)

    d = dbgout("xn", [S, D])
    if d is not None:
        def f(eng, d=d):
            eng.wait_ge(cV, ln_cV)
            for ci in range(SI):
                eng.dma_start(out=d[ci * 128:(ci + 1) * 128, :], in_=xn_t(ci)).then_inc(dA, 16)
        p.A(f); bump("dA", 16 * SI)
    d = dbgout("rowb", [128, 96])
    if d is not None:
        def f_rb(eng, d=d):
            eng.wait_ge(cV, ln_cV)
            eng.dma_start(out=d, in_=rowbuf).then_inc(dA, 16)
        p.A(f_rb); bump("dA", 16)
    d = dbgout("xq", [S, D], BF16)
    if d is not None:
        def f_xq(eng, d=d):
            eng.wait_ge(cV, ln_cV)
            for ci in range(SI):
                eng.dma_start(out=d[ci * 128:(ci + 1) * 128, :], in_=xq_t(ci)).then_inc(dA, 16)
        p.A(f_xq); bump("dA", 16 * SI)

    for dc in range(DC):
        def t_rw(eng, dc=dc):
            if dc == 0:
                eng.wait_ge(dS, x_dS)
            eng.wait_ge(cV, p.cnt["cV"])
            eng.transpose(pb[6][:, 0:E], rw_s[:, dc * 128:(dc + 1) * 128], IDENF[0:E, 0:E]).then_inc(cT, 1)
        p.T(t_rw); bump("cT")
        def v_rw(eng, dc=dc):
            eng.wait_ge(cT, p.cnt["cT"])
            chv(eng, eng.tensor_copy(rwT[:, dc * E:(dc + 1) * E], pb[6][:, 0:E]))
            eng.drain().then_inc(cV, 1)
        p.V(v_rw); bump("cV")

    for nb in range(S // 512):
        for dc in range(DC):
            for q in range(4):
                ci = nb * 4 + q
                bank = 6 + (q % 2)
                def t_tr(eng, ci=ci, dc=dc, bank=bank):
                    eng.wait_ge(cV, p.cnt["cV"])
                    eng.transpose(pb[bank][:, 0:128], xn_t(ci)[:, dc * 128:(dc + 1) * 128],
                                  IDENF).then_inc(cT, 1)
                p.T(t_tr); bump("cT")
                def v_tr(eng, q=q, bank=bank):
                    eng.wait_ge(cT, p.cnt["cT"])
                    chv(eng, eng.tensor_copy(xnT32[:, q * 128:(q + 1) * 128], pb[bank][:, 0:128]))
                    eng.drain().then_inc(cV, 1)
                p.V(v_tr); bump("cV")
            def t_mm(eng, dc=dc):
                eng.wait_ge(cV, p.cnt["cV"])
                eng.matmul(pb[5][0:E, :], rwT[:, dc * E:(dc + 1) * E], xnT32,
                           start=(dc == 0), stop=(dc == DC - 1)).then_inc(cT, 1)
            p.T(t_mm); bump("cT")
        def v_ev(eng, nb=nb):
            eng.wait_ge(cT, p.cnt["cT"])
            chv(eng, eng.tensor_copy(logTs, pb[5][0:E, :]))
            eng.drain().then_inc(cV, 1)
        p.V(v_ev); bump("cV")
        for q in range(4):
            ci = nb * 4 + q
            def t_pl(eng, q=q):
                eng.wait_ge(cV, p.cnt["cV"])
                eng.transpose(pb[6][:, 0:E], logTs[:, q * 128:(q + 1) * 128],
                              IDENF[0:E, 0:E]).then_inc(cT, 1)
            p.T(t_pl); bump("cT")
            def v_pl(eng, ci=ci):
                eng.wait_ge(cT, p.cnt["cT"])
                chv(eng, eng.tensor_scalar_mul(LOG[:, ci::16], pb[6][0:128, 0:E],
                                               RS2R[:, ci:ci + 1]))
                chv(eng, eng.engine_nop())
                eng.drain().then_inc(cV, 1)
            p.V(v_pl); bump("cV")

    d = dbgout("logits", [128, 128])
    if d is not None:
        def f(eng, d=d):
            eng.wait_ge(cV, p.cnt["cV"])
            eng.dma_start(out=d, in_=LOG).then_inc(dA, 16)
        p.A(f); bump("dA", 16)

    def v_lh(eng):
        eng.wait_ge(cV, p.cnt["cV"])
        for dc in range(DC):
            h = LH[:, dc * 18:dc * 18 + 9]
            lo = LH[:, dc * 18 + 9:dc * 18 + 18]
            eng.tensor_copy(h[:, 0:1], ONE16)
            eng.memset(lo[:, 0:1], 0.0)
            chv(eng, eng.tensor_copy(h[:, 1:9], rwT[:, dc * E:(dc + 1) * E]))
            chv(eng, eng.tensor_copy(rwlo, h[:, 1:9]))
            chv(eng, eng.tensor_sub(rwlo, rwT[:, dc * E:(dc + 1) * E], rwlo))
            chv(eng, eng.tensor_copy(lo[:, 1:9], rwlo))
        chv(eng, eng.engine_nop())
        eng.drain().then_inc(cV, 1)
    p.V(v_lh); bump("cV")

    def v_se(eng):
        chv(eng, eng.memset(SEST, 0.0))
        chv(eng, eng.reduce_sum(SEST[:, 0:1], rw_s, axis=AX.X))
        chv(eng, eng.engine_nop())
        eng.drain().then_inc(cV, 1)
    p.V(v_se); bump("cV")
    def t_se(eng):
        eng.wait_ge(cV, p.cnt["cV"])
        eng.transpose(pb[6][:, 0:E], SEST, IDENF[0:E, 0:E]).then_inc(cT, 1)
    p.T(t_se); bump("cT")
    def v_se2(eng):
        eng.wait_ge(cT, p.cnt["cT"])
        chv(eng, eng.tensor_copy(SER16, pb[6][0:1, 0:E]))
        eng.drain().then_inc(cV, 1)
    p.V(v_se2); bump("cV")
    def t_se2(eng):
        eng.wait_ge(cV, p.cnt["cV"])
        eng.matmul(pb[6][:, 0:E], ONEROW, SER16, start=True, stop=True).then_inc(cT, 1)
    p.T(t_se2); bump("cT")
    def v_se3(eng):
        eng.wait_ge(cT, p.cnt["cT"])
        chv(eng, eng.tensor_copy(SBC, pb[6][:, 0:E]))
        eng.drain().then_inc(cV, 1)
    p.V(v_se3); bump("cV")

    xhiT = p.sbuf("xhiT", [128, DC * S], BF16, scope="sF")
    fLb = p.sbuf("flb", [9, S], F32, scope="sF")
    FLO = p.sbuf("flo", [128, SI * 9], F32, scope="sF")
    fsel = p.sbuf("fsel", [128, 16 * 9], F32, scope="sF")
    fsel16 = p.sbuf("fsel16", [128, E], BF16, scope="sF")

    for b in range(NCORES):
        for dc in range(DC):
            def s_tr(eng, b=b, dc=dc):
                if b == 0 and dc == 0:
                    eng.wait_ge(dG, xf16_dG)
                if dc == 0:
                    eng.wait_ge(cV, p.cnt["cV"])
                eng.dma_start(out=xhiT[:, dc * S:(dc + 1) * S],
                              in_=xf16[b * S:(b + 1) * S, dc * 128:(dc + 1) * 128],
                              transpose=True).then_inc(dS, 16)
                eng.wait_ge(dS, bump("dS", 16))
            p.S(s_tr)
        for nb in range(4):
            for dc in range(DC):
                def t_mm(eng, nb=nb, dc=dc):
                    if dc == 0:
                        eng.wait_ge(dS, p.cnt["dS"])
                        eng.wait_ge(cV, p.cnt["cV"])
                    rhs = xhiT[:, dc * S + nb * 512: dc * S + (nb + 1) * 512]
                    eng.matmul(pb[4][0:9, :], LH[:, dc * 18:dc * 18 + 9], rhs,
                               start=(dc == 0), stop=False)
                    eng.matmul(pb[4][0:9, :], LH[:, dc * 18 + 9:dc * 18 + 18], rhs,
                               start=False, stop=(dc == DC - 1)).then_inc(cT, 1)
                p.T(t_mm); bump("cT")
            def v_ev(eng, nb=nb):
                eng.wait_ge(cT, p.cnt["cT"])
                chv(eng, eng.tensor_copy(fLb[:, nb * 512:(nb + 1) * 512], pb[4][0:9, :]))
                eng.drain().then_inc(cV, 1)
            p.V(v_ev); bump("cV")
        for ci in range(SI):
            def t_pl(eng, ci=ci):
                eng.wait_ge(cV, p.cnt["cV"])
                eng.transpose(pb[6][:, 0:9], fLb[:, ci * 128:(ci + 1) * 128], IDENF[0:9, 0:9]).then_inc(cT, 1)
            p.T(t_pl); bump("cT")
            def v_pl(eng, ci=ci):
                eng.wait_ge(cT, p.cnt["cT"])
                chv(eng, eng.tensor_copy(FLO[:, ci * 9:(ci + 1) * 9], pb[6][0:128, 0:9]))
                chv(eng, eng.engine_nop())
                eng.drain().then_inc(cV, 1)
            p.V(v_pl); bump("cV")
        def v_f(eng, b=b):
            eng.wait_ge(cV, p.cnt["cV"])
            eng.wait_ge(cT, p.cnt["cT"])
            mpl = ps(9); m1 = ps(10); m2 = ps(11)
            chv(eng, eng.tensor_scalar_mul(mpl, FLO[:, 0::9], 1.0 / D))
            for e in range(E):
                lf = FLO[:, 1 + e::9]
                chv(eng, eng.tensor_scalar_mul(ps(0), mpl, SBC[:, e:e + 1]))
                chv(eng, eng.tensor_sub(lf, lf, ps(0)))
            chv(eng, eng.tensor_max(m1, FLO[:, 1::9], FLO[:, 2::9]))
            for e in range(2, E):
                chv(eng, eng.tensor_max(m1, m1, FLO[:, 1 + e::9]))
            for e in range(E):
                lf = FLO[:, 1 + e::9]
                sc = fsel[:, e * 16:(e + 1) * 16]
                chv(eng, eng.tensor_tensor(sc, lf, m1, ALU.is_ge))
                chv(eng, eng.scalar_tensor_tensor(sc, sc, -BIG, lf, ALU.mult, ALU.add))
            chv(eng, eng.tensor_max(m2, fsel[:, 0:16], fsel[:, 16:32]))
            for e in range(2, E):
                chv(eng, eng.tensor_max(m2, m2, fsel[:, e * 16:(e + 1) * 16]))
            for e in range(E):
                lf = FLO[:, 1 + e::9]
                sc = fsel[:, e * 16:(e + 1) * 16]
                chv(eng, eng.tensor_tensor(sc, lf, m2, ALU.is_ge))
                chv(eng, eng.reduce_sum(fsel[:, 128 + e:129 + e], sc, axis=AX.X))
            chv(eng, eng.tensor_copy(fsel16, fsel[:, 128:128 + E]))
            chv(eng, eng.engine_nop())
            eng.drain().then_inc(cV, 1)
        p.V(v_f); bump("cV")
        def t_cnt(eng):
            eng.wait_ge(cV, p.cnt["cV"])
            eng.matmul(pb[6][0:1, 0:E], ONE16, fsel16, start=True, stop=True).then_inc(cT, 1)
        p.T(t_cnt); bump("cT")
        def v_cnt(eng, b=b):
            eng.wait_ge(cT, p.cnt["cT"])
            chv(eng, eng.tensor_copy(CNT[:, b * E:(b + 1) * E], pb[6][0:1, 0:E]))
            chv(eng, eng.engine_nop())
            eng.drain().then_inc(cV, 1)
        p.V(v_cnt); bump("cV")

    def v_cb(eng):
        eng.wait_ge(cV, p.cnt["cV"])
        tr = tv[0:1, 32:32 + NCORES * E]
        chv(eng, eng.tensor_mul(tr, CNT[:, 0:NCORES * E], MSK))
        for e in range(E):
            chv(eng, eng.reduce_sum(CB[:, e:e + 1], tr[:, e::E], axis=AX.X))
        chv(eng, eng.tensor_copy(CB16, CB))
        chv(eng, eng.engine_nop())
        eng.drain().then_inc(cV, 1)
    p.V(v_cb); bump("cV")
    def t_cb(eng):
        eng.wait_ge(cV, p.cnt["cV"])
        eng.matmul(pb[6][:, 0:E], ONEROW, CB16, start=True, stop=True).then_inc(cT, 1)
    p.T(t_cb); bump("cT")
    def v_cb2(eng):
        eng.wait_ge(cT, p.cnt["cT"])
        chv(eng, eng.tensor_copy(CBC, pb[6][:, 0:E]))
        eng.drain().then_inc(cV, 1)
    p.V(v_cb2); bump("cV")

    d = dbgout("cnt", [1, NCORES * E + E])
    if d is not None:
        def f(eng, d=d):
            eng.wait_ge(cV, p.cnt["cV"])
            eng.dma_start(out=d, in_=CNT).then_inc(dA, 16)
        p.A(f); bump("dA", 16)

    def Le(e): return LOG[:, e * 16:(e + 1) * 16]

    def v_top2(eng):
        eng.wait_ge(cV, p.cnt["cV"])
        m1 = ps(0); run = ps(11); t = ps(9); t2 = ps(10)
        chv(eng, eng.tensor_max(m1, Le(0), Le(1)))
        for e in range(2, E):
            chv(eng, eng.tensor_max(m1, m1, Le(e)))
        chv(eng, eng.memset(run, 0.0))
        for e in range(E):
            f1 = F1[:, e * 16:(e + 1) * 16]
            chv(eng, eng.tensor_tensor(t, Le(e), m1, ALU.is_ge))
            chv(eng, eng.tensor_mul(t2, t, run))
            chv(eng, eng.tensor_sub(f1, t, t2))
            chv(eng, eng.tensor_max(run, run, t))
        for e in range(E):
            f1 = F1[:, e * 16:(e + 1) * 16]
            l2 = L2[:, e * 16:(e + 1) * 16]
            chv(eng, eng.scalar_tensor_tensor(l2, f1, -BIG, Le(e), ALU.mult, ALU.add))
        m2 = ps(1)
        chv(eng, eng.tensor_max(m2, L2[:, 0:16], L2[:, 16:32]))
        for e in range(2, E):
            chv(eng, eng.tensor_max(m2, m2, L2[:, e * 16:(e + 1) * 16]))
        chv(eng, eng.memset(run, 0.0))
        for e in range(E):
            f2 = F2[:, e * 16:(e + 1) * 16]
            l2 = L2[:, e * 16:(e + 1) * 16]
            chv(eng, eng.tensor_tensor(t, l2, m2, ALU.is_ge))
            chv(eng, eng.tensor_mul(t2, t, run))
            chv(eng, eng.tensor_sub(f2, t, t2))
            chv(eng, eng.tensor_max(run, run, t))
        e1 = ps(2); e2 = ps(3)
        chv(eng, eng.memset(e1, 0.0))
        chv(eng, eng.memset(e2, 0.0))
        for e in range(1, E):
            chv(eng, eng.scalar_tensor_tensor(e1, F1[:, e * 16:(e + 1) * 16], float(e), e1,
                                              ALU.mult, ALU.add))
            chv(eng, eng.scalar_tensor_tensor(e2, F2[:, e * 16:(e + 1) * 16], float(e), e2,
                                              ALU.mult, ALU.add))
        chv(eng, eng.tensor_sub(ps(9), ps(0), ps(1)))
        chv(eng, eng.engine_nop())
        eng.drain().then_inc(cV, 1)
    p.V(v_top2); bump("cV")

    def a_sig(eng):
        eng.wait_ge(cV, p.cnt["cV"])
        cha(eng, eng.activation(ps(4), ps(9), AF.Sigmoid, bias=0.0, scale=1.0))
        eng.drain().then_inc(cA, 1)
    p.A(a_sig); bump("cA")

    for e in range(E):
        def v_rk(eng, e=e):
            if e == 0:
                eng.wait_ge(cV, p.cnt["cV"])
            else:
                eng.wait_ge(cT, p.cnt["cT"])
            sel = VAL[:, e * 16:(e + 1) * 16]
            chv(eng, eng.tensor_add(sel, F1[:, e * 16:(e + 1) * 16], F2[:, e * 16:(e + 1) * 16]))
            chv(eng, eng.tensor_copy(SEL16, sel))
            chv(eng, eng.engine_nop())
            eng.drain().then_inc(cV, 1)
        p.V(v_rk); bump("cV")
        def t_rk(eng, e=e):
            eng.wait_ge(cV, p.cnt["cV"])
            eng.matmul(pb[6][:, 0:16], SUP, SEL16, start=True, stop=True)
            eng.matmul(pb[7][:, 0:16], ONESF16, SEL16, start=True, stop=True).then_inc(cT, 1)
        p.T(t_rk); bump("cT")
        def v_rk2(eng, e=e):
            eng.wait_ge(cT, p.cnt["cT"])
            lsr = LSR[:, e * 16:(e + 1) * 16]
            ccm = CCM[:, e * 16:(e + 1) * 16]
            eng.tensor_copy(lsr, pb[6][:, 0:16])
            eng.tensor_copy(ccm, pb[7][:, 0:16])
            chv(eng, eng.engine_nop())
            eng.drain().then_inc(cV, 1)
        p.V(v_rk2); bump("cV")
    def v_rk3(eng):
        eng.wait_ge(cV, p.cnt["cV"])
        for e in range(E):
            lsr = LSR[:, e * 16:(e + 1) * 16]
            ccm = CCM[:, e * 16:(e + 1) * 16]
            for ci in range(1, SI):
                chv(eng, eng.tensor_add(ccm[:, ci:ci + 1], ccm[:, ci:ci + 1], ccm[:, ci - 1:ci]))
            for ci in range(1, SI):
                chv(eng, eng.tensor_add(lsr[:, ci:ci + 1], lsr[:, ci:ci + 1], ccm[:, ci - 1:ci]))
        chv(eng, eng.engine_nop())
        eng.drain().then_inc(cV, 1)
    p.V(v_rk3); bump("cV")

    def v_sigma(eng):
        eng.wait_ge(cV, p.cnt["cV"])
        eng.wait_ge(cA, p.cnt["cA"])
        t = ps(9); t2 = ps(10)
        for e in range(E):
            sel = VAL[:, e * 16:(e + 1) * 16]
            lsr = LSR[:, e * 16:(e + 1) * 16]
            chv(eng, eng.tensor_scalar_add(t, lsr, CBC[:, e:e + 1]))
            chv(eng, eng.tensor_scalar(t, t, float(CAP), None, ALU.is_lt))
            chv(eng, eng.tensor_mul(sel, sel, t))
        lsr1 = ps(5); lsr2 = ps(6); v1 = ps(7); v2 = ps(8)
        chv(eng, eng.memset(lsr1, 0.0)); chv(eng, eng.memset(lsr2, 0.0))
        chv(eng, eng.memset(v1, 0.0)); chv(eng, eng.memset(v2, 0.0))
        for e in range(E):
            f1 = F1[:, e * 16:(e + 1) * 16]; f2 = F2[:, e * 16:(e + 1) * 16]
            lsr = LSR[:, e * 16:(e + 1) * 16]; val = VAL[:, e * 16:(e + 1) * 16]
            chv(eng, eng.tensor_mul(t, f1, lsr)); chv(eng, eng.tensor_add(lsr1, lsr1, t))
            chv(eng, eng.tensor_mul(t, f2, lsr)); chv(eng, eng.tensor_add(lsr2, lsr2, t))
            chv(eng, eng.tensor_mul(t, f1, val)); chv(eng, eng.tensor_add(v1, v1, t))
            chv(eng, eng.tensor_mul(t, f2, val)); chv(eng, eng.tensor_add(v2, v2, t))
        chv(eng, eng.tensor_mul(WV[:, 0:16], ps(4), v1))
        chv(eng, eng.engine_nop())
        eng.drain().then_inc(cV, 1)
    p.V(v_sigma); bump("cV")

    def v_w2(eng):
        t = ps(9)
        chv(eng, eng.tensor_scalar(t, ps(4), -1.0, 1.0, ALU.mult, ALU.add))
        chv(eng, eng.tensor_mul(WV[:, 16:32], t, ps(8)))
        chv(eng, eng.engine_nop())
        eng.drain().then_inc(cV, 1)
    p.V(v_w2); bump("cV")

    def v_sig2(eng):
        t = ps(9); t2 = ps(10)
        for j, (ee, ll, vv, so, go) in enumerate(
                ((ps(2), ps(5), ps(7), 0, 2), (ps(3), ps(6), ps(8), 1, 3))):
            chv(eng, eng.tensor_scalar_min(t, ll, float(WSL - 1)))
            chv(eng, eng.scalar_tensor_tensor(t, ee, float(WSL), t, ALU.mult, ALU.add))
            chv(eng, eng.tensor_copy(pu(0), vv))
            chv(eng, eng.select(t2, pu(0), t, ps(12)))
            chv(eng, eng.tensor_copy(sigi(so), t2))
            chv(eng, eng.select(t2, pu(0), t, ps(12)))
            chv(eng, eng.tensor_copy(sigi(go), t2))
        chv(eng, eng.engine_nop())
        eng.drain().then_inc(cV, 1)
    p.V(v_sig2); bump("cV")

    d = dbgout("tables", [128, 16 * 8])
    if d is not None:
        def f(eng, d=d):
            eng.wait_ge(cV, p.cnt["cV"])
            eng.dma_start(out=d[:, 0:16], in_=ps(2)).then_inc(dA, 16)
            eng.dma_start(out=d[:, 16:32], in_=ps(3)).then_inc(dA, 16)
            eng.dma_start(out=d[:, 32:48], in_=ps(5)).then_inc(dA, 16)
            eng.dma_start(out=d[:, 48:64], in_=ps(6)).then_inc(dA, 16)
            eng.dma_start(out=d[:, 64:80], in_=ps(7)).then_inc(dA, 16)
            eng.dma_start(out=d[:, 80:96], in_=ps(8)).then_inc(dA, 16)
            eng.dma_start(out=d[:, 96:112], in_=WV[:, 0:16]).then_inc(dA, 16)
            eng.dma_start(out=d[:, 112:128], in_=WV[:, 16:32]).then_inc(dA, 16)
        p.A(f); bump("dA", 16 * 8)

    join_all()
    p.close_scope("sF")
    p.close_scope("sq")
    p.close_scope("sx")

    _emit_rest(p, locals())

    nc_final = p.close()
    return nc_final, p.dbg


def _emit_rest(p, g):
    nc = p.nc
    pb = g["pb"]
    IDENF = g["IDENF"]; ONEROW = g["ONEROW"]; E16A = g["E16A"]; ONE16 = g["ONE16"]
    RALL = g["RALL"]; BIASC = g["BIASC"]; GSC = g["GSC"]; SW = g["SW"]
    UO = g["UO"]; tv = g["tv"]; WV = g["WV"]; EPSC2 = g["EPSC"]
    sigi = g["sigi"]
    w_in = g["w_in"]; w16_d = g["w16_d"]
    xq_d = g["xq_d"]; xqo_d = g["xqo_d"]; xnb_d = g["xnb_d"]; xb = g["xb"]
    acc_d = g["acc_d"]; sums_d = g["sums_d"]; ctxT_d = g["ctxT_d"]
    mix_d = g["mix_d"]; E16T = g["E16T"]; IDENB = g["IDENB"]
    ztile = g["ztile"]; xnbt = g["xnbt"]
    sgb16_d = g["sgb16_d"]; sub16_d = g["sub16_d"]; sd_d = g["sd_d"]
    xg16 = g["xg16"]; eout_d = g["eout_d"]; out = g["out"]
    rg_d = g["rg_d"]; ru_d = g["ru_d"]; rd_d = g["rd_d"]
    sg_d = g["sg_d"]; su_d = g["su_d"]; sd16_d = g["sd16_d"]
    dbgout = g["dbgout"]; bump = g["bump"]; join_all = g["join_all"]; chv = g["chv"]; cha = g["cha"]
    dS, dG, dA, cT, cV, cA, cP = (p.sem(n) for n in
                                  ("dS", "dG", "dA", "cT", "cV", "cA", "cP"))

    def g_z(eng):
        eng.memset(ztile, 0.0).then_inc(p.sem("cP"), 1)
        eng.wait_ge(p.sem("cP"), bump("cP"))
        for r in range(NSLOT // 128 + 1):
            eng.dma_start(out=xg16[r * 128:(r + 1) * 128, :], in_=ztile).then_inc(dG, 16)
    bump("dG", 16 * (NSLOT // 128 + 1)); p.G(g_z)

    def g_cast(eng):
        for k in range(16):
            eng.dma_start(out=sd16_d[k * 512:(k + 1) * 512, :],
                          in_=sd_d[k * 512:(k + 1) * 512, :]).then_inc(dG, 16)
        for icc in range(16):
            eng.dma_start(out=sgb16_d[icc].rearrange("p (c i) -> p c i", i=512),
                          in_=sg_d[:, icc * 512:(icc + 1) * 512].rearrange(
                              "(c p) i -> p c i", p=128)).then_inc(dG, 16)
            eng.dma_start(out=sub16_d[icc].rearrange("p (c i) -> p c i", i=512),
                          in_=su_d[:, icc * 512:(icc + 1) * 512].rearrange(
                              "(c p) i -> p c i", p=128)).then_inc(dG, 16)
    bump("dG", 16 * 48); p.G(g_cast)

    def g_sc(eng):
        eng.wait_ge(dG, p.cnt["dG"])
        eng.wait_ge(dA, p.cnt["dA"])
        eng.wait_ge(cV, p.cnt["cV"])
        for ci in range(SI):
            eng.dma_start(out=xnbt, in_=xnb_d[ci * 128:(ci + 1) * 128, :]).then_inc(dG, 16)
            eng.wait_ge(dG, bump("dG", 16))
            for jj in range(2):
                eng.indirect_dma_start(
                    out=xg16, out_offset=IndirectOffsetOnAxis(sigi(jj)[:, ci:ci + 1], 0),
                    in_=xnbt, in_offset=None).then_inc(dG, 16)
                bump("dG", 16)
            eng.wait_ge(dG, p.cnt["dG"])
    p.G(g_sc)
    scat_dG = p.cnt["dG"]

    w32 = p.sbuf("w32", [128, DC * D], F32, scope="wq")
    qscr = p.sbuf("qscr", [128, D], F32, scope="wq")
    ZI2 = p.sbuf("zi2", [128, D], I32, scope="wq")
    w16b = p.sbuf("w16b", [128, D], BF16, scope="wq")
    ONERF = g["ONERF"]; ONECF = g["ONECF"]
    asum = tv[:, 64:65]
    ared = tv[:, 65:66]
    swcol = tv[:, 66:67]

    def v_onerf(eng):
        eng.memset(ONERF, 1.0)
        eng.memset(ONECF, 1.0)
        eng.tensor_copy(IDENB, IDENF)
        chv(eng, eng.engine_nop())
        eng.drain().then_inc(cV, 1)
    p.V(v_onerf); bump("cV")

    for wi, wn in enumerate(("wq", "wk", "wv", "wo")):
        def s_w(eng, wn=wn):
            eng.wait_ge(cV, p.cnt["cV"])
            for r in range(DC):
                eng.dma_start(out=w32[:, r * D:(r + 1) * D],
                              in_=w_in[wn][r * 128:(r + 1) * 128, :]).then_inc(dS, 16)
        p.S(s_w); bump("dS", 16 * DC)

        def v_as(eng):
            eng.wait_ge(dS, p.cnt["dS"])
            chv(eng, eng.memset(asum, 0.0))
            for r in range(DC):
                chv(eng, eng.reduce_sum(ared, w32[:, r * D:(r + 1) * D], axis=AX.X,
                                        apply_absolute_value=True))
                chv(eng, eng.tensor_add(asum, asum, ared))
            chv(eng, eng.engine_nop())
            eng.drain().then_inc(cV, 1)
        p.V(v_as); bump("cV")

        def t_tot(eng):
            eng.wait_ge(cV, p.cnt["cV"])
            eng.matmul(pb[6][0:1, 0:1], asum, ONECF, start=True, stop=True).then_inc(cT, 1)
        p.T(t_tot); bump("cT")

        def v_sw(eng, wi=wi):
            eng.wait_ge(cT, p.cnt["cT"])
            sws = SW[:, wi:wi + 1]
            chv(eng, eng.tensor_scalar_mul(sws, pb[6][0:1, 0:1], 1.0 / (D * D)))
            chv(eng, eng.tensor_scalar_max(sws, sws, 1e-5))
            chv(eng, eng.reciprocal(sws, sws))
            chv(eng, eng.engine_nop())
            eng.drain().then_inc(cV, 1)
        p.V(v_sw); bump("cV")

        def t_bc(eng, wi=wi):
            eng.wait_ge(cV, p.cnt["cV"])
            eng.matmul(pb[6][:, 0:1], ONERF, SW[:, wi:wi + 1], start=True, stop=True).then_inc(cT, 1)
        p.T(t_bc); bump("cT")

        def v_bc(eng):
            eng.wait_ge(cT, p.cnt["cT"])
            chv(eng, eng.tensor_copy(swcol, pb[6][:, 0:1]))
            eng.drain().then_inc(cV, 1)
        p.V(v_bc); bump("cV")

        for r in range(DC):
            def v_q(eng, r=r):
                if r == 0:
                    eng.wait_ge(cV, p.cnt["cV"])
                if r > 0:
                    eng.wait_ge(dA, p.cnt["dA"])
                chv(eng, eng.tensor_scalar_mul(qscr, w32[:, r * D:(r + 1) * D], swcol))
                chv(eng, eng.tensor_copy(ZI2, qscr))
                chv(eng, eng.tensor_scalar(w16b, ZI2, -1.0, 1.0, ALU.max, ALU.min))
                chv(eng, eng.engine_nop())
                eng.drain().then_inc(cV, 1)
            p.V(v_q); bump("cV")
            def a_wb(eng, wn=wn, r=r):
                eng.wait_ge(cV, p.cnt["cV"])
                eng.dma_start(out=w16_d[wn][r * 128:(r + 1) * 128, :], in_=w16b).then_inc(dA, 16)
            p.A(a_wb); bump("dA", 16)

    def v_gs(eng):
        eng.wait_ge(dA, p.cnt["dA"])
        a = tv[0:1, 70:71]; b = tv[0:1, 71:72]
        chv(eng, eng.tensor_mul(a, SW[:, 0:1], SW[:, 1:2]))
        chv(eng, eng.tensor_scalar_mul(a, a, 8.0))
        chv(eng, eng.reciprocal(a, a))
        chv(eng, eng.tensor_copy(tv[0:1, 72:73], a))
        chv(eng, eng.reciprocal(b, SW[:, 2:3]))
        chv(eng, eng.tensor_copy(tv[0:1, 73:74], b))
        chv(eng, eng.reciprocal(b, SW[:, 3:4]))
        chv(eng, eng.tensor_copy(tv[0:1, 74:75], b))
        chv(eng, eng.engine_nop())
        eng.drain().then_inc(cV, 1)
    p.V(v_gs); bump("cV")
    for i in range(3):
        def t_g(eng, i=i):
            eng.wait_ge(cV, p.cnt["cV"])
            eng.matmul(pb[6][:, 0:1], ONERF, tv[0:1, 72 + i:73 + i], start=True, stop=True).then_inc(cT, 1)
        p.T(t_g); bump("cT")
        def v_g(eng, i=i):
            eng.wait_ge(cT, p.cnt["cT"])
            chv(eng, eng.tensor_copy(GSC[:, i:i + 1], pb[6][:, 0:1]))
            eng.drain().then_inc(cV, 1)
        p.V(v_g); bump("cV")

    join_all()
    p.close_scope("wq")

    qT = p.sbuf("qT", [128, DC * S], BF16, scope="attB")
    kT = p.sbuf("kT", [128, DC * S], BF16, scope="attB")
    vT = p.sbuf("vT", [128, DC * S], BF16, scope="attB")
    SF = p.sbuf("sfull", [16, S], F32, scope="attB")
    xqT = p.sbuf("xqT", [128, DC * S], BF16, scope="attW")
    wqT = p.sbuf("wqT", [128, DC * D], BF16, scope="attW")
    wkT = p.sbuf("wkT", [128, DC * D], BF16, scope="attW")
    wvT = p.sbuf("wvT", [128, DC * D], BF16, scope="attW")

    def s_tp(eng):
        eng.wait_ge(dA, p.cnt["dA"])
        for dc in range(DC):
            eng.dma_start(out=xqT[:, dc * S:(dc + 1) * S],
                          in_=xq_d[:, dc * 128:(dc + 1) * 128], transpose=True).then_inc(dS, 16)
            eng.wait_ge(dS, bump("dS", 16))
        for wbuf, wn in ((wqT, "wq"), (wkT, "wk"), (wvT, "wv")):
            for dc in range(DC):
                eng.dma_start(out=wbuf[:, dc * D:(dc + 1) * D],
                              in_=w16_d[wn][:, dc * 128:(dc + 1) * 128],
                              transpose=True).then_inc(dS, 16)
                eng.wait_ge(dS, bump("dS", 16))
    p.S(s_tp)
    tp_dS = p.cnt["dS"]

    for which, wbuf, dst, gcol in (("q", wqT, qT, None), ("k", wkT, kT, 0), ("v", wvT, vT, 1)):
        for ec in range(DC):
            for nb in range(4):
                bank = (nb % 2) * 2 + (0 if which != "k" else 1)
                def t_p(eng, ec=ec, nb=nb, wbuf=wbuf, bank=bank):
                    if ec == 0 and nb == 0:
                        eng.wait_ge(dS, tp_dS)
                    eng.wait_ge(cV, max(p.cnt["cV"] - 1, 1))
                    for dc in range(DC):
                        mm = eng.matmul(pb[bank], wbuf[:, dc * D + ec * 128: dc * D + (ec + 1) * 128],
                                        xqT[:, dc * S + nb * 512: dc * S + (nb + 1) * 512],
                                        start=(dc == 0), stop=(dc == DC - 1))
                    mm.then_inc(cT, 1)
                p.T(t_p); bump("cT")
                def v_p(eng, ec=ec, nb=nb, dst=dst, gcol=gcol, bank=bank):
                    eng.wait_ge(cT, p.cnt["cT"])
                    o = dst[:, ec * S + nb * 512: ec * S + (nb + 1) * 512]
                    if gcol is None:
                        chv(eng, eng.tensor_copy(o, pb[bank]))
                    else:
                        chv(eng, eng.tensor_scalar_mul(o, pb[bank], GSC[:, gcol:gcol + 1]))
                    eng.drain().then_inc(cV, 1)
                p.V(v_p); bump("cV")

    join_all()
    p.close_scope("attW")

    krep = p.sbuf("krep", [128, 512], BF16, scope="hmx")
    vrepA = p.sbuf("vrepa", [128, 16 * 512], BF16, scope="hmx")
    PROD = p.sbuf("prod", [128, 2 * 512], BF16, scope="hmx")
    SCt = [p.sbuf(f"sct{t}", [16, 512], F32, scope="hmx") for t in range(16)]
    EBt = [p.sbuf(f"ebt{t}", [16, 512], BF16, scope="hmx") for t in range(16)]
    MX = p.sbuf("mx", [16, 512], F32, scope="hmx")
    TMPA = p.sbuf("tmpa", [128, 3 * 512], BF16, scope="hmx")
    ctxc = p.sbuf("ctxc", [128, 2 * 512], F32, scope="hmx")

    for sb in range(4):
        for t in range(16):
            def v_kr(eng, sb=sb, t=t):
                eng.wait_ge(cT, p.cnt["cT"])
                src_ = kT[:, (t // 2) * S + sb * 512:(t // 2) * S + (sb + 1) * 512]
                base = (t % 2) * 64
                chv(eng, eng.tensor_copy(krep[0:64, :], src_[base:base + 64, :]))
                chv(eng, eng.tensor_copy(krep[64:128, :], src_[base:base + 64, :]))
                eng.drain().then_inc(cV, 1)
            p.V(v_kr); bump("cV")
            for c in range(DC):
                def v_pr(eng, sb=sb, t=t, c=c):
                    if c >= 2:
                        eng.wait_ge(cT, p.cnt["cT"] - 1)
                    chv(eng, eng.tensor_mul(PROD[:, (c % 2) * 512:(c % 2 + 1) * 512],
                                            qT[:, c * S + sb * 512: c * S + (sb + 1) * 512], krep))
                    eng.drain().then_inc(cV, 1)
                p.V(v_pr); bump("cV")
                def t_sc(eng, t=t, c=c):
                    eng.wait_ge(cV, p.cnt["cV"])
                    eng.matmul(pb[4][0:16, :], E16A[:, c * 16:(c + 1) * 16],
                               PROD[:, (c % 2) * 512:(c % 2 + 1) * 512],
                               start=(c == 0), stop=(c == DC - 1)).then_inc(cT, 1)
                p.T(t_sc); bump("cT")
            def v_sc(eng, t=t):
                eng.wait_ge(cT, p.cnt["cT"])
                chv(eng, eng.tensor_copy(SCt[t], pb[4][0:16, :]))
                eng.drain().then_inc(cV, 1)
            p.V(v_sc); bump("cV")
        def v_sm(eng, sb=sb):
            chv(eng, eng.tensor_max(MX, SCt[0], SCt[1]))
            for t in range(2, 16):
                chv(eng, eng.tensor_max(MX, MX, SCt[t]))
            for t in range(16):
                chv(eng, eng.tensor_sub(SCt[t], SCt[t], MX))
            eng.drain().then_inc(cV, 1)
        p.V(v_sm); bump("cV")
        def a_ex(eng, sb=sb):
            eng.wait_ge(cV, p.cnt["cV"])
            for t in range(16):
                cha(eng, eng.activation(SCt[t], SCt[t], AF.Exp))
            eng.drain().then_inc(cA, 1)
        p.A(a_ex); bump("cA")
        def v_ss(eng, sb=sb):
            eng.wait_ge(cA, p.cnt["cA"])
            s_ = SF[:, sb * 512:(sb + 1) * 512]
            chv(eng, eng.tensor_add(s_, SCt[0], SCt[1]))
            for t in range(2, 16):
                chv(eng, eng.tensor_add(s_, s_, SCt[t]))
            for t in range(16):
                chv(eng, eng.tensor_copy(EBt[t], SCt[t]))
            eng.drain().then_inc(cV, 1)
        p.V(v_ss); bump("cV")
        def v_vra(eng, sb=sb):
            for t in range(16):
                src_ = vT[:, (t // 2) * S + sb * 512:(t // 2) * S + (sb + 1) * 512]
                base = (t % 2) * 64
                chv(eng, eng.tensor_copy(vrepA[0:64, t * 512:(t + 1) * 512],
                                         src_[base:base + 64, :]))
                chv(eng, eng.tensor_copy(vrepA[64:128, t * 512:(t + 1) * 512],
                                         src_[base:base + 64, :]))
            eng.drain().then_inc(cV, 1)
        p.V(v_vra); bump("cV")
        for c in range(DC):
            for t in range(16):
                def t_bc(eng, c=c, t=t):
                    if c == 0 and t < 3:
                        eng.wait_ge(cV, p.cnt["cV"])
                    else:
                        eng.wait_ge(cV, max(p.cnt["cV"] - 2, 1))
                    eng.matmul(pb[4 + t % 3][:, :], E16T[:, c * 128:(c + 1) * 128],
                               EBt[t], start=True, stop=True).then_inc(cT, 1)
                p.T(t_bc); bump("cT")
                def v_mu(eng, c=c, t=t):
                    eng.wait_ge(cT, p.cnt["cT"])
                    chv(eng, eng.tensor_mul(TMPA[:, (t % 3) * 512:(t % 3 + 1) * 512],
                                            pb[4 + t % 3], vrepA[:, t * 512:(t + 1) * 512]))
                    eng.drain().then_inc(cV, 1)
                p.V(v_mu); bump("cV")
                def t_ac(eng, c=c, t=t):
                    eng.wait_ge(cV, p.cnt["cV"])
                    eng.matmul(pb[7][:, :], IDENB,
                               TMPA[:, (t % 3) * 512:(t % 3 + 1) * 512],
                               start=(t == 0), stop=(t == 15)).then_inc(cT, 1)
                p.T(t_ac); bump("cT")
            def v_ev(eng, sb=sb, c=c):
                eng.wait_ge(cT, p.cnt["cT"])
                eng.wait_ge(dA, max(p.cnt["dA"] - 1, 0))
                chv(eng, eng.tensor_copy(ctxc[:, (c % 2) * 512:(c % 2 + 1) * 512], pb[7]))
                eng.drain().then_inc(cV, 1)
            p.V(v_ev); bump("cV")
            def a_cw2(eng, sb=sb, c=c):
                eng.wait_ge(cV, p.cnt["cV"])
                eng.dma_start(out=ctxT_d[c * 128:(c + 1) * 128, sb * 512:(sb + 1) * 512],
                              in_=ctxc[:, (c % 2) * 512:(c % 2 + 1) * 512]).then_inc(dA, 16)
            p.A(a_cw2); bump("dA", 16)

    for tci in range(SI):
        def t_r(eng, tci=tci):
            if tci == 0:
                eng.wait_ge(cV, p.cnt["cV"])
            eng.wait_ge(cV, p.cnt["cV"])
            eng.transpose(pb[6][:, 0:16], SF[:, tci * 128:(tci + 1) * 128], IDENF[0:16, 0:16]).then_inc(cT, 1)
        p.T(t_r); bump("cT")
        def v_r(eng, tci=tci):
            eng.wait_ge(cT, p.cnt["cT"])
            chv(eng, eng.reciprocal(RALL[:, tci * 16:(tci + 1) * 16], pb[6][:, 0:16]))
            eng.drain().then_inc(cV, 1)
        p.V(v_r); bump("cV")

    join_all()
    p.close_scope("hmx")
    p.close_scope("attB")

    cblk = p.sbuf("cblk", [64, S], F32, scope="mix")
    mbuf = p.sbuf("mbuf", [128, 2 * 64], F32, scope="mix")
    ctok = p.sbuf("ctok", [128, D], F32, scope="mix")
    zoq = p.sbuf("zoq", [128, D], F32, scope="mix")
    ZI3 = p.sbuf("zi3", [128, D], I32, scope="mix")
    xqo16 = p.sbuf("xqo16", [128, D], BF16, scope="mix")

    for ci in range(SI):
        def s_cb(eng, ci=ci):
            eng.wait_ge(dA, p.cnt["dA"])
            eng.wait_ge(cT, p.cnt["cT"])
            eng.dma_start(out=cblk, in_=ctxT_d[ci * 64:(ci + 1) * 64, :]).then_inc(dS, 16)
        p.S(s_cb); bump("dS", 16)
        for tci in range(SI):
            def t_mt(eng, ci=ci, tci=tci):
                if tci == 0:
                    eng.wait_ge(dS, p.cnt["dS"])
                eng.wait_ge(cV, max(p.cnt["cV"] - 1, 1))
                eng.transpose(pb[6 + tci % 2][:, 0:64],
                              cblk[:, tci * 128:(tci + 1) * 128],
                              IDENF[0:64, 0:64]).then_inc(cT, 1)
            p.T(t_mt); bump("cT")
            def v_mt(eng, ci=ci, tci=tci):
                eng.wait_ge(cT, p.cnt["cT"])
                eng.wait_ge(dA, max(p.cnt["dA"] - 1, 0))
                chv(eng, eng.tensor_scalar_mul(mbuf[:, (tci % 2) * 64:(tci % 2 + 1) * 64],
                                               pb[6 + tci % 2][:, 0:64],
                                               RALL[:, tci * 16 + ci:tci * 16 + ci + 1]))
                eng.drain().then_inc(cV, 1)
            p.V(v_mt); bump("cV")
            def a_mw(eng, ci=ci, tci=tci):
                eng.wait_ge(cV, p.cnt["cV"])
                eng.dma_start(out=mix_d[tci * 128:(tci + 1) * 128, :],
                              in_=mbuf[:, (tci % 2) * 64:(tci % 2 + 1) * 64]).then_inc(dA, 16)
            p.A(a_mw); bump("dA", 16)
        def s_mr(eng, ci=ci):
            eng.wait_ge(dA, p.cnt["dA"])
            eng.wait_ge(cV, p.cnt["cV"])
            eng.dma_start(out=ctok.rearrange("p (j dh) -> p j dh", dh=64),
                          in_=mix_d.rearrange("(pp j) dh -> pp j dh", j=16)).then_inc(dS, 16)
        p.S(s_mr); bump("dS", 16)
        c0 = tv[:, 80:81]; c1 = tv[:, 81:82]; c2 = tv[:, 82:83]; c3 = tv[:, 83:84]
        uo = UO[:, ci:ci + 1]
        def v_cq(eng, ci=ci, c0=c0, c1=c1):
            eng.wait_ge(dS, p.cnt["dS"])
            eng.wait_ge(dA, p.cnt["dA"])
            chv(eng, eng.scalar_tensor_tensor(zoq, ctok, 1.0, ctok, ALU.mult, ALU.mult))
            chv(eng, eng.reduce_sum(c0, zoq, axis=AX.X))
            chv(eng, eng.reduce_max(c1, ctok, axis=AX.X, apply_absolute_value=True))
            chv(eng, eng.tensor_scalar_mul(c0, c0, 1.0 / D))
            eng.drain().then_inc(cV, 1)
        p.V(v_cq); bump("cV")
        def a_cq(eng, c0=c0, c2=c2):
            eng.wait_ge(cV, p.cnt["cV"])
            cha(eng, eng.activation(c2, c0, AF.Sqrt, bias=EPSC2[:, 1:2], scale=1.0))
            eng.drain().then_inc(cA, 1)
        p.A(a_cq); bump("cA")
        def v_cq2(eng, ci=ci, c1=c1, c2=c2, c3=c3, uo=uo):
            eng.wait_ge(cA, p.cnt["cA"])
            chv(eng, eng.reciprocal(c2, c2))
            chv(eng, eng.tensor_mul(c3, c2, c1))
            chv(eng, eng.tensor_scalar_max(c3, c3, EPS_Q))
            chv(eng, eng.tensor_scalar_mul(uo, c3, 1.0 / 127.0))
            chv(eng, eng.reciprocal(c3, c3))
            chv(eng, eng.scalar_tensor_tensor(c2, c2, 127.0, c3, ALU.mult, ALU.mult))
            chv(eng, eng.tensor_scalar_mul(zoq, ctok, c2))
            chv(eng, eng.tensor_copy(ZI3, zoq))
            chv(eng, eng.tensor_scalar_mul(xqo16, ZI3, uo))
            eng.drain().then_inc(cV, 1)
        p.V(v_cq2); bump("cV")
        def a_cw(eng, ci=ci):
            eng.wait_ge(cV, p.cnt["cV"])
            eng.dma_start(out=xqo_d[ci * 128:(ci + 1) * 128, :], in_=xqo16).then_inc(dA, 16)
        p.A(a_cw); bump("dA", 16)

    join_all()
    p.close_scope("mix")

    xqoT = p.sbuf("xqoT", [128, DC * S], BF16, scope="wo")
    woT = p.sbuf("woT", [128, DC * D], BF16, scope="wo")
    xrld = p.sbuf("xrld", [128, D], F32, scope="wo")
    acct = p.sbuf("acct", [128, D], F32, scope="wo")

    def s_wo(eng):
        eng.wait_ge(dA, p.cnt["dA"])
        for dc in range(DC):
            eng.dma_start(out=xqoT[:, dc * S:(dc + 1) * S],
                          in_=xqo_d[:, dc * 128:(dc + 1) * 128], transpose=True).then_inc(dS, 16)
            eng.wait_ge(dS, bump("dS", 16))
        for dc in range(DC):
            eng.dma_start(out=woT[:, dc * D:(dc + 1) * D],
                          in_=w16_d["wo"][:, dc * 128:(dc + 1) * 128],
                          transpose=True).then_inc(dS, 16)
            eng.wait_ge(dS, bump("dS", 16))
    p.S(s_wo)
    wo_dS = p.cnt["dS"]

    for si in range(SI):
        def s_x(eng, si=si):
            eng.wait_ge(cV, p.cnt["cV"])
            eng.dma_start(out=xrld, in_=xb[si * 128:(si + 1) * 128, :]).then_inc(dS, 16)
        p.S(s_x); bump("dS", 16)
        def t_wo(eng, si=si):
            if si == 0:
                eng.wait_ge(dS, wo_dS)
            eng.wait_ge(cV, p.cnt["cV"])
            for dc in range(DC):
                eng.matmul(pb[0], xqoT[:, dc * S + si * 128: dc * S + (si + 1) * 128],
                           woT[:, dc * D: dc * D + 512],
                           start=(dc == 0), stop=(dc == DC - 1))
            for dc in range(DC):
                mm = eng.matmul(pb[1], xqoT[:, dc * S + si * 128: dc * S + (si + 1) * 128],
                                woT[:, dc * D + 512: (dc + 1) * D],
                                start=(dc == 0), stop=(dc == DC - 1))
            mm.then_inc(cT, 1)
        p.T(t_wo); bump("cT")
        def v_wo(eng, si=si):
            eng.wait_ge(cT, p.cnt["cT"])
            eng.wait_ge(dS, p.cnt["dS"])
            eng.wait_ge(dA, p.cnt["dA"])
            eng.scalar_tensor_tensor(acct[:, 0:512], pb[0], GSC[:, 2:3], xrld[:, 0:512],
                                     ALU.mult, ALU.add)
            eng.scalar_tensor_tensor(acct[:, 512:1024], pb[1], GSC[:, 2:3], xrld[:, 512:1024],
                                     ALU.mult, ALU.add)
            chv(eng, eng.engine_nop())
            eng.drain().then_inc(cV, 1)
        p.V(v_wo); bump("cV")
        def a_acc(eng, si=si):
            eng.wait_ge(cV, p.cnt["cV"])
            eng.dma_start(out=acc_d[si * 128:(si + 1) * 128, :], in_=acct).then_inc(dA, 16)
        p.A(a_acc); bump("dA", 16)

    d = dbgout("xqo", [S, D], BF16)
    if d is not None:
        def f_xqo(eng, d=d):
            eng.wait_ge(dA, p.cnt["dA"])
            eng.dma_start(out=d, in_=xqo_d).then_inc(dS, 16)
        p.S(f_xqo); bump("dS", 16)
    d = dbgout("ctxT", [D, S])
    if d is not None:
        def f_ctxT(eng, d=d):
            eng.wait_ge(dA, p.cnt["dA"])
            for k in range(8):
                eng.dma_start(out=d[k * 128:(k + 1) * 128, :],
                              in_=ctxT_d[k * 128:(k + 1) * 128, :]).then_inc(dS, 16)
        p.S(f_ctxT); bump("dS", 16 * 8)
    d = dbgout("sums", [1, NH * S])
    if d is not None:
        def f_sums(eng, d=d):
            eng.wait_ge(dA, p.cnt["dA"])
            eng.dma_start(out=d, in_=sums_d).then_inc(dS, 16)
        p.S(f_sums); bump("dS", 16)
    d = dbgout("accA", [S, D])
    if d is not None:
        def f_accA(eng, d=d):
            eng.wait_ge(dA, p.cnt["dA"])
            eng.dma_start(out=d, in_=acc_d).then_inc(dS, 16)
        p.S(f_accA); bump("dS", 16)
    join_all()
    p.close_scope("wo")

    xgT = p.sbuf("xgT", [128, DC * WSL], BF16, scope="moe")
    rg16 = [p.sbuf(f"rg16_{i}", [128, 4096], BF16, scope="moe") for i in range(2)]
    ru16 = [p.sbuf(f"ru16_{i}", [128, 4096], BF16, scope="moe") for i in range(2)]
    h_e = p.sbuf("h_e", [128, 32 * WSL], BF16, scope="moe")
    rd16 = p.sbuf("rd16", [128, 32 * D], BF16, scope="moe")
    sgt = p.sbuf("sgt", [128, 4 * 512], F32, scope="moe")
    eoutb = p.sbuf("eoutb", [128, D], F32, scope="moe")

    NBW = ((0, 512), (512, 256))

    for e in range(E):
        def s_xg(eng, e=e):
            if e == 0:
                eng.wait_ge(dG, scat_dG)
            eng.wait_ge(cT, p.cnt["cT"])
            for dc in range(DC):
                eng.dma_start(out=xgT[:, dc * WSL:(dc + 1) * WSL],
                              in_=xg16[e * WSL:(e + 1) * WSL, dc * 128:(dc + 1) * 128],
                              transpose=True).then_inc(dS, 16)
                eng.wait_ge(dS, bump("dS", 16))
        p.S(s_xg)
        def g_rd(eng, e=e):
            eng.wait_ge(cT, p.cnt["cT"])
            for ic in range(32):
                eng.dma_start(out=rd16[:, ic * D:(ic + 1) * D],
                              in_=rd_d[e, ic * 128:(ic + 1) * 128, :]).then_inc(dG, 16)
        p.G(g_rd); bump("dG", 16 * 32)
        xg_dS = p.cnt["dS"]

        for icc in range(DC):
            buf = icc % 2
            def g_w(eng, e=e, icc=icc, buf=buf):
                if icc >= 2:
                    eng.wait_ge(cT, p.cnt["cT"])
                eng.dma_start(out=rg16[buf].rearrange("p (c i) -> p c i", i=512),
                              in_=rg_d[e, :, icc * 512:(icc + 1) * 512].rearrange(
                                  "(c p) i -> p c i", p=128)).then_inc(dG, 16)
                eng.dma_start(out=ru16[buf].rearrange("p (c i) -> p c i", i=512),
                              in_=ru_d[e, :, icc * 512:(icc + 1) * 512].rearrange(
                                  "(c p) i -> p c i", p=128)).then_inc(dG, 16)
            p.G(g_w); bump("dG", 32)
            for ib in range(4):
                recs = []
                for nbs in range(2):
                    off, w = NBW[nbs]
                    kk = (icc * 8 + ib * 2 + nbs) % 4
                    first4 = (icc == 0 and ib < 2)
                    def t_gu(eng, e=e, icc=icc, ib=ib, nbs=nbs, buf=buf, kk=kk, off=off, w=w, first4=first4):
                        if ib == 0 and nbs == 0:
                            eng.wait_ge(dG, p.cnt["dG"])
                            if icc == 0:
                                eng.wait_ge(dS, xg_dS)
                        if first4:
                            eng.wait_ge(cV, p.cnt["cV"])
                        elif nbs == 0:
                            eng.wait_ge(cV, max(p.cnt["cV"] - 3, 1))
                        else:
                            eng.wait_ge(cV, max(p.cnt["cV"] - 2, 1))
                        for dc in range(DC):
                            eng.matmul(pb[2 * kk][:, 0:w],
                                       rg16[buf][:, dc * 512 + ib * 128: dc * 512 + (ib + 1) * 128],
                                       xgT[:, dc * WSL + off: dc * WSL + off + w],
                                       start=(dc == 0), stop=(dc == DC - 1))
                        for dc in range(DC):
                            mm = eng.matmul(pb[2 * kk + 1][:, 0:w],
                                            ru16[buf][:, dc * 512 + ib * 128: dc * 512 + (ib + 1) * 128],
                                            xgT[:, dc * WSL + off: dc * WSL + off + w],
                                            start=(dc == 0), stop=(dc == DC - 1))
                        mm.then_inc(cT, 1)
                    p.T(t_gu); bump("cT")
                    recs.append((p.cnt["cT"], kk, off, w))
                cvb = p.cnt["cV"]
                def a_si(eng, recs=recs, cvb=cvb):
                    for j, (ct, kk, off, w) in enumerate(recs):
                        eng.wait_ge(cT, ct)
                        if cvb - 3 + j >= 1:
                            eng.wait_ge(cV, cvb - 3 + j)
                        cha(eng, eng.activation(sgt[:, kk * 512: kk * 512 + w],
                                                pb[2 * kk][:, 0:w], AF.Silu))
                        eng.drain().then_inc(cA, 1)
                p.A(a_si); bump("cA", 2)
                cab = p.cnt["cA"]
                for j, (ct, kk, off, w) in enumerate(recs):
                    def v_h(eng, icc=icc, ib=ib, kk=kk, off=off, w=w, ca=cab - 1 + j):
                        eng.wait_ge(cA, ca)
                        ic32 = icc * 4 + ib
                        eng.tensor_mul(h_e[:, ic32 * WSL + off: ic32 * WSL + off + w],
                                       sgt[:, kk * 512: kk * 512 + w], pb[2 * kk + 1][:, 0:w])
                        chv(eng, eng.engine_nop())
                        eng.drain().then_inc(cV, 1)
                    p.V(v_h); bump("cV")
        for sc in range(WSL // 128):
            def t_dn(eng, e=e, sc=sc):
                if sc == 0:
                    eng.wait_ge(dG, p.cnt["dG"])
                eng.wait_ge(cV, p.cnt["cV"])
                for ic in range(32):
                    eng.matmul(pb[4], h_e[:, ic * WSL + sc * 128: ic * WSL + (sc + 1) * 128],
                               rd16[:, ic * D: ic * D + 512],
                               start=(ic == 0), stop=(ic == 31))
                for ic in range(32):
                    mm = eng.matmul(pb[5], h_e[:, ic * WSL + sc * 128: ic * WSL + (sc + 1) * 128],
                                    rd16[:, ic * D + 512: (ic + 1) * D],
                                    start=(ic == 0), stop=(ic == 31))
                mm.then_inc(cT, 1)
            p.T(t_dn); bump("cT")
            def v_dn(eng):
                eng.wait_ge(cT, p.cnt["cT"])
                eng.wait_ge(dA, p.cnt["dA"])
                eng.tensor_copy(eoutb[:, 0:512], pb[4])
                eng.tensor_copy(eoutb[:, 512:1024], pb[5])
                chv(eng, eng.engine_nop())
                eng.drain().then_inc(cV, 1)
            p.V(v_dn); bump("cV")
            def a_eo(eng, e=e, sc=sc):
                eng.wait_ge(cV, p.cnt["cV"])
                eng.dma_start(out=eout_d[e * WSL + sc * 128: e * WSL + (sc + 1) * 128, :],
                              in_=eoutb).then_inc(dA, 16)
            p.A(a_eo); bump("dA", 16)

    def v_z0(eng):
        eng.wait_ge(dA, p.cnt["dA"])
        eng.memset(eoutb[0:1, :], 0.0)
        chv(eng, eng.engine_nop())
        eng.drain().then_inc(cV, 1)
    p.V(v_z0); bump("cV")
    def a_z0(eng):
        eng.wait_ge(cV, p.cnt["cV"])
        eng.dma_start(out=eout_d[NSLOT:NSLOT + 1, :], in_=eoutb[0:1, :]).then_inc(dA, 16)
    p.A(a_z0); bump("dA", 16)

    join_all()
    p.close_scope("moe")

    xnT16 = p.sbuf("xnT16", [128, DC * S], BF16, scope="shc")
    sg16 = [p.sbuf(f"sg16_{i}", [128, 4096], BF16, scope="shc") for i in range(2)]
    su16 = [p.sbuf(f"su16_{i}", [128, 4096], BF16, scope="shc") for i in range(2)]
    h_sb = p.sbuf("h_sb", [128, 64 * 512], BF16, scope="shc")
    sdb = [p.sbuf(f"sdb_{i}", [128, D], BF16, scope="shc") for i in range(2)]
    shT = p.sbuf("shT", [128, 8 * 512], F32, scope="shc")
    sgt2 = p.sbuf("sgt2", [128, 4 * 512], F32, scope="shc")
    acc4 = p.sbuf("acc4", [128, 4 * D], F32, scope="shc")

    def s_xnt(eng):
        for dc in range(DC):
            eng.dma_start(out=xnT16[:, dc * S:(dc + 1) * S],
                          in_=xnb_d[:, dc * 128:(dc + 1) * 128], transpose=True).then_inc(dS, 16)
            eng.wait_ge(dS, bump("dS", 16))
    p.S(s_xnt)
    xnt_dS = p.cnt["dS"]

    for sb in range(4):
        for icc in range(16):
            buf = icc % 2
            def g_sg(eng, icc=icc, buf=buf):
                if icc >= 2:
                    eng.wait_ge(cT, p.cnt["cT"])
                eng.dma_start(out=sg16[buf], in_=sgb16_d[icc]).then_inc(dG, 16)
                eng.dma_start(out=su16[buf], in_=sub16_d[icc]).then_inc(dG, 16)
            p.G(g_sg); bump("dG", 32)
            for ibp in range(2):
                recs = []
                for jj in range(2):
                    ib = ibp * 2 + jj
                    kk = (icc * 4 + ib) % 4
                    first4 = (icc == 0)
                    def t_gu(eng, sb=sb, icc=icc, ib=ib, buf=buf, kk=kk, jj=jj, first4=first4):
                        if ib == 0:
                            eng.wait_ge(dG, p.cnt["dG"])
                            if icc == 0 and sb == 0:
                                eng.wait_ge(dS, xnt_dS)
                        if first4:
                            eng.wait_ge(cV, p.cnt["cV"])
                        elif jj == 0:
                            eng.wait_ge(cV, max(p.cnt["cV"] - 3, 1))
                        else:
                            eng.wait_ge(cV, max(p.cnt["cV"] - 2, 1))
                        for dc in range(DC):
                            eng.matmul(pb[2 * kk], sg16[buf][:, dc * 512 + ib * 128: dc * 512 + (ib + 1) * 128],
                                       xnT16[:, dc * S + sb * 512: dc * S + (sb + 1) * 512],
                                       start=(dc == 0), stop=(dc == DC - 1))
                        for dc in range(DC):
                            mm = eng.matmul(pb[2 * kk + 1], su16[buf][:, dc * 512 + ib * 128: dc * 512 + (ib + 1) * 128],
                                            xnT16[:, dc * S + sb * 512: dc * S + (sb + 1) * 512],
                                            start=(dc == 0), stop=(dc == DC - 1))
                        mm.then_inc(cT, 1)
                    p.T(t_gu); bump("cT")
                    recs.append((p.cnt["cT"], ib, kk))
                cvb = p.cnt["cV"]
                def a_si(eng, recs=recs, cvb=cvb):
                    for j, (ct, ib, kk) in enumerate(recs):
                        eng.wait_ge(cT, ct)
                        if cvb - 3 + j >= 1:
                            eng.wait_ge(cV, cvb - 3 + j)
                        cha(eng, eng.activation(sgt2[:, kk * 512:(kk + 1) * 512],
                                                pb[2 * kk], AF.Silu))
                        eng.drain().then_inc(cA, 1)
                p.A(a_si); bump("cA", 2)
                cab = p.cnt["cA"]
                for j, (ct, ib, kk) in enumerate(recs):
                    def v_h(eng, icc=icc, ib=ib, kk=kk, ca=cab - 1 + j):
                        eng.wait_ge(cA, ca)
                        ic64 = icc * 4 + ib
                        eng.tensor_mul(h_sb[:, ic64 * 512:(ic64 + 1) * 512],
                                       sgt2[:, kk * 512:(kk + 1) * 512], pb[2 * kk + 1])
                        chv(eng, eng.engine_nop())
                        eng.drain().then_inc(cV, 1)
                    p.V(v_h); bump("cV")
        for ic64 in range(64):
            buf = ic64 % 2
            def g_sd(eng, ic64=ic64, buf=buf):
                if ic64 >= 2:
                    eng.wait_ge(cT, p.cnt["cT"])
                eng.dma_start(out=sdb[buf],
                              in_=sd16_d[ic64 * 128:(ic64 + 1) * 128, :]).then_inc(dG, 16)
            p.G(g_sd); bump("dG", 16)
            def t_dn(eng, sb=sb, ic64=ic64, buf=buf):
                eng.wait_ge(dG, p.cnt["dG"])
                if ic64 == 0:
                    eng.wait_ge(cV, p.cnt["cV"])
                for dm in range(8):
                    mm = eng.matmul(pb[dm], sdb[buf][:, dm * 128:(dm + 1) * 128],
                                    h_sb[:, ic64 * 512:(ic64 + 1) * 512],
                                    start=(ic64 == 0), stop=(ic64 == 63))
                mm.then_inc(cT, 1)
            p.T(t_dn); bump("cT")
        def v_sh(eng):
            eng.wait_ge(cT, p.cnt["cT"])
            for dm in range(8):
                eng.tensor_copy(shT[:, dm * 512:(dm + 1) * 512], pb[dm])
            chv(eng, eng.engine_nop())
            eng.drain().then_inc(cV, 1)
        p.V(v_sh); bump("cV")
        def s_ac(eng, sb=sb):
            eng.wait_ge(cV, p.cnt["cV"])
            eng.wait_ge(dA, p.cnt["dA"])
            for q in range(4):
                eng.dma_start(out=acc4[:, q * D:(q + 1) * D],
                              in_=acc_d[sb * 512 + q * 128: sb * 512 + (q + 1) * 128, :]).then_inc(dS, 16)
        p.S(s_ac); bump("dS", 64)
        for q in range(4):
            for dm in range(8):
                bank = dm % 2
                def t_tr(eng, q=q, dm=dm, bank=bank):
                    if q == 0 and dm == 0:
                        eng.wait_ge(dS, p.cnt["dS"])
                    if dm >= 2 or q > 0:
                        eng.wait_ge(cV, p.cnt["cV"] - 1)
                    eng.transpose(pb[bank][:, 0:128],
                                  shT[:, dm * 512 + q * 128: dm * 512 + (q + 1) * 128],
                                  IDENF).then_inc(cT, 1)
                p.T(t_tr); bump("cT")
                def v_ad(eng, q=q, dm=dm, bank=bank):
                    eng.wait_ge(cT, p.cnt["cT"])
                    a = acc4[:, q * D + dm * 128: q * D + (dm + 1) * 128]
                    eng.tensor_add(a, a, pb[bank][:, 0:128])
                    chv(eng, eng.engine_nop())
                    eng.drain().then_inc(cV, 1)
                p.V(v_ad); bump("cV")
        def a_st(eng, sb=sb):
            eng.wait_ge(cV, p.cnt["cV"])
            for q in range(4):
                eng.dma_start(out=acc_d[sb * 512 + q * 128: sb * 512 + (q + 1) * 128, :],
                              in_=acc4[:, q * D:(q + 1) * D]).then_inc(dA, 16)
        p.A(a_st); bump("dA", 64)

    d = dbgout("accS", [S, D])
    if d is not None:
        def f_accS(eng, d=d):
            eng.wait_ge(dA, p.cnt["dA"])
            eng.dma_start(out=d, in_=acc_d).then_inc(dS, 16)
        p.S(f_accS); bump("dS", 16)
    d = dbgout("eout", [NSLOT, D])
    if d is not None:
        def f_eout(eng, d=d):
            eng.wait_ge(dA, p.cnt["dA"])
            for k in range(8):
                eng.dma_start(out=d[k * WSL:(k + 1) * WSL, :],
                              in_=eout_d[k * WSL:(k + 1) * WSL, :]).then_inc(dS, 16)
        p.S(f_eout); bump("dS", 16 * 8)
    join_all()
    p.close_scope("shc")

    accf = p.sbuf("accf", [128, D], F32, scope="fin")
    g1b = p.sbuf("g1b", [128, D], F32, scope="fin")
    g2b = p.sbuf("g2b", [128, D], F32, scope="fin")

    for ci in range(SI):
        def s_af(eng, ci=ci):
            eng.wait_ge(dS, p.cnt["dS"])
            eng.dma_start(out=accf, in_=acc_d[ci * 128:(ci + 1) * 128, :]).then_inc(dS, 16)
        p.S(s_af); bump("dS", 16)
        def g_g(eng, ci=ci):
            eng.wait_ge(cV, p.cnt["cV"])
            eng.indirect_dma_start(out=g1b, out_offset=None, in_=eout_d,
                                   in_offset=IndirectOffsetOnAxis(sigi(2)[:, ci:ci + 1], 0)).then_inc(dG, 16)
            eng.indirect_dma_start(out=g2b, out_offset=None, in_=eout_d,
                                   in_offset=IndirectOffsetOnAxis(sigi(3)[:, ci:ci + 1], 0)).then_inc(dG, 16)
        p.G(g_g); bump("dG", 32)
        def v_fin(eng, ci=ci):
            eng.wait_ge(dS, p.cnt["dS"])
            eng.wait_ge(dG, p.cnt["dG"])
            eng.scalar_tensor_tensor(accf, g1b, WV[:, ci:ci + 1], accf, ALU.mult, ALU.add)
            eng.scalar_tensor_tensor(accf, g2b, WV[:, 16 + ci:17 + ci], accf, ALU.mult, ALU.add)
            chv(eng, eng.engine_nop())
            eng.drain().then_inc(cV, 1)
        p.V(v_fin); bump("cV")
        def s_out(eng, ci=ci):
            eng.wait_ge(cV, p.cnt["cV"])
            eng.dma_start(out=out[ci * 128:(ci + 1) * 128, :], in_=accf).then_inc(dS, 16)
        p.S(s_out); bump("dS", 16)

    def s_end(eng):
        eng.wait_ge(dS, p.cnt["dS"])
    p.S(s_end)
    p.close_scope("fin")


_NC_CACHE = {}


def _consts():
    iden = np.eye(128, dtype=np.float32)
    sup = np.triu(np.ones((128, 128), np.float32), 1)
    e16 = np.zeros((128, 128), np.float32)
    for c in range(8):
        for pp in range(128):
            m = 2 * c + (1 if pp >= 64 else 0)
            e16[pp, c * 16 + m] = 1.0
    e16t = np.zeros((16, 1024), np.float32)
    for c in range(8):
        for mm in range(128):
            e16t[2 * c + mm // 64, c * 128 + mm] = 1.0
    import ml_dtypes
    return (iden, sup.astype(ml_dtypes.bfloat16), e16.astype(ml_dtypes.bfloat16),
            e16t.astype(ml_dtypes.bfloat16))


def kernel(**inputs):
    import ml_dtypes
    x = np.asarray(inputs["L0_attn_x"], np.float32)
    xflat = np.ascontiguousarray(x.reshape(T, D))
    iden, sup16, e16, e16t = _consts()
    if "nc" not in _NC_CACHE:
        _NC_CACHE["nc"], _NC_CACHE["dbg"] = build()
    nc = _NC_CACHE["nc"]
    in_maps = []
    for c in range(NCORES):
        mask = np.zeros((1, NCORES * E), np.float32)
        mask[0, :c * E] = 1.0
        in_maps.append(dict(
            xb=np.ascontiguousarray(xflat[c * S:(c + 1) * S]),
            xfull=xflat,
            wq=np.asarray(inputs["L0_attn_wq"], np.float32),
            wk=np.asarray(inputs["L0_attn_wk"], np.float32),
            wv=np.asarray(inputs["L0_attn_wv"], np.float32),
            wo=np.asarray(inputs["L0_attn_wo"], np.float32),
            rwin=np.asarray(inputs["L0_ffn_router_w"], np.float32),
            sg=np.asarray(inputs["L0_ffn_sg"], np.float32),
            su=np.asarray(inputs["L0_ffn_su"], np.float32),
            sd=np.asarray(inputs["L0_ffn_sd"], np.float32),
            rg=np.asarray(inputs["L0_ffn_rg"], np.float32),
            ru=np.asarray(inputs["L0_ffn_ru"], np.float32),
            rd=np.asarray(inputs["L0_ffn_rd"], np.float32),
            idenf=iden, sup16=sup16, e16all=e16, e16t=e16t, maskrep=mask,
        ))
    res = run_bass_kernel_spmd(nc, in_maps, core_ids=list(range(NCORES)))
    outs = [res.results[c]["out"] for c in range(NCORES)]
    return np.stack(outs, 0).astype(np.float32)


# version 04
# speedup vs baseline: 1.4151x; 1.2942x over previous
import numpy as np
from contextlib import ExitStack

import concourse.bass as bass
import concourse.mybir as mybir
from concourse.bass import IndirectOffsetOnAxis
from concourse.bass_utils import run_bass_kernel_spmd

F32 = mybir.dt.float32
BF16 = mybir.dt.bfloat16
I32 = mybir.dt.int32
U8 = mybir.dt.uint8
AF = mybir.ActivationFunctionType
ALU = mybir.AluOpType
AX = mybir.AxisListType

NCORES = 8
S = 2048
D = 1024
NH = 16
IS_ = 8192
IR = 4096
E = 8
CAP = 2560
WSL = 768
NSLOT = E * WSL
SI = S // 128
DC = D // 128
T = NCORES * S

EPS_LN = 1e-5
EPS_RMS_ATTN = 1e-6
EPS_RMS_FFN = float(np.finfo(np.float32).eps)
EPS_Q = 1e-5
BIG = 1e30


class P:
    def __init__(self):
        self.nc = bass.Bass("TRN2", target_bir_lowering=False, detect_race_conditions=False)
        self.es = ExitStack()
        self.scopes = {}
        self._sem = {}
        self.cnt = {}
        self.block = None
        self.dbg = {}

    def open_block(self):
        self.block = self.es.enter_context(self.nc.Block())

    def sem(self, name):
        if name not in self._sem:
            self._sem[name] = self.es.enter_context(self.nc.semaphore(name))
            self.cnt[name] = 0
        return self._sem[name]

    def T(self, fn): self.block.tensor(fn)
    def V(self, fn): self.block.vector(fn)
    def A(self, fn): self.block.scalar(fn)
    def G(self, fn): self.block.gpsimd(fn)
    def S(self, fn): self.block.sync(fn)

    def sbuf(self, name, shape, dt, scope=None):
        stack = self.es if scope is None else self.scopes.setdefault(scope, ExitStack())
        return stack.enter_context(self.nc.sbuf_tensor(name, list(shape), dt))[:, :]

    def close_scope(self, scope):
        if scope in self.scopes:
            self.scopes.pop(scope).close()

    def psum(self, name, shape, dt=F32):
        return self.es.enter_context(self.nc.psum_tensor(name, list(shape), dt))[:, :]

    def close(self):
        for s in list(self.scopes):
            self.close_scope(s)
        self.es.close()
        return self.nc


def build(dbg=()):
    dbg = set(dbg)
    p = P()
    nc = p.nc

    xb = nc.dram_tensor("xb", [S, D], F32, kind="ExternalInput").ap()
    xfull = nc.dram_tensor("xfull", [T, D], F32, kind="ExternalInput").ap()
    w_in = {n: nc.dram_tensor(n, [D, D], F32, kind="ExternalInput").ap()
            for n in ("wq", "wk", "wv", "wo")}
    rwin = nc.dram_tensor("rwin", [E, D], F32, kind="ExternalInput").ap()
    sg_d = nc.dram_tensor("sg", [D, IS_], F32, kind="ExternalInput").ap()
    su_d = nc.dram_tensor("su", [D, IS_], F32, kind="ExternalInput").ap()
    sd_d = nc.dram_tensor("sd", [IS_, D], F32, kind="ExternalInput").ap()
    rg_d = nc.dram_tensor("rg", [E, D, IR], F32, kind="ExternalInput").ap()
    ru_d = nc.dram_tensor("ru", [E, D, IR], F32, kind="ExternalInput").ap()
    rd_d = nc.dram_tensor("rd", [E, IR, D], F32, kind="ExternalInput").ap()
    idenf = nc.dram_tensor("idenf", [128, 128], F32, kind="ExternalInput").ap()
    sup16_d = nc.dram_tensor("sup16", [128, 128], BF16, kind="ExternalInput").ap()
    e16all_d = nc.dram_tensor("e16all", [128, 128], BF16, kind="ExternalInput").ap()
    e16t_d = nc.dram_tensor("e16t", [16, 1024], BF16, kind="ExternalInput").ap()
    maskrep_d = nc.dram_tensor("maskrep", [1, NCORES * E], F32, kind="ExternalInput").ap()
    out = nc.dram_tensor("out", [S, D], F32, kind="ExternalOutput").ap()

    xf16 = nc.dram_tensor("xf16", [T, D], BF16).ap()
    xnb_d = nc.dram_tensor("xnb_d", [S, D], BF16).ap()
    xq_d = nc.dram_tensor("xq_d", [S, D], BF16).ap()
    xqo_d = nc.dram_tensor("xqo_d", [S, D], BF16).ap()
    w16_d = {n: nc.dram_tensor(n + "16d", [D, D], BF16).ap()
             for n in ("wq", "wk", "wv", "wo")}
    acc_d = nc.dram_tensor("acc_d", [S, D], F32).ap()
    xg16 = nc.dram_tensor("xg16", [NSLOT + 128, D], BF16).ap()
    eout_d = nc.dram_tensor("eout_d", [NSLOT + 1, D], F32).ap()
    sums_d = nc.dram_tensor("sums_d", [1, NH * S], F32).ap()
    sd16_d = nc.dram_tensor("sd16_d", [IS_, D], BF16).ap()
    sgb16_d = nc.dram_tensor("sgb16_d", [16, 128, 8 * 512], BF16).ap()
    sub16_d = nc.dram_tensor("sub16_d", [16, 128, 8 * 512], BF16).ap()
    ctxT_d = nc.dram_tensor("ctxT_d", [D, S], F32).ap()
    mix_d = nc.dram_tensor("mix_d", [S, 64], F32).ap()

    def dbgout(tag, shape, dt=F32):
        if tag in dbg:
            ap = nc.dram_tensor("dbg_" + tag, list(shape), dt, kind="ExternalOutput").ap()
            p.dbg[tag] = ap
            return ap
        return None

    pb = [p.psum(f"pb{i}", [128, 512]) for i in range(8)]

    IDENF = p.sbuf("idenf_s", [128, 128], F32)
    C16 = p.sbuf("c16", [128, 392], BF16)
    SUP = C16[:, 0:128]
    E16A = C16[:, 128:256]
    ONESF16 = C16[:, 256:384]
    ONE16 = C16[:, 384:385]
    ONEROW = p.sbuf("onerow", [1, 128], BF16)
    E16T = p.sbuf("e16t_s", [16, 1024], BF16)
    IDENB = p.sbuf("idenb", [128, 128], BF16)
    ztile = p.sbuf("ztile_p", [128, D], BF16)
    xnbt = p.sbuf("xnbt_p", [128, D], BF16)

    rowbuf = p.sbuf("rowbuf", [128, 96], F32)
    MEAN = rowbuf[:, 0:16]; RSTD = rowbuf[:, 16:32]; RS2R = rowbuf[:, 32:48]
    UQ = rowbuf[:, 48:64]; UO = rowbuf[:, 64:80]
    RALL = p.sbuf("rall", [128, 256], F32)
    BIASC = p.sbuf("biasc", [128, 16], F32)
    GSC = p.sbuf("gsc", [128, 4], F32)
    SW = p.sbuf("swt", [1, 16], F32)
    CNT = p.sbuf("cntall", [1, NCORES * E + E], F32)
    CB = CNT[:, NCORES * E:NCORES * E + E]
    MSK = p.sbuf("mskrep", [1, NCORES * E], F32)
    CBC = p.sbuf("cbc", [128, E], F32)
    SBC = p.sbuf("sbc", [128, E], F32)
    tv = p.sbuf("tvs", [128, 96], F32)
    EPSC = p.sbuf("epsc", [128, 4], F32)

    LOG = p.sbuf("plLOG", [128, 128], F32)
    F1 = p.sbuf("plF1", [128, 128], F32)
    F2 = p.sbuf("plF2", [128, 128], F32)
    L2 = p.sbuf("plL2", [128, 128], F32)
    LSR = p.sbuf("plLSR", [128, 128], F32)
    VAL = p.sbuf("plVAL", [128, 128], F32)
    CCM = p.sbuf("plCCM", [128, 128], F32)
    PSC = p.sbuf("plPSC", [128, 16 * 14], F32)
    def ps(i): return PSC[:, 16 * i:16 * (i + 1)]
    PU = p.sbuf("planesu", [128, 16 * 3], U8)
    def pu(i): return PU[:, 16 * i:16 * (i + 1)]
    SIGI = p.sbuf("sigi", [128, 16 * 4], I32)
    def sigi(i): return SIGI[:, 16 * i:16 * (i + 1)]
    WV = p.sbuf("plWV", [128, 32], F32)
    rw_s = p.sbuf("rw_s", [E, D], F32)
    rwT = p.sbuf("rwT", [128, DC * E], F32)
    LH = p.sbuf("lh", [128, DC * 18], BF16)
    rwlo = p.sbuf("rwlo", [128, E], F32)
    SEST = p.sbuf("sest", [E, 128], F32)
    SER16 = p.sbuf("ser16", [1, E], BF16)
    SEL16 = p.sbuf("sel16", [128, 16], BF16)
    CB16 = p.sbuf("cb16", [1, E], BF16)
    ONERF = p.sbuf("onerf", [1, 128], F32)
    ONECF = p.sbuf("onecf", [128, 1], F32)

    dS = p.sem("dS"); dG = p.sem("dG"); dA = p.sem("dA")
    cT = p.sem("cT"); cV = p.sem("cV"); cA = p.sem("cA"); cP = p.sem("cP")
    SEMNAMES = ["dS", "dG", "dA", "cT", "cV", "cA", "cP"]

    def bump(name, k=1):
        p.cnt[name] += k
        return p.cnt[name]

    def join_all():
        for nm, emitfn in (("cT", p.T), ("cV", p.V), ("cA", p.A), ("cP", p.G)):
            bump(nm)
            def f(eng, nm=nm):
                eng.drain().then_inc(p.sem(nm), 1)
            emitfn(f)
        snap = {nm: p.cnt[nm] for nm in SEMNAMES}
        for emitfn in (p.T, p.V, p.A, p.G, p.S):
            def f(eng, snap=snap):
                for nm, v in snap.items():
                    if v:
                        eng.wait_ge(p.sem(nm), v)
            emitfn(f)

    def chv(eng, ins):
        ins.then_inc(p.sem("cW"), 1)
        eng.wait_ge(p.sem("cW"), bump("cW"))

    def cha(eng, ins):
        ins.then_inc(p.sem("cX"), 1)
        eng.wait_ge(p.sem("cX"), bump("cX"))

    p.open_block()

    xn = p.sbuf("xn", [128, SI * D], F32, scope="sx")
    scrA = p.sbuf("scrA", [128, D], F32, scope="sx")
    scrB = p.sbuf("scrB", [128, D], F32, scope="sx")
    ZI = p.sbuf("zi", [128, D], I32, scope="sx")
    logTs = p.sbuf("logTs", [E, 512], F32, scope="sx")
    xnT32 = p.sbuf("xnT32", [128, 512], F32, scope="sx")
    xq16 = p.sbuf("xq16", [128, SI * D], BF16, scope="sq")
    xnb = p.sbuf("xnb", [128, SI * D], BF16, scope="sq")
    def xn_t(ci): return xn[:, ci * D:(ci + 1) * D]
    def xq_t(ci): return xq16[:, ci * D:(ci + 1) * D]
    def xnb_t(ci): return xnb[:, ci * D:(ci + 1) * D]

    def s0_s(eng):
        eng.dma_start(out=IDENF, in_=idenf).then_inc(dS, 16)
        eng.dma_start(out=SUP, in_=sup16_d).then_inc(dS, 16)
        eng.dma_start(out=E16A, in_=e16all_d).then_inc(dS, 16)
        eng.dma_start(out=E16T, in_=e16t_d).then_inc(dS, 16)
        eng.dma_start(out=MSK, in_=maskrep_d).then_inc(dS, 16)
        eng.dma_start(out=rw_s, in_=rwin).then_inc(dS, 16)
        for ci in range(SI):
            eng.dma_start(out=xn_t(ci), in_=xb[ci * 128:(ci + 1) * 128, :]).then_inc(dS, 16)
    bump("dS", 16 * (6 + SI)); p.S(s0_s)
    x_dS = p.cnt["dS"]

    def s0_v(eng):
        eng.memset(ONE16, 1.0)
        eng.memset(ONEROW, 1.0)
        eng.memset(ONESF16, 1.0)
        eng.memset(ps(13), 0.0)
        eng.memset(ps(12), float(NSLOT))
        eng.memset(EPSC[:, 0:1], EPS_LN)
        eng.memset(EPSC[:, 1:2], EPS_RMS_ATTN)
        eng.memset(EPSC[:, 2:3], EPS_RMS_FFN)
        eng.memset(EPSC[:, 3:4], 0.0)
        chv(eng, eng.engine_nop())
        eng.drain().then_inc(cV, 1)
    p.V(s0_v); bump("cV")

    NFC = 32
    rows_per = T // NFC
    def s0_g(eng):
        for k in range(NFC):
            eng.dma_start(out=xf16[k * rows_per:(k + 1) * rows_per, :],
                          in_=xfull[k * rows_per:(k + 1) * rows_per, :]).then_inc(dG, 16)
    bump("dG", 16 * NFC); p.G(s0_g)
    xf16_dG = p.cnt["dG"]

    for ci in range(SI):
        m = MEAN[:, ci:ci + 1]; r = RSTD[:, ci:ci + 1]; uq = UQ[:, ci:ci + 1]
        c0 = tv[:, 0:1]; c1 = tv[:, 1:2]; c2 = tv[:, 2:3]; c3 = tv[:, 3:4]
        c4 = tv[:, 4:5]; c5 = tv[:, 5:6]; c6 = tv[:, 6:7]; c7 = tv[:, 7:8]
        c8 = tv[:, 8:9]; c9 = tv[:, 9:10]; c10 = tv[:, 10:11]; c11 = tv[:, 11:12]

        def v1(eng, ci=ci, m=m, c0=c0, c1=c1, c2=c2):
            if ci == 0:
                eng.wait_ge(dS, x_dS)
            x = xn_t(ci)
            chv(eng, eng.scalar_tensor_tensor(scrA, x, 1.0, x, ALU.mult, ALU.mult))
            chv(eng, eng.reduce_sum(c0, scrA, axis=AX.X))
            chv(eng, eng.reduce_sum(m, x, axis=AX.X))
            chv(eng, eng.tensor_scalar_mul(m, m, 1.0 / D))
            chv(eng, eng.tensor_scalar_mul(c0, c0, 1.0 / D))
            chv(eng, eng.tensor_mul(c1, m, m))
            chv(eng, eng.tensor_sub(c2, c0, c1))
            chv(eng, eng.engine_nop())
            eng.drain().then_inc(cV, 1)
        p.V(v1); bump("cV")

        def a1(eng, c2=c2, c3=c3):
            eng.wait_ge(cV, p.cnt["cV"])
            cha(eng, eng.activation(c3, c2, AF.Sqrt, bias=EPSC[:, 0:1], scale=1.0))
            eng.drain().then_inc(cA, 1)
        p.A(a1); bump("cA")

        def v2(eng, r=r, c3=c3, c4=c4, m=m):
            eng.wait_ge(cA, p.cnt["cA"])
            chv(eng, eng.reciprocal(r, c3))
            chv(eng, eng.scalar_tensor_tensor(c4, m, -1.0, r, ALU.mult, ALU.mult))
            chv(eng, eng.engine_nop())
            eng.drain().then_inc(cV, 1)
        p.V(v2); bump("cV")

        def v3(eng, ci=ci, c4=c4, c5=c5, c6=c6, r=r, c2=c2):
            x = xn_t(ci)
            chv(eng, eng.tensor_scalar(x, x, r, c4, ALU.mult, ALU.add))
            eng.tensor_copy(xnb_t(ci), x)
            chv(eng, eng.reduce_max(c5, x, axis=AX.X, apply_absolute_value=True))
            chv(eng, eng.tensor_mul(c6, c2, r))
            chv(eng, eng.tensor_mul(c6, c6, r))
            eng.drain().then_inc(cV, 1)
        p.V(v3); bump("cV")

        def a3(eng, c6=c6, c7=c7, c8=c8):
            eng.wait_ge(cV, p.cnt["cV"])
            cha(eng, eng.activation(c7, c6, AF.Sqrt, bias=EPSC[:, 1:2], scale=1.0))
            cha(eng, eng.activation(c8, c6, AF.Sqrt, bias=EPSC[:, 2:3], scale=1.0))
            eng.drain().then_inc(cA, 1)
        p.A(a3); bump("cA")

        def v4(eng, ci=ci, c5=c5, c7=c7, c8=c8, c9=c9, c10=c10, c11=c11, uq=uq, r=r):
            eng.wait_ge(cA, p.cnt["cA"])
            x = xn_t(ci)
            chv(eng, eng.reciprocal(c9, c7))
            chv(eng, eng.reciprocal(c10, c8))
            chv(eng, eng.tensor_copy(RS2R[:, ci:ci + 1], c10))
            chv(eng, eng.tensor_mul(c10, c9, c5))
            chv(eng, eng.tensor_scalar_max(c10, c10, EPS_Q))
            chv(eng, eng.tensor_scalar_mul(uq, c10, 1.0 / 127.0))
            chv(eng, eng.reciprocal(c11, c10))
            chv(eng, eng.scalar_tensor_tensor(c9, c9, 127.0, c11, ALU.mult, ALU.mult))
            chv(eng, eng.tensor_scalar_mul(scrB, x, c9))
            chv(eng, eng.tensor_copy(ZI, scrB))
            chv(eng, eng.tensor_scalar_mul(xq_t(ci), ZI, uq))
            chv(eng, eng.engine_nop())
            eng.drain().then_inc(cV, 1)
        p.V(v4); bump("cV")
    ln_cV = p.cnt["cV"]

    def s1_wb(eng):
        eng.wait_ge(cV, ln_cV)
        for ci in range(SI):
            eng.dma_start(out=xnb_d[ci * 128:(ci + 1) * 128, :], in_=xnb_t(ci)).then_inc(dA, 16)
            eng.dma_start(out=xq_d[ci * 128:(ci + 1) * 128, :], in_=xq_t(ci)).then_inc(dA, 16)
    p.A(s1_wb); bump("dA", 16 * 2 * SI)

    d = dbgout("xn", [S, D])
    if d is not None:
        def f(eng, d=d):
            eng.wait_ge(cV, ln_cV)
            for ci in range(SI):
                eng.dma_start(out=d[ci * 128:(ci + 1) * 128, :], in_=xn_t(ci)).then_inc(dA, 16)
        p.A(f); bump("dA", 16 * SI)
    d = dbgout("rowb", [128, 96])
    if d is not None:
        def f_rb(eng, d=d):
            eng.wait_ge(cV, ln_cV)
            eng.dma_start(out=d, in_=rowbuf).then_inc(dA, 16)
        p.A(f_rb); bump("dA", 16)
    d = dbgout("xq", [S, D], BF16)
    if d is not None:
        def f_xq(eng, d=d):
            eng.wait_ge(cV, ln_cV)
            for ci in range(SI):
                eng.dma_start(out=d[ci * 128:(ci + 1) * 128, :], in_=xq_t(ci)).then_inc(dA, 16)
        p.A(f_xq); bump("dA", 16 * SI)

    for dc in range(DC):
        def t_rw(eng, dc=dc):
            if dc == 0:
                eng.wait_ge(dS, x_dS)
            eng.wait_ge(cV, p.cnt["cV"])
            eng.transpose(pb[6][:, 0:E], rw_s[:, dc * 128:(dc + 1) * 128], IDENF[0:E, 0:E]).then_inc(cT, 1)
        p.T(t_rw); bump("cT")
        def v_rw(eng, dc=dc):
            eng.wait_ge(cT, p.cnt["cT"])
            chv(eng, eng.tensor_copy(rwT[:, dc * E:(dc + 1) * E], pb[6][:, 0:E]))
            eng.drain().then_inc(cV, 1)
        p.V(v_rw); bump("cV")

    for nb in range(S // 512):
        for dc in range(DC):
            for q in range(4):
                ci = nb * 4 + q
                bank = 6 + (q % 2)
                def t_tr(eng, ci=ci, dc=dc, bank=bank):
                    eng.wait_ge(cV, p.cnt["cV"])
                    eng.transpose(pb[bank][:, 0:128], xn_t(ci)[:, dc * 128:(dc + 1) * 128],
                                  IDENF).then_inc(cT, 1)
                p.T(t_tr); bump("cT")
                def v_tr(eng, q=q, bank=bank):
                    eng.wait_ge(cT, p.cnt["cT"])
                    chv(eng, eng.tensor_copy(xnT32[:, q * 128:(q + 1) * 128], pb[bank][:, 0:128]))
                    eng.drain().then_inc(cV, 1)
                p.V(v_tr); bump("cV")
            def t_mm(eng, dc=dc):
                eng.wait_ge(cV, p.cnt["cV"])
                eng.matmul(pb[5][0:E, :], rwT[:, dc * E:(dc + 1) * E], xnT32,
                           start=(dc == 0), stop=(dc == DC - 1)).then_inc(cT, 1)
            p.T(t_mm); bump("cT")
        def v_ev(eng, nb=nb):
            eng.wait_ge(cT, p.cnt["cT"])
            chv(eng, eng.tensor_copy(logTs, pb[5][0:E, :]))
            eng.drain().then_inc(cV, 1)
        p.V(v_ev); bump("cV")
        for q in range(4):
            ci = nb * 4 + q
            def t_pl(eng, q=q):
                eng.wait_ge(cV, p.cnt["cV"])
                eng.transpose(pb[6][:, 0:E], logTs[:, q * 128:(q + 1) * 128],
                              IDENF[0:E, 0:E]).then_inc(cT, 1)
            p.T(t_pl); bump("cT")
            def v_pl(eng, ci=ci):
                eng.wait_ge(cT, p.cnt["cT"])
                chv(eng, eng.tensor_scalar_mul(LOG[:, ci::16], pb[6][0:128, 0:E],
                                               RS2R[:, ci:ci + 1]))
                chv(eng, eng.engine_nop())
                eng.drain().then_inc(cV, 1)
            p.V(v_pl); bump("cV")

    d = dbgout("logits", [128, 128])
    if d is not None:
        def f(eng, d=d):
            eng.wait_ge(cV, p.cnt["cV"])
            eng.dma_start(out=d, in_=LOG).then_inc(dA, 16)
        p.A(f); bump("dA", 16)

    def v_lh(eng):
        eng.wait_ge(cV, p.cnt["cV"])
        for dc in range(DC):
            h = LH[:, dc * 18:dc * 18 + 9]
            lo = LH[:, dc * 18 + 9:dc * 18 + 18]
            eng.tensor_copy(h[:, 0:1], ONE16)
            eng.memset(lo[:, 0:1], 0.0)
            chv(eng, eng.tensor_copy(h[:, 1:9], rwT[:, dc * E:(dc + 1) * E]))
            chv(eng, eng.tensor_copy(rwlo, h[:, 1:9]))
            chv(eng, eng.tensor_sub(rwlo, rwT[:, dc * E:(dc + 1) * E], rwlo))
            chv(eng, eng.tensor_copy(lo[:, 1:9], rwlo))
        chv(eng, eng.engine_nop())
        eng.drain().then_inc(cV, 1)
    p.V(v_lh); bump("cV")

    def v_se(eng):
        chv(eng, eng.memset(SEST, 0.0))
        chv(eng, eng.reduce_sum(SEST[:, 0:1], rw_s, axis=AX.X))
        chv(eng, eng.engine_nop())
        eng.drain().then_inc(cV, 1)
    p.V(v_se); bump("cV")
    def t_se(eng):
        eng.wait_ge(cV, p.cnt["cV"])
        eng.transpose(pb[6][:, 0:E], SEST, IDENF[0:E, 0:E]).then_inc(cT, 1)
    p.T(t_se); bump("cT")
    def v_se2(eng):
        eng.wait_ge(cT, p.cnt["cT"])
        chv(eng, eng.tensor_copy(SER16, pb[6][0:1, 0:E]))
        eng.drain().then_inc(cV, 1)
    p.V(v_se2); bump("cV")
    def t_se2(eng):
        eng.wait_ge(cV, p.cnt["cV"])
        eng.matmul(pb[6][:, 0:E], ONEROW, SER16, start=True, stop=True).then_inc(cT, 1)
    p.T(t_se2); bump("cT")
    def v_se3(eng):
        eng.wait_ge(cT, p.cnt["cT"])
        chv(eng, eng.tensor_copy(SBC, pb[6][:, 0:E]))
        eng.drain().then_inc(cV, 1)
    p.V(v_se3); bump("cV")

    xhiT = p.sbuf("xhiT", [128, DC * S], BF16, scope="sF")
    fLb = p.sbuf("flb", [9, S], F32, scope="sF")
    FLO = p.sbuf("flo", [128, SI * 9], F32, scope="sF")
    fsel = p.sbuf("fsel", [128, 16 * 9], F32, scope="sF")
    fsel16 = p.sbuf("fsel16", [128, E], BF16, scope="sF")

    for b in range(NCORES):
        for dc in range(DC):
            def s_tr(eng, b=b, dc=dc):
                if b == 0 and dc == 0:
                    eng.wait_ge(dG, xf16_dG)
                if dc == 0:
                    eng.wait_ge(cV, p.cnt["cV"])
                eng.dma_start(out=xhiT[:, dc * S:(dc + 1) * S],
                              in_=xf16[b * S:(b + 1) * S, dc * 128:(dc + 1) * 128],
                              transpose=True).then_inc(dS, 16)
                eng.wait_ge(dS, bump("dS", 16))
            p.S(s_tr)
        for nb in range(4):
            for dc in range(DC):
                def t_mm(eng, nb=nb, dc=dc):
                    if dc == 0:
                        eng.wait_ge(dS, p.cnt["dS"])
                        eng.wait_ge(cV, p.cnt["cV"])
                    rhs = xhiT[:, dc * S + nb * 512: dc * S + (nb + 1) * 512]
                    eng.matmul(pb[4][0:9, :], LH[:, dc * 18:dc * 18 + 9], rhs,
                               start=(dc == 0), stop=False)
                    eng.matmul(pb[4][0:9, :], LH[:, dc * 18 + 9:dc * 18 + 18], rhs,
                               start=False, stop=(dc == DC - 1)).then_inc(cT, 1)
                p.T(t_mm); bump("cT")
            def v_ev(eng, nb=nb):
                eng.wait_ge(cT, p.cnt["cT"])
                chv(eng, eng.tensor_copy(fLb[:, nb * 512:(nb + 1) * 512], pb[4][0:9, :]))
                eng.drain().then_inc(cV, 1)
            p.V(v_ev); bump("cV")
        for ci in range(SI):
            def t_pl(eng, ci=ci):
                eng.wait_ge(cV, p.cnt["cV"])
                eng.transpose(pb[6][:, 0:9], fLb[:, ci * 128:(ci + 1) * 128], IDENF[0:9, 0:9]).then_inc(cT, 1)
            p.T(t_pl); bump("cT")
            def v_pl(eng, ci=ci):
                eng.wait_ge(cT, p.cnt["cT"])
                chv(eng, eng.tensor_copy(FLO[:, ci * 9:(ci + 1) * 9], pb[6][0:128, 0:9]))
                chv(eng, eng.engine_nop())
                eng.drain().then_inc(cV, 1)
            p.V(v_pl); bump("cV")
        def v_f(eng, b=b):
            eng.wait_ge(cV, p.cnt["cV"])
            eng.wait_ge(cT, p.cnt["cT"])
            mpl = ps(9); m1 = ps(10); m2 = ps(11)
            chv(eng, eng.tensor_scalar_mul(mpl, FLO[:, 0::9], 1.0 / D))
            for e in range(E):
                lf = FLO[:, 1 + e::9]
                chv(eng, eng.tensor_scalar_mul(ps(0), mpl, SBC[:, e:e + 1]))
                chv(eng, eng.tensor_sub(lf, lf, ps(0)))
            chv(eng, eng.tensor_max(m1, FLO[:, 1::9], FLO[:, 2::9]))
            for e in range(2, E):
                chv(eng, eng.tensor_max(m1, m1, FLO[:, 1 + e::9]))
            for e in range(E):
                lf = FLO[:, 1 + e::9]
                sc = fsel[:, e * 16:(e + 1) * 16]
                chv(eng, eng.tensor_tensor(sc, lf, m1, ALU.is_ge))
                chv(eng, eng.scalar_tensor_tensor(sc, sc, -BIG, lf, ALU.mult, ALU.add))
            chv(eng, eng.tensor_max(m2, fsel[:, 0:16], fsel[:, 16:32]))
            for e in range(2, E):
                chv(eng, eng.tensor_max(m2, m2, fsel[:, e * 16:(e + 1) * 16]))
            for e in range(E):
                lf = FLO[:, 1 + e::9]
                sc = fsel[:, e * 16:(e + 1) * 16]
                chv(eng, eng.tensor_tensor(sc, lf, m2, ALU.is_ge))
                chv(eng, eng.reduce_sum(fsel[:, 128 + e:129 + e], sc, axis=AX.X))
            chv(eng, eng.tensor_copy(fsel16, fsel[:, 128:128 + E]))
            chv(eng, eng.engine_nop())
            eng.drain().then_inc(cV, 1)
        p.V(v_f); bump("cV")
        def t_cnt(eng):
            eng.wait_ge(cV, p.cnt["cV"])
            eng.matmul(pb[6][0:1, 0:E], ONE16, fsel16, start=True, stop=True).then_inc(cT, 1)
        p.T(t_cnt); bump("cT")
        def v_cnt(eng, b=b):
            eng.wait_ge(cT, p.cnt["cT"])
            chv(eng, eng.tensor_copy(CNT[:, b * E:(b + 1) * E], pb[6][0:1, 0:E]))
            chv(eng, eng.engine_nop())
            eng.drain().then_inc(cV, 1)
        p.V(v_cnt); bump("cV")

    def v_cb(eng):
        eng.wait_ge(cV, p.cnt["cV"])
        tr = tv[0:1, 32:32 + NCORES * E]
        chv(eng, eng.tensor_mul(tr, CNT[:, 0:NCORES * E], MSK))
        for e in range(E):
            chv(eng, eng.reduce_sum(CB[:, e:e + 1], tr[:, e::E], axis=AX.X))
        chv(eng, eng.tensor_copy(CB16, CB))
        chv(eng, eng.engine_nop())
        eng.drain().then_inc(cV, 1)
    p.V(v_cb); bump("cV")
    def t_cb(eng):
        eng.wait_ge(cV, p.cnt["cV"])
        eng.matmul(pb[6][:, 0:E], ONEROW, CB16, start=True, stop=True).then_inc(cT, 1)
    p.T(t_cb); bump("cT")
    def v_cb2(eng):
        eng.wait_ge(cT, p.cnt["cT"])
        chv(eng, eng.tensor_copy(CBC, pb[6][:, 0:E]))
        eng.drain().then_inc(cV, 1)
    p.V(v_cb2); bump("cV")

    d = dbgout("cnt", [1, NCORES * E + E])
    if d is not None:
        def f(eng, d=d):
            eng.wait_ge(cV, p.cnt["cV"])
            eng.dma_start(out=d, in_=CNT).then_inc(dA, 16)
        p.A(f); bump("dA", 16)

    def Le(e): return LOG[:, e * 16:(e + 1) * 16]

    def v_top2(eng):
        eng.wait_ge(cV, p.cnt["cV"])
        m1 = ps(0); run = ps(11); t = ps(9); t2 = ps(10)
        chv(eng, eng.tensor_max(m1, Le(0), Le(1)))
        for e in range(2, E):
            chv(eng, eng.tensor_max(m1, m1, Le(e)))
        chv(eng, eng.memset(run, 0.0))
        for e in range(E):
            f1 = F1[:, e * 16:(e + 1) * 16]
            chv(eng, eng.tensor_tensor(t, Le(e), m1, ALU.is_ge))
            chv(eng, eng.tensor_mul(t2, t, run))
            chv(eng, eng.tensor_sub(f1, t, t2))
            chv(eng, eng.tensor_max(run, run, t))
        for e in range(E):
            f1 = F1[:, e * 16:(e + 1) * 16]
            l2 = L2[:, e * 16:(e + 1) * 16]
            chv(eng, eng.scalar_tensor_tensor(l2, f1, -BIG, Le(e), ALU.mult, ALU.add))
        m2 = ps(1)
        chv(eng, eng.tensor_max(m2, L2[:, 0:16], L2[:, 16:32]))
        for e in range(2, E):
            chv(eng, eng.tensor_max(m2, m2, L2[:, e * 16:(e + 1) * 16]))
        chv(eng, eng.memset(run, 0.0))
        for e in range(E):
            f2 = F2[:, e * 16:(e + 1) * 16]
            l2 = L2[:, e * 16:(e + 1) * 16]
            chv(eng, eng.tensor_tensor(t, l2, m2, ALU.is_ge))
            chv(eng, eng.tensor_mul(t2, t, run))
            chv(eng, eng.tensor_sub(f2, t, t2))
            chv(eng, eng.tensor_max(run, run, t))
        e1 = ps(2); e2 = ps(3)
        chv(eng, eng.memset(e1, 0.0))
        chv(eng, eng.memset(e2, 0.0))
        for e in range(1, E):
            chv(eng, eng.scalar_tensor_tensor(e1, F1[:, e * 16:(e + 1) * 16], float(e), e1,
                                              ALU.mult, ALU.add))
            chv(eng, eng.scalar_tensor_tensor(e2, F2[:, e * 16:(e + 1) * 16], float(e), e2,
                                              ALU.mult, ALU.add))
        chv(eng, eng.tensor_sub(ps(9), ps(0), ps(1)))
        chv(eng, eng.engine_nop())
        eng.drain().then_inc(cV, 1)
    p.V(v_top2); bump("cV")

    def a_sig(eng):
        eng.wait_ge(cV, p.cnt["cV"])
        cha(eng, eng.activation(ps(4), ps(9), AF.Sigmoid, bias=0.0, scale=1.0))
        eng.drain().then_inc(cA, 1)
    p.A(a_sig); bump("cA")

    for e in range(E):
        def v_rk(eng, e=e):
            if e == 0:
                eng.wait_ge(cV, p.cnt["cV"])
            else:
                eng.wait_ge(cT, p.cnt["cT"])
            sel = VAL[:, e * 16:(e + 1) * 16]
            chv(eng, eng.tensor_add(sel, F1[:, e * 16:(e + 1) * 16], F2[:, e * 16:(e + 1) * 16]))
            chv(eng, eng.tensor_copy(SEL16, sel))
            chv(eng, eng.engine_nop())
            eng.drain().then_inc(cV, 1)
        p.V(v_rk); bump("cV")
        def t_rk(eng, e=e):
            eng.wait_ge(cV, p.cnt["cV"])
            eng.matmul(pb[6][:, 0:16], SUP, SEL16, start=True, stop=True)
            eng.matmul(pb[7][:, 0:16], ONESF16, SEL16, start=True, stop=True).then_inc(cT, 1)
        p.T(t_rk); bump("cT")
        def v_rk2(eng, e=e):
            eng.wait_ge(cT, p.cnt["cT"])
            lsr = LSR[:, e * 16:(e + 1) * 16]
            ccm = CCM[:, e * 16:(e + 1) * 16]
            eng.tensor_copy(lsr, pb[6][:, 0:16])
            eng.tensor_copy(ccm, pb[7][:, 0:16])
            chv(eng, eng.engine_nop())
            eng.drain().then_inc(cV, 1)
        p.V(v_rk2); bump("cV")
    def v_rk3(eng):
        eng.wait_ge(cV, p.cnt["cV"])
        for e in range(E):
            lsr = LSR[:, e * 16:(e + 1) * 16]
            ccm = CCM[:, e * 16:(e + 1) * 16]
            for ci in range(1, SI):
                chv(eng, eng.tensor_add(ccm[:, ci:ci + 1], ccm[:, ci:ci + 1], ccm[:, ci - 1:ci]))
            for ci in range(1, SI):
                chv(eng, eng.tensor_add(lsr[:, ci:ci + 1], lsr[:, ci:ci + 1], ccm[:, ci - 1:ci]))
        chv(eng, eng.engine_nop())
        eng.drain().then_inc(cV, 1)
    p.V(v_rk3); bump("cV")

    def v_sigma(eng):
        eng.wait_ge(cV, p.cnt["cV"])
        eng.wait_ge(cA, p.cnt["cA"])
        t = ps(9); t2 = ps(10)
        for e in range(E):
            sel = VAL[:, e * 16:(e + 1) * 16]
            lsr = LSR[:, e * 16:(e + 1) * 16]
            chv(eng, eng.tensor_scalar_add(t, lsr, CBC[:, e:e + 1]))
            chv(eng, eng.tensor_scalar(t, t, float(CAP), None, ALU.is_lt))
            chv(eng, eng.tensor_mul(sel, sel, t))
        lsr1 = ps(5); lsr2 = ps(6); v1 = ps(7); v2 = ps(8)
        chv(eng, eng.memset(lsr1, 0.0)); chv(eng, eng.memset(lsr2, 0.0))
        chv(eng, eng.memset(v1, 0.0)); chv(eng, eng.memset(v2, 0.0))
        for e in range(E):
            f1 = F1[:, e * 16:(e + 1) * 16]; f2 = F2[:, e * 16:(e + 1) * 16]
            lsr = LSR[:, e * 16:(e + 1) * 16]; val = VAL[:, e * 16:(e + 1) * 16]
            chv(eng, eng.tensor_mul(t, f1, lsr)); chv(eng, eng.tensor_add(lsr1, lsr1, t))
            chv(eng, eng.tensor_mul(t, f2, lsr)); chv(eng, eng.tensor_add(lsr2, lsr2, t))
            chv(eng, eng.tensor_mul(t, f1, val)); chv(eng, eng.tensor_add(v1, v1, t))
            chv(eng, eng.tensor_mul(t, f2, val)); chv(eng, eng.tensor_add(v2, v2, t))
        chv(eng, eng.tensor_mul(WV[:, 0:16], ps(4), v1))
        chv(eng, eng.engine_nop())
        eng.drain().then_inc(cV, 1)
    p.V(v_sigma); bump("cV")

    def v_w2(eng):
        t = ps(9)
        chv(eng, eng.tensor_scalar(t, ps(4), -1.0, 1.0, ALU.mult, ALU.add))
        chv(eng, eng.tensor_mul(WV[:, 16:32], t, ps(8)))
        chv(eng, eng.engine_nop())
        eng.drain().then_inc(cV, 1)
    p.V(v_w2); bump("cV")

    def v_sig2(eng):
        t = ps(9); t2 = ps(10)
        for j, (ee, ll, vv, so, go) in enumerate(
                ((ps(2), ps(5), ps(7), 0, 2), (ps(3), ps(6), ps(8), 1, 3))):
            chv(eng, eng.tensor_scalar_min(t, ll, float(WSL - 1)))
            chv(eng, eng.scalar_tensor_tensor(t, ee, float(WSL), t, ALU.mult, ALU.add))
            chv(eng, eng.tensor_copy(pu(0), vv))
            chv(eng, eng.select(t2, pu(0), t, ps(12)))
            chv(eng, eng.tensor_copy(sigi(so), t2))
            chv(eng, eng.select(t2, pu(0), t, ps(12)))
            chv(eng, eng.tensor_copy(sigi(go), t2))
        chv(eng, eng.engine_nop())
        eng.drain().then_inc(cV, 1)
    p.V(v_sig2); bump("cV")

    d = dbgout("tables", [128, 16 * 8])
    if d is not None:
        def f(eng, d=d):
            eng.wait_ge(cV, p.cnt["cV"])
            eng.dma_start(out=d[:, 0:16], in_=ps(2)).then_inc(dA, 16)
            eng.dma_start(out=d[:, 16:32], in_=ps(3)).then_inc(dA, 16)
            eng.dma_start(out=d[:, 32:48], in_=ps(5)).then_inc(dA, 16)
            eng.dma_start(out=d[:, 48:64], in_=ps(6)).then_inc(dA, 16)
            eng.dma_start(out=d[:, 64:80], in_=ps(7)).then_inc(dA, 16)
            eng.dma_start(out=d[:, 80:96], in_=ps(8)).then_inc(dA, 16)
            eng.dma_start(out=d[:, 96:112], in_=WV[:, 0:16]).then_inc(dA, 16)
            eng.dma_start(out=d[:, 112:128], in_=WV[:, 16:32]).then_inc(dA, 16)
        p.A(f); bump("dA", 16 * 8)

    join_all()
    p.close_scope("sF")
    p.close_scope("sq")
    p.close_scope("sx")

    _emit_rest(p, locals())

    nc_final = p.close()
    return nc_final, p.dbg


def _emit_rest(p, g):
    nc = p.nc
    pb = g["pb"]
    IDENF = g["IDENF"]; ONEROW = g["ONEROW"]; E16A = g["E16A"]; ONE16 = g["ONE16"]
    RALL = g["RALL"]; BIASC = g["BIASC"]; GSC = g["GSC"]; SW = g["SW"]
    UO = g["UO"]; tv = g["tv"]; WV = g["WV"]; EPSC2 = g["EPSC"]
    sigi = g["sigi"]
    w_in = g["w_in"]; w16_d = g["w16_d"]
    xq_d = g["xq_d"]; xqo_d = g["xqo_d"]; xnb_d = g["xnb_d"]; xb = g["xb"]
    acc_d = g["acc_d"]; sums_d = g["sums_d"]; ctxT_d = g["ctxT_d"]
    mix_d = g["mix_d"]; E16T = g["E16T"]; IDENB = g["IDENB"]
    ztile = g["ztile"]; xnbt = g["xnbt"]
    sgb16_d = g["sgb16_d"]; sub16_d = g["sub16_d"]; sd_d = g["sd_d"]
    xg16 = g["xg16"]; eout_d = g["eout_d"]; out = g["out"]
    rg_d = g["rg_d"]; ru_d = g["ru_d"]; rd_d = g["rd_d"]
    sg_d = g["sg_d"]; su_d = g["su_d"]; sd16_d = g["sd16_d"]
    dbgout = g["dbgout"]; bump = g["bump"]; join_all = g["join_all"]; chv = g["chv"]; cha = g["cha"]
    dS, dG, dA, cT, cV, cA, cP = (p.sem(n) for n in
                                  ("dS", "dG", "dA", "cT", "cV", "cA", "cP"))

    def g_z(eng):
        eng.memset(ztile, 0.0).then_inc(p.sem("cP"), 1)
        eng.wait_ge(p.sem("cP"), bump("cP"))
        for r in range(NSLOT // 128 + 1):
            eng.dma_start(out=xg16[r * 128:(r + 1) * 128, :], in_=ztile).then_inc(dG, 16)
    bump("dG", 16 * (NSLOT // 128 + 1)); p.G(g_z)

    def g_cast(eng):
        for k in range(16):
            eng.dma_start(out=sd16_d[k * 512:(k + 1) * 512, :],
                          in_=sd_d[k * 512:(k + 1) * 512, :]).then_inc(dG, 16)
        for icc in range(16):
            eng.dma_start(out=sgb16_d[icc].rearrange("p (c i) -> p c i", i=512),
                          in_=sg_d[:, icc * 512:(icc + 1) * 512].rearrange(
                              "(c p) i -> p c i", p=128)).then_inc(dG, 16)
            eng.dma_start(out=sub16_d[icc].rearrange("p (c i) -> p c i", i=512),
                          in_=su_d[:, icc * 512:(icc + 1) * 512].rearrange(
                              "(c p) i -> p c i", p=128)).then_inc(dG, 16)
    bump("dG", 16 * 48); p.G(g_cast)

    def g_sc(eng):
        eng.wait_ge(dG, p.cnt["dG"])
        eng.wait_ge(dA, p.cnt["dA"])
        eng.wait_ge(cV, p.cnt["cV"])
        for ci in range(SI):
            eng.dma_start(out=xnbt, in_=xnb_d[ci * 128:(ci + 1) * 128, :]).then_inc(dG, 16)
            eng.wait_ge(dG, bump("dG", 16))
            for jj in range(2):
                eng.indirect_dma_start(
                    out=xg16, out_offset=IndirectOffsetOnAxis(sigi(jj)[:, ci:ci + 1], 0),
                    in_=xnbt, in_offset=None).then_inc(dG, 16)
                bump("dG", 16)
            eng.wait_ge(dG, p.cnt["dG"])
    p.G(g_sc)
    scat_dG = p.cnt["dG"]

    w32 = p.sbuf("w32", [128, DC * D], F32, scope="wq")
    qscr = p.sbuf("qscr", [128, D], F32, scope="wq")
    ZI2 = p.sbuf("zi2", [128, D], I32, scope="wq")
    w16b = p.sbuf("w16b", [128, D], BF16, scope="wq")
    ONERF = g["ONERF"]; ONECF = g["ONECF"]
    asum = tv[:, 64:65]
    ared = tv[:, 65:66]
    swcol = tv[:, 66:67]

    def v_onerf(eng):
        eng.memset(ONERF, 1.0)
        eng.memset(ONECF, 1.0)
        eng.tensor_copy(IDENB, IDENF)
        chv(eng, eng.engine_nop())
        eng.drain().then_inc(cV, 1)
    p.V(v_onerf); bump("cV")

    for wi, wn in enumerate(("wq", "wk", "wv", "wo")):
        def s_w(eng, wn=wn):
            eng.wait_ge(cV, p.cnt["cV"])
            for r in range(DC):
                eng.dma_start(out=w32[:, r * D:(r + 1) * D],
                              in_=w_in[wn][r * 128:(r + 1) * 128, :]).then_inc(dS, 16)
        p.S(s_w); bump("dS", 16 * DC)

        def v_as(eng):
            eng.wait_ge(dS, p.cnt["dS"])
            chv(eng, eng.memset(asum, 0.0))
            for r in range(DC):
                chv(eng, eng.reduce_sum(ared, w32[:, r * D:(r + 1) * D], axis=AX.X,
                                        apply_absolute_value=True))
                chv(eng, eng.tensor_add(asum, asum, ared))
            chv(eng, eng.engine_nop())
            eng.drain().then_inc(cV, 1)
        p.V(v_as); bump("cV")

        def t_tot(eng):
            eng.wait_ge(cV, p.cnt["cV"])
            eng.matmul(pb[6][0:1, 0:1], asum, ONECF, start=True, stop=True).then_inc(cT, 1)
        p.T(t_tot); bump("cT")

        def v_sw(eng, wi=wi):
            eng.wait_ge(cT, p.cnt["cT"])
            sws = SW[:, wi:wi + 1]
            chv(eng, eng.tensor_scalar_mul(sws, pb[6][0:1, 0:1], 1.0 / (D * D)))
            chv(eng, eng.tensor_scalar_max(sws, sws, 1e-5))
            chv(eng, eng.reciprocal(sws, sws))
            chv(eng, eng.engine_nop())
            eng.drain().then_inc(cV, 1)
        p.V(v_sw); bump("cV")

        def t_bc(eng, wi=wi):
            eng.wait_ge(cV, p.cnt["cV"])
            eng.matmul(pb[6][:, 0:1], ONERF, SW[:, wi:wi + 1], start=True, stop=True).then_inc(cT, 1)
        p.T(t_bc); bump("cT")

        def v_bc(eng):
            eng.wait_ge(cT, p.cnt["cT"])
            chv(eng, eng.tensor_copy(swcol, pb[6][:, 0:1]))
            eng.drain().then_inc(cV, 1)
        p.V(v_bc); bump("cV")

        for r in range(DC):
            def v_q(eng, r=r):
                if r == 0:
                    eng.wait_ge(cV, p.cnt["cV"])
                if r > 0:
                    eng.wait_ge(dA, p.cnt["dA"])
                chv(eng, eng.tensor_scalar_mul(qscr, w32[:, r * D:(r + 1) * D], swcol))
                chv(eng, eng.tensor_copy(ZI2, qscr))
                chv(eng, eng.tensor_scalar(w16b, ZI2, -1.0, 1.0, ALU.max, ALU.min))
                chv(eng, eng.engine_nop())
                eng.drain().then_inc(cV, 1)
            p.V(v_q); bump("cV")
            def a_wb(eng, wn=wn, r=r):
                eng.wait_ge(cV, p.cnt["cV"])
                eng.dma_start(out=w16_d[wn][r * 128:(r + 1) * 128, :], in_=w16b).then_inc(dA, 16)
            p.A(a_wb); bump("dA", 16)

    def v_gs(eng):
        eng.wait_ge(dA, p.cnt["dA"])
        a = tv[0:1, 70:71]; b = tv[0:1, 71:72]
        chv(eng, eng.tensor_mul(a, SW[:, 0:1], SW[:, 1:2]))
        chv(eng, eng.tensor_scalar_mul(a, a, 8.0))
        chv(eng, eng.reciprocal(a, a))
        chv(eng, eng.tensor_copy(tv[0:1, 72:73], a))
        chv(eng, eng.reciprocal(b, SW[:, 2:3]))
        chv(eng, eng.tensor_copy(tv[0:1, 73:74], b))
        chv(eng, eng.reciprocal(b, SW[:, 3:4]))
        chv(eng, eng.tensor_copy(tv[0:1, 74:75], b))
        chv(eng, eng.engine_nop())
        eng.drain().then_inc(cV, 1)
    p.V(v_gs); bump("cV")
    for i in range(3):
        def t_g(eng, i=i):
            eng.wait_ge(cV, p.cnt["cV"])
            eng.matmul(pb[6][:, 0:1], ONERF, tv[0:1, 72 + i:73 + i], start=True, stop=True).then_inc(cT, 1)
        p.T(t_g); bump("cT")
        def v_g(eng, i=i):
            eng.wait_ge(cT, p.cnt["cT"])
            chv(eng, eng.tensor_copy(GSC[:, i:i + 1], pb[6][:, 0:1]))
            eng.drain().then_inc(cV, 1)
        p.V(v_g); bump("cV")

    join_all()
    p.close_scope("wq")

    qT = p.sbuf("qT", [128, DC * S], BF16, scope="attB")
    kT = p.sbuf("kT", [128, DC * S], BF16, scope="attB")
    vT = p.sbuf("vT", [128, DC * S], BF16, scope="attB")
    SF = p.sbuf("sfull", [16, S], F32, scope="attB")
    xqT = p.sbuf("xqT", [128, DC * S], BF16, scope="attW")
    wqT = p.sbuf("wqT", [128, DC * D], BF16, scope="attW")
    wkT = p.sbuf("wkT", [128, DC * D], BF16, scope="attW")
    wvT = p.sbuf("wvT", [128, DC * D], BF16, scope="attW")

    def s_tp(eng):
        eng.wait_ge(dA, p.cnt["dA"])
        for dc in range(DC):
            eng.dma_start(out=xqT[:, dc * S:(dc + 1) * S],
                          in_=xq_d[:, dc * 128:(dc + 1) * 128], transpose=True).then_inc(dS, 16)
            eng.wait_ge(dS, bump("dS", 16))
        for wbuf, wn in ((wqT, "wq"), (wkT, "wk"), (wvT, "wv")):
            for dc in range(DC):
                eng.dma_start(out=wbuf[:, dc * D:(dc + 1) * D],
                              in_=w16_d[wn][:, dc * 128:(dc + 1) * 128],
                              transpose=True).then_inc(dS, 16)
                eng.wait_ge(dS, bump("dS", 16))
    p.S(s_tp)
    tp_dS = p.cnt["dS"]

    for which, wbuf, dst, gcol in (("q", wqT, qT, None), ("k", wkT, kT, 0), ("v", wvT, vT, 1)):
        for ec in range(DC):
            for nb in range(4):
                bank = (nb % 2) * 2 + (0 if which != "k" else 1)
                def t_p(eng, ec=ec, nb=nb, wbuf=wbuf, bank=bank):
                    if ec == 0 and nb == 0:
                        eng.wait_ge(dS, tp_dS)
                    eng.wait_ge(cV, max(p.cnt["cV"] - 1, 1))
                    for dc in range(DC):
                        mm = eng.matmul(pb[bank], wbuf[:, dc * D + ec * 128: dc * D + (ec + 1) * 128],
                                        xqT[:, dc * S + nb * 512: dc * S + (nb + 1) * 512],
                                        start=(dc == 0), stop=(dc == DC - 1))
                    mm.then_inc(cT, 1)
                p.T(t_p); bump("cT")
                def v_p(eng, ec=ec, nb=nb, dst=dst, gcol=gcol, bank=bank):
                    eng.wait_ge(cT, p.cnt["cT"])
                    o = dst[:, ec * S + nb * 512: ec * S + (nb + 1) * 512]
                    if gcol is None:
                        chv(eng, eng.tensor_copy(o, pb[bank]))
                    else:
                        chv(eng, eng.tensor_scalar_mul(o, pb[bank], GSC[:, gcol:gcol + 1]))
                    eng.drain().then_inc(cV, 1)
                p.V(v_p); bump("cV")

    join_all()
    p.close_scope("attW")

    krep = p.sbuf("krep", [128, 512], BF16, scope="hmx")
    vrepA = p.sbuf("vrepa", [128, 16 * 512], BF16, scope="hmx")
    PROD = p.sbuf("prod", [128, 2 * 512], BF16, scope="hmx")
    SCt = [p.sbuf(f"sct{t}", [16, 512], F32, scope="hmx") for t in range(16)]
    EBt = [p.sbuf(f"ebt{t}", [16, 512], BF16, scope="hmx") for t in range(16)]
    MX = p.sbuf("mx", [16, 512], F32, scope="hmx")
    TMPA = p.sbuf("tmpa", [128, 3 * 512], BF16, scope="hmx")
    ctxc = p.sbuf("ctxc", [128, 2 * 512], F32, scope="hmx")

    for sb in range(4):
        for t in range(16):
            def v_kr(eng, sb=sb, t=t):
                eng.wait_ge(cT, p.cnt["cT"])
                src_ = kT[:, (t // 2) * S + sb * 512:(t // 2) * S + (sb + 1) * 512]
                base = (t % 2) * 64
                chv(eng, eng.tensor_copy(krep[0:64, :], src_[base:base + 64, :]))
                chv(eng, eng.tensor_copy(krep[64:128, :], src_[base:base + 64, :]))
                eng.drain().then_inc(cV, 1)
            p.V(v_kr); bump("cV")
            for c in range(DC):
                def v_pr(eng, sb=sb, t=t, c=c):
                    if c >= 2:
                        eng.wait_ge(cT, p.cnt["cT"] - 1)
                    chv(eng, eng.tensor_mul(PROD[:, (c % 2) * 512:(c % 2 + 1) * 512],
                                            qT[:, c * S + sb * 512: c * S + (sb + 1) * 512], krep))
                    eng.drain().then_inc(cV, 1)
                p.V(v_pr); bump("cV")
                def t_sc(eng, t=t, c=c):
                    eng.wait_ge(cV, p.cnt["cV"])
                    eng.matmul(pb[4][0:16, :], E16A[:, c * 16:(c + 1) * 16],
                               PROD[:, (c % 2) * 512:(c % 2 + 1) * 512],
                               start=(c == 0), stop=(c == DC - 1)).then_inc(cT, 1)
                p.T(t_sc); bump("cT")
            def v_sc(eng, t=t):
                eng.wait_ge(cT, p.cnt["cT"])
                chv(eng, eng.tensor_copy(SCt[t], pb[4][0:16, :]))
                eng.drain().then_inc(cV, 1)
            p.V(v_sc); bump("cV")
        def v_sm(eng, sb=sb):
            chv(eng, eng.tensor_max(MX, SCt[0], SCt[1]))
            for t in range(2, 16):
                chv(eng, eng.tensor_max(MX, MX, SCt[t]))
            for t in range(16):
                chv(eng, eng.tensor_sub(SCt[t], SCt[t], MX))
            eng.drain().then_inc(cV, 1)
        p.V(v_sm); bump("cV")
        def a_ex(eng, sb=sb):
            eng.wait_ge(cV, p.cnt["cV"])
            for t in range(16):
                cha(eng, eng.activation(SCt[t], SCt[t], AF.Exp))
            eng.drain().then_inc(cA, 1)
        p.A(a_ex); bump("cA")
        def v_ss(eng, sb=sb):
            eng.wait_ge(cA, p.cnt["cA"])
            s_ = SF[:, sb * 512:(sb + 1) * 512]
            chv(eng, eng.tensor_add(s_, SCt[0], SCt[1]))
            for t in range(2, 16):
                chv(eng, eng.tensor_add(s_, s_, SCt[t]))
            for t in range(16):
                chv(eng, eng.tensor_copy(EBt[t], SCt[t]))
            eng.drain().then_inc(cV, 1)
        p.V(v_ss); bump("cV")
        def v_vra(eng, sb=sb):
            for t in range(16):
                src_ = vT[:, (t // 2) * S + sb * 512:(t // 2) * S + (sb + 1) * 512]
                base = (t % 2) * 64
                chv(eng, eng.tensor_copy(vrepA[0:64, t * 512:(t + 1) * 512],
                                         src_[base:base + 64, :]))
                chv(eng, eng.tensor_copy(vrepA[64:128, t * 512:(t + 1) * 512],
                                         src_[base:base + 64, :]))
            eng.drain().then_inc(cV, 1)
        p.V(v_vra); bump("cV")
        for c in range(DC):
            for t in range(16):
                def t_bc(eng, c=c, t=t):
                    if c == 0 and t < 3:
                        eng.wait_ge(cV, p.cnt["cV"])
                    else:
                        eng.wait_ge(cV, max(p.cnt["cV"] - 2, 1))
                    eng.matmul(pb[4 + t % 3][:, :], E16T[:, c * 128:(c + 1) * 128],
                               EBt[t], start=True, stop=True).then_inc(cT, 1)
                p.T(t_bc); bump("cT")
                def v_mu(eng, c=c, t=t):
                    eng.wait_ge(cT, p.cnt["cT"])
                    chv(eng, eng.tensor_mul(TMPA[:, (t % 3) * 512:(t % 3 + 1) * 512],
                                            pb[4 + t % 3], vrepA[:, t * 512:(t + 1) * 512]))
                    eng.drain().then_inc(cV, 1)
                p.V(v_mu); bump("cV")
                def t_ac(eng, c=c, t=t):
                    eng.wait_ge(cV, p.cnt["cV"])
                    eng.matmul(pb[7][:, :], IDENB,
                               TMPA[:, (t % 3) * 512:(t % 3 + 1) * 512],
                               start=(t == 0), stop=(t == 15)).then_inc(cT, 1)
                p.T(t_ac); bump("cT")
            def v_ev(eng, sb=sb, c=c):
                eng.wait_ge(cT, p.cnt["cT"])
                eng.wait_ge(dA, max(p.cnt["dA"] - 1, 0))
                chv(eng, eng.tensor_copy(ctxc[:, (c % 2) * 512:(c % 2 + 1) * 512], pb[7]))
                eng.drain().then_inc(cV, 1)
            p.V(v_ev); bump("cV")
            def a_cw2(eng, sb=sb, c=c):
                eng.wait_ge(cV, p.cnt["cV"])
                eng.dma_start(out=ctxT_d[c * 128:(c + 1) * 128, sb * 512:(sb + 1) * 512],
                              in_=ctxc[:, (c % 2) * 512:(c % 2 + 1) * 512]).then_inc(dA, 16)
            p.A(a_cw2); bump("dA", 16)

    for tci in range(SI):
        def t_r(eng, tci=tci):
            if tci == 0:
                eng.wait_ge(cV, p.cnt["cV"])
            eng.wait_ge(cV, p.cnt["cV"])
            eng.transpose(pb[6][:, 0:16], SF[:, tci * 128:(tci + 1) * 128], IDENF[0:16, 0:16]).then_inc(cT, 1)
        p.T(t_r); bump("cT")
        def v_r(eng, tci=tci):
            eng.wait_ge(cT, p.cnt["cT"])
            chv(eng, eng.reciprocal(RALL[:, tci * 16:(tci + 1) * 16], pb[6][:, 0:16]))
            eng.drain().then_inc(cV, 1)
        p.V(v_r); bump("cV")

    join_all()
    p.close_scope("hmx")
    p.close_scope("attB")

    cblk = p.sbuf("cblk", [64, S], F32, scope="mix")
    mbuf = p.sbuf("mbuf", [128, 2 * 64], F32, scope="mix")
    ctok = p.sbuf("ctok", [128, D], F32, scope="mix")
    zoq = p.sbuf("zoq", [128, D], F32, scope="mix")
    ZI3 = p.sbuf("zi3", [128, D], I32, scope="mix")
    xqo16 = p.sbuf("xqo16", [128, D], BF16, scope="mix")

    for ci in range(SI):
        def s_cb(eng, ci=ci):
            eng.wait_ge(dA, p.cnt["dA"])
            eng.wait_ge(cT, p.cnt["cT"])
            eng.dma_start(out=cblk, in_=ctxT_d[ci * 64:(ci + 1) * 64, :]).then_inc(dS, 16)
        p.S(s_cb); bump("dS", 16)
        for tci in range(SI):
            def t_mt(eng, ci=ci, tci=tci):
                if tci == 0:
                    eng.wait_ge(dS, p.cnt["dS"])
                eng.wait_ge(cV, max(p.cnt["cV"] - 1, 1))
                eng.transpose(pb[6 + tci % 2][:, 0:64],
                              cblk[:, tci * 128:(tci + 1) * 128],
                              IDENF[0:64, 0:64]).then_inc(cT, 1)
            p.T(t_mt); bump("cT")
            def v_mt(eng, ci=ci, tci=tci):
                eng.wait_ge(cT, p.cnt["cT"])
                eng.wait_ge(dA, max(p.cnt["dA"] - 1, 0))
                chv(eng, eng.tensor_scalar_mul(mbuf[:, (tci % 2) * 64:(tci % 2 + 1) * 64],
                                               pb[6 + tci % 2][:, 0:64],
                                               RALL[:, tci * 16 + ci:tci * 16 + ci + 1]))
                eng.drain().then_inc(cV, 1)
            p.V(v_mt); bump("cV")
            def a_mw(eng, ci=ci, tci=tci):
                eng.wait_ge(cV, p.cnt["cV"])
                eng.dma_start(out=mix_d[tci * 128:(tci + 1) * 128, :],
                              in_=mbuf[:, (tci % 2) * 64:(tci % 2 + 1) * 64]).then_inc(dA, 16)
            p.A(a_mw); bump("dA", 16)
        def s_mr(eng, ci=ci):
            eng.wait_ge(dA, p.cnt["dA"])
            eng.wait_ge(cV, p.cnt["cV"])
            eng.dma_start(out=ctok.rearrange("p (j dh) -> p j dh", dh=64),
                          in_=mix_d.rearrange("(pp j) dh -> pp j dh", j=16)).then_inc(dS, 16)
        p.S(s_mr); bump("dS", 16)
        c0 = tv[:, 80:81]; c1 = tv[:, 81:82]; c2 = tv[:, 82:83]; c3 = tv[:, 83:84]
        uo = UO[:, ci:ci + 1]
        def v_cq(eng, ci=ci, c0=c0, c1=c1):
            eng.wait_ge(dS, p.cnt["dS"])
            eng.wait_ge(dA, p.cnt["dA"])
            chv(eng, eng.scalar_tensor_tensor(zoq, ctok, 1.0, ctok, ALU.mult, ALU.mult))
            chv(eng, eng.reduce_sum(c0, zoq, axis=AX.X))
            chv(eng, eng.reduce_max(c1, ctok, axis=AX.X, apply_absolute_value=True))
            chv(eng, eng.tensor_scalar_mul(c0, c0, 1.0 / D))
            eng.drain().then_inc(cV, 1)
        p.V(v_cq); bump("cV")
        def a_cq(eng, c0=c0, c2=c2):
            eng.wait_ge(cV, p.cnt["cV"])
            cha(eng, eng.activation(c2, c0, AF.Sqrt, bias=EPSC2[:, 1:2], scale=1.0))
            eng.drain().then_inc(cA, 1)
        p.A(a_cq); bump("cA")
        def v_cq2(eng, ci=ci, c1=c1, c2=c2, c3=c3, uo=uo):
            eng.wait_ge(cA, p.cnt["cA"])
            chv(eng, eng.reciprocal(c2, c2))
            chv(eng, eng.tensor_mul(c3, c2, c1))
            chv(eng, eng.tensor_scalar_max(c3, c3, EPS_Q))
            chv(eng, eng.tensor_scalar_mul(uo, c3, 1.0 / 127.0))
            chv(eng, eng.reciprocal(c3, c3))
            chv(eng, eng.scalar_tensor_tensor(c2, c2, 127.0, c3, ALU.mult, ALU.mult))
            chv(eng, eng.tensor_scalar_mul(zoq, ctok, c2))
            chv(eng, eng.tensor_copy(ZI3, zoq))
            chv(eng, eng.tensor_scalar_mul(xqo16, ZI3, uo))
            eng.drain().then_inc(cV, 1)
        p.V(v_cq2); bump("cV")
        def a_cw(eng, ci=ci):
            eng.wait_ge(cV, p.cnt["cV"])
            eng.dma_start(out=xqo_d[ci * 128:(ci + 1) * 128, :], in_=xqo16).then_inc(dA, 16)
        p.A(a_cw); bump("dA", 16)

    join_all()
    p.close_scope("mix")

    xqoT = p.sbuf("xqoT", [128, DC * S], BF16, scope="wo")
    woT = p.sbuf("woT", [128, DC * D], BF16, scope="wo")
    xrld = p.sbuf("xrld", [128, D], F32, scope="wo")
    acct = p.sbuf("acct", [128, D], F32, scope="wo")

    def s_wo(eng):
        eng.wait_ge(dA, p.cnt["dA"])
        for dc in range(DC):
            eng.dma_start(out=xqoT[:, dc * S:(dc + 1) * S],
                          in_=xqo_d[:, dc * 128:(dc + 1) * 128], transpose=True).then_inc(dS, 16)
            eng.wait_ge(dS, bump("dS", 16))
        for dc in range(DC):
            eng.dma_start(out=woT[:, dc * D:(dc + 1) * D],
                          in_=w16_d["wo"][:, dc * 128:(dc + 1) * 128],
                          transpose=True).then_inc(dS, 16)
            eng.wait_ge(dS, bump("dS", 16))
    p.S(s_wo)
    wo_dS = p.cnt["dS"]

    for si in range(SI):
        def s_x(eng, si=si):
            eng.wait_ge(cV, p.cnt["cV"])
            eng.dma_start(out=xrld, in_=xb[si * 128:(si + 1) * 128, :]).then_inc(dS, 16)
        p.S(s_x); bump("dS", 16)
        def t_wo(eng, si=si):
            if si == 0:
                eng.wait_ge(dS, wo_dS)
            eng.wait_ge(cV, p.cnt["cV"])
            for dc in range(DC):
                eng.matmul(pb[0], xqoT[:, dc * S + si * 128: dc * S + (si + 1) * 128],
                           woT[:, dc * D: dc * D + 512],
                           start=(dc == 0), stop=(dc == DC - 1))
            for dc in range(DC):
                mm = eng.matmul(pb[1], xqoT[:, dc * S + si * 128: dc * S + (si + 1) * 128],
                                woT[:, dc * D + 512: (dc + 1) * D],
                                start=(dc == 0), stop=(dc == DC - 1))
            mm.then_inc(cT, 1)
        p.T(t_wo); bump("cT")
        def v_wo(eng, si=si):
            eng.wait_ge(cT, p.cnt["cT"])
            eng.wait_ge(dS, p.cnt["dS"])
            eng.wait_ge(dA, p.cnt["dA"])
            eng.scalar_tensor_tensor(acct[:, 0:512], pb[0], GSC[:, 2:3], xrld[:, 0:512],
                                     ALU.mult, ALU.add)
            eng.scalar_tensor_tensor(acct[:, 512:1024], pb[1], GSC[:, 2:3], xrld[:, 512:1024],
                                     ALU.mult, ALU.add)
            chv(eng, eng.engine_nop())
            eng.drain().then_inc(cV, 1)
        p.V(v_wo); bump("cV")
        def a_acc(eng, si=si):
            eng.wait_ge(cV, p.cnt["cV"])
            eng.dma_start(out=acc_d[si * 128:(si + 1) * 128, :], in_=acct).then_inc(dA, 16)
        p.A(a_acc); bump("dA", 16)

    d = dbgout("xqo", [S, D], BF16)
    if d is not None:
        def f_xqo(eng, d=d):
            eng.wait_ge(dA, p.cnt["dA"])
            eng.dma_start(out=d, in_=xqo_d).then_inc(dS, 16)
        p.S(f_xqo); bump("dS", 16)
    d = dbgout("ctxT", [D, S])
    if d is not None:
        def f_ctxT(eng, d=d):
            eng.wait_ge(dA, p.cnt["dA"])
            for k in range(8):
                eng.dma_start(out=d[k * 128:(k + 1) * 128, :],
                              in_=ctxT_d[k * 128:(k + 1) * 128, :]).then_inc(dS, 16)
        p.S(f_ctxT); bump("dS", 16 * 8)
    d = dbgout("sums", [1, NH * S])
    if d is not None:
        def f_sums(eng, d=d):
            eng.wait_ge(dA, p.cnt["dA"])
            eng.dma_start(out=d, in_=sums_d).then_inc(dS, 16)
        p.S(f_sums); bump("dS", 16)
    d = dbgout("accA", [S, D])
    if d is not None:
        def f_accA(eng, d=d):
            eng.wait_ge(dA, p.cnt["dA"])
            eng.dma_start(out=d, in_=acc_d).then_inc(dS, 16)
        p.S(f_accA); bump("dS", 16)
    join_all()
    p.close_scope("wo")

    xgT2 = [p.sbuf(f"xgT{i}", [128, DC * WSL], BF16, scope="moe") for i in range(2)]
    gu_ct = []
    rg16 = [p.sbuf(f"rg16_{i}", [128, 4096], BF16, scope="moe") for i in range(2)]
    ru16 = [p.sbuf(f"ru16_{i}", [128, 4096], BF16, scope="moe") for i in range(2)]
    h_e = p.sbuf("h_e", [128, 32 * WSL], BF16, scope="moe")
    rd16 = p.sbuf("rd16", [128, 32 * D], BF16, scope="moe")
    sgt = p.sbuf("sgt", [128, 4 * 512], F32, scope="moe")
    eoutb = p.sbuf("eoutb", [128, D], F32, scope="moe")

    NBW = ((0, 512), (512, 256))

    for e in range(E):
        xgT = xgT2[e % 2]
        def s_xg(eng, e=e):
            if e == 0:
                eng.wait_ge(dG, scat_dG)
            if e >= 2:
                eng.wait_ge(cT, gu_ct[e - 2])
            for dc in range(DC):
                eng.dma_start(out=xgT2[e % 2][:, dc * WSL:(dc + 1) * WSL],
                              in_=xg16[e * WSL:(e + 1) * WSL, dc * 128:(dc + 1) * 128],
                              transpose=True).then_inc(dS, 16)
                eng.wait_ge(dS, bump("dS", 16))
        p.S(s_xg)
        def g_rd(eng, e=e):
            eng.wait_ge(cT, p.cnt["cT"])
            for ic in range(32):
                eng.dma_start(out=rd16[:, ic * D:(ic + 1) * D],
                              in_=rd_d[e, ic * 128:(ic + 1) * 128, :]).then_inc(dG, 16)
        p.G(g_rd); bump("dG", 16 * 32)
        xg_dS = p.cnt["dS"]

        for icc in range(DC):
            buf = icc % 2
            def g_w(eng, e=e, icc=icc, buf=buf):
                if icc >= 2:
                    eng.wait_ge(cT, max(p.cnt["cT"] - 8, 1))
                eng.dma_start(out=rg16[buf].rearrange("p (c i) -> p c i", i=512),
                              in_=rg_d[e, :, icc * 512:(icc + 1) * 512].rearrange(
                                  "(c p) i -> p c i", p=128)).then_inc(dG, 16)
                eng.dma_start(out=ru16[buf].rearrange("p (c i) -> p c i", i=512),
                              in_=ru_d[e, :, icc * 512:(icc + 1) * 512].rearrange(
                                  "(c p) i -> p c i", p=128)).then_inc(dG, 16)
            p.G(g_w); bump("dG", 32)
            for ib in range(4):
                recs = []
                for nbs in range(2):
                    off, w = NBW[nbs]
                    kk = (icc * 8 + ib * 2 + nbs) % 4
                    first4 = (icc == 0 and ib < 2)
                    def t_gu(eng, e=e, icc=icc, ib=ib, nbs=nbs, buf=buf, kk=kk, off=off, w=w, first4=first4):
                        if ib == 0 and nbs == 0:
                            eng.wait_ge(dG, p.cnt["dG"])
                            if icc == 0:
                                eng.wait_ge(dS, xg_dS)
                        if first4:
                            eng.wait_ge(cV, p.cnt["cV"])
                        elif nbs == 0:
                            eng.wait_ge(cV, max(p.cnt["cV"] - 3, 1))
                        else:
                            eng.wait_ge(cV, max(p.cnt["cV"] - 2, 1))
                        for dc in range(DC):
                            eng.matmul(pb[2 * kk][:, 0:w],
                                       rg16[buf][:, dc * 512 + ib * 128: dc * 512 + (ib + 1) * 128],
                                       xgT[:, dc * WSL + off: dc * WSL + off + w],
                                       start=(dc == 0), stop=(dc == DC - 1))
                        for dc in range(DC):
                            mm = eng.matmul(pb[2 * kk + 1][:, 0:w],
                                            ru16[buf][:, dc * 512 + ib * 128: dc * 512 + (ib + 1) * 128],
                                            xgT[:, dc * WSL + off: dc * WSL + off + w],
                                            start=(dc == 0), stop=(dc == DC - 1))
                        mm.then_inc(cT, 1)
                    p.T(t_gu); bump("cT")
                    recs.append((p.cnt["cT"], kk, off, w))
                cvb = p.cnt["cV"]
                def a_si(eng, recs=recs, cvb=cvb):
                    for j, (ct, kk, off, w) in enumerate(recs):
                        eng.wait_ge(cT, ct)
                        if cvb - 3 + j >= 1:
                            eng.wait_ge(cV, cvb - 3 + j)
                        cha(eng, eng.activation(sgt[:, kk * 512: kk * 512 + w],
                                                pb[2 * kk][:, 0:w], AF.Silu))
                        eng.drain().then_inc(cA, 1)
                p.A(a_si); bump("cA", 2)
                cab = p.cnt["cA"]
                for j, (ct, kk, off, w) in enumerate(recs):
                    def v_h(eng, icc=icc, ib=ib, kk=kk, off=off, w=w, ca=cab - 1 + j):
                        eng.wait_ge(cA, ca)
                        ic32 = icc * 4 + ib
                        eng.tensor_mul(h_e[:, ic32 * WSL + off: ic32 * WSL + off + w],
                                       sgt[:, kk * 512: kk * 512 + w], pb[2 * kk + 1][:, 0:w])
                        chv(eng, eng.engine_nop())
                        eng.drain().then_inc(cV, 1)
                    p.V(v_h); bump("cV")
        gu_ct.append(p.cnt["cT"])
        for sc in range(WSL // 128):
            def t_dn(eng, e=e, sc=sc):
                if sc == 0:
                    eng.wait_ge(dG, p.cnt["dG"])
                eng.wait_ge(cV, p.cnt["cV"])
                for ic in range(32):
                    eng.matmul(pb[4], h_e[:, ic * WSL + sc * 128: ic * WSL + (sc + 1) * 128],
                               rd16[:, ic * D: ic * D + 512],
                               start=(ic == 0), stop=(ic == 31))
                for ic in range(32):
                    mm = eng.matmul(pb[5], h_e[:, ic * WSL + sc * 128: ic * WSL + (sc + 1) * 128],
                                    rd16[:, ic * D + 512: (ic + 1) * D],
                                    start=(ic == 0), stop=(ic == 31))
                mm.then_inc(cT, 1)
            p.T(t_dn); bump("cT")
            def v_dn(eng):
                eng.wait_ge(cT, p.cnt["cT"])
                eng.wait_ge(dA, p.cnt["dA"])
                eng.tensor_copy(eoutb[:, 0:512], pb[4])
                eng.tensor_copy(eoutb[:, 512:1024], pb[5])
                chv(eng, eng.engine_nop())
                eng.drain().then_inc(cV, 1)
            p.V(v_dn); bump("cV")
            def a_eo(eng, e=e, sc=sc):
                eng.wait_ge(cV, p.cnt["cV"])
                eng.dma_start(out=eout_d[e * WSL + sc * 128: e * WSL + (sc + 1) * 128, :],
                              in_=eoutb).then_inc(dA, 16)
            p.A(a_eo); bump("dA", 16)

    def v_z0(eng):
        eng.wait_ge(dA, p.cnt["dA"])
        eng.memset(eoutb[0:1, :], 0.0)
        chv(eng, eng.engine_nop())
        eng.drain().then_inc(cV, 1)
    p.V(v_z0); bump("cV")
    def a_z0(eng):
        eng.wait_ge(cV, p.cnt["cV"])
        eng.dma_start(out=eout_d[NSLOT:NSLOT + 1, :], in_=eoutb[0:1, :]).then_inc(dA, 16)
    p.A(a_z0); bump("dA", 16)

    join_all()
    p.close_scope("moe")

    xnT16 = p.sbuf("xnT16", [128, DC * S], BF16, scope="shc")
    sg16 = [p.sbuf(f"sg16_{i}", [128, 4096], BF16, scope="shc") for i in range(2)]
    su16 = [p.sbuf(f"su16_{i}", [128, 4096], BF16, scope="shc") for i in range(2)]
    h_sb = p.sbuf("h_sb", [128, 64 * 512], BF16, scope="shc")
    sdb = [p.sbuf(f"sdb_{i}", [128, D], BF16, scope="shc") for i in range(4)]
    shT = p.sbuf("shT", [128, 8 * 512], F32, scope="shc")
    sgt2 = p.sbuf("sgt2", [128, 4 * 512], F32, scope="shc")
    acc4 = p.sbuf("acc4", [128, 4 * D], F32, scope="shc")

    def s_xnt(eng):
        for dc in range(DC):
            eng.dma_start(out=xnT16[:, dc * S:(dc + 1) * S],
                          in_=xnb_d[:, dc * 128:(dc + 1) * 128], transpose=True).then_inc(dS, 16)
            eng.wait_ge(dS, bump("dS", 16))
    p.S(s_xnt)
    xnt_dS = p.cnt["dS"]

    for sb in range(4):
        for icc in range(16):
            buf = icc % 2
            def g_sg(eng, icc=icc, buf=buf):
                if icc >= 2:
                    eng.wait_ge(cT, max(p.cnt["cT"] - 4, 1))
                eng.dma_start(out=sg16[buf], in_=sgb16_d[icc]).then_inc(dG, 16)
                eng.dma_start(out=su16[buf], in_=sub16_d[icc]).then_inc(dG, 16)
            p.G(g_sg); bump("dG", 32)
            for ibp in range(2):
                recs = []
                for jj in range(2):
                    ib = ibp * 2 + jj
                    kk = (icc * 4 + ib) % 4
                    first4 = (icc == 0)
                    def t_gu(eng, sb=sb, icc=icc, ib=ib, buf=buf, kk=kk, jj=jj, first4=first4):
                        if ib == 0:
                            eng.wait_ge(dG, p.cnt["dG"])
                            if icc == 0 and sb == 0:
                                eng.wait_ge(dS, xnt_dS)
                        if first4:
                            eng.wait_ge(cV, p.cnt["cV"])
                        elif jj == 0:
                            eng.wait_ge(cV, max(p.cnt["cV"] - 3, 1))
                        else:
                            eng.wait_ge(cV, max(p.cnt["cV"] - 2, 1))
                        for dc in range(DC):
                            eng.matmul(pb[2 * kk], sg16[buf][:, dc * 512 + ib * 128: dc * 512 + (ib + 1) * 128],
                                       xnT16[:, dc * S + sb * 512: dc * S + (sb + 1) * 512],
                                       start=(dc == 0), stop=(dc == DC - 1))
                        for dc in range(DC):
                            mm = eng.matmul(pb[2 * kk + 1], su16[buf][:, dc * 512 + ib * 128: dc * 512 + (ib + 1) * 128],
                                            xnT16[:, dc * S + sb * 512: dc * S + (sb + 1) * 512],
                                            start=(dc == 0), stop=(dc == DC - 1))
                        mm.then_inc(cT, 1)
                    p.T(t_gu); bump("cT")
                    recs.append((p.cnt["cT"], ib, kk))
                cvb = p.cnt["cV"]
                def a_si(eng, recs=recs, cvb=cvb):
                    for j, (ct, ib, kk) in enumerate(recs):
                        eng.wait_ge(cT, ct)
                        if cvb - 3 + j >= 1:
                            eng.wait_ge(cV, cvb - 3 + j)
                        cha(eng, eng.activation(sgt2[:, kk * 512:(kk + 1) * 512],
                                                pb[2 * kk], AF.Silu))
                        eng.drain().then_inc(cA, 1)
                p.A(a_si); bump("cA", 2)
                cab = p.cnt["cA"]
                for j, (ct, ib, kk) in enumerate(recs):
                    def v_h(eng, icc=icc, ib=ib, kk=kk, ca=cab - 1 + j):
                        eng.wait_ge(cA, ca)
                        ic64 = icc * 4 + ib
                        eng.tensor_mul(h_sb[:, ic64 * 512:(ic64 + 1) * 512],
                                       sgt2[:, kk * 512:(kk + 1) * 512], pb[2 * kk + 1])
                        chv(eng, eng.engine_nop())
                        eng.drain().then_inc(cV, 1)
                    p.V(v_h); bump("cV")
        for ic64 in range(64):
            buf = ic64 % 4
            def g_sd(eng, ic64=ic64, buf=buf):
                if ic64 >= 4:
                    eng.wait_ge(cT, max(p.cnt["cT"] - 3, 1))
                eng.dma_start(out=sdb[buf],
                              in_=sd16_d[ic64 * 128:(ic64 + 1) * 128, :]).then_inc(dG, 16)
            p.G(g_sd); bump("dG", 16)
            def t_dn(eng, sb=sb, ic64=ic64, buf=buf):
                eng.wait_ge(dG, p.cnt["dG"])
                if ic64 == 0:
                    eng.wait_ge(cV, p.cnt["cV"])
                for dm in range(8):
                    mm = eng.matmul(pb[dm], sdb[buf][:, dm * 128:(dm + 1) * 128],
                                    h_sb[:, ic64 * 512:(ic64 + 1) * 512],
                                    start=(ic64 == 0), stop=(ic64 == 63))
                mm.then_inc(cT, 1)
            p.T(t_dn); bump("cT")
        def v_sh(eng):
            eng.wait_ge(cT, p.cnt["cT"])
            for dm in range(8):
                eng.tensor_copy(shT[:, dm * 512:(dm + 1) * 512], pb[dm])
            chv(eng, eng.engine_nop())
            eng.drain().then_inc(cV, 1)
        p.V(v_sh); bump("cV")
        def s_ac(eng, sb=sb):
            eng.wait_ge(cV, p.cnt["cV"])
            eng.wait_ge(dA, p.cnt["dA"])
            for q in range(4):
                eng.dma_start(out=acc4[:, q * D:(q + 1) * D],
                              in_=acc_d[sb * 512 + q * 128: sb * 512 + (q + 1) * 128, :]).then_inc(dS, 16)
        p.S(s_ac); bump("dS", 64)
        for q in range(4):
            for dm in range(8):
                bank = dm % 2
                def t_tr(eng, q=q, dm=dm, bank=bank):
                    if q == 0 and dm == 0:
                        eng.wait_ge(dS, p.cnt["dS"])
                    if dm >= 2 or q > 0:
                        eng.wait_ge(cV, p.cnt["cV"] - 1)
                    eng.transpose(pb[bank][:, 0:128],
                                  shT[:, dm * 512 + q * 128: dm * 512 + (q + 1) * 128],
                                  IDENF).then_inc(cT, 1)
                p.T(t_tr); bump("cT")
                def v_ad(eng, q=q, dm=dm, bank=bank):
                    eng.wait_ge(cT, p.cnt["cT"])
                    a = acc4[:, q * D + dm * 128: q * D + (dm + 1) * 128]
                    eng.tensor_add(a, a, pb[bank][:, 0:128])
                    chv(eng, eng.engine_nop())
                    eng.drain().then_inc(cV, 1)
                p.V(v_ad); bump("cV")
        def a_st(eng, sb=sb):
            eng.wait_ge(cV, p.cnt["cV"])
            for q in range(4):
                eng.dma_start(out=acc_d[sb * 512 + q * 128: sb * 512 + (q + 1) * 128, :],
                              in_=acc4[:, q * D:(q + 1) * D]).then_inc(dA, 16)
        p.A(a_st); bump("dA", 64)

    d = dbgout("accS", [S, D])
    if d is not None:
        def f_accS(eng, d=d):
            eng.wait_ge(dA, p.cnt["dA"])
            eng.dma_start(out=d, in_=acc_d).then_inc(dS, 16)
        p.S(f_accS); bump("dS", 16)
    d = dbgout("eout", [NSLOT, D])
    if d is not None:
        def f_eout(eng, d=d):
            eng.wait_ge(dA, p.cnt["dA"])
            for k in range(8):
                eng.dma_start(out=d[k * WSL:(k + 1) * WSL, :],
                              in_=eout_d[k * WSL:(k + 1) * WSL, :]).then_inc(dS, 16)
        p.S(f_eout); bump("dS", 16 * 8)
    join_all()
    p.close_scope("shc")

    accf = p.sbuf("accf", [128, D], F32, scope="fin")
    g1b = p.sbuf("g1b", [128, D], F32, scope="fin")
    g2b = p.sbuf("g2b", [128, D], F32, scope="fin")

    for ci in range(SI):
        def s_af(eng, ci=ci):
            eng.wait_ge(dS, p.cnt["dS"])
            eng.dma_start(out=accf, in_=acc_d[ci * 128:(ci + 1) * 128, :]).then_inc(dS, 16)
        p.S(s_af); bump("dS", 16)
        def g_g(eng, ci=ci):
            eng.wait_ge(cV, p.cnt["cV"])
            eng.indirect_dma_start(out=g1b, out_offset=None, in_=eout_d,
                                   in_offset=IndirectOffsetOnAxis(sigi(2)[:, ci:ci + 1], 0)).then_inc(dG, 16)
            eng.indirect_dma_start(out=g2b, out_offset=None, in_=eout_d,
                                   in_offset=IndirectOffsetOnAxis(sigi(3)[:, ci:ci + 1], 0)).then_inc(dG, 16)
        p.G(g_g); bump("dG", 32)
        def v_fin(eng, ci=ci):
            eng.wait_ge(dS, p.cnt["dS"])
            eng.wait_ge(dG, p.cnt["dG"])
            eng.scalar_tensor_tensor(accf, g1b, WV[:, ci:ci + 1], accf, ALU.mult, ALU.add)
            eng.scalar_tensor_tensor(accf, g2b, WV[:, 16 + ci:17 + ci], accf, ALU.mult, ALU.add)
            chv(eng, eng.engine_nop())
            eng.drain().then_inc(cV, 1)
        p.V(v_fin); bump("cV")
        def s_out(eng, ci=ci):
            eng.wait_ge(cV, p.cnt["cV"])
            eng.dma_start(out=out[ci * 128:(ci + 1) * 128, :], in_=accf).then_inc(dS, 16)
        p.S(s_out); bump("dS", 16)

    def s_end(eng):
        eng.wait_ge(dS, p.cnt["dS"])
    p.S(s_end)
    p.close_scope("fin")


_NC_CACHE = {}


def _consts():
    iden = np.eye(128, dtype=np.float32)
    sup = np.triu(np.ones((128, 128), np.float32), 1)
    e16 = np.zeros((128, 128), np.float32)
    for c in range(8):
        for pp in range(128):
            m = 2 * c + (1 if pp >= 64 else 0)
            e16[pp, c * 16 + m] = 1.0
    e16t = np.zeros((16, 1024), np.float32)
    for c in range(8):
        for mm in range(128):
            e16t[2 * c + mm // 64, c * 128 + mm] = 1.0
    import ml_dtypes
    return (iden, sup.astype(ml_dtypes.bfloat16), e16.astype(ml_dtypes.bfloat16),
            e16t.astype(ml_dtypes.bfloat16))


def kernel(**inputs):
    import ml_dtypes
    x = np.asarray(inputs["L0_attn_x"], np.float32)
    xflat = np.ascontiguousarray(x.reshape(T, D))
    iden, sup16, e16, e16t = _consts()
    if "nc" not in _NC_CACHE:
        _NC_CACHE["nc"], _NC_CACHE["dbg"] = build()
    nc = _NC_CACHE["nc"]
    in_maps = []
    for c in range(NCORES):
        mask = np.zeros((1, NCORES * E), np.float32)
        mask[0, :c * E] = 1.0
        in_maps.append(dict(
            xb=np.ascontiguousarray(xflat[c * S:(c + 1) * S]),
            xfull=xflat,
            wq=np.asarray(inputs["L0_attn_wq"], np.float32),
            wk=np.asarray(inputs["L0_attn_wk"], np.float32),
            wv=np.asarray(inputs["L0_attn_wv"], np.float32),
            wo=np.asarray(inputs["L0_attn_wo"], np.float32),
            rwin=np.asarray(inputs["L0_ffn_router_w"], np.float32),
            sg=np.asarray(inputs["L0_ffn_sg"], np.float32),
            su=np.asarray(inputs["L0_ffn_su"], np.float32),
            sd=np.asarray(inputs["L0_ffn_sd"], np.float32),
            rg=np.asarray(inputs["L0_ffn_rg"], np.float32),
            ru=np.asarray(inputs["L0_ffn_ru"], np.float32),
            rd=np.asarray(inputs["L0_ffn_rd"], np.float32),
            idenf=iden, sup16=sup16, e16all=e16, e16t=e16t, maskrep=mask,
        ))
    res = run_bass_kernel_spmd(nc, in_maps, core_ids=list(range(NCORES)))
    outs = [res.results[c]["out"] for c in range(NCORES)]
    return np.stack(outs, 0).astype(np.float32)


# version 05
# speedup vs baseline: 1.4556x; 1.0286x over previous
import numpy as np
from contextlib import ExitStack

import concourse.bass as bass
import concourse.mybir as mybir
from concourse.bass import IndirectOffsetOnAxis
from concourse.bass_utils import run_bass_kernel_spmd

F32 = mybir.dt.float32
BF16 = mybir.dt.bfloat16
I32 = mybir.dt.int32
U8 = mybir.dt.uint8
AF = mybir.ActivationFunctionType
ALU = mybir.AluOpType
AX = mybir.AxisListType

NCORES = 8
S = 2048
D = 1024
NH = 16
IS_ = 8192
IR = 4096
E = 8
CAP = 2560
WSL = 768
NSLOT = E * WSL
SI = S // 128
DC = D // 128
T = NCORES * S

EPS_LN = 1e-5
EPS_RMS_ATTN = 1e-6
EPS_RMS_FFN = float(np.finfo(np.float32).eps)
EPS_Q = 1e-5
BIG = 1e30


class P:
    def __init__(self):
        self.nc = bass.Bass("TRN2", target_bir_lowering=False, detect_race_conditions=False)
        self.es = ExitStack()
        self.scopes = {}
        self._sem = {}
        self.cnt = {}
        self.block = None
        self.dbg = {}

    def open_block(self):
        self.block = self.es.enter_context(self.nc.Block())

    def sem(self, name):
        if name not in self._sem:
            self._sem[name] = self.es.enter_context(self.nc.semaphore(name))
            self.cnt[name] = 0
        return self._sem[name]

    def T(self, fn): self.block.tensor(fn)
    def V(self, fn): self.block.vector(fn)
    def A(self, fn): self.block.scalar(fn)
    def G(self, fn): self.block.gpsimd(fn)
    def S(self, fn): self.block.sync(fn)

    def sbuf(self, name, shape, dt, scope=None):
        stack = self.es if scope is None else self.scopes.setdefault(scope, ExitStack())
        return stack.enter_context(self.nc.sbuf_tensor(name, list(shape), dt))[:, :]

    def close_scope(self, scope):
        if scope in self.scopes:
            self.scopes.pop(scope).close()

    def psum(self, name, shape, dt=F32):
        return self.es.enter_context(self.nc.psum_tensor(name, list(shape), dt))[:, :]

    def close(self):
        for s in list(self.scopes):
            self.close_scope(s)
        self.es.close()
        return self.nc


def build(dbg=()):
    dbg = set(dbg)
    p = P()
    nc = p.nc

    xb = nc.dram_tensor("xb", [S, D], F32, kind="ExternalInput").ap()
    xfull = nc.dram_tensor("xfull", [T, D], F32, kind="ExternalInput").ap()
    w_in = {n: nc.dram_tensor(n, [D, D], F32, kind="ExternalInput").ap()
            for n in ("wq", "wk", "wv", "wo")}
    rwin = nc.dram_tensor("rwin", [E, D], F32, kind="ExternalInput").ap()
    sg_d = nc.dram_tensor("sg", [D, IS_], F32, kind="ExternalInput").ap()
    su_d = nc.dram_tensor("su", [D, IS_], F32, kind="ExternalInput").ap()
    sd_d = nc.dram_tensor("sd", [IS_, D], F32, kind="ExternalInput").ap()
    rg_d = nc.dram_tensor("rg", [E, D, IR], F32, kind="ExternalInput").ap()
    ru_d = nc.dram_tensor("ru", [E, D, IR], F32, kind="ExternalInput").ap()
    rd_d = nc.dram_tensor("rd", [E, IR, D], F32, kind="ExternalInput").ap()
    idenf = nc.dram_tensor("idenf", [128, 128], F32, kind="ExternalInput").ap()
    sup16_d = nc.dram_tensor("sup16", [128, 128], BF16, kind="ExternalInput").ap()
    e16all_d = nc.dram_tensor("e16all", [128, 128], BF16, kind="ExternalInput").ap()
    e16t_d = nc.dram_tensor("e16t", [16, 1024], BF16, kind="ExternalInput").ap()
    maskrep_d = nc.dram_tensor("maskrep", [1, NCORES * E], F32, kind="ExternalInput").ap()
    out = nc.dram_tensor("out", [S, D], F32, kind="ExternalOutput").ap()

    xf16 = nc.dram_tensor("xf16", [T, D], BF16).ap()
    xnb_d = nc.dram_tensor("xnb_d", [S, D], BF16).ap()
    xq_d = nc.dram_tensor("xq_d", [S, D], BF16).ap()
    xqo_d = nc.dram_tensor("xqo_d", [S, D], BF16).ap()
    w16_d = {n: nc.dram_tensor(n + "16d", [D, D], BF16).ap()
             for n in ("wq", "wk", "wv", "wo")}
    acc_d = nc.dram_tensor("acc_d", [S, D], F32).ap()
    xg16 = nc.dram_tensor("xg16", [NSLOT + 128, D], BF16).ap()
    eout_d = nc.dram_tensor("eout_d", [NSLOT + 1, D], F32).ap()
    sums_d = nc.dram_tensor("sums_d", [1, NH * S], F32).ap()
    sd16_d = nc.dram_tensor("sd16_d", [IS_, D], BF16).ap()
    sgb16_d = nc.dram_tensor("sgb16_d", [16, 128, 8 * 512], BF16).ap()
    sub16_d = nc.dram_tensor("sub16_d", [16, 128, 8 * 512], BF16).ap()
    ctxT_d = nc.dram_tensor("ctxT_d", [D, S], F32).ap()
    mix_d = nc.dram_tensor("mix_d", [S, 64], F32).ap()

    def dbgout(tag, shape, dt=F32):
        if tag in dbg:
            ap = nc.dram_tensor("dbg_" + tag, list(shape), dt, kind="ExternalOutput").ap()
            p.dbg[tag] = ap
            return ap
        return None

    pb = [p.psum(f"pb{i}", [128, 512]) for i in range(8)]

    IDENF = p.sbuf("idenf_s", [128, 128], F32)
    C16 = p.sbuf("c16", [128, 392], BF16)
    SUP = C16[:, 0:128]
    E16A = C16[:, 128:256]
    ONESF16 = C16[:, 256:384]
    ONE16 = C16[:, 384:385]
    ONEROW = p.sbuf("onerow", [1, 128], BF16)
    E16T = p.sbuf("e16t_s", [16, 1024], BF16)
    IDENB = p.sbuf("idenb", [128, 128], BF16)
    ztile = p.sbuf("ztile_p", [128, D], BF16)
    xnbt = p.sbuf("xnbt_p", [128, D], BF16)

    rowbuf = p.sbuf("rowbuf", [128, 96], F32)
    MEAN = rowbuf[:, 0:16]; RSTD = rowbuf[:, 16:32]; RS2R = rowbuf[:, 32:48]
    UQ = rowbuf[:, 48:64]; UO = rowbuf[:, 64:80]
    RALL = p.sbuf("rall", [128, 256], F32)
    BIASC = p.sbuf("biasc", [128, 16], F32)
    GSC = p.sbuf("gsc", [128, 4], F32)
    SW = p.sbuf("swt", [1, 16], F32)
    CNT = p.sbuf("cntall", [1, NCORES * E + E], F32)
    CB = CNT[:, NCORES * E:NCORES * E + E]
    MSK = p.sbuf("mskrep", [1, NCORES * E], F32)
    CBC = p.sbuf("cbc", [128, E], F32)
    SBC = p.sbuf("sbc", [128, E], F32)
    tv = p.sbuf("tvs", [128, 96], F32)
    EPSC = p.sbuf("epsc", [128, 4], F32)

    LOG = p.sbuf("plLOG", [128, 128], F32)
    F1 = p.sbuf("plF1", [128, 128], F32)
    F2 = p.sbuf("plF2", [128, 128], F32)
    L2 = p.sbuf("plL2", [128, 128], F32)
    LSR = p.sbuf("plLSR", [128, 128], F32)
    VAL = p.sbuf("plVAL", [128, 128], F32)
    CCM = p.sbuf("plCCM", [128, 128], F32)
    PSC = p.sbuf("plPSC", [128, 16 * 14], F32)
    def ps(i): return PSC[:, 16 * i:16 * (i + 1)]
    PU = p.sbuf("planesu", [128, 16 * 3], U8)
    def pu(i): return PU[:, 16 * i:16 * (i + 1)]
    SIGI = p.sbuf("sigi", [128, 16 * 4], I32)
    def sigi(i): return SIGI[:, 16 * i:16 * (i + 1)]
    WV = p.sbuf("plWV", [128, 32], F32)
    rw_s = p.sbuf("rw_s", [E, D], F32)
    rwT = p.sbuf("rwT", [128, DC * E], F32)
    LH = p.sbuf("lh", [128, DC * 18], BF16)
    rwlo = p.sbuf("rwlo", [128, E], F32)
    SEST = p.sbuf("sest", [E, 128], F32)
    SER16 = p.sbuf("ser16", [1, E], BF16)
    SEL16 = p.sbuf("sel16", [128, 16], BF16)
    CB16 = p.sbuf("cb16", [1, E], BF16)
    ONERF = p.sbuf("onerf", [1, 128], F32)
    ONECF = p.sbuf("onecf", [128, 1], F32)

    dS = p.sem("dS"); dG = p.sem("dG"); dA = p.sem("dA")
    cT = p.sem("cT"); cV = p.sem("cV"); cA = p.sem("cA"); cP = p.sem("cP")
    SEMNAMES = ["dS", "dG", "dA", "cT", "cV", "cA", "cP"]

    def bump(name, k=1):
        p.cnt[name] += k
        return p.cnt[name]

    def join_all():
        for nm, emitfn in (("cT", p.T), ("cV", p.V), ("cA", p.A), ("cP", p.G)):
            bump(nm)
            def f(eng, nm=nm):
                eng.drain().then_inc(p.sem(nm), 1)
            emitfn(f)
        snap = {nm: p.cnt[nm] for nm in SEMNAMES}
        for emitfn in (p.T, p.V, p.A, p.G, p.S):
            def f(eng, snap=snap):
                for nm, v in snap.items():
                    if v:
                        eng.wait_ge(p.sem(nm), v)
            emitfn(f)

    def chv(eng, ins):
        ins.then_inc(p.sem("cW"), 1)
        eng.wait_ge(p.sem("cW"), bump("cW"))

    def cha(eng, ins):
        ins.then_inc(p.sem("cX"), 1)
        eng.wait_ge(p.sem("cX"), bump("cX"))

    p.open_block()

    xn = p.sbuf("xn", [128, SI * D], F32, scope="sx")
    scrA = p.sbuf("scrA", [128, D], F32, scope="sx")
    scrB = p.sbuf("scrB", [128, D], F32, scope="sx")
    ZI = p.sbuf("zi", [128, D], I32, scope="sx")
    logTs = p.sbuf("logTs", [E, 512], F32, scope="sx")
    xnT32 = p.sbuf("xnT32", [128, 512], F32, scope="sx")
    xq16 = p.sbuf("xq16", [128, SI * D], BF16, scope="sq")
    xnb = p.sbuf("xnb", [128, SI * D], BF16, scope="sq")
    def xn_t(ci): return xn[:, ci * D:(ci + 1) * D]
    def xq_t(ci): return xq16[:, ci * D:(ci + 1) * D]
    def xnb_t(ci): return xnb[:, ci * D:(ci + 1) * D]

    def s0_s(eng):
        eng.dma_start(out=IDENF, in_=idenf).then_inc(dS, 16)
        eng.dma_start(out=SUP, in_=sup16_d).then_inc(dS, 16)
        eng.dma_start(out=E16A, in_=e16all_d).then_inc(dS, 16)
        eng.dma_start(out=E16T, in_=e16t_d).then_inc(dS, 16)
        eng.dma_start(out=MSK, in_=maskrep_d).then_inc(dS, 16)
        eng.dma_start(out=rw_s, in_=rwin).then_inc(dS, 16)
        for ci in range(SI):
            eng.dma_start(out=xn_t(ci), in_=xb[ci * 128:(ci + 1) * 128, :]).then_inc(dS, 16)
    bump("dS", 16 * (6 + SI)); p.S(s0_s)
    x_dS = p.cnt["dS"]

    def s0_v(eng):
        eng.memset(ONE16, 1.0)
        eng.memset(ONEROW, 1.0)
        eng.memset(ONESF16, 1.0)
        eng.memset(ps(13), 0.0)
        eng.memset(ps(12), float(NSLOT))
        eng.memset(EPSC[:, 0:1], EPS_LN)
        eng.memset(EPSC[:, 1:2], EPS_RMS_ATTN)
        eng.memset(EPSC[:, 2:3], EPS_RMS_FFN)
        eng.memset(EPSC[:, 3:4], 0.0)
        chv(eng, eng.engine_nop())
        eng.drain().then_inc(cV, 1)
    p.V(s0_v); bump("cV")

    NFC = 32
    rows_per = T // NFC
    def s0_g(eng):
        for k in range(NFC):
            eng.dma_start(out=xf16[k * rows_per:(k + 1) * rows_per, :],
                          in_=xfull[k * rows_per:(k + 1) * rows_per, :]).then_inc(dG, 16)
    bump("dG", 16 * NFC); p.G(s0_g)
    xf16_dG = p.cnt["dG"]

    for ci in range(SI):
        m = MEAN[:, ci:ci + 1]; r = RSTD[:, ci:ci + 1]; uq = UQ[:, ci:ci + 1]
        c0 = tv[:, 0:1]; c1 = tv[:, 1:2]; c2 = tv[:, 2:3]; c3 = tv[:, 3:4]
        c4 = tv[:, 4:5]; c5 = tv[:, 5:6]; c6 = tv[:, 6:7]; c7 = tv[:, 7:8]
        c8 = tv[:, 8:9]; c9 = tv[:, 9:10]; c10 = tv[:, 10:11]; c11 = tv[:, 11:12]

        def v1(eng, ci=ci, m=m, c0=c0, c1=c1, c2=c2):
            if ci == 0:
                eng.wait_ge(dS, x_dS)
            x = xn_t(ci)
            chv(eng, eng.scalar_tensor_tensor(scrA, x, 1.0, x, ALU.mult, ALU.mult))
            chv(eng, eng.reduce_sum(c0, scrA, axis=AX.X))
            chv(eng, eng.reduce_sum(m, x, axis=AX.X))
            chv(eng, eng.tensor_scalar_mul(m, m, 1.0 / D))
            chv(eng, eng.tensor_scalar_mul(c0, c0, 1.0 / D))
            chv(eng, eng.tensor_mul(c1, m, m))
            chv(eng, eng.tensor_sub(c2, c0, c1))
            chv(eng, eng.engine_nop())
            eng.drain().then_inc(cV, 1)
        p.V(v1); bump("cV")

        def a1(eng, c2=c2, c3=c3):
            eng.wait_ge(cV, p.cnt["cV"])
            cha(eng, eng.activation(c3, c2, AF.Sqrt, bias=EPSC[:, 0:1], scale=1.0))
            eng.drain().then_inc(cA, 1)
        p.A(a1); bump("cA")

        def v2(eng, r=r, c3=c3, c4=c4, m=m):
            eng.wait_ge(cA, p.cnt["cA"])
            chv(eng, eng.reciprocal(r, c3))
            chv(eng, eng.scalar_tensor_tensor(c4, m, -1.0, r, ALU.mult, ALU.mult))
            chv(eng, eng.engine_nop())
            eng.drain().then_inc(cV, 1)
        p.V(v2); bump("cV")

        def v3(eng, ci=ci, c4=c4, c5=c5, c6=c6, r=r, c2=c2):
            x = xn_t(ci)
            chv(eng, eng.tensor_scalar(x, x, r, c4, ALU.mult, ALU.add))
            eng.tensor_copy(xnb_t(ci), x)
            chv(eng, eng.reduce_max(c5, x, axis=AX.X, apply_absolute_value=True))
            chv(eng, eng.tensor_mul(c6, c2, r))
            chv(eng, eng.tensor_mul(c6, c6, r))
            eng.drain().then_inc(cV, 1)
        p.V(v3); bump("cV")

        def a3(eng, c6=c6, c7=c7, c8=c8):
            eng.wait_ge(cV, p.cnt["cV"])
            cha(eng, eng.activation(c7, c6, AF.Sqrt, bias=EPSC[:, 1:2], scale=1.0))
            cha(eng, eng.activation(c8, c6, AF.Sqrt, bias=EPSC[:, 2:3], scale=1.0))
            eng.drain().then_inc(cA, 1)
        p.A(a3); bump("cA")

        def v4(eng, ci=ci, c5=c5, c7=c7, c8=c8, c9=c9, c10=c10, c11=c11, uq=uq, r=r):
            eng.wait_ge(cA, p.cnt["cA"])
            x = xn_t(ci)
            chv(eng, eng.reciprocal(c9, c7))
            chv(eng, eng.reciprocal(c10, c8))
            chv(eng, eng.tensor_copy(RS2R[:, ci:ci + 1], c10))
            chv(eng, eng.tensor_mul(c10, c9, c5))
            chv(eng, eng.tensor_scalar_max(c10, c10, EPS_Q))
            chv(eng, eng.tensor_scalar_mul(uq, c10, 1.0 / 127.0))
            chv(eng, eng.reciprocal(c11, c10))
            chv(eng, eng.scalar_tensor_tensor(c9, c9, 127.0, c11, ALU.mult, ALU.mult))
            chv(eng, eng.tensor_scalar_mul(scrB, x, c9))
            chv(eng, eng.tensor_copy(ZI, scrB))
            chv(eng, eng.tensor_scalar_mul(xq_t(ci), ZI, uq))
            chv(eng, eng.engine_nop())
            eng.drain().then_inc(cV, 1)
        p.V(v4); bump("cV")
    ln_cV = p.cnt["cV"]

    def s1_wb(eng):
        eng.wait_ge(cV, ln_cV)
        for ci in range(SI):
            eng.dma_start(out=xnb_d[ci * 128:(ci + 1) * 128, :], in_=xnb_t(ci)).then_inc(dA, 16)
            eng.dma_start(out=xq_d[ci * 128:(ci + 1) * 128, :], in_=xq_t(ci)).then_inc(dA, 16)
    p.A(s1_wb); bump("dA", 16 * 2 * SI)

    d = dbgout("xn", [S, D])
    if d is not None:
        def f(eng, d=d):
            eng.wait_ge(cV, ln_cV)
            for ci in range(SI):
                eng.dma_start(out=d[ci * 128:(ci + 1) * 128, :], in_=xn_t(ci)).then_inc(dA, 16)
        p.A(f); bump("dA", 16 * SI)
    d = dbgout("rowb", [128, 96])
    if d is not None:
        def f_rb(eng, d=d):
            eng.wait_ge(cV, ln_cV)
            eng.dma_start(out=d, in_=rowbuf).then_inc(dA, 16)
        p.A(f_rb); bump("dA", 16)
    d = dbgout("xq", [S, D], BF16)
    if d is not None:
        def f_xq(eng, d=d):
            eng.wait_ge(cV, ln_cV)
            for ci in range(SI):
                eng.dma_start(out=d[ci * 128:(ci + 1) * 128, :], in_=xq_t(ci)).then_inc(dA, 16)
        p.A(f_xq); bump("dA", 16 * SI)

    for dc in range(DC):
        def t_rw(eng, dc=dc):
            if dc == 0:
                eng.wait_ge(dS, x_dS)
            eng.wait_ge(cV, p.cnt["cV"])
            eng.transpose(pb[6][:, 0:E], rw_s[:, dc * 128:(dc + 1) * 128], IDENF[0:E, 0:E]).then_inc(cT, 1)
        p.T(t_rw); bump("cT")
        def v_rw(eng, dc=dc):
            eng.wait_ge(cT, p.cnt["cT"])
            chv(eng, eng.tensor_copy(rwT[:, dc * E:(dc + 1) * E], pb[6][:, 0:E]))
            eng.drain().then_inc(cV, 1)
        p.V(v_rw); bump("cV")

    for nb in range(S // 512):
        for dc in range(DC):
            for q in range(4):
                ci = nb * 4 + q
                bank = 6 + (q % 2)
                def t_tr(eng, ci=ci, dc=dc, bank=bank):
                    eng.wait_ge(cV, p.cnt["cV"])
                    eng.transpose(pb[bank][:, 0:128], xn_t(ci)[:, dc * 128:(dc + 1) * 128],
                                  IDENF).then_inc(cT, 1)
                p.T(t_tr); bump("cT")
                def v_tr(eng, q=q, bank=bank):
                    eng.wait_ge(cT, p.cnt["cT"])
                    chv(eng, eng.tensor_copy(xnT32[:, q * 128:(q + 1) * 128], pb[bank][:, 0:128]))
                    eng.drain().then_inc(cV, 1)
                p.V(v_tr); bump("cV")
            def t_mm(eng, dc=dc):
                eng.wait_ge(cV, p.cnt["cV"])
                eng.matmul(pb[5][0:E, :], rwT[:, dc * E:(dc + 1) * E], xnT32,
                           start=(dc == 0), stop=(dc == DC - 1)).then_inc(cT, 1)
            p.T(t_mm); bump("cT")
        def v_ev(eng, nb=nb):
            eng.wait_ge(cT, p.cnt["cT"])
            chv(eng, eng.tensor_copy(logTs, pb[5][0:E, :]))
            eng.drain().then_inc(cV, 1)
        p.V(v_ev); bump("cV")
        for q in range(4):
            ci = nb * 4 + q
            def t_pl(eng, q=q):
                eng.wait_ge(cV, p.cnt["cV"])
                eng.transpose(pb[6][:, 0:E], logTs[:, q * 128:(q + 1) * 128],
                              IDENF[0:E, 0:E]).then_inc(cT, 1)
            p.T(t_pl); bump("cT")
            def v_pl(eng, ci=ci):
                eng.wait_ge(cT, p.cnt["cT"])
                chv(eng, eng.tensor_scalar_mul(LOG[:, ci::16], pb[6][0:128, 0:E],
                                               RS2R[:, ci:ci + 1]))
                chv(eng, eng.engine_nop())
                eng.drain().then_inc(cV, 1)
            p.V(v_pl); bump("cV")

    d = dbgout("logits", [128, 128])
    if d is not None:
        def f(eng, d=d):
            eng.wait_ge(cV, p.cnt["cV"])
            eng.dma_start(out=d, in_=LOG).then_inc(dA, 16)
        p.A(f); bump("dA", 16)

    def v_lh(eng):
        eng.wait_ge(cV, p.cnt["cV"])
        for dc in range(DC):
            h = LH[:, dc * 18:dc * 18 + 9]
            lo = LH[:, dc * 18 + 9:dc * 18 + 18]
            eng.tensor_copy(h[:, 0:1], ONE16)
            eng.memset(lo[:, 0:1], 0.0)
            chv(eng, eng.tensor_copy(h[:, 1:9], rwT[:, dc * E:(dc + 1) * E]))
            chv(eng, eng.tensor_copy(rwlo, h[:, 1:9]))
            chv(eng, eng.tensor_sub(rwlo, rwT[:, dc * E:(dc + 1) * E], rwlo))
            chv(eng, eng.tensor_copy(lo[:, 1:9], rwlo))
        chv(eng, eng.engine_nop())
        eng.drain().then_inc(cV, 1)
    p.V(v_lh); bump("cV")

    def v_se(eng):
        chv(eng, eng.memset(SEST, 0.0))
        chv(eng, eng.reduce_sum(SEST[:, 0:1], rw_s, axis=AX.X))
        chv(eng, eng.engine_nop())
        eng.drain().then_inc(cV, 1)
    p.V(v_se); bump("cV")
    def t_se(eng):
        eng.wait_ge(cV, p.cnt["cV"])
        eng.transpose(pb[6][:, 0:E], SEST, IDENF[0:E, 0:E]).then_inc(cT, 1)
    p.T(t_se); bump("cT")
    def v_se2(eng):
        eng.wait_ge(cT, p.cnt["cT"])
        chv(eng, eng.tensor_copy(SER16, pb[6][0:1, 0:E]))
        eng.drain().then_inc(cV, 1)
    p.V(v_se2); bump("cV")
    def t_se2(eng):
        eng.wait_ge(cV, p.cnt["cV"])
        eng.matmul(pb[6][:, 0:E], ONEROW, SER16, start=True, stop=True).then_inc(cT, 1)
    p.T(t_se2); bump("cT")
    def v_se3(eng):
        eng.wait_ge(cT, p.cnt["cT"])
        chv(eng, eng.tensor_copy(SBC, pb[6][:, 0:E]))
        eng.drain().then_inc(cV, 1)
    p.V(v_se3); bump("cV")

    xhiT = p.sbuf("xhiT", [128, DC * S], BF16, scope="sF")
    fLb = p.sbuf("flb", [9, S], F32, scope="sF")
    FLO = p.sbuf("flo", [128, SI * 9], F32, scope="sF")
    fsel = p.sbuf("fsel", [128, 16 * 9], F32, scope="sF")
    fsel16 = p.sbuf("fsel16", [128, E], BF16, scope="sF")

    for b in range(NCORES):
        for dc in range(DC):
            def s_tr(eng, b=b, dc=dc):
                if b == 0 and dc == 0:
                    eng.wait_ge(dG, xf16_dG)
                if dc == 0:
                    eng.wait_ge(cV, p.cnt["cV"])
                eng.dma_start(out=xhiT[:, dc * S:(dc + 1) * S],
                              in_=xf16[b * S:(b + 1) * S, dc * 128:(dc + 1) * 128],
                              transpose=True).then_inc(dS, 16)
                eng.wait_ge(dS, bump("dS", 16))
            p.S(s_tr)
        for nb in range(4):
            for dc in range(DC):
                def t_mm(eng, nb=nb, dc=dc):
                    if dc == 0:
                        eng.wait_ge(dS, p.cnt["dS"])
                        eng.wait_ge(cV, p.cnt["cV"])
                    rhs = xhiT[:, dc * S + nb * 512: dc * S + (nb + 1) * 512]
                    eng.matmul(pb[4][0:9, :], LH[:, dc * 18:dc * 18 + 9], rhs,
                               start=(dc == 0), stop=False)
                    eng.matmul(pb[4][0:9, :], LH[:, dc * 18 + 9:dc * 18 + 18], rhs,
                               start=False, stop=(dc == DC - 1)).then_inc(cT, 1)
                p.T(t_mm); bump("cT")
            def v_ev(eng, nb=nb):
                eng.wait_ge(cT, p.cnt["cT"])
                chv(eng, eng.tensor_copy(fLb[:, nb * 512:(nb + 1) * 512], pb[4][0:9, :]))
                eng.drain().then_inc(cV, 1)
            p.V(v_ev); bump("cV")
        for ci in range(SI):
            def t_pl(eng, ci=ci):
                eng.wait_ge(cV, max(p.cnt["cV"] - 1, 1))
                eng.transpose(pb[6 + ci % 2][:, 0:9], fLb[:, ci * 128:(ci + 1) * 128],
                              IDENF[0:9, 0:9]).then_inc(cT, 1)
            p.T(t_pl); bump("cT")
            def v_pl(eng, ci=ci):
                eng.wait_ge(cT, p.cnt["cT"])
                chv(eng, eng.tensor_copy(FLO[:, ci * 9:(ci + 1) * 9], pb[6 + ci % 2][0:128, 0:9]))
                chv(eng, eng.engine_nop())
                eng.drain().then_inc(cV, 1)
            p.V(v_pl); bump("cV")
        def v_f(eng, b=b):
            eng.wait_ge(cV, p.cnt["cV"])
            eng.wait_ge(cT, p.cnt["cT"])
            mpl = ps(9); m1 = ps(10); m2 = ps(11)
            chv(eng, eng.tensor_scalar_mul(mpl, FLO[:, 0::9], 1.0 / D))
            for e in range(E):
                lf = FLO[:, 1 + e::9]
                chv(eng, eng.tensor_scalar_mul(ps(0), mpl, SBC[:, e:e + 1]))
                chv(eng, eng.tensor_sub(lf, lf, ps(0)))
            chv(eng, eng.tensor_max(m1, FLO[:, 1::9], FLO[:, 2::9]))
            for e in range(2, E):
                chv(eng, eng.tensor_max(m1, m1, FLO[:, 1 + e::9]))
            for e in range(E):
                lf = FLO[:, 1 + e::9]
                sc = fsel[:, e * 16:(e + 1) * 16]
                chv(eng, eng.tensor_tensor(sc, lf, m1, ALU.is_ge))
                chv(eng, eng.scalar_tensor_tensor(sc, sc, -BIG, lf, ALU.mult, ALU.add))
            chv(eng, eng.tensor_max(m2, fsel[:, 0:16], fsel[:, 16:32]))
            for e in range(2, E):
                chv(eng, eng.tensor_max(m2, m2, fsel[:, e * 16:(e + 1) * 16]))
            for e in range(E):
                lf = FLO[:, 1 + e::9]
                sc = fsel[:, e * 16:(e + 1) * 16]
                chv(eng, eng.tensor_tensor(sc, lf, m2, ALU.is_ge))
                chv(eng, eng.reduce_sum(fsel[:, 128 + e:129 + e], sc, axis=AX.X))
            chv(eng, eng.tensor_copy(fsel16, fsel[:, 128:128 + E]))
            chv(eng, eng.engine_nop())
            eng.drain().then_inc(cV, 1)
        p.V(v_f); bump("cV")
        def t_cnt(eng):
            eng.wait_ge(cV, p.cnt["cV"])
            eng.matmul(pb[6][0:1, 0:E], ONE16, fsel16, start=True, stop=True).then_inc(cT, 1)
        p.T(t_cnt); bump("cT")
        def v_cnt(eng, b=b):
            eng.wait_ge(cT, p.cnt["cT"])
            chv(eng, eng.tensor_copy(CNT[:, b * E:(b + 1) * E], pb[6][0:1, 0:E]))
            chv(eng, eng.engine_nop())
            eng.drain().then_inc(cV, 1)
        p.V(v_cnt); bump("cV")

    def v_cb(eng):
        eng.wait_ge(cV, p.cnt["cV"])
        tr = tv[0:1, 32:32 + NCORES * E]
        chv(eng, eng.tensor_mul(tr, CNT[:, 0:NCORES * E], MSK))
        for e in range(E):
            chv(eng, eng.reduce_sum(CB[:, e:e + 1], tr[:, e::E], axis=AX.X))
        chv(eng, eng.tensor_copy(CB16, CB))
        chv(eng, eng.engine_nop())
        eng.drain().then_inc(cV, 1)
    p.V(v_cb); bump("cV")
    def t_cb(eng):
        eng.wait_ge(cV, p.cnt["cV"])
        eng.matmul(pb[6][:, 0:E], ONEROW, CB16, start=True, stop=True).then_inc(cT, 1)
    p.T(t_cb); bump("cT")
    def v_cb2(eng):
        eng.wait_ge(cT, p.cnt["cT"])
        chv(eng, eng.tensor_copy(CBC, pb[6][:, 0:E]))
        eng.drain().then_inc(cV, 1)
    p.V(v_cb2); bump("cV")

    d = dbgout("cnt", [1, NCORES * E + E])
    if d is not None:
        def f(eng, d=d):
            eng.wait_ge(cV, p.cnt["cV"])
            eng.dma_start(out=d, in_=CNT).then_inc(dA, 16)
        p.A(f); bump("dA", 16)

    def Le(e): return LOG[:, e * 16:(e + 1) * 16]

    def v_top2(eng):
        eng.wait_ge(cV, p.cnt["cV"])
        m1 = ps(0); run = ps(11); t = ps(9); t2 = ps(10)
        chv(eng, eng.tensor_max(m1, Le(0), Le(1)))
        for e in range(2, E):
            chv(eng, eng.tensor_max(m1, m1, Le(e)))
        chv(eng, eng.memset(run, 0.0))
        for e in range(E):
            f1 = F1[:, e * 16:(e + 1) * 16]
            chv(eng, eng.tensor_tensor(t, Le(e), m1, ALU.is_ge))
            chv(eng, eng.tensor_mul(t2, t, run))
            chv(eng, eng.tensor_sub(f1, t, t2))
            chv(eng, eng.tensor_max(run, run, t))
        for e in range(E):
            f1 = F1[:, e * 16:(e + 1) * 16]
            l2 = L2[:, e * 16:(e + 1) * 16]
            chv(eng, eng.scalar_tensor_tensor(l2, f1, -BIG, Le(e), ALU.mult, ALU.add))
        m2 = ps(1)
        chv(eng, eng.tensor_max(m2, L2[:, 0:16], L2[:, 16:32]))
        for e in range(2, E):
            chv(eng, eng.tensor_max(m2, m2, L2[:, e * 16:(e + 1) * 16]))
        chv(eng, eng.memset(run, 0.0))
        for e in range(E):
            f2 = F2[:, e * 16:(e + 1) * 16]
            l2 = L2[:, e * 16:(e + 1) * 16]
            chv(eng, eng.tensor_tensor(t, l2, m2, ALU.is_ge))
            chv(eng, eng.tensor_mul(t2, t, run))
            chv(eng, eng.tensor_sub(f2, t, t2))
            chv(eng, eng.tensor_max(run, run, t))
        e1 = ps(2); e2 = ps(3)
        chv(eng, eng.memset(e1, 0.0))
        chv(eng, eng.memset(e2, 0.0))
        for e in range(1, E):
            chv(eng, eng.scalar_tensor_tensor(e1, F1[:, e * 16:(e + 1) * 16], float(e), e1,
                                              ALU.mult, ALU.add))
            chv(eng, eng.scalar_tensor_tensor(e2, F2[:, e * 16:(e + 1) * 16], float(e), e2,
                                              ALU.mult, ALU.add))
        chv(eng, eng.tensor_sub(ps(9), ps(0), ps(1)))
        chv(eng, eng.engine_nop())
        eng.drain().then_inc(cV, 1)
    p.V(v_top2); bump("cV")

    def a_sig(eng):
        eng.wait_ge(cV, p.cnt["cV"])
        cha(eng, eng.activation(ps(4), ps(9), AF.Sigmoid, bias=0.0, scale=1.0))
        eng.drain().then_inc(cA, 1)
    p.A(a_sig); bump("cA")

    for e in range(E):
        def v_rk(eng, e=e):
            if e == 0:
                eng.wait_ge(cV, p.cnt["cV"])
            else:
                eng.wait_ge(cT, p.cnt["cT"])
            sel = VAL[:, e * 16:(e + 1) * 16]
            chv(eng, eng.tensor_add(sel, F1[:, e * 16:(e + 1) * 16], F2[:, e * 16:(e + 1) * 16]))
            chv(eng, eng.tensor_copy(SEL16, sel))
            chv(eng, eng.engine_nop())
            eng.drain().then_inc(cV, 1)
        p.V(v_rk); bump("cV")
        def t_rk(eng, e=e):
            eng.wait_ge(cV, p.cnt["cV"])
            eng.matmul(pb[6][:, 0:16], SUP, SEL16, start=True, stop=True)
            eng.matmul(pb[7][:, 0:16], ONESF16, SEL16, start=True, stop=True).then_inc(cT, 1)
        p.T(t_rk); bump("cT")
        def v_rk2(eng, e=e):
            eng.wait_ge(cT, p.cnt["cT"])
            lsr = LSR[:, e * 16:(e + 1) * 16]
            ccm = CCM[:, e * 16:(e + 1) * 16]
            eng.tensor_copy(lsr, pb[6][:, 0:16])
            eng.tensor_copy(ccm, pb[7][:, 0:16])
            chv(eng, eng.engine_nop())
            eng.drain().then_inc(cV, 1)
        p.V(v_rk2); bump("cV")
    def v_rk3(eng):
        eng.wait_ge(cV, p.cnt["cV"])
        for e in range(E):
            lsr = LSR[:, e * 16:(e + 1) * 16]
            ccm = CCM[:, e * 16:(e + 1) * 16]
            for ci in range(1, SI):
                chv(eng, eng.tensor_add(ccm[:, ci:ci + 1], ccm[:, ci:ci + 1], ccm[:, ci - 1:ci]))
            for ci in range(1, SI):
                chv(eng, eng.tensor_add(lsr[:, ci:ci + 1], lsr[:, ci:ci + 1], ccm[:, ci - 1:ci]))
        chv(eng, eng.engine_nop())
        eng.drain().then_inc(cV, 1)
    p.V(v_rk3); bump("cV")

    def v_sigma(eng):
        eng.wait_ge(cV, p.cnt["cV"])
        eng.wait_ge(cA, p.cnt["cA"])
        t = ps(9); t2 = ps(10)
        for e in range(E):
            sel = VAL[:, e * 16:(e + 1) * 16]
            lsr = LSR[:, e * 16:(e + 1) * 16]
            chv(eng, eng.tensor_scalar_add(t, lsr, CBC[:, e:e + 1]))
            chv(eng, eng.tensor_scalar(t, t, float(CAP), None, ALU.is_lt))
            chv(eng, eng.tensor_mul(sel, sel, t))
        lsr1 = ps(5); lsr2 = ps(6); v1 = ps(7); v2 = ps(8)
        chv(eng, eng.memset(lsr1, 0.0)); chv(eng, eng.memset(lsr2, 0.0))
        chv(eng, eng.memset(v1, 0.0)); chv(eng, eng.memset(v2, 0.0))
        for e in range(E):
            f1 = F1[:, e * 16:(e + 1) * 16]; f2 = F2[:, e * 16:(e + 1) * 16]
            lsr = LSR[:, e * 16:(e + 1) * 16]; val = VAL[:, e * 16:(e + 1) * 16]
            chv(eng, eng.tensor_mul(t, f1, lsr)); chv(eng, eng.tensor_add(lsr1, lsr1, t))
            chv(eng, eng.tensor_mul(t, f2, lsr)); chv(eng, eng.tensor_add(lsr2, lsr2, t))
            chv(eng, eng.tensor_mul(t, f1, val)); chv(eng, eng.tensor_add(v1, v1, t))
            chv(eng, eng.tensor_mul(t, f2, val)); chv(eng, eng.tensor_add(v2, v2, t))
        chv(eng, eng.tensor_mul(WV[:, 0:16], ps(4), v1))
        chv(eng, eng.engine_nop())
        eng.drain().then_inc(cV, 1)
    p.V(v_sigma); bump("cV")

    def v_w2(eng):
        t = ps(9)
        chv(eng, eng.tensor_scalar(t, ps(4), -1.0, 1.0, ALU.mult, ALU.add))
        chv(eng, eng.tensor_mul(WV[:, 16:32], t, ps(8)))
        chv(eng, eng.engine_nop())
        eng.drain().then_inc(cV, 1)
    p.V(v_w2); bump("cV")

    def v_sig2(eng):
        t = ps(9); t2 = ps(10)
        for j, (ee, ll, vv, so, go) in enumerate(
                ((ps(2), ps(5), ps(7), 0, 2), (ps(3), ps(6), ps(8), 1, 3))):
            chv(eng, eng.tensor_scalar_min(t, ll, float(WSL - 1)))
            chv(eng, eng.scalar_tensor_tensor(t, ee, float(WSL), t, ALU.mult, ALU.add))
            chv(eng, eng.tensor_copy(pu(0), vv))
            chv(eng, eng.select(t2, pu(0), t, ps(12)))
            chv(eng, eng.tensor_copy(sigi(so), t2))
            chv(eng, eng.select(t2, pu(0), t, ps(12)))
            chv(eng, eng.tensor_copy(sigi(go), t2))
        chv(eng, eng.engine_nop())
        eng.drain().then_inc(cV, 1)
    p.V(v_sig2); bump("cV")

    d = dbgout("tables", [128, 16 * 8])
    if d is not None:
        def f(eng, d=d):
            eng.wait_ge(cV, p.cnt["cV"])
            eng.dma_start(out=d[:, 0:16], in_=ps(2)).then_inc(dA, 16)
            eng.dma_start(out=d[:, 16:32], in_=ps(3)).then_inc(dA, 16)
            eng.dma_start(out=d[:, 32:48], in_=ps(5)).then_inc(dA, 16)
            eng.dma_start(out=d[:, 48:64], in_=ps(6)).then_inc(dA, 16)
            eng.dma_start(out=d[:, 64:80], in_=ps(7)).then_inc(dA, 16)
            eng.dma_start(out=d[:, 80:96], in_=ps(8)).then_inc(dA, 16)
            eng.dma_start(out=d[:, 96:112], in_=WV[:, 0:16]).then_inc(dA, 16)
            eng.dma_start(out=d[:, 112:128], in_=WV[:, 16:32]).then_inc(dA, 16)
        p.A(f); bump("dA", 16 * 8)

    join_all()
    p.close_scope("sF")
    p.close_scope("sq")
    p.close_scope("sx")

    _emit_rest(p, locals())

    nc_final = p.close()
    return nc_final, p.dbg


def _emit_rest(p, g):
    nc = p.nc
    pb = g["pb"]
    IDENF = g["IDENF"]; ONEROW = g["ONEROW"]; E16A = g["E16A"]; ONE16 = g["ONE16"]
    RALL = g["RALL"]; BIASC = g["BIASC"]; GSC = g["GSC"]; SW = g["SW"]
    UO = g["UO"]; tv = g["tv"]; WV = g["WV"]; EPSC2 = g["EPSC"]
    sigi = g["sigi"]
    w_in = g["w_in"]; w16_d = g["w16_d"]
    xq_d = g["xq_d"]; xqo_d = g["xqo_d"]; xnb_d = g["xnb_d"]; xb = g["xb"]
    acc_d = g["acc_d"]; sums_d = g["sums_d"]; ctxT_d = g["ctxT_d"]
    mix_d = g["mix_d"]; E16T = g["E16T"]; IDENB = g["IDENB"]
    ztile = g["ztile"]; xnbt = g["xnbt"]
    sgb16_d = g["sgb16_d"]; sub16_d = g["sub16_d"]; sd_d = g["sd_d"]
    xg16 = g["xg16"]; eout_d = g["eout_d"]; out = g["out"]
    rg_d = g["rg_d"]; ru_d = g["ru_d"]; rd_d = g["rd_d"]
    sg_d = g["sg_d"]; su_d = g["su_d"]; sd16_d = g["sd16_d"]
    dbgout = g["dbgout"]; bump = g["bump"]; join_all = g["join_all"]; chv = g["chv"]; cha = g["cha"]
    dS, dG, dA, cT, cV, cA, cP = (p.sem(n) for n in
                                  ("dS", "dG", "dA", "cT", "cV", "cA", "cP"))

    def g_z(eng):
        eng.memset(ztile, 0.0).then_inc(p.sem("cP"), 1)
        eng.wait_ge(p.sem("cP"), bump("cP"))
        for r in range(NSLOT // 128 + 1):
            eng.dma_start(out=xg16[r * 128:(r + 1) * 128, :], in_=ztile).then_inc(dG, 16)
    bump("dG", 16 * (NSLOT // 128 + 1)); p.G(g_z)

    def g_cast(eng):
        for k in range(16):
            eng.dma_start(out=sd16_d[k * 512:(k + 1) * 512, :],
                          in_=sd_d[k * 512:(k + 1) * 512, :]).then_inc(dG, 16)
        for icc in range(16):
            eng.dma_start(out=sgb16_d[icc].rearrange("p (c i) -> p c i", i=512),
                          in_=sg_d[:, icc * 512:(icc + 1) * 512].rearrange(
                              "(c p) i -> p c i", p=128)).then_inc(dG, 16)
            eng.dma_start(out=sub16_d[icc].rearrange("p (c i) -> p c i", i=512),
                          in_=su_d[:, icc * 512:(icc + 1) * 512].rearrange(
                              "(c p) i -> p c i", p=128)).then_inc(dG, 16)
    bump("dG", 16 * 48); p.G(g_cast)

    def g_sc(eng):
        eng.wait_ge(dG, p.cnt["dG"])
        eng.wait_ge(dA, p.cnt["dA"])
        eng.wait_ge(cV, p.cnt["cV"])
        for ci in range(SI):
            eng.dma_start(out=xnbt, in_=xnb_d[ci * 128:(ci + 1) * 128, :]).then_inc(dG, 16)
            eng.wait_ge(dG, bump("dG", 16))
            for jj in range(2):
                eng.indirect_dma_start(
                    out=xg16, out_offset=IndirectOffsetOnAxis(sigi(jj)[:, ci:ci + 1], 0),
                    in_=xnbt, in_offset=None).then_inc(dG, 16)
                bump("dG", 16)
            eng.wait_ge(dG, p.cnt["dG"])
    p.G(g_sc)
    scat_dG = p.cnt["dG"]

    w32 = p.sbuf("w32", [128, DC * D], F32, scope="wq")
    qscr = p.sbuf("qscr", [128, D], F32, scope="wq")
    ZI2 = p.sbuf("zi2", [128, D], I32, scope="wq")
    w16b = p.sbuf("w16b", [128, D], BF16, scope="wq")
    ONERF = g["ONERF"]; ONECF = g["ONECF"]
    asum = tv[:, 64:65]
    ared = tv[:, 65:66]
    swcol = tv[:, 66:67]

    def v_onerf(eng):
        eng.memset(ONERF, 1.0)
        eng.memset(ONECF, 1.0)
        eng.tensor_copy(IDENB, IDENF)
        chv(eng, eng.engine_nop())
        eng.drain().then_inc(cV, 1)
    p.V(v_onerf); bump("cV")

    for wi, wn in enumerate(("wq", "wk", "wv", "wo")):
        def s_w(eng, wn=wn):
            eng.wait_ge(cV, p.cnt["cV"])
            for r in range(DC):
                eng.dma_start(out=w32[:, r * D:(r + 1) * D],
                              in_=w_in[wn][r * 128:(r + 1) * 128, :]).then_inc(dS, 16)
        p.S(s_w); bump("dS", 16 * DC)

        def v_as(eng):
            eng.wait_ge(dS, p.cnt["dS"])
            chv(eng, eng.memset(asum, 0.0))
            for r in range(DC):
                chv(eng, eng.reduce_sum(ared, w32[:, r * D:(r + 1) * D], axis=AX.X,
                                        apply_absolute_value=True))
                chv(eng, eng.tensor_add(asum, asum, ared))
            chv(eng, eng.engine_nop())
            eng.drain().then_inc(cV, 1)
        p.V(v_as); bump("cV")

        def t_tot(eng):
            eng.wait_ge(cV, p.cnt["cV"])
            eng.matmul(pb[6][0:1, 0:1], asum, ONECF, start=True, stop=True).then_inc(cT, 1)
        p.T(t_tot); bump("cT")

        def v_sw(eng, wi=wi):
            eng.wait_ge(cT, p.cnt["cT"])
            sws = SW[:, wi:wi + 1]
            chv(eng, eng.tensor_scalar_mul(sws, pb[6][0:1, 0:1], 1.0 / (D * D)))
            chv(eng, eng.tensor_scalar_max(sws, sws, 1e-5))
            chv(eng, eng.reciprocal(sws, sws))
            chv(eng, eng.engine_nop())
            eng.drain().then_inc(cV, 1)
        p.V(v_sw); bump("cV")

        def t_bc(eng, wi=wi):
            eng.wait_ge(cV, p.cnt["cV"])
            eng.matmul(pb[6][:, 0:1], ONERF, SW[:, wi:wi + 1], start=True, stop=True).then_inc(cT, 1)
        p.T(t_bc); bump("cT")

        def v_bc(eng):
            eng.wait_ge(cT, p.cnt["cT"])
            chv(eng, eng.tensor_copy(swcol, pb[6][:, 0:1]))
            eng.drain().then_inc(cV, 1)
        p.V(v_bc); bump("cV")

        for r in range(DC):
            def v_q(eng, r=r):
                if r == 0:
                    eng.wait_ge(cV, p.cnt["cV"])
                if r > 0:
                    eng.wait_ge(dA, p.cnt["dA"])
                chv(eng, eng.tensor_scalar_mul(qscr, w32[:, r * D:(r + 1) * D], swcol))
                chv(eng, eng.tensor_copy(ZI2, qscr))
                chv(eng, eng.tensor_scalar(w16b, ZI2, -1.0, 1.0, ALU.max, ALU.min))
                chv(eng, eng.engine_nop())
                eng.drain().then_inc(cV, 1)
            p.V(v_q); bump("cV")
            def a_wb(eng, wn=wn, r=r):
                eng.wait_ge(cV, p.cnt["cV"])
                eng.dma_start(out=w16_d[wn][r * 128:(r + 1) * 128, :], in_=w16b).then_inc(dA, 16)
            p.A(a_wb); bump("dA", 16)

    def v_gs(eng):
        eng.wait_ge(dA, p.cnt["dA"])
        a = tv[0:1, 70:71]; b = tv[0:1, 71:72]
        chv(eng, eng.tensor_mul(a, SW[:, 0:1], SW[:, 1:2]))
        chv(eng, eng.tensor_scalar_mul(a, a, 8.0))
        chv(eng, eng.reciprocal(a, a))
        chv(eng, eng.tensor_copy(tv[0:1, 72:73], a))
        chv(eng, eng.reciprocal(b, SW[:, 2:3]))
        chv(eng, eng.tensor_copy(tv[0:1, 73:74], b))
        chv(eng, eng.reciprocal(b, SW[:, 3:4]))
        chv(eng, eng.tensor_copy(tv[0:1, 74:75], b))
        chv(eng, eng.engine_nop())
        eng.drain().then_inc(cV, 1)
    p.V(v_gs); bump("cV")
    for i in range(3):
        def t_g(eng, i=i):
            eng.wait_ge(cV, p.cnt["cV"])
            eng.matmul(pb[6][:, 0:1], ONERF, tv[0:1, 72 + i:73 + i], start=True, stop=True).then_inc(cT, 1)
        p.T(t_g); bump("cT")
        def v_g(eng, i=i):
            eng.wait_ge(cT, p.cnt["cT"])
            chv(eng, eng.tensor_copy(GSC[:, i:i + 1], pb[6][:, 0:1]))
            eng.drain().then_inc(cV, 1)
        p.V(v_g); bump("cV")

    join_all()
    p.close_scope("wq")

    qT = p.sbuf("qT", [128, DC * S], BF16, scope="attB")
    kT = p.sbuf("kT", [128, DC * S], BF16, scope="attB")
    vT = p.sbuf("vT", [128, DC * S], BF16, scope="attB")
    SF = p.sbuf("sfull", [16, S], F32, scope="attB")
    xqT = p.sbuf("xqT", [128, DC * S], BF16, scope="attW")
    wqT = p.sbuf("wqT", [128, DC * D], BF16, scope="attW")
    wkT = p.sbuf("wkT", [128, DC * D], BF16, scope="attW")
    wvT = p.sbuf("wvT", [128, DC * D], BF16, scope="attW")

    def s_tp(eng):
        eng.wait_ge(dA, p.cnt["dA"])
        for dc in range(DC):
            eng.dma_start(out=xqT[:, dc * S:(dc + 1) * S],
                          in_=xq_d[:, dc * 128:(dc + 1) * 128], transpose=True).then_inc(dS, 16)
            eng.wait_ge(dS, bump("dS", 16))
        for wbuf, wn in ((wqT, "wq"), (wkT, "wk"), (wvT, "wv")):
            for dc in range(DC):
                eng.dma_start(out=wbuf[:, dc * D:(dc + 1) * D],
                              in_=w16_d[wn][:, dc * 128:(dc + 1) * 128],
                              transpose=True).then_inc(dS, 16)
                eng.wait_ge(dS, bump("dS", 16))
    p.S(s_tp)
    tp_dS = p.cnt["dS"]

    for which, wbuf, dst, gcol in (("q", wqT, qT, None), ("k", wkT, kT, 0), ("v", wvT, vT, 1)):
        for ec in range(DC):
            for nb in range(4):
                bank = (nb % 2) * 2 + (0 if which != "k" else 1)
                def t_p(eng, ec=ec, nb=nb, wbuf=wbuf, bank=bank):
                    if ec == 0 and nb == 0:
                        eng.wait_ge(dS, tp_dS)
                    eng.wait_ge(cV, max(p.cnt["cV"] - 1, 1))
                    for dc in range(DC):
                        mm = eng.matmul(pb[bank], wbuf[:, dc * D + ec * 128: dc * D + (ec + 1) * 128],
                                        xqT[:, dc * S + nb * 512: dc * S + (nb + 1) * 512],
                                        start=(dc == 0), stop=(dc == DC - 1))
                    mm.then_inc(cT, 1)
                p.T(t_p); bump("cT")
                def v_p(eng, ec=ec, nb=nb, dst=dst, gcol=gcol, bank=bank):
                    eng.wait_ge(cT, p.cnt["cT"])
                    o = dst[:, ec * S + nb * 512: ec * S + (nb + 1) * 512]
                    if gcol is None:
                        chv(eng, eng.tensor_copy(o, pb[bank]))
                    else:
                        chv(eng, eng.tensor_scalar_mul(o, pb[bank], GSC[:, gcol:gcol + 1]))
                    eng.drain().then_inc(cV, 1)
                p.V(v_p); bump("cV")

    join_all()
    p.close_scope("attW")

    krep = p.sbuf("krep", [128, 512], BF16, scope="hmx")
    vrepA = p.sbuf("vrepa", [128, 16 * 512], BF16, scope="hmx")
    PROD = p.sbuf("prod", [128, 4 * 512], BF16, scope="hmx")
    SCt = [p.sbuf(f"sct{t}", [16, 512], F32, scope="hmx") for t in range(16)]
    EBt = [p.sbuf(f"ebt{t}", [16, 512], BF16, scope="hmx") for t in range(16)]
    MX = p.sbuf("mx", [16, 512], F32, scope="hmx")
    TMPA = p.sbuf("tmpa", [128, 3 * 512], BF16, scope="hmx")
    ctxc = p.sbuf("ctxc", [128, 2 * 512], F32, scope="hmx")

    for sb in range(4):
        for t in range(16):
            def v_kr(eng, sb=sb, t=t):
                src_ = kT[:, (t // 2) * S + sb * 512:(t // 2) * S + (sb + 1) * 512]
                base = (t % 2) * 64
                chv(eng, eng.tensor_copy(krep[0:64, :], src_[base:base + 64, :]))
                chv(eng, eng.tensor_copy(krep[64:128, :], src_[base:base + 64, :]))
                eng.drain().then_inc(cV, 1)
            p.V(v_kr); bump("cV")
            for c in range(DC):
                def v_pr(eng, sb=sb, t=t, c=c):
                    eng.wait_ge(cT, max(p.cnt["cT"] - 3, 1))
                    chv(eng, eng.tensor_mul(PROD[:, (c % 4) * 512:(c % 4 + 1) * 512],
                                            qT[:, c * S + sb * 512: c * S + (sb + 1) * 512], krep))
                    eng.drain().then_inc(cV, 1)
                p.V(v_pr); bump("cV")
                def t_sc(eng, t=t, c=c):
                    eng.wait_ge(cV, p.cnt["cV"])
                    eng.matmul(pb[4][0:16, :], E16A[:, c * 16:(c + 1) * 16],
                               PROD[:, (c % 4) * 512:(c % 4 + 1) * 512],
                               start=(c == 0), stop=(c == DC - 1)).then_inc(cT, 1)
                p.T(t_sc); bump("cT")
            def v_sc(eng, t=t):
                eng.wait_ge(cT, p.cnt["cT"])
                chv(eng, eng.tensor_copy(SCt[t], pb[4][0:16, :]))
                eng.drain().then_inc(cV, 1)
            p.V(v_sc); bump("cV")
        def v_sm(eng, sb=sb):
            chv(eng, eng.tensor_max(MX, SCt[0], SCt[1]))
            for t in range(2, 16):
                chv(eng, eng.tensor_max(MX, MX, SCt[t]))
            for t in range(16):
                chv(eng, eng.tensor_sub(SCt[t], SCt[t], MX))
            eng.drain().then_inc(cV, 1)
        p.V(v_sm); bump("cV")
        def a_ex(eng, sb=sb):
            eng.wait_ge(cV, p.cnt["cV"])
            for t in range(16):
                cha(eng, eng.activation(SCt[t], SCt[t], AF.Exp))
            eng.drain().then_inc(cA, 1)
        p.A(a_ex); bump("cA")
        def v_ss(eng, sb=sb):
            eng.wait_ge(cA, p.cnt["cA"])
            s_ = SF[:, sb * 512:(sb + 1) * 512]
            chv(eng, eng.tensor_add(s_, SCt[0], SCt[1]))
            for t in range(2, 16):
                chv(eng, eng.tensor_add(s_, s_, SCt[t]))
            for t in range(16):
                chv(eng, eng.tensor_copy(EBt[t], SCt[t]))
            eng.drain().then_inc(cV, 1)
        p.V(v_ss); bump("cV")
        def v_vra(eng, sb=sb):
            for t in range(16):
                src_ = vT[:, (t // 2) * S + sb * 512:(t // 2) * S + (sb + 1) * 512]
                base = (t % 2) * 64
                chv(eng, eng.tensor_copy(vrepA[0:64, t * 512:(t + 1) * 512],
                                         src_[base:base + 64, :]))
                chv(eng, eng.tensor_copy(vrepA[64:128, t * 512:(t + 1) * 512],
                                         src_[base:base + 64, :]))
            eng.drain().then_inc(cV, 1)
        p.V(v_vra); bump("cV")
        for c in range(DC):
            for t in range(16):
                def t_bc(eng, c=c, t=t):
                    if c == 0 and t < 3:
                        eng.wait_ge(cV, p.cnt["cV"])
                    else:
                        eng.wait_ge(cV, max(p.cnt["cV"] - 2, 1))
                    eng.matmul(pb[4 + t % 3][:, :], E16T[:, c * 128:(c + 1) * 128],
                               EBt[t], start=True, stop=True).then_inc(cT, 1)
                p.T(t_bc); bump("cT")
                def v_mu(eng, c=c, t=t):
                    eng.wait_ge(cT, p.cnt["cT"])
                    chv(eng, eng.tensor_mul(TMPA[:, (t % 3) * 512:(t % 3 + 1) * 512],
                                            pb[4 + t % 3], vrepA[:, t * 512:(t + 1) * 512]))
                    eng.drain().then_inc(cV, 1)
                p.V(v_mu); bump("cV")
                def t_ac(eng, c=c, t=t):
                    eng.wait_ge(cV, p.cnt["cV"])
                    eng.matmul(pb[7][:, :], IDENB,
                               TMPA[:, (t % 3) * 512:(t % 3 + 1) * 512],
                               start=(t == 0), stop=(t == 15)).then_inc(cT, 1)
                p.T(t_ac); bump("cT")
            def v_ev(eng, sb=sb, c=c):
                eng.wait_ge(cT, p.cnt["cT"])
                eng.wait_ge(dA, max(p.cnt["dA"] - 1, 0))
                chv(eng, eng.tensor_copy(ctxc[:, (c % 2) * 512:(c % 2 + 1) * 512], pb[7]))
                eng.drain().then_inc(cV, 1)
            p.V(v_ev); bump("cV")
            def a_cw2(eng, sb=sb, c=c):
                eng.wait_ge(cV, p.cnt["cV"])
                eng.dma_start(out=ctxT_d[c * 128:(c + 1) * 128, sb * 512:(sb + 1) * 512],
                              in_=ctxc[:, (c % 2) * 512:(c % 2 + 1) * 512]).then_inc(dA, 16)
            p.A(a_cw2); bump("dA", 16)

    for tci in range(SI):
        def t_r(eng, tci=tci):
            if tci == 0:
                eng.wait_ge(cV, p.cnt["cV"])
            eng.wait_ge(cV, p.cnt["cV"])
            eng.transpose(pb[6][:, 0:16], SF[:, tci * 128:(tci + 1) * 128], IDENF[0:16, 0:16]).then_inc(cT, 1)
        p.T(t_r); bump("cT")
        def v_r(eng, tci=tci):
            eng.wait_ge(cT, p.cnt["cT"])
            chv(eng, eng.reciprocal(RALL[:, tci * 16:(tci + 1) * 16], pb[6][:, 0:16]))
            eng.drain().then_inc(cV, 1)
        p.V(v_r); bump("cV")

    join_all()
    p.close_scope("hmx")
    p.close_scope("attB")

    cblk2 = [p.sbuf(f"cblk{i}", [64, S], F32, scope="mix") for i in range(2)]
    mix_ct = []
    mbuf = p.sbuf("mbuf", [128, 2 * 64], F32, scope="mix")
    ctok = p.sbuf("ctok", [128, D], F32, scope="mix")
    zoq = p.sbuf("zoq", [128, D], F32, scope="mix")
    ZI3 = p.sbuf("zi3", [128, D], I32, scope="mix")
    xqo16 = p.sbuf("xqo16", [128, D], BF16, scope="mix")

    for ci in range(SI):
        def s_cb(eng, ci=ci):
            if ci < 2:
                eng.wait_ge(dA, p.cnt["dA"])
            else:
                eng.wait_ge(cT, mix_ct[ci - 2])
            eng.dma_start(out=cblk2[ci % 2], in_=ctxT_d[ci * 64:(ci + 1) * 64, :]).then_inc(dS, 16)
        p.S(s_cb); bump("dS", 16)
        for tci in range(SI):
            def t_mt(eng, ci=ci, tci=tci):
                if tci == 0:
                    eng.wait_ge(dS, p.cnt["dS"])
                eng.wait_ge(cV, max(p.cnt["cV"] - 1, 1))
                eng.transpose(pb[6 + tci % 2][:, 0:64],
                              cblk2[ci % 2][:, tci * 128:(tci + 1) * 128],
                              IDENF[0:64, 0:64]).then_inc(cT, 1)
            p.T(t_mt); bump("cT")
            def v_mt(eng, ci=ci, tci=tci):
                eng.wait_ge(cT, p.cnt["cT"])
                eng.wait_ge(dA, max(p.cnt["dA"] - 1, 0))
                chv(eng, eng.tensor_scalar_mul(mbuf[:, (tci % 2) * 64:(tci % 2 + 1) * 64],
                                               pb[6 + tci % 2][:, 0:64],
                                               RALL[:, tci * 16 + ci:tci * 16 + ci + 1]))
                eng.drain().then_inc(cV, 1)
            p.V(v_mt); bump("cV")
            def a_mw(eng, ci=ci, tci=tci):
                eng.wait_ge(cV, p.cnt["cV"])
                eng.dma_start(out=mix_d[tci * 128:(tci + 1) * 128, :],
                              in_=mbuf[:, (tci % 2) * 64:(tci % 2 + 1) * 64]).then_inc(dA, 16)
            p.A(a_mw); bump("dA", 16)
        mix_ct.append(p.cnt["cT"])
        def s_mr(eng, ci=ci):
            eng.wait_ge(dA, p.cnt["dA"])
            eng.wait_ge(cV, p.cnt["cV"])
            eng.dma_start(out=ctok.rearrange("p (j dh) -> p j dh", dh=64),
                          in_=mix_d.rearrange("(pp j) dh -> pp j dh", j=16)).then_inc(dS, 16)
        p.S(s_mr); bump("dS", 16)
        c0 = tv[:, 80:81]; c1 = tv[:, 81:82]; c2 = tv[:, 82:83]; c3 = tv[:, 83:84]
        uo = UO[:, ci:ci + 1]
        def v_cq(eng, ci=ci, c0=c0, c1=c1):
            eng.wait_ge(dS, p.cnt["dS"])
            eng.wait_ge(dA, p.cnt["dA"])
            chv(eng, eng.scalar_tensor_tensor(zoq, ctok, 1.0, ctok, ALU.mult, ALU.mult))
            chv(eng, eng.reduce_sum(c0, zoq, axis=AX.X))
            chv(eng, eng.reduce_max(c1, ctok, axis=AX.X, apply_absolute_value=True))
            chv(eng, eng.tensor_scalar_mul(c0, c0, 1.0 / D))
            eng.drain().then_inc(cV, 1)
        p.V(v_cq); bump("cV")
        def a_cq(eng, c0=c0, c2=c2):
            eng.wait_ge(cV, p.cnt["cV"])
            cha(eng, eng.activation(c2, c0, AF.Sqrt, bias=EPSC2[:, 1:2], scale=1.0))
            eng.drain().then_inc(cA, 1)
        p.A(a_cq); bump("cA")
        def v_cq2(eng, ci=ci, c1=c1, c2=c2, c3=c3, uo=uo):
            eng.wait_ge(cA, p.cnt["cA"])
            chv(eng, eng.reciprocal(c2, c2))
            chv(eng, eng.tensor_mul(c3, c2, c1))
            chv(eng, eng.tensor_scalar_max(c3, c3, EPS_Q))
            chv(eng, eng.tensor_scalar_mul(uo, c3, 1.0 / 127.0))
            chv(eng, eng.reciprocal(c3, c3))
            chv(eng, eng.scalar_tensor_tensor(c2, c2, 127.0, c3, ALU.mult, ALU.mult))
            chv(eng, eng.tensor_scalar_mul(zoq, ctok, c2))
            chv(eng, eng.tensor_copy(ZI3, zoq))
            chv(eng, eng.tensor_scalar_mul(xqo16, ZI3, uo))
            eng.drain().then_inc(cV, 1)
        p.V(v_cq2); bump("cV")
        def a_cw(eng, ci=ci):
            eng.wait_ge(cV, p.cnt["cV"])
            eng.dma_start(out=xqo_d[ci * 128:(ci + 1) * 128, :], in_=xqo16).then_inc(dA, 16)
        p.A(a_cw); bump("dA", 16)

    join_all()
    p.close_scope("mix")

    xqoT = p.sbuf("xqoT", [128, DC * S], BF16, scope="wo")
    woT = p.sbuf("woT", [128, DC * D], BF16, scope="wo")
    xrld = p.sbuf("xrld", [128, D], F32, scope="wo")
    acct = p.sbuf("acct", [128, D], F32, scope="wo")

    def s_wo(eng):
        eng.wait_ge(dA, p.cnt["dA"])
        for dc in range(DC):
            eng.dma_start(out=xqoT[:, dc * S:(dc + 1) * S],
                          in_=xqo_d[:, dc * 128:(dc + 1) * 128], transpose=True).then_inc(dS, 16)
            eng.wait_ge(dS, bump("dS", 16))
        for dc in range(DC):
            eng.dma_start(out=woT[:, dc * D:(dc + 1) * D],
                          in_=w16_d["wo"][:, dc * 128:(dc + 1) * 128],
                          transpose=True).then_inc(dS, 16)
            eng.wait_ge(dS, bump("dS", 16))
    p.S(s_wo)
    wo_dS = p.cnt["dS"]

    for si in range(SI):
        def s_x(eng, si=si):
            eng.wait_ge(cV, p.cnt["cV"])
            eng.dma_start(out=xrld, in_=xb[si * 128:(si + 1) * 128, :]).then_inc(dS, 16)
        p.S(s_x); bump("dS", 16)
        def t_wo(eng, si=si):
            if si == 0:
                eng.wait_ge(dS, wo_dS)
            eng.wait_ge(cV, p.cnt["cV"])
            for dc in range(DC):
                eng.matmul(pb[0], xqoT[:, dc * S + si * 128: dc * S + (si + 1) * 128],
                           woT[:, dc * D: dc * D + 512],
                           start=(dc == 0), stop=(dc == DC - 1))
            for dc in range(DC):
                mm = eng.matmul(pb[1], xqoT[:, dc * S + si * 128: dc * S + (si + 1) * 128],
                                woT[:, dc * D + 512: (dc + 1) * D],
                                start=(dc == 0), stop=(dc == DC - 1))
            mm.then_inc(cT, 1)
        p.T(t_wo); bump("cT")
        def v_wo(eng, si=si):
            eng.wait_ge(cT, p.cnt["cT"])
            eng.wait_ge(dS, p.cnt["dS"])
            eng.wait_ge(dA, p.cnt["dA"])
            eng.scalar_tensor_tensor(acct[:, 0:512], pb[0], GSC[:, 2:3], xrld[:, 0:512],
                                     ALU.mult, ALU.add)
            eng.scalar_tensor_tensor(acct[:, 512:1024], pb[1], GSC[:, 2:3], xrld[:, 512:1024],
                                     ALU.mult, ALU.add)
            chv(eng, eng.engine_nop())
            eng.drain().then_inc(cV, 1)
        p.V(v_wo); bump("cV")
        def a_acc(eng, si=si):
            eng.wait_ge(cV, p.cnt["cV"])
            eng.dma_start(out=acc_d[si * 128:(si + 1) * 128, :], in_=acct).then_inc(dA, 16)
        p.A(a_acc); bump("dA", 16)

    d = dbgout("xqo", [S, D], BF16)
    if d is not None:
        def f_xqo(eng, d=d):
            eng.wait_ge(dA, p.cnt["dA"])
            eng.dma_start(out=d, in_=xqo_d).then_inc(dS, 16)
        p.S(f_xqo); bump("dS", 16)
    d = dbgout("ctxT", [D, S])
    if d is not None:
        def f_ctxT(eng, d=d):
            eng.wait_ge(dA, p.cnt["dA"])
            for k in range(8):
                eng.dma_start(out=d[k * 128:(k + 1) * 128, :],
                              in_=ctxT_d[k * 128:(k + 1) * 128, :]).then_inc(dS, 16)
        p.S(f_ctxT); bump("dS", 16 * 8)
    d = dbgout("sums", [1, NH * S])
    if d is not None:
        def f_sums(eng, d=d):
            eng.wait_ge(dA, p.cnt["dA"])
            eng.dma_start(out=d, in_=sums_d).then_inc(dS, 16)
        p.S(f_sums); bump("dS", 16)
    d = dbgout("accA", [S, D])
    if d is not None:
        def f_accA(eng, d=d):
            eng.wait_ge(dA, p.cnt["dA"])
            eng.dma_start(out=d, in_=acc_d).then_inc(dS, 16)
        p.S(f_accA); bump("dS", 16)
    join_all()
    p.close_scope("wo")

    xgT2 = [p.sbuf(f"xgT{i}", [128, DC * WSL], BF16, scope="moe") for i in range(2)]
    gu_ct = []
    rg16 = [p.sbuf(f"rg16_{i}", [128, 4096], BF16, scope="moe") for i in range(2)]
    ru16 = [p.sbuf(f"ru16_{i}", [128, 4096], BF16, scope="moe") for i in range(2)]
    h_e = p.sbuf("h_e", [128, 32 * WSL], BF16, scope="moe")
    rd16 = p.sbuf("rd16", [128, 32 * D], BF16, scope="moe")
    sgt = p.sbuf("sgt", [128, 4 * 512], F32, scope="moe")
    eoutb = p.sbuf("eoutb", [128, D], F32, scope="moe")

    NBW = ((0, 512), (512, 256))

    for e in range(E):
        xgT = xgT2[e % 2]
        def s_xg(eng, e=e):
            if e == 0:
                eng.wait_ge(dG, scat_dG)
            if e >= 2:
                eng.wait_ge(cT, gu_ct[e - 2])
            for dc in range(DC):
                eng.dma_start(out=xgT2[e % 2][:, dc * WSL:(dc + 1) * WSL],
                              in_=xg16[e * WSL:(e + 1) * WSL, dc * 128:(dc + 1) * 128],
                              transpose=True).then_inc(dS, 16)
                eng.wait_ge(dS, bump("dS", 16))
        p.S(s_xg)
        def g_rd(eng, e=e):
            eng.wait_ge(cT, p.cnt["cT"])
            for ic in range(32):
                eng.dma_start(out=rd16[:, ic * D:(ic + 1) * D],
                              in_=rd_d[e, ic * 128:(ic + 1) * 128, :]).then_inc(dG, 16)
        p.G(g_rd); bump("dG", 16 * 32)
        xg_dS = p.cnt["dS"]

        for icc in range(DC):
            buf = icc % 2
            def g_w(eng, e=e, icc=icc, buf=buf):
                if icc >= 2:
                    eng.wait_ge(cT, max(p.cnt["cT"] - 8, 1))
                eng.dma_start(out=rg16[buf].rearrange("p (c i) -> p c i", i=512),
                              in_=rg_d[e, :, icc * 512:(icc + 1) * 512].rearrange(
                                  "(c p) i -> p c i", p=128)).then_inc(dG, 16)
                eng.dma_start(out=ru16[buf].rearrange("p (c i) -> p c i", i=512),
                              in_=ru_d[e, :, icc * 512:(icc + 1) * 512].rearrange(
                                  "(c p) i -> p c i", p=128)).then_inc(dG, 16)
            p.G(g_w); bump("dG", 32)
            for ib in range(4):
                recs = []
                for nbs in range(2):
                    off, w = NBW[nbs]
                    kk = (icc * 8 + ib * 2 + nbs) % 4
                    first4 = (icc == 0 and ib < 2)
                    def t_gu(eng, e=e, icc=icc, ib=ib, nbs=nbs, buf=buf, kk=kk, off=off, w=w, first4=first4):
                        if ib == 0 and nbs == 0:
                            eng.wait_ge(dG, p.cnt["dG"])
                            if icc == 0:
                                eng.wait_ge(dS, xg_dS)
                        if first4:
                            eng.wait_ge(cV, p.cnt["cV"])
                        elif nbs == 0:
                            eng.wait_ge(cV, max(p.cnt["cV"] - 3, 1))
                        else:
                            eng.wait_ge(cV, max(p.cnt["cV"] - 2, 1))
                        for dc in range(DC):
                            eng.matmul(pb[2 * kk][:, 0:w],
                                       rg16[buf][:, dc * 512 + ib * 128: dc * 512 + (ib + 1) * 128],
                                       xgT[:, dc * WSL + off: dc * WSL + off + w],
                                       start=(dc == 0), stop=(dc == DC - 1))
                        for dc in range(DC):
                            mm = eng.matmul(pb[2 * kk + 1][:, 0:w],
                                            ru16[buf][:, dc * 512 + ib * 128: dc * 512 + (ib + 1) * 128],
                                            xgT[:, dc * WSL + off: dc * WSL + off + w],
                                            start=(dc == 0), stop=(dc == DC - 1))
                        mm.then_inc(cT, 1)
                    p.T(t_gu); bump("cT")
                    recs.append((p.cnt["cT"], kk, off, w))
                cvb = p.cnt["cV"]
                def a_si(eng, recs=recs, cvb=cvb):
                    for j, (ct, kk, off, w) in enumerate(recs):
                        eng.wait_ge(cT, ct)
                        if cvb - 3 + j >= 1:
                            eng.wait_ge(cV, cvb - 3 + j)
                        cha(eng, eng.activation(sgt[:, kk * 512: kk * 512 + w],
                                                pb[2 * kk][:, 0:w], AF.Silu))
                        eng.drain().then_inc(cA, 1)
                p.A(a_si); bump("cA", 2)
                cab = p.cnt["cA"]
                for j, (ct, kk, off, w) in enumerate(recs):
                    def v_h(eng, icc=icc, ib=ib, kk=kk, off=off, w=w, ca=cab - 1 + j):
                        eng.wait_ge(cA, ca)
                        ic32 = icc * 4 + ib
                        eng.tensor_mul(h_e[:, ic32 * WSL + off: ic32 * WSL + off + w],
                                       sgt[:, kk * 512: kk * 512 + w], pb[2 * kk + 1][:, 0:w])
                        chv(eng, eng.engine_nop())
                        eng.drain().then_inc(cV, 1)
                    p.V(v_h); bump("cV")
        gu_ct.append(p.cnt["cT"])
        for sc in range(WSL // 128):
            def t_dn(eng, e=e, sc=sc):
                if sc == 0:
                    eng.wait_ge(dG, p.cnt["dG"])
                eng.wait_ge(cV, p.cnt["cV"])
                for ic in range(32):
                    eng.matmul(pb[4], h_e[:, ic * WSL + sc * 128: ic * WSL + (sc + 1) * 128],
                               rd16[:, ic * D: ic * D + 512],
                               start=(ic == 0), stop=(ic == 31))
                for ic in range(32):
                    mm = eng.matmul(pb[5], h_e[:, ic * WSL + sc * 128: ic * WSL + (sc + 1) * 128],
                                    rd16[:, ic * D + 512: (ic + 1) * D],
                                    start=(ic == 0), stop=(ic == 31))
                mm.then_inc(cT, 1)
            p.T(t_dn); bump("cT")
            def v_dn(eng):
                eng.wait_ge(cT, p.cnt["cT"])
                eng.wait_ge(dA, p.cnt["dA"])
                eng.tensor_copy(eoutb[:, 0:512], pb[4])
                eng.tensor_copy(eoutb[:, 512:1024], pb[5])
                chv(eng, eng.engine_nop())
                eng.drain().then_inc(cV, 1)
            p.V(v_dn); bump("cV")
            def a_eo(eng, e=e, sc=sc):
                eng.wait_ge(cV, p.cnt["cV"])
                eng.dma_start(out=eout_d[e * WSL + sc * 128: e * WSL + (sc + 1) * 128, :],
                              in_=eoutb).then_inc(dA, 16)
            p.A(a_eo); bump("dA", 16)

    def v_z0(eng):
        eng.wait_ge(dA, p.cnt["dA"])
        eng.memset(eoutb[0:1, :], 0.0)
        chv(eng, eng.engine_nop())
        eng.drain().then_inc(cV, 1)
    p.V(v_z0); bump("cV")
    def a_z0(eng):
        eng.wait_ge(cV, p.cnt["cV"])
        eng.dma_start(out=eout_d[NSLOT:NSLOT + 1, :], in_=eoutb[0:1, :]).then_inc(dA, 16)
    p.A(a_z0); bump("dA", 16)

    join_all()
    p.close_scope("moe")

    xnT16 = p.sbuf("xnT16", [128, DC * S], BF16, scope="shc")
    sg16 = [p.sbuf(f"sg16_{i}", [128, 4096], BF16, scope="shc") for i in range(2)]
    su16 = [p.sbuf(f"su16_{i}", [128, 4096], BF16, scope="shc") for i in range(2)]
    h_sb = p.sbuf("h_sb", [128, 64 * 512], BF16, scope="shc")
    sdb = [p.sbuf(f"sdb_{i}", [128, D], BF16, scope="shc") for i in range(4)]
    shT = p.sbuf("shT", [128, 8 * 512], F32, scope="shc")
    sgt2 = p.sbuf("sgt2", [128, 4 * 512], F32, scope="shc")
    acc4 = p.sbuf("acc4", [128, 4 * D], F32, scope="shc")

    def s_xnt(eng):
        for dc in range(DC):
            eng.dma_start(out=xnT16[:, dc * S:(dc + 1) * S],
                          in_=xnb_d[:, dc * 128:(dc + 1) * 128], transpose=True).then_inc(dS, 16)
            eng.wait_ge(dS, bump("dS", 16))
    p.S(s_xnt)
    xnt_dS = p.cnt["dS"]

    for sb in range(4):
        for icc in range(16):
            buf = icc % 2
            def g_sg(eng, icc=icc, buf=buf):
                if icc >= 2:
                    eng.wait_ge(cT, max(p.cnt["cT"] - 4, 1))
                eng.dma_start(out=sg16[buf], in_=sgb16_d[icc]).then_inc(dG, 16)
                eng.dma_start(out=su16[buf], in_=sub16_d[icc]).then_inc(dG, 16)
            p.G(g_sg); bump("dG", 32)
            for ibp in range(2):
                recs = []
                for jj in range(2):
                    ib = ibp * 2 + jj
                    kk = (icc * 4 + ib) % 4
                    first4 = (icc == 0)
                    def t_gu(eng, sb=sb, icc=icc, ib=ib, buf=buf, kk=kk, jj=jj, first4=first4):
                        if ib == 0:
                            eng.wait_ge(dG, p.cnt["dG"])
                            if icc == 0 and sb == 0:
                                eng.wait_ge(dS, xnt_dS)
                        if first4:
                            eng.wait_ge(cV, p.cnt["cV"])
                        elif jj == 0:
                            eng.wait_ge(cV, max(p.cnt["cV"] - 3, 1))
                        else:
                            eng.wait_ge(cV, max(p.cnt["cV"] - 2, 1))
                        for dc in range(DC):
                            eng.matmul(pb[2 * kk], sg16[buf][:, dc * 512 + ib * 128: dc * 512 + (ib + 1) * 128],
                                       xnT16[:, dc * S + sb * 512: dc * S + (sb + 1) * 512],
                                       start=(dc == 0), stop=(dc == DC - 1))
                        for dc in range(DC):
                            mm = eng.matmul(pb[2 * kk + 1], su16[buf][:, dc * 512 + ib * 128: dc * 512 + (ib + 1) * 128],
                                            xnT16[:, dc * S + sb * 512: dc * S + (sb + 1) * 512],
                                            start=(dc == 0), stop=(dc == DC - 1))
                        mm.then_inc(cT, 1)
                    p.T(t_gu); bump("cT")
                    recs.append((p.cnt["cT"], ib, kk))
                cvb = p.cnt["cV"]
                def a_si(eng, recs=recs, cvb=cvb):
                    for j, (ct, ib, kk) in enumerate(recs):
                        eng.wait_ge(cT, ct)
                        if cvb - 3 + j >= 1:
                            eng.wait_ge(cV, cvb - 3 + j)
                        cha(eng, eng.activation(sgt2[:, kk * 512:(kk + 1) * 512],
                                                pb[2 * kk], AF.Silu))
                        eng.drain().then_inc(cA, 1)
                p.A(a_si); bump("cA", 2)
                cab = p.cnt["cA"]
                for j, (ct, ib, kk) in enumerate(recs):
                    def v_h(eng, icc=icc, ib=ib, kk=kk, ca=cab - 1 + j):
                        eng.wait_ge(cA, ca)
                        ic64 = icc * 4 + ib
                        eng.tensor_mul(h_sb[:, ic64 * 512:(ic64 + 1) * 512],
                                       sgt2[:, kk * 512:(kk + 1) * 512], pb[2 * kk + 1])
                        chv(eng, eng.engine_nop())
                        eng.drain().then_inc(cV, 1)
                    p.V(v_h); bump("cV")
        for ic64 in range(64):
            buf = ic64 % 4
            def g_sd(eng, ic64=ic64, buf=buf):
                if ic64 >= 4:
                    eng.wait_ge(cT, max(p.cnt["cT"] - 3, 1))
                eng.dma_start(out=sdb[buf],
                              in_=sd16_d[ic64 * 128:(ic64 + 1) * 128, :]).then_inc(dG, 16)
            p.G(g_sd); bump("dG", 16)
            def t_dn(eng, sb=sb, ic64=ic64, buf=buf):
                eng.wait_ge(dG, p.cnt["dG"])
                if ic64 == 0:
                    eng.wait_ge(cV, p.cnt["cV"])
                for dm in range(8):
                    mm = eng.matmul(pb[dm], sdb[buf][:, dm * 128:(dm + 1) * 128],
                                    h_sb[:, ic64 * 512:(ic64 + 1) * 512],
                                    start=(ic64 == 0), stop=(ic64 == 63))
                mm.then_inc(cT, 1)
            p.T(t_dn); bump("cT")
        def v_sh(eng):
            eng.wait_ge(cT, p.cnt["cT"])
            for dm in range(8):
                eng.tensor_copy(shT[:, dm * 512:(dm + 1) * 512], pb[dm])
            chv(eng, eng.engine_nop())
            eng.drain().then_inc(cV, 1)
        p.V(v_sh); bump("cV")
        def s_ac(eng, sb=sb):
            eng.wait_ge(cV, p.cnt["cV"])
            eng.wait_ge(dA, p.cnt["dA"])
            for q in range(4):
                eng.dma_start(out=acc4[:, q * D:(q + 1) * D],
                              in_=acc_d[sb * 512 + q * 128: sb * 512 + (q + 1) * 128, :]).then_inc(dS, 16)
        p.S(s_ac); bump("dS", 64)
        for q in range(4):
            for dm in range(8):
                bank = dm % 2
                def t_tr(eng, q=q, dm=dm, bank=bank):
                    if q == 0 and dm == 0:
                        eng.wait_ge(dS, p.cnt["dS"])
                    if dm >= 2 or q > 0:
                        eng.wait_ge(cV, p.cnt["cV"] - 1)
                    eng.transpose(pb[bank][:, 0:128],
                                  shT[:, dm * 512 + q * 128: dm * 512 + (q + 1) * 128],
                                  IDENF).then_inc(cT, 1)
                p.T(t_tr); bump("cT")
                def v_ad(eng, q=q, dm=dm, bank=bank):
                    eng.wait_ge(cT, p.cnt["cT"])
                    a = acc4[:, q * D + dm * 128: q * D + (dm + 1) * 128]
                    eng.tensor_add(a, a, pb[bank][:, 0:128])
                    chv(eng, eng.engine_nop())
                    eng.drain().then_inc(cV, 1)
                p.V(v_ad); bump("cV")
        def a_st(eng, sb=sb):
            eng.wait_ge(cV, p.cnt["cV"])
            for q in range(4):
                eng.dma_start(out=acc_d[sb * 512 + q * 128: sb * 512 + (q + 1) * 128, :],
                              in_=acc4[:, q * D:(q + 1) * D]).then_inc(dA, 16)
        p.A(a_st); bump("dA", 64)

    d = dbgout("accS", [S, D])
    if d is not None:
        def f_accS(eng, d=d):
            eng.wait_ge(dA, p.cnt["dA"])
            eng.dma_start(out=d, in_=acc_d).then_inc(dS, 16)
        p.S(f_accS); bump("dS", 16)
    d = dbgout("eout", [NSLOT, D])
    if d is not None:
        def f_eout(eng, d=d):
            eng.wait_ge(dA, p.cnt["dA"])
            for k in range(8):
                eng.dma_start(out=d[k * WSL:(k + 1) * WSL, :],
                              in_=eout_d[k * WSL:(k + 1) * WSL, :]).then_inc(dS, 16)
        p.S(f_eout); bump("dS", 16 * 8)
    join_all()
    p.close_scope("shc")

    accf = p.sbuf("accf", [128, D], F32, scope="fin")
    g1b = p.sbuf("g1b", [128, D], F32, scope="fin")
    g2b = p.sbuf("g2b", [128, D], F32, scope="fin")

    for ci in range(SI):
        def s_af(eng, ci=ci):
            eng.wait_ge(dS, p.cnt["dS"])
            eng.dma_start(out=accf, in_=acc_d[ci * 128:(ci + 1) * 128, :]).then_inc(dS, 16)
        p.S(s_af); bump("dS", 16)
        def g_g(eng, ci=ci):
            eng.wait_ge(cV, p.cnt["cV"])
            eng.indirect_dma_start(out=g1b, out_offset=None, in_=eout_d,
                                   in_offset=IndirectOffsetOnAxis(sigi(2)[:, ci:ci + 1], 0)).then_inc(dG, 16)
            eng.indirect_dma_start(out=g2b, out_offset=None, in_=eout_d,
                                   in_offset=IndirectOffsetOnAxis(sigi(3)[:, ci:ci + 1], 0)).then_inc(dG, 16)
        p.G(g_g); bump("dG", 32)
        def v_fin(eng, ci=ci):
            eng.wait_ge(dS, p.cnt["dS"])
            eng.wait_ge(dG, p.cnt["dG"])
            eng.scalar_tensor_tensor(accf, g1b, WV[:, ci:ci + 1], accf, ALU.mult, ALU.add)
            eng.scalar_tensor_tensor(accf, g2b, WV[:, 16 + ci:17 + ci], accf, ALU.mult, ALU.add)
            chv(eng, eng.engine_nop())
            eng.drain().then_inc(cV, 1)
        p.V(v_fin); bump("cV")
        def s_out(eng, ci=ci):
            eng.wait_ge(cV, p.cnt["cV"])
            eng.dma_start(out=out[ci * 128:(ci + 1) * 128, :], in_=accf).then_inc(dS, 16)
        p.S(s_out); bump("dS", 16)

    def s_end(eng):
        eng.wait_ge(dS, p.cnt["dS"])
    p.S(s_end)
    p.close_scope("fin")


_NC_CACHE = {}


def _consts():
    iden = np.eye(128, dtype=np.float32)
    sup = np.triu(np.ones((128, 128), np.float32), 1)
    e16 = np.zeros((128, 128), np.float32)
    for c in range(8):
        for pp in range(128):
            m = 2 * c + (1 if pp >= 64 else 0)
            e16[pp, c * 16 + m] = 1.0
    e16t = np.zeros((16, 1024), np.float32)
    for c in range(8):
        for mm in range(128):
            e16t[2 * c + mm // 64, c * 128 + mm] = 1.0
    import ml_dtypes
    return (iden, sup.astype(ml_dtypes.bfloat16), e16.astype(ml_dtypes.bfloat16),
            e16t.astype(ml_dtypes.bfloat16))


def kernel(**inputs):
    import ml_dtypes
    x = np.asarray(inputs["L0_attn_x"], np.float32)
    xflat = np.ascontiguousarray(x.reshape(T, D))
    iden, sup16, e16, e16t = _consts()
    if "nc" not in _NC_CACHE:
        _NC_CACHE["nc"], _NC_CACHE["dbg"] = build()
    nc = _NC_CACHE["nc"]
    in_maps = []
    for c in range(NCORES):
        mask = np.zeros((1, NCORES * E), np.float32)
        mask[0, :c * E] = 1.0
        in_maps.append(dict(
            xb=np.ascontiguousarray(xflat[c * S:(c + 1) * S]),
            xfull=xflat,
            wq=np.asarray(inputs["L0_attn_wq"], np.float32),
            wk=np.asarray(inputs["L0_attn_wk"], np.float32),
            wv=np.asarray(inputs["L0_attn_wv"], np.float32),
            wo=np.asarray(inputs["L0_attn_wo"], np.float32),
            rwin=np.asarray(inputs["L0_ffn_router_w"], np.float32),
            sg=np.asarray(inputs["L0_ffn_sg"], np.float32),
            su=np.asarray(inputs["L0_ffn_su"], np.float32),
            sd=np.asarray(inputs["L0_ffn_sd"], np.float32),
            rg=np.asarray(inputs["L0_ffn_rg"], np.float32),
            ru=np.asarray(inputs["L0_ffn_ru"], np.float32),
            rd=np.asarray(inputs["L0_ffn_rd"], np.float32),
            idenf=iden, sup16=sup16, e16all=e16, e16t=e16t, maskrep=mask,
        ))
    res = run_bass_kernel_spmd(nc, in_maps, core_ids=list(range(NCORES)))
    outs = [res.results[c]["out"] for c in range(NCORES)]
    return np.stack(outs, 0).astype(np.float32)
